# version 03
# baseline (speedup 1.0000x reference)
import numpy as np
from contextlib import ExitStack

import concourse.bass as bass
import concourse.mybir as mybir
from concourse.bass_utils import run_bass_kernel_spmd

F32 = mybir.dt.float32
BF16 = mybir.dt.bfloat16
AF = mybir.ActivationFunctionType
ALU = mybir.AluOpType
AX = mybir.AxisListType

T = 2048
NT = 4
TT = 512
D = 1024
KC = 8

ENGINES = ("PE", "DVE", "ACT", "POOL", "SP")
DMA_QUEUES = ("SP", "ACT", "POOL")


class Op:
    __slots__ = ("eng", "fn", "reads", "writes", "kind", "inc_needed",
                 "deps", "seq", "inc_no", "dma_no", "tag")

    def __init__(self, eng, fn, reads, writes, kind, tag=""):
        self.eng = eng
        self.fn = fn
        self.reads = reads
        self.writes = writes
        self.kind = kind
        self.inc_needed = kind == "dma"
        self.deps = []
        self.seq = None
        self.inc_no = None
        self.dma_no = None
        self.tag = tag


class B:

    def __init__(self, nc, st):
        self.nc = nc
        self.st = st
        self.q = {e: [] for e in ENGINES}
        self.last_writer = {}
        self.readers = {}
        self.esem = {e: st.enter_context(nc.semaphore(f"sem_{e}")) for e in ENGINES}
        self.esem2 = {e: [self.esem[e]] + [
            st.enter_context(nc.semaphore(f"sem_{e}b{i}")) for i in range(1, 3)]
            for e in ENGINES}
        self.NDSEM = 6
        self.dsem = {e: [st.enter_context(nc.semaphore(f"dsem_{e}{i}"))
                         for i in range(self.NDSEM)] for e in DMA_QUEUES}
        self.n_ops = 0
        self._consts = {}
        self.cpool = st.enter_context(nc.sbuf_tensor("cpool", [128, 16], F32))
        self._cpool_n = 0

    def const(self, val):
        if val not in self._consts:
            idx = self._cpool_n
            assert idx < 16
            self._cpool_n += 1
            ap = self.cpool[:, idx:idx + 1]
            self.nc.const_aps.register(F32, float(val), ap)
            self.dve(lambda v, ap=ap, val=val: v.memset(ap, float(val)),
                     reads=(), writes=("cpool",), tag="const")
            self._consts[val] = ap
        return self._consts[val]

    def _op(self, eng, fn, reads, writes, kind, tag=""):
        op = Op(eng, fn, tuple(reads), tuple(writes), kind, tag)
        op.seq = len(self.q[eng])
        self.q[eng].append(op)
        self.n_ops += 1
        deps = {}

        def add_dep(p):
            if p is None or p is op:
                return
            if p.eng == op.eng and p.eng == "PE" and p.kind == "compute" and kind == "compute":
                return
            deps[id(p)] = p

        for k in reads:
            add_dep(self.last_writer.get(k))
        for k in writes:
            add_dep(self.last_writer.get(k))
            for r in self.readers.get(k, ()):
                add_dep(r)
        op.deps = list(deps.values())
        for k in writes:
            self.last_writer[k] = op
            self.readers[k] = []
        for k in reads:
            self.readers.setdefault(k, []).append(op)
        return op

    def pe(self, fn, reads, writes, tag=""):
        return self._op("PE", fn, reads, writes, "compute", tag)

    def dve(self, fn, reads, writes, tag=""):
        return self._op("DVE", fn, reads, writes, "compute", tag)

    def act(self, fn, reads, writes, tag=""):
        return self._op("ACT", fn, reads, writes, "compute", tag)

    def pool(self, fn, reads, writes, tag=""):
        return self._op("POOL", fn, reads, writes, "compute", tag)

    def sp(self, fn, reads, writes, tag=""):
        return self._op("SP", fn, reads, writes, "compute", tag)

    def dma(self, queue, dst, src, reads, writes, tag=""):
        fn = lambda e: e.dma_start(dst, src)
        return self._op(queue, fn, reads, writes, "dma", tag)

    def barrier_on(self, eng, producers, tag="barrier"):
        op = self._op(eng, None, (), (), "wait", tag)
        for p in producers:
            if p is not None:
                op.deps.append(p)
        return op

    def finalize(self):
        for e in ENGINES:
            for op in self.q[e]:
                for p in op.deps:
                    if p.kind == "compute":
                        p.inc_needed = True
        for e in ENGINES:
            n_inc = 0
            n_dma = 0
            for op in self.q[e]:
                if op.kind == "dma":
                    n_dma += 1
                    op.dma_no = n_dma
                elif op.inc_needed:
                    n_inc += 1
                    op.inc_no = n_inc

        SEM_BAND = 50000

        def sem_val(p):
            if p.kind == "dma":
                slot = (p.dma_no - 1) % self.NDSEM
                s = (p.dma_no - 1) // self.NDSEM
                return (self.dsem[p.eng][slot], 16 * (s + 1))
            band, val = divmod(p.inc_no - 1, SEM_BAND)
            return (self.esem2[p.eng][band], val + 1)

        nc = self.nc
        nc.all_engine_barrier()
        for ss in self.esem2.values():
            for s in ss:
                nc.gpsimd.sem_clear(s)
        for ss in self.dsem.values():
            for s in ss:
                nc.gpsimd.sem_clear(s)
        nc.all_engine_barrier()
        blk = self.st.enter_context(nc.Block())
        builder = self

        def emit_engine(proxy, e):
            hw = {}
            for op in builder.q[e]:
                waits = {}
                for p in op.deps:
                    s, v = sem_val(p)
                    key = id(s)
                    if hw.get(key, 0) >= v:
                        continue
                    if key not in waits or waits[key][1] < v:
                        waits[key] = (s, v)
                if op.kind == "dma":
                    slot = (op.dma_no - 1) % builder.NDSEM
                    s_ = (op.dma_no - 1) // builder.NDSEM
                    if s_ > 0:
                        gsem = builder.dsem[e][slot]
                        gval = 16 * s_
                        if hw.get(id(gsem), 0) < gval:
                            waits[id(gsem)] = (gsem, gval)
                for s, v in waits.values():
                    proxy.wait_ge(s, v)
                    hw[id(s)] = v
                if op.kind == "wait":
                    continue
                inst = op.fn(proxy)
                if op.kind == "dma":
                    slot = (op.dma_no - 1) % builder.NDSEM
                    inst.then_inc(builder.dsem[e][slot], 16)
                elif op.inc_needed:
                    inst.then_inc(builder.esem2[e][(op.inc_no - 1) // 50000], 1)

        if self.q["PE"]:
            @blk.tensor
            def _(t):
                emit_engine(t, "PE")
        if self.q["DVE"]:
            @blk.vector
            def _(v):
                emit_engine(v, "DVE")
        if self.q["ACT"]:
            @blk.scalar
            def _(s):
                emit_engine(s, "ACT")
        if self.q["POOL"]:
            @blk.gpsimd
            def _(g):
                emit_engine(g, "POOL")
        if self.q["SP"]:
            @blk.sync
            def _(sy):
                emit_engine(sy, "SP")


class Tiles:

    def __init__(self, nc, st):
        self.nc = nc
        self.st = st
        self.t = {}

    def sb(self, key, shape, dtype):
        assert key not in self.t
        h = self.st.enter_context(self.nc.sbuf_tensor(key, shape, dtype))
        self.t[key] = h
        return h

    def ps(self, key, shape, dtype=F32):
        assert key not in self.t
        h = self.st.enter_context(self.nc.psum_tensor(key, shape, dtype))
        self.t[key] = h
        return h

    def __getitem__(self, key):
        return self.t[key]




def as_ap(x):
    return x if isinstance(x, bass.AP) else x[:, :]


def mm_dense(b, name, w_hbm, din, dout, rhs, psum_pool, wbufs, epilogue,
             dq="POOL", nt=NT, tt=TT, blk_cols=128, js=None):
    kcd = din // 128
    njall = dout // blk_cols
    js = list(range(njall)) if js is None else js
    src = w_hbm.rearrange("(k p) n -> p k n", p=128)
    depth = len(wbufs)
    wb = [(as_ap(h), k) for h, k in wbufs]

    def dma_block(ji):
        j = js[ji]
        wh, wk = wb[ji % depth]
        dst = wh.rearrange("p (k c) -> p k c", c=blk_cols)
        b.dma(dq, dst, src[:, :, j * blk_cols:(j + 1) * blk_cols],
              reads=(), writes=(wk,), tag=f"{name}dma{j}")

    nj = len(js)
    for ji in range(min(depth, nj)):
        dma_block(ji)
    for ji in range(nj):
        j = js[ji]
        wh, wk = wb[ji % depth]
        banks = [psum_pool[(ji * nt + t) % len(psum_pool)] for t in range(nt)]
        for k in range(kcd):
            lhsT = wh[:, k * blk_cols:(k + 1) * blk_cols]
            rh, rk = rhs[k]
            for t in range(nt):
                ph, pk = banks[t]
                b.pe(lambda e, ph=ph, lhsT=lhsT, rh=rh, t=t, k=k:
                     e.matmul(ph[0:blk_cols, :tt], lhsT,
                              rh[:, t * tt:(t + 1) * tt],
                              start=(k == 0), stop=(k == kcd - 1)),
                     reads=(wk, rk), writes=(pk,), tag=f"{name}mm{j}_{k}_{t}")
        if ji + depth < nj:
            dma_block(ji + depth)
        for t in range(nt):
            ph, pk = banks[t]
            epilogue(j, t, ph, pk)


def reduce_channels(b, name, srcs, ps, t_idx, op_engine="pe", tt=TT,
                    ones=None, scale=None):
    oh, ok = ones
    ph, pk = ps
    n = len(srcs)
    for i, (sh, sk) in enumerate(srcs):
        b.pe(lambda e, sh=sh, i=i: e.matmul(
            ph[0:1, :tt], oh[:, 0:1], sh[:, t_idx * tt:(t_idx + 1) * tt],
            start=(i == 0), stop=(i == n - 1)),
            reads=(ok, sk), writes=(pk,), tag=f"{name}rc{i}")


def bcast_row(b, name, row, ps, ones_row, tt=TT):
    rh, rk, csl = row
    ph, pk = ps
    oh, ok = ones_row
    b.pe(lambda e: e.matmul(ph[:, :tt], oh[0:1, :], rh[0:1, csl],
                            start=True, stop=True),
         reads=(ok, rk), writes=(pk,), tag=f"{name}bc")


NEG_BIG = -30000.0


class Prog:

    def __init__(self, nc, st, ins):
        self.nc = nc
        self.st = st
        self.ins = ins
        self.b = B(nc, st)
        self.tl = Tiles(nc, st)
        t = self.tl
        self.xn = [(t.sb(f"xn{k}", [128, T], BF16), f"xn{k}") for k in range(KC)]
        self.gb = [(t.sb(f"gbt{k}", [128, T], BF16), f"gbt{k}") for k in range(KC)]
        self.bb = [(t.sb(f"bb{k}", [128, T], BF16), f"bb{k}") for k in range(16)]
        self.fb = [(t.sb(f"fbt{k}", [128, T], F32), f"fbt{k}") for k in range(3)]
        self.sq = (t.sb("sqt", [128, 2 * TT], F32), "sqt")
        self.rows = (t.sb("rowst", [1, 2 * T], F32), "rowst")
        self.stat = (t.sb("statt", [128, 64], F32), "statt")
        self.onec = (t.sb("onect", [128, 1], F32), "onect")
        self.onecb = (t.sb("onecbt", [128, 1], BF16), "onecbt")
        self.oner = (t.sb("onert", [1, 128], F32), "onert")
        self.idf = (t.sb("idft", [128, 128], F32), "idft")
        self.idb = (t.sb("idbt", [128, 128], BF16), "idbt")
        self.nbI = (t.sb("nbIt", [128, 128], BF16), "nbIt")
        self.u01 = (t.sb("u01t", [128, 1280], BF16), "u01t")
        self.u01off = (0, 512, 896, 1152)
        self.lm = (t.sb("lmt", [128, 128], BF16), "lmt")
        self.wbA = (t.sb("wbAt", [128, 2048], BF16), "wbAt")
        self.wbB = (t.sb("wbBt", [128, 2048], BF16), "wbBt")
        self.wfull = (t.sb("wfullt", [128, 8192], BF16), "wfullt")
        self.pb = [(t.ps(f"pbk{i}", [128, TT], F32), f"pbk{i}") for i in range(8)]
        self.smalls = {}

    def small(self, name, shape, dtype=F32):
        if name not in self.smalls:
            self.smalls[name] = (self.tl.sb(name, shape, dtype), name)
        return self.smalls[name]

    def fbb(self, i):
        h, k = self.fb[i]
        return h, k

    def pbt(self, i):
        h, k = self.pb[i]
        return h, k


def fbi_bf(h, c0, c1):
    return h[:, :].bitcast(BF16)[:, c0:c1]


def emit_prologue(P):
    b = P.b
    b.dve(lambda v: v.memset(P.onec[0][:, :], 1.0), (), (P.onec[1],))
    b.dve(lambda v: v.memset(P.onecb[0][:, :], 1.0), (), (P.onecb[1],))
    b.dve(lambda v: v.memset(P.oner[0][:, :], 1.0), (), (P.oner[1],))
    for th, tk in (P.idf, P.idb), (P.nbI, P.nbI):
        pass
    b.pool(lambda g: g.memset(P.idf[0][:, :], 1.0), (), (P.idf[1],))
    b.pool(lambda g: g.affine_select(P.idf[0][:, :], P.idf[0][:, :],
                                     pattern=[[-1, 128]], base=0,
                                     channel_multiplier=1,
                                     compare_op=ALU.is_equal, fill=0.0),
           (P.idf[1],), (P.idf[1],))
    b.pool(lambda g: g.memset(P.idb[0][:, :], 1.0), (), (P.idb[1],))
    b.pool(lambda g: g.affine_select(P.idb[0][:, :], P.idb[0][:, :],
                                     pattern=[[-1, 128]], base=0,
                                     channel_multiplier=1,
                                     compare_op=ALU.is_equal, fill=0.0),
           (P.idb[1],), (P.idb[1],))
    b.pool(lambda g: g.memset(P.nbI[0][:, :], NEG_BIG), (), (P.nbI[1],))
    b.pool(lambda g: g.affine_select(P.nbI[0][:, :], P.nbI[0][:, :],
                                     pattern=[[-1, 128]], base=0,
                                     channel_multiplier=1,
                                     compare_op=ALU.is_equal, fill=0.0),
           (P.nbI[1],), (P.nbI[1],))
    for v in range(4):
        off = P.u01off[v]
        w = 512 - 128 * v
        b.pool(lambda g, off=off, w=w: g.memset(P.u01[0][:, off:off + w], 0.0),
               (), (P.u01[1],))
        b.pool(lambda g, off=off, w=w: g.affine_select(
            P.u01[0][:, off:off + w], P.u01[0][:, off:off + w],
            pattern=[[-1, w]], base=0, channel_multiplier=1,
            compare_op=ALU.is_ge, fill=1.0), (P.u01[1],), (P.u01[1],))
    b.pool(lambda g: g.memset(P.lm[0][:, :], 0.0), (), (P.lm[1],))
    b.pool(lambda g: g.affine_select(P.lm[0][:, :], P.lm[0][:, :],
                                     pattern=[[1, 128]], base=0,
                                     channel_multiplier=-1,
                                     compare_op=ALU.is_ge, fill=1.0),
           (P.lm[1],), (P.lm[1],))


def load_smallvec(P, name, hbm_name, shape, dtype=F32, dq="SP"):
    tile = P.small(name, list(shape), dtype)
    b = P.b
    b.dma(dq, tile[0][:, :], P.ins[hbm_name], (), (tile[1],), tag=f"ld_{name}")
    return tile


def emit_outer_ln(P, src, pref, src_keys=None):
    b = P.b
    eps = b.const(1e-5)
    xb = []
    for k in range(KC):
        fh, fk = P.fb[k % 2]
        b.dma("SP", fh[:, :], src[k * 128:(k + 1) * 128, :],
              (() if src_keys is None else (src_keys[k],)), (fk,),
              tag=f"{pref}lnx{k}")
        bh, bk = P.bb[8 + k]
        b.act(lambda s, fh=fh, bh=bh: s.activation(bh[:, :], fh[:, :], AF.Copy),
              (fk,), (bk,), tag=f"{pref}lncast{k}")
        xb.append((bh, bk))
    rows_h, rows_k = P.rows
    for t in range(NT):
        for k in range(KC):
            bh, bk = xb[k]
            b.pe(lambda e, bh=bh, k=k, t=t: e.matmul(
                P.pb[0][0][0:1, :TT], P.onecb[0][:, 0:1],
                bh[:, t * TT:(t + 1) * TT],
                start=(k == 0), stop=(k == KC - 1)),
                (P.onecb[1], bk), (P.pb[0][1],), tag=f"{pref}lnsx{k}_{t}")
            sh, sk = P.sq
            b.act(lambda s, bh=bh, t=t, sh=sh: s.activation(
                sh[:, :TT], bh[:, t * TT:(t + 1) * TT], AF.Square),
                (bk,), (sk,), tag=f"{pref}lnsq{k}_{t}")
            b.pe(lambda e, sh=sh, k=k: e.matmul(
                P.pb[1][0][0:1, :TT], P.onec[0][:, 0:1], sh[:, :TT],
                start=(k == 0), stop=(k == KC - 1)),
                (P.onec[1], sk), (P.pb[1][1],), tag=f"{pref}lnsq2{k}_{t}")
        b.dve(lambda v, t=t: v.tensor_scalar_mul(
            rows_h[0:1, t * TT:(t + 1) * TT], P.pb[0][0][0:1, :TT], 1.0 / D),
            (P.pb[0][1],), (rows_k,), tag=f"{pref}lnmu{t}")
        f2h, f2k = P.fb[2]
        b.dve(lambda v, t=t, f2h=f2h: v.tensor_scalar_mul(
            f2h[0:1, t * TT:(t + 1) * TT],
            P.pb[1][0][0:1, :TT], 1.0 / D),
            (P.pb[1][1],), (f2k,), tag=f"{pref}lnms{t}")
    f2h, f2k = P.fb[2]
    b.act(lambda s: s.activation(rows_h[0:1, T:2 * T], rows_h[0:1, 0:T],
                                 AF.Square), (rows_k,), (rows_k,),
          tag=f"{pref}lnmu2")
    b.dve(lambda v: v.tensor_sub(rows_h[0:1, T:2 * T], f2h[0:1, 0:T],
                                 rows_h[0:1, T:2 * T]),
          (rows_k, f2k), (rows_k,), tag=f"{pref}lnvar")
    b.act(lambda s: s.activation(rows_h[0:1, T:2 * T],
                                 rows_h[0:1, T:2 * T], AF.Ln,
                                 bias=eps[0:1, :]),
          (rows_k, "cpool"), (rows_k,), tag=f"{pref}lnln")
    b.act(lambda s: s.activation(rows_h[0:1, T:2 * T],
                                 rows_h[0:1, T:2 * T], AF.Exp, scale=-0.5),
          (rows_k,), (rows_k,), tag=f"{pref}lnexp")
    for t in range(NT):
        bcast_row(b, f"{pref}lnbmu{t}", (rows_h, rows_k, slice(t * TT, (t + 1) * TT)),
                  P.pb[2], (P.oner[0], P.oner[1]))
        bcast_row(b, f"{pref}lnbr{t}", (rows_h, rows_k, slice(T + t * TT, T + (t + 1) * TT)),
                  P.pb[3], (P.oner[0], P.oner[1]))
        for k in range(KC):
            bh, bk = xb[k]
            xh, xk = P.xn[k]
            fh, fk = P.fb[2]
            b.dve(lambda v, bh=bh, fh=fh, t=t: v.tensor_sub(
                fh[:, t * TT:(t + 1) * TT], bh[:, t * TT:(t + 1) * TT],
                P.pb[2][0][:, :TT]),
                (bk, P.pb[2][1]), (fk,), tag=f"{pref}lnsub{k}_{t}")
            b.dve(lambda v, fh=fh, xh=xh, t=t: v.tensor_mul(
                xh[:, t * TT:(t + 1) * TT], fh[:, t * TT:(t + 1) * TT],
                P.pb[3][0][:, :TT]),
                (fk, P.pb[3][1]), (xk,), tag=f"{pref}lnmul{k}_{t}")


def emit_rms(P, pref, src, dst, w2d, eps_val, kcin=KC):
    b = P.b
    eps = b.const(eps_val)
    rows_h, rows_k = P.rows
    for t in range(NT):
        for k in range(kcin):
            bh, bk = src[k]
            sh, sk = P.sq
            b.act(lambda s, bh=bh, sh=sh, t=t: s.activation(
                sh[:, :TT], bh[:, t * TT:(t + 1) * TT], AF.Square),
                (bk,), (sk,), tag=f"{pref}sq{k}_{t}")
            b.pe(lambda e, sh=sh, k=k: e.matmul(
                P.pb[0][0][0:1, :TT], P.onec[0][:, 0:1], sh[:, :TT],
                start=(k == 0), stop=(k == kcin - 1)),
                (P.onec[1], sk), (P.pb[0][1],), tag=f"{pref}rs{k}_{t}")
        b.dve(lambda v, t=t: v.tensor_scalar_mul(
            rows_h[0:1, t * TT:(t + 1) * TT], P.pb[0][0][0:1, :TT],
            1.0 / (128 * kcin)),
            (P.pb[0][1],), (rows_k,), tag=f"{pref}ms{t}")
    b.act(lambda s: s.activation(rows_h[0:1, 0:T], rows_h[0:1, 0:T], AF.Ln,
                                 bias=eps[0:1, :]),
          (rows_k, "cpool"), (rows_k,), tag=f"{pref}ln")
    b.act(lambda s: s.activation(rows_h[0:1, 0:T], rows_h[0:1, 0:T], AF.Exp,
                                 scale=-0.5), (rows_k,), (rows_k,),
          tag=f"{pref}exp")
    for t in range(NT):
        bcast_row(b, f"{pref}br{t}", (rows_h, rows_k, slice(t * TT, (t + 1) * TT)),
                  P.pb[2], (P.oner[0], P.oner[1]))
        for k in range(kcin):
            bh, bk = src[k]
            dh, dk = dst[k]
            if w2d is None:
                b.dve(lambda v, bh=bh, dh=dh, t=t: v.tensor_mul(
                    dh[:, t * TT:(t + 1) * TT], bh[:, t * TT:(t + 1) * TT],
                    P.pb[2][0][:, :TT]),
                    (bk, P.pb[2][1]), (dk,), tag=f"{pref}n{k}_{t}")
            else:
                wh, wk = w2d
                b.dve(lambda v, bh=bh, dh=dh, wh=wh, t=t, k=k:
                      v.scalar_tensor_tensor(
                          dh[:, t * TT:(t + 1) * TT],
                          bh[:, t * TT:(t + 1) * TT], wh[:, k:k + 1],
                          P.pb[2][0][:, :TT], ALU.mult, ALU.mult),
                      (bk, wk, P.pb[2][1]), (dk,), tag=f"{pref}n{k}_{t}")


def emit_attn(P, pref, qspec, kspec, vt, vstride, vch, out_cb, scale, ptile,
              pts, window=False, sink=None):
    b = P.b
    st_h, st_k = P.stat
    sc = b.const(scale)
    vth, vtk = vt
    pth, ptk = ptile
    psh, psk = pts
    for hh in (0, 1):
        qs = qspec(hh)
        ks = kspec(hh)
        for qi in range(16):
            if window:
                kts = [qi - 1, qi] if qi > 0 else [0]
                ktw = 128
            else:
                kts = list(range(qi // 4 + 1))
                ktw = TT
            nk = len(kts)
            for i, kt in enumerate(kts):
                ph, pk = P.pb[i]
                npc = len(qs)
                if window:
                    mask = (P.u01[0][:, 0:128], P.u01[1], 0) if kt == qi else \
                           (P.lm[0][:, 0:128], P.lm[1], 0)
                else:
                    v4 = qi % 4
                    mask = ((P.u01[0][:, P.u01off[v4]:P.u01off[v4] + 512 - 128 * v4],
                             P.u01[1], 128 * v4) if kt == kts[-1] else None)
                for ip in range(npc):
                    qh, qk, qb, cw = qs[ip]
                    kh, kk, kb, _ = ks[ip]
                    is_last = (ip == npc - 1) and mask is None
                    b.pe(lambda e, ph=ph, qh=qh, kh=kh, qb=qb, kb=kb, qi=qi,
                         kt=kt, ip=ip, is_last=is_last, ktw=ktw:
                         e.matmul(ph[:, :ktw],
                                  qh[:, qi * 128:(qi + 1) * 128],
                                  kh[:, kt * ktw:kt * ktw + ktw],
                                  start=(ip == 0), stop=is_last,
                                  tile_position=(qb, 0)),
                         (qk, kk), (pk,), tag=f"{pref}sc{hh}_{qi}_{i}_{ip}")
                if mask is not None:
                    map_, mk, moff = mask
                    b.pe(lambda e, ph=ph, map_=map_, ktw=ktw, moff=moff:
                         e.matmul(ph[:, moff:ktw], P.nbI[0][:, :], map_,
                                  start=False, stop=True),
                        (P.nbI[1], mk), (pk,), tag=f"{pref}msk{hh}_{qi}_{i}")
            for i in range(nk):
                ph, pk = P.pb[i]
                w = ktw
                b.dve(lambda v, ph=ph, i=i, w=w: v.reduce_max(
                    st_h[:, (4 if i == 0 else 5):(5 if i == 0 else 6)],
                    ph[:, :w], axis=AX.X),
                    (pk,), (st_k,), tag=f"{pref}mx{hh}_{qi}_{i}")
                if i > 0:
                    b.dve(lambda v: v.tensor_max(st_h[:, 4:5], st_h[:, 4:5],
                                                 st_h[:, 5:6]),
                          (st_k,), (st_k,), tag=f"{pref}mxc{hh}_{qi}_{i}")
            b.dve(lambda v: v.tensor_scalar_mul(st_h[:, 6:7], st_h[:, 4:5],
                                                -scale),
                  (st_k,), (st_k,), tag=f"{pref}nm{hh}_{qi}")
            for i in range(nk):
                ph, pk = P.pb[i]
                b.act(lambda s, ph=ph, i=i, w=ktw: s.activation(
                    pth[:, i * TT:i * TT + w], ph[:, :w], AF.Exp,
                    bias=st_h[:, 6:7], scale=sc[:, :],
                    accum_out=st_h[:, 8 + i:9 + i]),
                    (pk, st_k, "cpool"), (ptk, st_k),
                    tag=f"{pref}ex{hh}_{qi}_{i}")
            for i in range(1, nk):
                b.dve(lambda v, i=i: v.tensor_add(st_h[:, 8:9], st_h[:, 8:9],
                                                  st_h[:, 8 + i:9 + i]),
                      (st_k,), (st_k,), tag=f"{pref}sm{hh}_{qi}_{i}")
            if sink is not None:
                skh, skk, hidx = sink(hh)
                b.act(lambda s, skh=skh, hidx=hidx: s.activation(
                    st_h[:, 9:10], st_h[:, 4:5], AF.Exp,
                    bias=skh[:, hidx:hidx + 1], scale=-1.0),
                    (st_k, skk, "cpool"), (st_k,), tag=f"{pref}snk{hh}_{qi}")
                b.dve(lambda v: v.tensor_add(st_h[:, 8:9], st_h[:, 8:9],
                                             st_h[:, 9:10]),
                      (st_k,), (st_k,), tag=f"{pref}snka{hh}_{qi}")
            b.dve(lambda v: v.reciprocal(st_h[:, 10:11], st_h[:, 8:9]),
                  (st_k,), (st_k,), tag=f"{pref}rcp{hh}_{qi}")
            for i in range(nk):
                b.dve(lambda v, i=i, w=ktw: v.tensor_scalar_mul(
                    pth[:, i * TT:i * TT + w], pth[:, i * TT:i * TT + w],
                    st_h[:, 10:11]),
                    (ptk, st_k), (ptk,), tag=f"{pref}nn{hh}_{qi}_{i}")
            sub128 = []
            if window:
                for i, kt in enumerate(kts):
                    sub128.append((i * TT, kt))
            else:
                for s in range(qi + 1):
                    sub128.append(((s // 4) * TT + (s % 4) * 128, s))
            pvh, pvk = P.pb[4 if (qi % 2 == 0) else 5]
            n128 = len(sub128)
            for s, (src_col, kt128) in enumerate(sub128):
                tph, tpk = P.pb[6 + (s % 2)]
                slot = (s // 2) % 4
                b.pe(lambda e, src_col=src_col, slot=slot, tph=tph:
                     e.transpose(
                         tph[:, :].bitcast(BF16)[:, slot * 128:(slot + 1) * 128],
                         pth[:, src_col:src_col + 128], P.idb[0][:, :]),
                     (ptk, P.idb[1]), (tpk,), tag=f"{pref}tp{hh}_{qi}_{s}")
                b.act(lambda sx, slot=slot, s=s, tph=tph: sx.activation(
                    psh[:, s * 128:(s + 1) * 128],
                    tph[:, :].bitcast(BF16)[:, slot * 128:(slot + 1) * 128],
                    AF.Copy),
                    (tpk,), (psk,), tag=f"{pref}tpe{hh}_{qi}_{s}")
            for s, (src_col, kt128) in enumerate(sub128):
                b.pe(lambda e, kt128=kt128, s=s, pvh=pvh, hh=hh, n128=n128:
                     e.matmul(pvh[hh * 64:hh * 64 + 64, 0:128],
                              vth[:, kt128 * vstride + vch(hh):
                                  kt128 * vstride + vch(hh) + 64],
                              psh[:, s * 128:(s + 1) * 128],
                              start=(s == 0), stop=(s == n128 - 1),
                              tile_position=(0, hh * 64)),
                     (vtk, psk), (pvk,), tag=f"{pref}pv{hh}_{qi}_{s}")
            out_cb(hh, qi, P.pb[4 if (qi % 2 == 0) else 5])


def emit_layer2(P, xsrc, xdst, src_keys=None, dst_keys=None):
    b = P.b
    ins = P.ins
    emit_outer_ln(P, xsrc, "l2o", src_keys)
    anw = load_smallvec(P, "l2anw", "l2_attn_norm_w", (128, 8))
    fnw = load_smallvec(P, "l2fnw", "l2_ffn_norm_w", (128, 8))
    qkb = load_smallvec(P, "l2qkb", "l2_qkv_b2d", (128, 10))
    pjb = load_smallvec(P, "l2pjb", "l2_proj_b2d", (128, 8))
    snk = load_smallvec(P, "l2snk", "l2_sink_bc", (128, 16))
    gbv = load_smallvec(P, "l2gbv", "l2_gate_b2d", (32, 1))
    bqv = load_smallvec(P, "l2bq", "l2_bq2d", (128, 8))
    bkv = load_smallvec(P, "l2bk", "l2_bk2d", (128, 8))

    h = [P.bb[k] for k in range(8)]
    emit_rms(P, "l2h", P.xn, h, anw, 1e-5)

    def epi_qkv(j, t, ph, pk):
        if j < 8:
            dh, dk = P.bb[8 + j]
        else:
            dh, dk = (fbi_bf(P.fb[0][0], (j - 8) * T, (j - 7) * T), P.fb[0][1])
        b.act(lambda s, ph=ph, dh=dh, t=t, j=j: s.activation(
            dh[:, t * TT:(t + 1) * TT], ph[:, :TT], AF.Identity,
            bias=qkb[0][:, j:j + 1], scale=1.0),
            (pk, qkb[1], "cpool"), (dk,), tag=f"l2qkv_e{j}_{t}")
    mm_dense(b, "l2qkv", ins["l2_qkv_w"], D, 1536, h, P.pb[:6],
             [(P.wbA[0][:, s * 1024:(s + 1) * 1024], P.wbA[1] + f":{s}")
              for s in range(2)], epi_qkv, js=list(range(10)))

    b.dma("POOL", P.wbB[0][:, 0:2048].rearrange("p (k c) -> p k c", c=256),
          ins["l2_wv_cols"].rearrange("(k p) n -> p k n", p=128),
          (), (P.wbB[1] + ":0", P.wbB[1] + ":1"), tag="l2wv")
    b.dma("SP", P.rows[0][0:1, T:T + 256], ins["l2_bv_row"],
          (), (P.rows[1],), tag="l2bvr")
    vt = (fbi_bf(P.fb[2][0], 0, 2 * T), P.fb[2][1])
    for ti in range(16):
        ph, pk = P.pb[ti % 2]
        for k in range(KC):
            b.pe(lambda e, ph=ph, k=k, ti=ti: e.matmul(
                ph[:, 0:256], P.xn[k][0][:, ti * 128:(ti + 1) * 128],
                P.wbB[0][:, k * 256:(k + 1) * 256],
                start=(k == 0), stop=False),
                (P.xn[k][1], P.wbB[1] + f":{k // 4}"), (pk,), tag=f"l2v{ti}_{k}")
        b.pe(lambda e, ph=ph: e.matmul(
            ph[:, 0:256], P.oner[0][0:1, :], P.rows[0][0:1, T:T + 256],
            start=False, stop=True),
            (P.oner[1], P.rows[1]), (pk,), tag=f"l2vb{ti}")
        vth, vtk = vt
        b.act(lambda s, ph=ph, ti=ti, vth=vth: s.activation(
            vth[:, ti * 256:(ti + 1) * 256], ph[:, 0:256], AF.Copy),
            (pk,), (vtk,), tag=f"l2ve{ti}")

    sinT = P.bb[0]
    cosT = P.bb[1]
    b.dma("POOL", sinT[0][:, :], ins["l2_rope_sin"], (), (sinT[1],), tag="l2sin")
    b.dma("POOL", cosT[0][:, :], ins["l2_rope_cos"], (), (cosT[1],), tag="l2cos")
    pairs = [(P.bb[8 + j], P.bb[12 + j]) for j in range(4)]
    pairs.append(((fbi_bf(P.fb[0][0], 0, T), P.fb[0][1]),
                  (fbi_bf(P.fb[0][0], T, 2 * T), P.fb[0][1])))
    f1h, f1k = P.fb[1]
    for pi, ((q1h, q1k), (q2h, q2k)) in enumerate(pairs):
        for th in range(2):
            c = slice(th * 1024, (th + 1) * 1024)
            sA = f1h[:, 0:1024]
            sB = f1h[:, 1024:2048]
            b.dve(lambda v, q1h=q1h, c=c, sA=sA: v.tensor_mul(
                sA, q1h[:, c], sinT[0][:, c]),
                (q1k, sinT[1]), (f1k,), tag=f"l2r1_{pi}_{th}")
            b.dve(lambda v, q1h=q1h, c=c, sB=sB: v.tensor_mul(
                sB, q1h[:, c], cosT[0][:, c]),
                (q1k, cosT[1]), (f1k,), tag=f"l2r2_{pi}_{th}")
            b.dve(lambda v, q1h=q1h, q2h=q2h, c=c: v.tensor_mul(
                q1h[:, c], q2h[:, c], sinT[0][:, c]),
                (q2k, sinT[1]), (q1k,), tag=f"l2r3_{pi}_{th}")
            b.dve(lambda v, q1h=q1h, c=c, sB=sB: v.tensor_sub(
                q1h[:, c], sB, q1h[:, c]),
                (f1k,), (q1k,), tag=f"l2r4_{pi}_{th}")
            b.dve(lambda v, q2h=q2h, c=c: v.tensor_mul(
                q2h[:, c], q2h[:, c], cosT[0][:, c]),
                (q2k, cosT[1]), (q2k,), tag=f"l2r5_{pi}_{th}")
            b.dve(lambda v, q2h=q2h, c=c, sA=sA: v.tensor_add(
                q2h[:, c], q2h[:, c], sA),
                (f1k,), (q2k,), tag=f"l2r6_{pi}_{th}")

    ytiles = [P.bb[2 + j] for j in range(6)]
    ytiles.append((fbi_bf(P.fb[1][0], 0, T), P.fb[1][1]))
    ytiles.append((fbi_bf(P.fb[1][0], T, 2 * T), P.fb[1][1]))
    kx1 = fbi_bf(P.fb[0][0], 0, T)
    kx2 = fbi_bf(P.fb[0][0], T, 2 * T)
    ptile = (P.bb[0][0], P.bb[0][1])
    pts = (P.bb[1][0], P.bb[1][1])
    for j in range(8):
        def qspec(hh, j=j):
            hsel = 2 * j + hh
            base = (hsel // 4) * 32
            q1h, q1k = P.bb[8 + hsel % 4]
            q2h, q2k = P.bb[12 + hsel % 4]
            return [(q1h[base:base + 32, :], q1k, base, 32),
                    (q2h[base:base + 32, :], q2k, base, 32)]

        def kspec(hh, j=j):
            gq = (2 * j + hh) // 4
            base = gq * 32
            return [(kx1[base:base + 32, :], P.fb[0][1], base, 32),
                    (kx2[base:base + 32, :], P.fb[0][1], base, 32)]

        def out_cb(hh, qi, pv, j=j):
            pvh, pvk = pv
            yh, yk = ytiles[j]
            b.dve(lambda v, pvh=pvh, yh=yh, hh=hh, qi=qi: v.tensor_copy(
                yh[hh * 64:(hh + 1) * 64, qi * 128:(qi + 1) * 128],
                pvh[hh * 64:(hh + 1) * 64, 0:128]),
                (pvk,), (yk,), tag=f"l2y{j}_{hh}_{qi}")

        emit_attn(P, f"l2w{j}", qspec, kspec, vt, 256,
                  lambda hh, j=j: ((2 * j + hh) // 4) * 64, out_cb, 0.125,
                  ptile, pts, window=True,
                  sink=lambda hh, j=j: (snk[0], snk[1], 2 * j + hh))

    def epi_proj(j, t, ph, pk):
        sh, sk = P.sq
        b.act(lambda s, ph=ph, sh=sh, j=j: s.activation(
            sh[:, 0:TT], ph[:, :TT], AF.Identity, bias=pjb[0][:, j:j + 1],
            scale=1.0), (pk, pjb[1], "cpool"), (sk,), tag=f"l2pj_{j}_{t}")
        dh, dk = P.bb[8 + j]
        b.dve(lambda v, sh=sh, dh=dh, t=t, j=j: v.tensor_add(
            dh[:, t * TT:(t + 1) * TT], sh[:, 0:TT],
            P.xn[j][0][:, t * TT:(t + 1) * TT]),
            (sk, P.xn[j][1]), (dk,), tag=f"l2x2_{j}_{t}")
    mm_dense(b, "l2pj", ins["l2_proj_w"], D, D, ytiles, P.pb[:6],
             [(P.wbA[0][:, s * 1024:(s + 1) * 1024], P.wbA[1] + f":{s}")
              for s in range(2)], epi_proj)

    x2 = [P.bb[8 + j] for j in range(8)]
    if DEBUG_DUMP:
        for j in range(8):
            b.dma("POOL", P.ins["dbg_x2"][j * 128:(j + 1) * 128, :],
                  x2[j][0][:, :], (x2[j][1],), (f"dbgx2_{j}",),
                  tag=f"dbgx2_{j}")
    emit_rms(P, "l2h2", x2, [P.bb[j] for j in range(8)], fnw, 1e-5)
    for j in range(8):
        b.dma("SP", P.ins["x2scr"][j * 128:(j + 1) * 128, :], x2[j][0][:, :],
              (x2[j][1],), (f"x2s{j}",), tag=f"l2x2s{j}")
    h2 = [P.bb[j] for j in range(8)]

    def epi_gate(j, t, ph, pk):
        f1h, f1k = P.fb[2]
        b.dve(lambda v, ph=ph, t=t: v.tensor_scalar_add(
            f1h[0:32, t * TT:(t + 1) * TT], ph[0:32, :TT], gbv[0][0:32, 0:1]),
            (pk, gbv[1]), (f1k,), tag=f"l2gl{t}")
    mm_dense(b, "l2g", ins["l2_gate_w"], D, 32, h2, P.pb[:6],
             [(P.wbB[0][:, 0:256], P.wbB[1] + ":0")], epi_gate, blk_cols=32)
    f1h, f1k = P.fb[2]
    f3h, f3k = P.fb[1]
    st_h, st_k = P.stat
    tkh, tkk = P.small("tktile", [128, 64], F32)
    for ti in range(16):
        ph, pk = P.pb[6]
        b.pe(lambda e, ti=ti, ph=ph: e.transpose(
            ph[0:128, 0:32], f1h[0:32, ti * 128:(ti + 1) * 128],
            P.idf[0][0:32, 0:32]), (f1k, P.idf[1]), (pk,), tag=f"l2gt{ti}")
        b.dve(lambda v, ti=ti, ph=ph: v.tensor_copy(
            f3h[:, ti * 32:(ti + 1) * 32], ph[0:128, 0:32]),
            (pk,), (f3k,), tag=f"l2gte{ti}")
    if DEBUG_DUMP:
        b.dma("SP", P.ins["dbg_gl"], f1h[0:32, :], (f1k,), ("dbggl",),
              tag="dbggl")
        b.dma("SP", P.ins["dbg_glt"], f3h[:, 0:512], (f3k,), ("dbgglt",),
              tag="dbgglt")
    for ti in range(16):
        gsl = lambda: f3h[:, ti * 32:(ti + 1) * 32]
        b.dve(lambda v, ti=ti: v.tensor_copy(tkh[:, 0:32],
                                             f3h[:, ti * 32:(ti + 1) * 32]),
              (f3k,), (tkk,), tag=f"l2tk0_{ti}")
        for r in range(4):
            b.dve(lambda v, r=r: v.reduce_max(st_h[:, 12 + r:13 + r],
                                              tkh[:, 0:32], axis=AX.X),
                  (tkk,), (st_k,), tag=f"l2tkm{ti}_{r}")
            if r < 3:
                b.dve(lambda v, r=r: v.tensor_scalar(
                    tkh[:, 32:64], tkh[:, 0:32], st_h[:, 12 + r:13 + r],
                    NEG_BIG, ALU.is_ge, ALU.mult),
                    (tkk, st_k), (tkk,), tag=f"l2tks{ti}_{r}")
                b.dve(lambda v: v.tensor_add(tkh[:, 0:32], tkh[:, 0:32],
                                             tkh[:, 32:64]),
                      (tkk,), (tkk,), tag=f"l2tka{ti}_{r}")
        b.dve(lambda v: v.tensor_scalar_mul(st_h[:, 16:17], st_h[:, 12:13],
                                            -1.0), (st_k,), (st_k,),
              tag=f"l2tkn{ti}")
        b.act(lambda s, ti=ti: s.activation(tkh[:, 0:32],
                                            f3h[:, ti * 32:(ti + 1) * 32],
                                            AF.Exp, bias=st_h[:, 16:17],
                                            scale=1.0),
              (f3k, st_k, "cpool"), (tkk,), tag=f"l2tke{ti}")
        b.dve(lambda v, ti=ti: v.tensor_scalar(
            tkh[:, 32:64], f3h[:, ti * 32:(ti + 1) * 32], st_h[:, 15:16],
            None, ALU.is_ge), (f3k, st_k), (tkk,), tag=f"l2tkg{ti}")
        b.dve(lambda v: v.tensor_mul(tkh[:, 0:32], tkh[:, 0:32],
                                     tkh[:, 32:64]),
              (tkk,), (tkk,), tag=f"l2tkw{ti}")
        b.dve(lambda v: v.reduce_sum(st_h[:, 17:18], tkh[:, 0:32], axis=AX.X),
              (tkk,), (st_k,), tag=f"l2tkss{ti}")
        b.dve(lambda v: v.reciprocal(st_h[:, 18:19], st_h[:, 17:18]),
              (st_k,), (st_k,), tag=f"l2tkr{ti}")
        b.dve(lambda v: v.tensor_scalar_mul(tkh[:, 0:32], tkh[:, 0:32],
                                            st_h[:, 18:19]),
              (tkk, st_k), (tkk,), tag=f"l2tkf{ti}")
        ph, pk = P.pb[6]
        b.pe(lambda e, ph=ph: e.transpose(ph[0:32, 0:128], tkh[:, 0:32],
                                          P.idf[0][:, :]),
             (tkk, P.idf[1]), (pk,), tag=f"l2ct{ti}")
        f0h, f0k = P.fb[0]
        b.dve(lambda v, ti=ti, ph=ph: v.tensor_copy(
            f0h[0:32, ti * 128:(ti + 1) * 128], ph[0:32, 0:128]),
            (pk,), (f0k,), tag=f"l2cte{ti}")
    b.dma("SP", P.ins["cescr"], P.fb[0][0][0:32, :], (P.fb[0][1],), ("cescr",),
          tag="l2ce")
    if DEBUG_DUMP:
        b.dma("SP", P.ins["dbg_ce"], P.fb[0][0][0:32, :], (P.fb[0][1],),
              ("dbgce",), tag="dbgce")

    hid = [P.bb[8 + k] for k in range(8)]
    cebc_h, cebc_k = P.fb[1]
    upslots = [(P.wbA[0][:, s * 1024:(s + 1) * 1024], P.wbA[1] + f":{s}")
               for s in range(2)] + \
              [(P.wbB[0][:, s * 1024:(s + 1) * 1024], P.wbB[1] + f":{s}")
               for s in range(2)]
    dnslots = [(P.wfull[0][:, s * 1024:(s + 1) * 1024], P.wfull[1] + f":{s}")
               for s in range(8)]
    wup_r = ins["l2_wup"].rearrange("e (k p) n -> e p k n", p=128)
    wdn_r = ins["l2_wdn"].rearrange("e (k p) n -> e p k n", p=128)
    for e in range(32):
        b.dma("SP", P.rows[0][0:1, 0:T], P.ins["cescr"][e:e + 1, :],
              ("cescr",), (P.rows[1],), tag=f"l2cer{e}")
        for t in range(NT):
            bcast_row(b, f"l2ceb{e}_{t}",
                      (P.rows[0], P.rows[1], slice(t * TT, (t + 1) * TT)),
                      P.pb[6], (P.oner[0], P.oner[1]))
            b.act(lambda s, t=t: s.activation(
                cebc_h[:, t * TT:(t + 1) * TT], P.pb[6][0][:, :TT], AF.Copy),
                (P.pb[6][1],), (cebc_k,), tag=f"l2cee{e}_{t}")
        hid_h = [h_ for h_, _ in hid]
        seq = []
        for jj in range(8):
            seq.append(jj * 128)
            seq.append(1024 + jj * 128)
        nseq = len(seq)

        def dma_up(si, e=e):
            col0 = seq[si]
            wh, wk = upslots[si % 4]
            b.dma("POOL", wh.rearrange("p (k c) -> p k c", c=128),
                  wup_r[e, :, :, col0:col0 + 128], (), (wk,),
                  tag=f"l2up{e}_{si}")
        for si in range(min(4, nseq)):
            dma_up(si)
        for jj in range(8):
            for half in (0, 1):
                si = jj * 2 + half
                wh, wk = upslots[si % 4]
                for k in range(KC):
                    for t in range(NT):
                        ps, psk = P.pb[half * 4 + t]
                        b.pe(lambda e2, ps=ps, wh=wh, k=k, t=t: e2.matmul(
                            ps[:, :TT], wh[:, k * 128:(k + 1) * 128],
                            h2[k][0][:, t * TT:(t + 1) * TT],
                            start=(k == 0), stop=(k == KC - 1)),
                            (wk, h2[k][1]), (psk,), tag=f"l2um{e}_{si}_{k}_{t}")
                if si + 4 < nseq:
                    dma_up(si + 4)
            hh_, hk_ = hid[jj]
            for t in range(NT):
                sh, sk = P.sq
                ss = ((jj * 4 + t) % 2) * TT
                b.act(lambda s, t=t, sh=sh, ss=ss: s.activation(
                    sh[:, ss:ss + TT], P.pb[4 + t][0][:, :TT], AF.Silu),
                    (P.pb[4 + t][1],), (sk,), tag=f"l2sg{e}_{jj}_{t}")
                b.dve(lambda v, t=t, hh_=hh_, sh=sh, ss=ss: v.tensor_mul(
                    hh_[:, t * TT:(t + 1) * TT], P.pb[t][0][:, :TT],
                    sh[:, ss:ss + TT]),
                    (P.pb[t][1], sk), (hk_,), tag=f"l2hu{e}_{jj}_{t}")
                b.dve(lambda v, t=t, hh_=hh_: v.tensor_mul(
                    hh_[:, t * TT:(t + 1) * TT], hh_[:, t * TT:(t + 1) * TT],
                    cebc_h[:, t * TT:(t + 1) * TT]),
                    (hk_, cebc_k), (hk_,), tag=f"l2hc{e}_{jj}_{t}")
        def dma_dn(j, e=e):
            wh, wk = dnslots[j % 8]
            b.dma("POOL", wh.rearrange("p (k c) -> p k c", c=128),
                  wdn_r[e, :, :, j * 128:(j + 1) * 128], (), (wk,),
                  tag=f"l2dn{e}_{j}")
        for j in range(min(8, 8)):
            dma_dn(j)
        for j in range(8):
            wh, wk = dnslots[j % 8]
            for k in range(KC):
                for t in range(NT):
                    ps, psk = P.pb[(j % 2) * 4 + t]
                    hh_, hk_ = hid[k]
                    b.pe(lambda e2, ps=ps, wh=wh, k=k, t=t, hh_=hh_:
                         e2.matmul(ps[:, :TT], wh[:, k * 128:(k + 1) * 128],
                                   hh_[:, t * TT:(t + 1) * TT],
                                   start=(k == 0), stop=(k == KC - 1)),
                         (wk, hk_), (psk,), tag=f"l2dm{e}_{j}_{k}_{t}")
            gh, gk = P.gb[j]
            for t in range(NT):
                ps, psk = P.pb[(j % 2) * 4 + t]
                if e == 0:
                    b.act(lambda s, ps=ps, gh=gh, t=t: s.activation(
                        gh[:, t * TT:(t + 1) * TT], ps[:, :TT], AF.Copy),
                        (psk,), (gk,), tag=f"l2y2i{j}_{t}")
                else:
                    b.dve(lambda v, ps=ps, gh=gh, t=t: v.tensor_add(
                        gh[:, t * TT:(t + 1) * TT],
                        gh[:, t * TT:(t + 1) * TT], ps[:, :TT]),
                        (psk, gk), (gk,), tag=f"l2y2a{e}_{j}_{t}")

    if DEBUG_DUMP:
        for j in range(8):
            b.dma("POOL", P.ins["dbg_y2"][j * 128:(j + 1) * 128, :],
                  P.gb[j][0][:, :], (P.gb[j][1],), (f"dbgy2_{j}",),
                  tag=f"dbgy2_{j}")
    for j in range(8):
        rh, rk = P.bb[8 + (j % 2)]
        b.dma("POOL", rh[:, :], P.ins["x2scr"][j * 128:(j + 1) * 128, :],
              (f"x2s{j}",), (rk,), tag=f"l2x2r{j}")
        gh, gk = P.gb[j]
        b.dve(lambda v, rh=rh, gh=gh: v.tensor_add(gh[:, :], gh[:, :],
                                                   rh[:, :]),
              (rk, gk), (gk,), tag=f"l2gadd{j}")
        b.act(lambda s, gh=gh: s.activation(gh[:, :], gh[:, :], AF.Silu),
              (gk,), (gk,), tag=f"l2gsilu{j}")
        if DEBUG_DUMP:
            b.dma("POOL", P.ins["dbg_g"][j * 128:(j + 1) * 128, :], gh[:, :],
                  (gk,), (f"dbgg{j}",), tag=f"dbgg{j}")
            b.dma("POOL", P.ins["dbg_xn"][j * 128:(j + 1) * 128, :],
                  P.xn[j][0][:, :], (P.xn[j][1],), (f"dbgxn{j}",),
                  tag=f"dbgxn{j}")

    b.dma("POOL", P.wfull[0][:, :].rearrange("p (k c) -> p k c", c=1024),
          ins["l2_wvT"].rearrange("(k p) n -> p k n", p=128),
          (), tuple(P.wfull[1] + f":{s}" for s in range(8)), tag="l2wvT")
    b.dma("SP", P.rows[0][0:1, T:T + 1024], ins["l2_bv_row2"],
          (), (P.rows[1],), tag="l2bvr2")
    wqslots = [(P.wbA[0][:, s * 1024:(s + 1) * 1024], P.wbA[1] + f":{s}")
               for s in range(2)]
    wkslots = [(P.wbB[0][:, s * 1024:(s + 1) * 1024], P.wbB[1] + f":{s}")
               for s in range(2)]
    wq_r = ins["l2_wqT"].rearrange("(k p) n -> p k n", p=128)
    wk_r = ins["l2_wkT"].rearrange("(k p) n -> p k n", p=128)

    def dma_qk(j):
        wh, wk = wqslots[j % 2]
        b.dma("POOL", wh.rearrange("p (k c) -> p k c", c=128),
              wq_r[:, :, j * 128:(j + 1) * 128], (), (wk,), tag=f"l2wq{j}")
        wh2, wk2 = wkslots[j % 2]
        b.dma("POOL", wh2.rearrange("p (k c) -> p k c", c=128),
              wk_r[:, :, j * 128:(j + 1) * 128], (), (wk2,), tag=f"l2wk{j}")
    dma_qk(0)
    for j in range(8):
        if j + 1 < 8:
            dma_qk(j + 1)
        for (slots, dstt, bias2d) in ((wqslots, P.bb[0], bqv),
                                      (wkslots, P.bb[1], bkv)):
            wh, wk = slots[j % 2]
            dh, dk = dstt
            for t in range(NT):
                ps, psk = P.pb[t % 2]
                for k in range(KC):
                    b.pe(lambda e, ps=ps, wh=wh, k=k, t=t: e.matmul(
                        ps[:, :TT], wh[:, k * 128:(k + 1) * 128],
                        P.xn[k][0][:, t * TT:(t + 1) * TT],
                        start=(k == 0), stop=(k == KC - 1)),
                        (wk, P.xn[k][1]), (psk,), tag=f"l2qk{j}_{t}_{k}")
                b.act(lambda s, ps=ps, dh=dh, t=t, j=j, bias2d=bias2d:
                      s.activation(dh[:, t * TT:(t + 1) * TT], ps[:, :TT],
                                   AF.Identity, bias=bias2d[0][:, j:j + 1],
                                   scale=1.0),
                      (psk, bias2d[1], "cpool"), (dk,), tag=f"l2qke{j}_{t}")
        for ti in range(16):
            ps, psk = P.pb[2 + (ti % 2)]
            for k in range(KC):
                b.pe(lambda e, ps=ps, k=k, ti=ti, j=j: e.matmul(
                    ps[:, 0:128], P.xn[k][0][:, ti * 128:(ti + 1) * 128],
                    P.wfull[0][:, k * 1024 + j * 128:k * 1024 + (j + 1) * 128],
                    start=(k == 0), stop=False),
                    (P.xn[k][1], P.wfull[1] + f":{k}"), (psk,),
                    tag=f"l2sv{j}_{ti}_{k}")
            b.pe(lambda e, ps=ps, j=j: e.matmul(
                ps[:, 0:128], P.oner[0][0:1, :],
                P.rows[0][0:1, T + j * 128:T + (j + 1) * 128],
                start=False, stop=True),
                (P.oner[1], P.rows[1]), (psk,), tag=f"l2svb{j}_{ti}")
            b.act(lambda s, ps=ps, ti=ti: s.activation(
                P.bb[2][0][:, ti * 128:(ti + 1) * 128], ps[:, 0:128],
                AF.Copy), (psk,), (P.bb[2][1],), tag=f"l2sve{j}_{ti}")

        if DEBUG_DUMP and j == 0:
            for nm, tt_ in (("dbg_q", P.bb[0]), ("dbg_k", P.bb[1]),
                            ("dbg_v", P.bb[2])):
                b.dma("POOL", P.ins[nm], tt_[0][:, :], (tt_[1],),
                      (nm,), tag=nm)

        def qspec(hh):
            return [(P.bb[0][0][hh * 64:(hh + 1) * 64, :], P.bb[0][1],
                     hh * 64, 64)]

        def kspec(hh):
            return [(P.bb[1][0][hh * 64:(hh + 1) * 64, :], P.bb[1][1],
                     hh * 64, 64)]

        def out_cb(hh, qi, pv, j=j):
            pvh, pvk = pv
            b.dve(lambda v, pvh=pvh, hh=hh, qi=qi: v.tensor_copy(
                P.fb[2][0][hh * 64:(hh + 1) * 64, qi * 128:(qi + 1) * 128],
                pvh[hh * 64:(hh + 1) * 64, 0:128]),
                (pvk,), (P.fb[2][1],), tag=f"l2s{j}_{hh}_{qi}")

        emit_attn(P, f"l2a{j}", qspec, kspec,
                  (P.bb[2][0], P.bb[2][1]), 128, lambda hh: hh * 64,
                  out_cb, 0.125, (P.bb[3][0], P.bb[3][1]),
                  (P.bb[4][0], P.bb[4][1]), window=False, sink=None)
        f0h, f0k = P.fb[0]
        b.dma("SP", f0h[:, :], xsrc[j * 128:(j + 1) * 128, :],
              (() if src_keys is None else (src_keys[j],)), (f0k,),
              tag=f"l2cx{j}")
        if DEBUG_DUMP:
            b.dma("SP", P.ins["dbg_s"][j * 128:(j + 1) * 128, :],
                  P.fb[2][0][:, :], (P.fb[2][1],), (f"dbgs{j}",),
                  tag=f"dbgs{j}")
        b.dve(lambda v, j=j: v.tensor_mul(P.fb[2][0][:, :], P.fb[2][0][:, :],
                                          P.gb[j][0][:, :]),
              (P.fb[2][1], P.gb[j][1]), (P.fb[2][1],), tag=f"l2cm{j}")
        b.dve(lambda v: v.tensor_add(P.fb[2][0][:, :], P.fb[2][0][:, :],
                                     f0h[:, :]),
              (P.fb[2][1], f0k), (P.fb[2][1],), tag=f"l2ca{j}")
        b.dma("SP", xdst[j * 128:(j + 1) * 128, :], P.fb[2][0][:, :],
              (P.fb[2][1],), (dst_keys[j],), tag=f"l2cw{j}")


def _perm_half(n_heads, hd):
    idx = []
    for h in range(n_heads):
        idx.extend(h * hd + 2 * np.arange(hd // 2))
    for h in range(n_heads):
        idx.extend(h * hd + 2 * np.arange(hd // 2) + 1)
    return np.array(idx)


def _col2d(v):
    return np.ascontiguousarray(v.reshape(-1, 128).T)


def prep_l2(ins, d):
    C = np.ascontiguousarray
    d["l2_attn_norm_w"] = _col2d(ins["L2_ffn_attn_norm_w"])
    d["l2_ffn_norm_w"] = _col2d(ins["L2_ffn_ffn_norm_w"])
    qkv_w = np.asarray(ins["L2_ffn_qkv_w"])
    qkv_b = np.asarray(ins["L2_ffn_qkv_b"])
    qp_x1, qp_x2 = [], []
    for j in range(4):
        for g in range(4):
            h = g * 4 + j
            qp_x1.extend(h * 64 + 2 * np.arange(32))
            qp_x2.extend(h * 64 + 2 * np.arange(32) + 1)
    qp = np.array(qp_x1 + qp_x2)
    kp = 1024 + _perm_half(4, 64)
    cols = np.concatenate([qp, kp])
    d["l2_qkv_w"] = C(qkv_w[:, cols])
    d["l2_qkv_b2d"] = C(qkv_b[cols].reshape(10, 128).T)
    d["l2_wv_cols"] = C(qkv_w[:, 1280:1536])
    d["l2_bv_row"] = C(qkv_b[1280:1536].reshape(1, 256))
    d["l2_proj_w"] = C(np.asarray(ins["L2_ffn_proj_w"]))
    d["l2_proj_b2d"] = _col2d(np.asarray(ins["L2_ffn_proj_b"]))
    d["l2_sink_bc"] = C(np.broadcast_to(np.asarray(ins["L2_ffn_attn_sink"]),
                                        (128, 16)))
    d["l2_gate_w"] = C(np.asarray(ins["L2_ffn_gate_w"]))
    d["l2_gate_b2d"] = C(np.asarray(ins["L2_ffn_gate_b"]).reshape(32, 1))
    d["l2_wup"] = np.asarray(ins["L2_ffn_W_up"])
    d["l2_wdn"] = np.asarray(ins["L2_ffn_W_down"])
    DH = 64
    pos = np.arange(T, dtype=np.float64) / 32.0
    invf = 1.0 / (150000.0 ** (np.arange(0, DH, 2, dtype=np.float64) / DH))
    ang = pos[None, :] * invf[:, None]
    sin = np.sin(ang).astype(np.float32)
    cos = np.cos(ang).astype(np.float32)
    d["l2_rope_sin"] = C(np.tile(sin, (4, 1)))
    d["l2_rope_cos"] = C(np.tile(cos, (4, 1)))
    d["l2_wqT"] = C(np.asarray(ins["L2_attn_Wq"]).T)
    d["l2_wkT"] = C(np.asarray(ins["L2_attn_Wk"]).T)
    d["l2_wvT"] = C(np.asarray(ins["L2_attn_Wv"]).T)
    d["l2_bq2d"] = _col2d(np.asarray(ins["L2_attn_bq"]))
    d["l2_bk2d"] = _col2d(np.asarray(ins["L2_attn_bk"]))
    d["l2_bv_row2"] = C(np.asarray(ins["L2_attn_bv"]).reshape(1, 1024))


L2_DEVINS = [
    ("l2_attn_norm_w", [128, 8]), ("l2_ffn_norm_w", [128, 8]),
    ("l2_qkv_w", [1024, 1280]), ("l2_qkv_b2d", [128, 10]),
    ("l2_wv_cols", [1024, 256]), ("l2_bv_row", [1, 256]),
    ("l2_proj_w", [1024, 1024]), ("l2_proj_b2d", [128, 8]),
    ("l2_sink_bc", [128, 16]), ("l2_gate_w", [1024, 32]),
    ("l2_gate_b2d", [32, 1]), ("l2_wup", [32, 1024, 2048]),
    ("l2_wdn", [32, 1024, 1024]), ("l2_rope_sin", [128, 2048]),
    ("l2_rope_cos", [128, 2048]), ("l2_wqT", [1024, 1024]),
    ("l2_wkT", [1024, 1024]), ("l2_wvT", [1024, 1024]),
    ("l2_bq2d", [128, 8]), ("l2_bk2d", [128, 8]), ("l2_bv_row2", [1, 1024]),
]

LAYER_DEVINS = {2: L2_DEVINS}
LAYER_PREP = {2: prep_l2}
LAYER_EMIT = {}


def build_program(layer_ids):
    nc = bass.Bass("TRN2", target_bir_lowering=False)
    ins = {}
    ins["x_in"] = nc.dram_tensor("x_in", [D, T], F32, kind="ExternalInput").ap()
    ins["x_out"] = nc.dram_tensor("x_out", [D, T], F32, kind="ExternalOutput").ap()
    ins["xres"] = nc.dram_tensor("xres", [D, T], F32).ap()
    ins["x2scr"] = nc.dram_tensor("x2scr", [D, T], BF16).ap()
    ins["cescr"] = nc.dram_tensor("cescr", [32, T], F32).ap()
    ins["rowscr"] = nc.dram_tensor("rowscr", [64, T], F32).ap()
    ins["vscr"] = nc.dram_tensor("vscr", [T, 2048], BF16).ap()
    ins["outscr"] = nc.dram_tensor("outscr", [2048, T], BF16).ap()
    ins["hscr"] = nc.dram_tensor("hscr", [2816, T], BF16).ap()
    ins["hscr2"] = nc.dram_tensor("hscr2", [2048, T], BF16).ap()
    ins["xfscr"] = nc.dram_tensor("xfscr", [D, T], BF16).ap()
    if DEBUG_DUMP:
        for nm in ("dbg_xn", "dbg_g", "dbg_s"):
            ins[nm] = nc.dram_tensor(nm, [D, T], F32,
                                     kind="ExternalOutput").ap()
        for nm in ("dbg_q", "dbg_k", "dbg_v"):
            ins[nm] = nc.dram_tensor(nm, [128, T], F32,
                                     kind="ExternalOutput").ap()
        for nm in ("dbg_t0", "dbg_t1", "dbg_t2", "dbg_t3"):
            ins[nm] = nc.dram_tensor(nm, [128, T], F32,
                                     kind="ExternalOutput").ap()
        ins["dbg_t4"] = nc.dram_tensor("dbg_t4", [128, 1024], F32,
                                       kind="ExternalOutput").ap()
        for nm in ("dbg_x2", "dbg_y2"):
            ins[nm] = nc.dram_tensor(nm, [D, T], F32,
                                     kind="ExternalOutput").ap()
        ins["dbg_ce"] = nc.dram_tensor("dbg_ce", [32, T], F32,
                                       kind="ExternalOutput").ap()
        ins["dbg_gl"] = nc.dram_tensor("dbg_gl", [32, T], F32,
                                       kind="ExternalOutput").ap()
        ins["dbg_glt"] = nc.dram_tensor("dbg_glt", [128, 512], F32,
                                        kind="ExternalOutput").ap()
    for l in layer_ids:
        for name, shape in LAYER_DEVINS[l]:
            if name in ins:
                continue
            ins[name] = nc.dram_tensor(name, shape, F32,
                                       kind="ExternalInput").ap()
    with ExitStack() as st:
        P = Prog(nc, st, ins)
        emit_prologue(P)
        nl = len(layer_ids)
        for i, l in enumerate(layer_ids):
            if i == 0:
                xsrc, src_keys = ins["x_in"], None
            else:
                xsrc, src_keys = ins["xres"], [f"xres{j}" for j in range(8)]
            if i == nl - 1:
                xdst, dst_keys = ins["x_out"], [f"xout{j}" for j in range(8)]
            else:
                xdst, dst_keys = ins["xres"], [f"xres{j}" for j in range(8)]
            LAYER_EMIT[l](P, xsrc, xdst, src_keys, dst_keys)
        b = P.b
        outs = [b.last_writer[f"xout{j}"] for j in range(8)]
        if DEBUG_DUMP:
            for j in range(8):
                for pre in ("dbgg", "dbgxn", "dbgs"):
                    if f"{pre}{j}" in b.last_writer:
                        outs.append(b.last_writer[f"{pre}{j}"])
            for nm in ("dbg_q", "dbg_k", "dbg_v", "dbgce", "dbggl", "dbgglt"):
                if nm in b.last_writer:
                    outs.append(b.last_writer[nm])
            for j in range(8):
                for pre in ("dbgx2_", "dbgy2_"):
                    if f"{pre}{j}" in b.last_writer:
                        outs.append(b.last_writer[f"{pre}{j}"])
        b.barrier_on("SP", outs, tag="final")
        b.finalize()
    return nc


FUSED_GROUPS = [[0, 1, 2, 3]]
DEBUG_DUMP = False


def kernel(**inputs):
    d = {}
    for l in set(sum(FUSED_GROUPS, [])):
        LAYER_PREP[l](inputs, d)
    x = np.asarray(inputs["L0_attn_u"])
    ncores = 8
    xcur = [np.ascontiguousarray(x[c].T) for c in range(ncores)]
    for grp in FUSED_GROUPS:
        nc = build_program(grp)
        names = {n for n, _ in sum((LAYER_DEVINS[l] for l in grp), [])}
        in_maps = []
        for c in range(ncores):
            m = {n: d[n] for n in names}
            m["x_in"] = xcur[c]
            in_maps.append(m)
        res = run_bass_kernel_spmd(nc, in_maps, core_ids=list(range(ncores)))
        xcur = [r["x_out"] for r in res.results]
    out = np.stack([xc.T for xc in xcur]).astype(np.float32)
    return np.ascontiguousarray(out)


LAYER_EMIT[2] = emit_layer2


def emit_topk_probs(P, pref, glsrc, cedst, E, K, eps=0.0):
    b = P.b
    st_h, st_k = P.stat
    glh, glk = glsrc
    ceh, cek = cedst
    tkh, tkk = P.small("tktile", [128, 64], F32)
    glth, gltk = P.stat[0][:, 32:64], P.stat[1]
    for ti in range(16):
        ph, pk = P.pb[6]
        b.pe(lambda e, ti=ti, ph=ph: e.transpose(
            ph[0:128, 0:E], glh[0:E, ti * 128:(ti + 1) * 128],
            P.idf[0][0:E, 0:E]), (glk, P.idf[1]), (pk,), tag=f"{pref}t{ti}")
        b.dve(lambda v, ph=ph: v.tensor_copy(glth[:, 0:E], ph[0:128, 0:E]),
              (pk,), (gltk,), tag=f"{pref}te{ti}")
        b.dve(lambda v: v.tensor_copy(tkh[:, 0:E], glth[:, 0:E]),
              (gltk,), (tkk,), tag=f"{pref}c{ti}")
        for r in range(K):
            b.dve(lambda v, r=r: v.reduce_max(st_h[:, 12 + r:13 + r],
                                              tkh[:, 0:E], axis=AX.X),
                  (tkk,), (st_k,), tag=f"{pref}m{ti}_{r}")
            if r < K - 1:
                b.dve(lambda v, r=r: v.tensor_scalar(
                    tkh[:, 32:32 + E], tkh[:, 0:E], st_h[:, 12 + r:13 + r],
                    NEG_BIG, ALU.is_ge, ALU.mult),
                    (tkk, st_k), (tkk,), tag=f"{pref}s{ti}_{r}")
                b.dve(lambda v: v.tensor_add(tkh[:, 0:E], tkh[:, 0:E],
                                             tkh[:, 32:32 + E]),
                      (tkk,), (tkk,), tag=f"{pref}a{ti}_{r}")
        b.dve(lambda v: v.tensor_scalar_mul(st_h[:, 16:17], st_h[:, 12:13],
                                            -1.0), (st_k,), (st_k,),
              tag=f"{pref}n{ti}")
        b.act(lambda s: s.activation(tkh[:, 0:E], glth[:, 0:E], AF.Exp,
                                     bias=st_h[:, 16:17], scale=1.0,
                                     accum_out=st_h[:, 19:20]),
              (gltk, st_k, "cpool"), (tkk, st_k), tag=f"{pref}e{ti}")
        b.dve(lambda v: v.tensor_scalar(
            tkh[:, 32:32 + E], glth[:, 0:E], st_h[:, 12 + K - 1:13 + K - 1],
            None, ALU.is_ge), (gltk, st_k), (tkk,), tag=f"{pref}g{ti}")
        b.dve(lambda v: v.tensor_mul(tkh[:, 0:E], tkh[:, 0:E],
                                     tkh[:, 32:32 + E]),
              (tkk,), (tkk,), tag=f"{pref}w{ti}")
        b.dve(lambda v: v.reduce_sum(st_h[:, 17:18], tkh[:, 0:E], axis=AX.X),
              (tkk,), (st_k,), tag=f"{pref}ss{ti}")
        if eps:
            b.dve(lambda v, eps=eps: v.tensor_scalar(
                st_h[:, 20:21], st_h[:, 19:20], eps, None, ALU.mult),
                (st_k,), (st_k,), tag=f"{pref}ep{ti}")
            b.dve(lambda v: v.tensor_add(st_h[:, 17:18], st_h[:, 17:18],
                                         st_h[:, 20:21]),
                  (st_k,), (st_k,), tag=f"{pref}ea{ti}")
        b.dve(lambda v: v.reciprocal(st_h[:, 18:19], st_h[:, 17:18]),
              (st_k,), (st_k,), tag=f"{pref}r{ti}")
        b.dve(lambda v: v.tensor_scalar_mul(tkh[:, 0:E], tkh[:, 0:E],
                                            st_h[:, 18:19]),
              (tkk, st_k), (tkk,), tag=f"{pref}f{ti}")
        ph2, pk2 = P.pb[6]
        b.pe(lambda e, ph2=ph2: e.transpose(ph2[0:E, 0:128], tkh[:, 0:E],
                                            P.idf[0][:, :]),
             (tkk, P.idf[1]), (pk2,), tag=f"{pref}tb{ti}")
        b.dve(lambda v, ti=ti, ph2=ph2: v.tensor_copy(
            ceh[0:E, ti * 128:(ti + 1) * 128], ph2[0:E, 0:128]),
            (pk2,), (cek,), tag=f"{pref}ce{ti}")
    b.dma("SP", P.ins["cescr"][0:E, :], ceh[0:E, :], (cek,), ("cescr",),
          tag=f"{pref}dma")


def emit_moe_std(P, pref, nE, dff, wup_r, wdn_r, src, hid, acc, upslots,
                 dnslots, acc_init_first=True):
    b = P.b
    kd = dff // 128
    cebc_h, cebc_k = P.fb[1]
    nus = len(upslots)
    for e in range(nE):
        b.dma("SP", P.rows[0][0:1, 0:T], P.ins["cescr"][e:e + 1, :],
              ("cescr",), (P.rows[1],), tag=f"{pref}cer{e}")
        for t in range(NT):
            bcast_row(b, f"{pref}ceb{e}_{t}",
                      (P.rows[0], P.rows[1], slice(t * TT, (t + 1) * TT)),
                      P.pb[6], (P.oner[0], P.oner[1]))
            b.act(lambda s, t=t: s.activation(
                cebc_h[:, t * TT:(t + 1) * TT], P.pb[6][0][:, :TT], AF.Copy),
                (P.pb[6][1],), (cebc_k,), tag=f"{pref}cee{e}_{t}")
        seq = []
        for jj in range(kd):
            seq.append(jj * 128)
            seq.append(dff + jj * 128)
        nseq = len(seq)

        def dma_up(si, e=e):
            col0 = seq[si]
            wh, wk = upslots[si % nus]
            b.dma("POOL", wh.rearrange("p (k c) -> p k c", c=128),
                  wup_r[e, :, :, col0:col0 + 128], (), (wk,),
                  tag=f"{pref}up{e}_{si}")
        for si in range(min(nus, nseq)):
            dma_up(si)
        for jj in range(kd):
            for half in (0, 1):
                si = jj * 2 + half
                wh, wk = upslots[si % nus]
                for k in range(KC):
                    for t in range(NT):
                        ps, psk = P.pb[half * 4 + t] if half == 0 else P.pb[4 + t]
                        ps, psk = P.pb[half * 4 + t]
                        b.pe(lambda e2, ps=ps, wh=wh, k=k, t=t: e2.matmul(
                            ps[:, :TT], wh[:, k * 128:(k + 1) * 128],
                            src[k][0][:, t * TT:(t + 1) * TT],
                            start=(k == 0), stop=(k == KC - 1)),
                            (wk, src[k][1]), (psk,),
                            tag=f"{pref}um{e}_{si}_{k}_{t}")
                if si + nus < nseq:
                    dma_up(si + nus)
            hh_, hk_ = hid[jj]
            for t in range(NT):
                sh, sk = P.sq
                ss = ((jj * 4 + t) % 2) * TT
                b.act(lambda s, t=t, sh=sh, ss=ss: s.activation(
                    sh[:, ss:ss + TT], P.pb[t][0][:, :TT], AF.Silu),
                    (P.pb[t][1],), (sk,), tag=f"{pref}sg{e}_{jj}_{t}")
                b.dve(lambda v, t=t, hh_=hh_, sh=sh, ss=ss: v.tensor_mul(
                    hh_[:, t * TT:(t + 1) * TT], P.pb[4 + t][0][:, :TT],
                    sh[:, ss:ss + TT]),
                    (P.pb[4 + t][1], sk), (hk_,), tag=f"{pref}hu{e}_{jj}_{t}")
                b.dve(lambda v, t=t, hh_=hh_: v.tensor_mul(
                    hh_[:, t * TT:(t + 1) * TT], hh_[:, t * TT:(t + 1) * TT],
                    cebc_h[:, t * TT:(t + 1) * TT]),
                    (hk_, cebc_k), (hk_,), tag=f"{pref}hc{e}_{jj}_{t}")
        nds = len(dnslots)

        def dma_dn(j, e=e):
            wh, wk = dnslots[j % nds]
            b.dma("POOL", wh[:, 0:kd * 128].rearrange("p (k c) -> p k c",
                                                      c=128),
                  wdn_r[e, :, :, j * 128:(j + 1) * 128], (), (wk,),
                  tag=f"{pref}dn{e}_{j}")
        for j in range(min(nds, 8)):
            dma_dn(j)
        for j in range(8):
            wh, wk = dnslots[j % nds]
            for k in range(kd):
                for t in range(NT):
                    ps, psk = P.pb[(j % 2) * 4 + t]
                    hh_, hk_ = hid[k]
                    b.pe(lambda e2, ps=ps, wh=wh, k=k, t=t, hh_=hh_:
                         e2.matmul(ps[:, :TT], wh[:, k * 128:(k + 1) * 128],
                                   hh_[:, t * TT:(t + 1) * TT],
                                   start=(k == 0), stop=(k == kd - 1)),
                         (wk, hk_), (psk,), tag=f"{pref}dm{e}_{j}_{k}_{t}")
            if j + nds < 8:
                dma_dn(j + nds)
            gh, gk = acc[j]
            for t in range(NT):
                ps, psk = P.pb[(j % 2) * 4 + t]
                if e == 0 and acc_init_first:
                    b.act(lambda s, ps=ps, gh=gh, t=t: s.activation(
                        gh[:, t * TT:(t + 1) * TT], ps[:, :TT], AF.Copy),
                        (psk,), (gk,), tag=f"{pref}y2i{j}_{t}")
                else:
                    b.dve(lambda v, ps=ps, gh=gh, t=t: v.tensor_add(
                        gh[:, t * TT:(t + 1) * TT],
                        gh[:, t * TT:(t + 1) * TT], ps[:, :TT]),
                        (psk, gk), (gk,), tag=f"{pref}y2a{e}_{j}_{t}")


def emit_layer3(P, xsrc, xdst, src_keys=None, dst_keys=None):
    b = P.b
    ins = P.ins
    emit_outer_ln(P, xsrc, "l3o", src_keys)

    rl = P.fb[2]
    def epi_rt(j, t, ph, pk):
        b.dve(lambda v, ph=ph, t=t: v.tensor_copy(
            rl[0][0:8, t * TT:(t + 1) * TT], ph[0:8, :TT]),
            (pk,), (rl[1],), tag=f"l3rt{t}")
    mm_dense(b, "l3r", ins["l3_router_w"], D, 8, P.xn, P.pb[:6],
             [(P.wbA[0][:, 0:64], P.wbA[1] + ":0")], epi_rt, blk_cols=8)
    emit_topk_probs(P, "l3tk", rl, rl, 8, 2, eps=0.0)
    upslots = [(P.wbA[0][:, s * 1024:(s + 1) * 1024], P.wbA[1] + f":{s}")
               for s in range(2)] + \
              [(P.wbB[0][:, s * 1024:(s + 1) * 1024], P.wbB[1] + f":{s}")
               for s in range(2)]
    dnslots = [(P.wfull[0][:, s * 1024:(s + 1) * 1024], P.wfull[1] + f":{s}")
               for s in range(8)]
    wup_r = ins["l3_up"].rearrange("e (k p) n -> e p k n", p=128)
    wdn_r = ins["l3_dn"].rearrange("e (k p) n -> e p k n", p=128)
    emit_moe_std(P, "l3m", 8, 1024, wup_r, wdn_r, P.xn,
                 [P.bb[8 + k] for k in range(8)], P.gb, upslots, dnslots)
    for j in range(8):
        gh, gk = P.gb[j]
        b.act(lambda s, gh=gh: s.activation(gh[:, :], gh[:, :], AF.Silu),
              (gk,), (gk,), tag=f"l3gs{j}")

    ln1w = load_smallvec(P, "l3ln1w", "l3_ln1_w2d", (128, 8))
    ln1b = load_smallvec(P, "l3ln1b", "l3_ln1_b2d", (128, 8))
    ln2w = load_smallvec(P, "l3ln2w", "l3_ln2_w2d", (128, 8))
    ln2b = load_smallvec(P, "l3ln2b", "l3_ln2_b2d", (128, 8))
    bqk2 = load_smallvec(P, "l3bqk", "l3_bqk2d", (128, 2))
    gam = load_smallvec(P, "l3gam", "l3_gamma2d", (128, 8))
    bet = load_smallvec(P, "l3bet", "l3_beta2d", (128, 8))
    bg2 = load_smallvec(P, "l3bg", "l3_bg2d", (128, 16))
    bo2 = load_smallvec(P, "l3bo", "l3_bo2d", (128, 8))
    b64 = P.small("l3b64", [64, 64], BF16)
    b.dma("POOL", b64[0][:, :], ins["l3_bias64"], (), (b64[1],), tag="l3b64")

    rows_h, rows_k = P.rows
    f2h, f2k = P.fb[2]
    eps5 = b.const(1e-5)
    for t in range(NT):
        for k in range(KC):
            xh, xk = P.xn[k]
            b.pe(lambda e, xh=xh, k=k, t=t: e.matmul(
                P.pb[0][0][0:1, :TT], P.onecb[0][:, 0:1],
                xh[:, t * TT:(t + 1) * TT],
                start=(k == 0), stop=(k == KC - 1)),
                (P.onecb[1], xk), (P.pb[0][1],), tag=f"l3lsx{k}_{t}")
            sh, sk = P.sq
            b.act(lambda s, xh=xh, sh=sh, t=t: s.activation(
                sh[:, :TT], xh[:, t * TT:(t + 1) * TT], AF.Square),
                (xk,), (sk,), tag=f"l3lsq{k}_{t}")
            b.pe(lambda e, sh=sh, k=k: e.matmul(
                P.pb[1][0][0:1, :TT], P.onec[0][:, 0:1], sh[:, :TT],
                start=(k == 0), stop=(k == KC - 1)),
                (P.onec[1], sk), (P.pb[1][1],), tag=f"l3lsq2{k}_{t}")
        b.dve(lambda v, t=t: v.tensor_scalar_mul(
            rows_h[0:1, t * TT:(t + 1) * TT], P.pb[0][0][0:1, :TT], 1.0 / D),
            (P.pb[0][1],), (rows_k,), tag=f"l3lmu{t}")
        b.dve(lambda v, t=t: v.tensor_scalar_mul(
            f2h[0:1, t * TT:(t + 1) * TT], P.pb[1][0][0:1, :TT], 1.0 / D),
            (P.pb[1][1],), (f2k,), tag=f"l3lms{t}")
    b.act(lambda s: s.activation(rows_h[0:1, T:2 * T], rows_h[0:1, 0:T],
                                 AF.Square), (rows_k,), (rows_k,), tag="l3lm2")
    b.dve(lambda v: v.tensor_sub(rows_h[0:1, T:2 * T], f2h[0:1, 0:T],
                                 rows_h[0:1, T:2 * T]),
          (rows_k, f2k), (rows_k,), tag="l3lvar")
    b.act(lambda s: s.activation(rows_h[0:1, T:2 * T], rows_h[0:1, T:2 * T],
                                 AF.Ln, bias=eps5[0:1, :]),
          (rows_k, "cpool"), (rows_k,), tag="l3lln")
    b.act(lambda s: s.activation(rows_h[0:1, T:2 * T], rows_h[0:1, T:2 * T],
                                 AF.Exp, scale=-0.5), (rows_k,), (rows_k,),
          tag="l3lexp")
    for t in range(NT):
        bcast_row(b, f"l3bm{t}", (rows_h, rows_k, slice(t * TT, (t + 1) * TT)),
                  P.pb[2], (P.oner[0], P.oner[1]))
        bcast_row(b, f"l3br{t}", (rows_h, rows_k, slice(T + t * TT, T + (t + 1) * TT)),
                  P.pb[3], (P.oner[0], P.oner[1]))
        for k in range(KC):
            xh, xk = P.xn[k]
            dh, dk = P.bb[k]
            b.dve(lambda v, xh=xh, t=t: v.tensor_sub(
                f2h[:, t * TT:(t + 1) * TT], xh[:, t * TT:(t + 1) * TT],
                P.pb[2][0][:, :TT]),
                (xk, P.pb[2][1]), (f2k,), tag=f"l3ls{k}_{t}")
            b.dve(lambda v, dh=dh, k=k, t=t: v.scalar_tensor_tensor(
                f2h[:, t * TT:(t + 1) * TT], f2h[:, t * TT:(t + 1) * TT],
                ln1w[0][:, k:k + 1], P.pb[3][0][:, :TT], ALU.mult, ALU.mult),
                (f2k, ln1w[1], P.pb[3][1]), (f2k,), tag=f"l3lm{k}_{t}")
            b.dve(lambda v, dh=dh, k=k, t=t: v.tensor_scalar_add(
                dh[:, t * TT:(t + 1) * TT], f2h[:, t * TT:(t + 1) * TT],
                ln1b[0][:, k:k + 1]),
                (f2k, ln1b[1]), (dk,), tag=f"l3lb{k}_{t}")
    xl = [P.bb[k] for k in range(KC)]

    def epi_qk(j, t, ph, pk):
        dh, dk = P.bb[8 + j]
        b.act(lambda s, ph=ph, dh=dh, t=t, j=j: s.activation(
            dh[:, t * TT:(t + 1) * TT], ph[:, :TT], AF.Silu,
            bias=bqk2[0][:, j:j + 1], scale=1.0),
            (pk, bqk2[1], "cpool"), (dk,), tag=f"l3qk{j}_{t}")
    mm_dense(b, "l3qk", ins["l3_wqkT"], D, 256, xl, P.pb[:6],
             [(P.wbA[0][:, s * 1024:(s + 1) * 1024], P.wbA[1] + f":{s}")
              for s in range(2)], epi_qk)

    sinT = P.fb[0]
    cosT = P.fb[1]
    b.dma("SP", sinT[0][:, :], ins["l3_sin"], (), (sinT[1],), tag="l3sin")
    b.dma("SP", cosT[0][:, :], ins["l3_cos"], (), (cosT[1],), tag="l3cos")
    gbs = P.small("l3gbs", [128, 4], F32)
    for c in range(2):
        b.dve(lambda v, c=c: v.tensor_scalar_mul(
            gbs[0][:, c:c + 1], gam[0][:, c:c + 1], 1.0 / 64),
            (gam[1],), (gbs[1],), tag=f"l3gs{c}")
        b.dve(lambda v, c=c: v.tensor_scalar_mul(
            gbs[0][:, 2 + c:3 + c], bet[0][:, c:c + 1], 1.0 / 64),
            (bet[1],), (gbs[1],), tag=f"l3bs{c}")

    def build_mat(i, dsts, gcols, bcols, gsrc, bsrc, pref):
        pre = []
        for c in range(2):
            qh, qkey = P.bb[8 + c]
            dh, dk = dsts[c]
            b.dve(lambda v, qh=qh, dh=dh, c=c: v.tensor_scalar(
                dh[:, :], qh[:, :], gsrc[0][:, gcols[c]:gcols[c] + 1],
                bsrc[0][:, bcols[c]:bcols[c] + 1], ALU.mult, ALU.add),
                (qkey, gsrc[1], bsrc[1]), (dk,), tag=f"{pref}p{c}")
        (d0h, d0k), (d1h, d1k) = dsts
        for th in range(2):
            cs = slice(th * 1024, (th + 1) * 1024)
            sA = f2h[:, 0:1024]
            sB = f2h[:, 1024:2048]
            b.dve(lambda v, cs=cs, sA=sA: v.tensor_mul(
                sA, d0h[:, cs], sinT[0][:, cs]),
                (d0k, sinT[1]), (f2k,), tag=f"{pref}r1_{th}")
            b.dve(lambda v, cs=cs, sB=sB: v.tensor_mul(
                sB, d0h[:, cs], cosT[0][:, cs]),
                (d0k, cosT[1]), (f2k,), tag=f"{pref}r2_{th}")
            b.dve(lambda v, cs=cs: v.tensor_mul(
                d0h[:, cs], d1h[:, cs], sinT[0][:, cs]),
                (d1k, sinT[1]), (d0k,), tag=f"{pref}r3_{th}")
            b.dve(lambda v, cs=cs, sB=sB: v.tensor_sub(
                d0h[:, cs], sB, d0h[:, cs]),
                (f2k,), (d0k,), tag=f"{pref}r4_{th}")
            b.dve(lambda v, cs=cs: v.tensor_mul(
                d1h[:, cs], d1h[:, cs], cosT[0][:, cs]),
                (d1k, cosT[1]), (d1k,), tag=f"{pref}r5_{th}")
            b.dve(lambda v, cs=cs, sA=sA: v.tensor_add(
                d1h[:, cs], d1h[:, cs], sA),
                (f2k,), (d1k,), tag=f"{pref}r6_{th}")

    build_mat(0, [P.bb[10], P.bb[11]], [0, 1], [0, 1], gbs, gbs, "l3qq")
    build_mat(1, [P.bb[14], P.bb[15]], [2, 3], [2, 3], gam, bet, "l3lq")
    build_mat(2, [P.bb[12], P.bb[13]], [4, 5], [4, 5], gam, bet, "l3qkk")
    build_mat(3, [P.bb[8], P.bb[9]], [6, 7], [6, 7], gam, bet, "l3lk")

    att = (fbi_bf(P.fb[2][0], 0, T), P.fb[2][1])
    for g in range(32):
        ph, pk = P.pb[g % 2]
        ob = (g % 2) * 64
        for c in range(2):
            b.pe(lambda e, ph=ph, g=g, c=c, ob=ob: e.matmul(
                ph[ob:ob + 64, 0:64], P.bb[12 + c][0][:, g * 64:(g + 1) * 64],
                P.bb[10 + c][0][:, g * 64:(g + 1) * 64],
                start=(c == 0), stop=False, tile_position=(0, ob)),
                (P.bb[12 + c][1], P.bb[10 + c][1]), (pk,), tag=f"l3sc{g}_{c}")
        b.pe(lambda e, ph=ph, ob=ob: e.matmul(
            ph[ob:ob + 64, 0:64], P.idb[0][0:64, 0:64], b64[0][:, :],
            start=False, stop=True, tile_position=(0, ob)),
            (P.idb[1], b64[1]), (pk,), tag=f"l3scb{g}")
        sh, sk = P.sq
        b.dve(lambda v, ph=ph, ob=ob: v.tensor_scalar_max(
            sh[ob:ob + 64, 0:64], ph[ob:ob + 64, 0:64], 0.0),
              (pk,), (sk,), tag=f"l3rl{g}")
        ah, ak = att
        b.act(lambda s, g=g, ah=ah, sh=sh, ob=ob: s.activation(
            ah[ob:ob + 64, g * 64:(g + 1) * 64], sh[ob:ob + 64, 0:64],
            AF.Square), (sk,), (ak,), tag=f"l3r2{g}")

    lkt = [(fbi_bf(P.fb[0][0], 0, T), P.fb[0][1]),
           (fbi_bf(P.fb[0][0], T, 2 * T), P.fb[0][1])]
    for c in range(2):
        for ti in range(16):
            ph, pk = P.pb[6]
            b.pe(lambda e, c=c, ti=ti, ph=ph: e.transpose(
                ph[:, :].bitcast(BF16)[:, 0:128],
                P.bb[8 + c][0][:, ti * 128:(ti + 1) * 128], P.idb[0][:, :]),
                (P.bb[8 + c][1], P.idb[1]), (pk,), tag=f"l3lt{c}_{ti}")
            dh, dk = lkt[c]
            b.act(lambda s, dh=dh, ti=ti, ph=ph: s.activation(
                dh[:, ti * 128:(ti + 1) * 128],
                ph[:, :].bitcast(BF16)[:, 0:128], AF.Copy),
                (pk,), (dk,), tag=f"l3lte{c}_{ti}")

    b.dma("SP", rows_h[0:1, 0:T], ins["l3_bv_row"], (), (rows_k,),
          tag="l3bvr")
    linkv = [P.bb[8], P.bb[9]]
    wv_r = ins["l3_wvT"].rearrange("(k p) n -> p k n", p=128)
    vth, vtk = P.bb[13][0][:, 0:TT], P.bb[13][1]
    for vb in range(4):
        s0 = (vb % 2) * 4
        wh = P.wfull[0][:, s0 * 1024:s0 * 1024 + 4096]
        wks = tuple(P.wfull[1] + f":{s0 + i}" for i in range(4))
        wk = wks[0]
        b.dma("POOL", wh.rearrange("p (k c) -> p k c", c=TT),
              wv_r[:, :, vb * TT:(vb + 1) * TT], (), wks, tag=f"l3wv{vb}")
        for ti in range(16):
            ps, psk = P.pb[2 + (ti % 2)]
            for k in range(KC):
                b.pe(lambda e, ps=ps, wh=wh, k=k, ti=ti: e.matmul(
                    ps[:, :TT], xl[k][0][:, ti * 128:(ti + 1) * 128],
                    wh[:, k * TT:(k + 1) * TT], start=(k == 0), stop=False),
                    (xl[k][1], wks[k // 2]), (psk,), tag=f"l3v{vb}_{ti}_{k}")
            b.pe(lambda e, ps=ps, vb=vb: e.matmul(
                ps[:, :TT], P.oner[0][0:1, :],
                rows_h[0:1, vb * TT:(vb + 1) * TT],
                start=False, stop=True),
                (P.oner[1], rows_k), (psk,), tag=f"l3vb{vb}_{ti}")
            b.act(lambda s, ps=ps, vth=vth: s.activation(
                vth[:, :], ps[:, :TT], AF.Silu), (psk,), (vtk,),
                tag=f"l3ve{vb}_{ti}")
            for c in range(2):
                ps2, psk2 = P.pb[4 + c]
                b.pe(lambda e, ps2=ps2, c=c, ti=ti, vth=vth: e.matmul(
                    ps2[:, :TT], lkt[c][0][:, ti * 128:(ti + 1) * 128],
                    vth[:, :], start=(ti == 0), stop=(ti == 15)),
                    (lkt[c][1], vtk), (psk2,), tag=f"l3kv{vb}_{ti}_{c}")
            b.dma("POOL", P.ins["vscr"][ti * 128:(ti + 1) * 128,
                                         vb * TT:(vb + 1) * TT],
                  vth[:, :], (vtk,), (f"vscr_{vb}_{ti}",), tag=f"l3vs{vb}_{ti}")
        for c in range(2):
            ps2, psk2 = P.pb[4 + c]
            dh, dk = linkv[c]
            b.dve(lambda v, ps2=ps2, dh=dh, vb=vb: v.tensor_scalar_mul(
                dh[:, vb * TT:(vb + 1) * TT], ps2[:, :TT], 1.0 / T),
                (psk2,), (dk,), tag=f"l3kve{vb}_{c}")


    wg_r = ins["l3_wgT"].rearrange("(k p) n -> p k n", p=128)
    gt = P.bb[10]
    oc = P.bb[11]
    vte = P.bb[12]
    vscr_keys = tuple(f"vscr_{vb}_{ti}" for vb in range(4) for ti in range(16))
    for ec in range(16):
        wh, wk = ((P.wbA[0][:, 0:1024], P.wbA[1] + ":0") if ec % 2 == 0 else
                  (P.wbA[0][:, 1024:2048], P.wbA[1] + ":1"))
        b.dma("POOL", wh.rearrange("p (k c) -> p k c", c=128),
              wg_r[:, :, ec * 128:(ec + 1) * 128], (), (wk,),
              tag=f"l3wg{ec}")
        for t in range(NT):
            ps, psk = P.pb[t % 2]
            for k in range(KC):
                b.pe(lambda e, ps=ps, wh=wh, k=k, t=t: e.matmul(
                    ps[:, :TT], wh[:, k * 128:(k + 1) * 128],
                    xl[k][0][:, t * TT:(t + 1) * TT],
                    start=(k == 0), stop=(k == KC - 1)),
                    (wk, xl[k][1]), (psk,), tag=f"l3g{ec}_{t}_{k}")
            b.act(lambda s, ps=ps, t=t, ec=ec: s.activation(
                gt[0][:, t * TT:(t + 1) * TT], ps[:, :TT], AF.Silu,
                bias=bg2[0][:, ec:ec + 1], scale=1.0),
                (psk, bg2[1], "cpool"), (gt[1],), tag=f"l3ge{ec}_{t}")
        b.dma("POOL", vte[0][:, :],
              P.ins["vscr"].rearrange("(ti p) c -> p ti c", p=128)
              [:, :, ec * 128:(ec + 1) * 128],
              vscr_keys, (f"vte{ec}",), tag=f"l3vl{ec}")
        ah, ak = att
        for g in range(32):
            ps, psk = P.pb[2 + (g % 2)]
            ob = (g % 2) * 64
            b.pe(lambda e, ps=ps, g=g, ob=ob: e.matmul(
                ps[:, 0:64],
                vte[0][ob:ob + 64, (g // 2) * 128:(g // 2) * 128 + 128],
                ah[ob:ob + 64, g * 64:(g + 1) * 64],
                start=True, stop=False, tile_position=(ob, 0)),
                ("vte" + str(ec) if False else f"vte{ec}", ak), (psk,),
                tag=f"l3q{ec}_{g}")
            for c in range(2):
                dh, dk = linkv[c]
                b.pe(lambda e, ps=ps, dh=dh, c=c, g=g, ec=ec: e.matmul(
                    ps[:, 0:64], dh[:, ec * 128:(ec + 1) * 128],
                    P.bb[14 + c][0][:, g * 64:(g + 1) * 64],
                    start=False, stop=(c == 1)),
                    (dk, P.bb[14 + c][1]), (psk,), tag=f"l3l{ec}_{g}_{c}")
            b.dve(lambda v, ps=ps, g=g: v.tensor_mul(
                oc[0][:, g * 64:(g + 1) * 64], ps[:, 0:64],
                gt[0][:, g * 64:(g + 1) * 64]),
                (psk, gt[1]), (oc[1],), tag=f"l3o{ec}_{g}")
        b.dma("POOL", P.ins["outscr"][ec * 128:(ec + 1) * 128, :],
              oc[0][:, :], (oc[1],), (f"outscr{ec}",), tag=f"l3os{ec}")

    wo_r = ins["l3_woT"].rearrange("(k p) n -> p k n", p=128)
    spre = [P.bb[10 + i] for i in range(6)] +            [(fbi_bf(P.fb[0][0], 0, T), P.fb[0][1]),
            (fbi_bf(P.fb[0][0], T, 2 * T), P.fb[0][1])]
    rhrot = [P.bb[8], P.bb[9]]
    for jp in range(4):
        wks = []
        for jj in range(2):
            j = jp * 2 + jj
            s0 = jj * 4 if jp % 2 == 0 else 4 * jj
            s0 = jj * 4
            whj = P.wfull[0][:, s0 * 1024:s0 * 1024 + 2048]
            wkj = tuple(P.wfull[1] + f":{s0 + i}" for i in range(2))
            b.dma("POOL", whj.rearrange("p (k c) -> p k c", c=128),
                  wo_r[:, :, j * 128:(j + 1) * 128], (), wkj,
                  tag=f"l3wo{j}")
            wks.append((whj, wkj))
        for k in range(16):
            rh, rk = rhrot[k % 2]
            b.dma("POOL", rh[:, :],
                  P.ins["outscr"][k * 128:(k + 1) * 128, :],
                  (f"outscr{k}",), (rk,), tag=f"l3or{jp}_{k}")
            for jj in range(2):
                whj, wkj = wks[jj]
                for t in range(NT):
                    ps, psk = P.pb[jj * 4 + t]
                    b.pe(lambda e, ps=ps, whj=whj, k=k, t=t, rh=rh: e.matmul(
                        ps[:, :TT], whj[:, k * 128:(k + 1) * 128],
                        rh[:, t * TT:(t + 1) * TT],
                        start=(k == 0), stop=(k == 15)),
                        (wkj[k // 8], rk), (psk,), tag=f"l3wm{jp}_{jj}_{k}_{t}")
        for jj in range(2):
            j = jp * 2 + jj
            dh, dk = spre[j]
            for t in range(NT):
                ps, psk = P.pb[jj * 4 + t]
                b.act(lambda s, ps=ps, dh=dh, t=t, j=j: s.activation(
                    dh[:, t * TT:(t + 1) * TT], ps[:, :TT], AF.Identity,
                    bias=bo2[0][:, j:j + 1], scale=1.0),
                    (psk, bo2[1], "cpool"), (dk,), tag=f"l3sp{j}_{t}")

    for t in range(NT):
        for k in range(KC):
            sh_, sk_ = spre[k]
            b.pe(lambda e, sh_=sh_, k=k, t=t: e.matmul(
                P.pb[0][0][0:1, :TT], P.onecb[0][:, 0:1],
                sh_[:, t * TT:(t + 1) * TT],
                start=(k == 0), stop=(k == KC - 1)),
                (P.onecb[1], sk_), (P.pb[0][1],), tag=f"l3n2x{k}_{t}")
            sqh, sqk = P.sq
            b.act(lambda s, sh_=sh_, sqh=sqh, t=t: s.activation(
                sqh[:, :TT], sh_[:, t * TT:(t + 1) * TT], AF.Square),
                (sk_,), (sqk,), tag=f"l3n2q{k}_{t}")
            b.pe(lambda e, sqh=sqh, k=k: e.matmul(
                P.pb[1][0][0:1, :TT], P.onec[0][:, 0:1], sqh[:, :TT],
                start=(k == 0), stop=(k == KC - 1)),
                (P.onec[1], sqk), (P.pb[1][1],), tag=f"l3n2q2{k}_{t}")
        b.dve(lambda v, t=t: v.tensor_scalar_mul(
            rows_h[0:1, t * TT:(t + 1) * TT], P.pb[0][0][0:1, :TT], 1.0 / D),
            (P.pb[0][1],), (rows_k,), tag=f"l3n2mu{t}")
        b.dve(lambda v, t=t: v.tensor_scalar_mul(
            f2h[0:1, t * TT:(t + 1) * TT], P.pb[1][0][0:1, :TT], 1.0 / D),
            (P.pb[1][1],), (f2k,), tag=f"l3n2ms{t}")
    b.act(lambda s: s.activation(rows_h[0:1, T:2 * T], rows_h[0:1, 0:T],
                                 AF.Square), (rows_k,), (rows_k,), tag="l3n2m2")
    b.dve(lambda v: v.tensor_sub(rows_h[0:1, T:2 * T], f2h[0:1, 0:T],
                                 rows_h[0:1, T:2 * T]),
          (rows_k, f2k), (rows_k,), tag="l3n2var")
    b.act(lambda s: s.activation(rows_h[0:1, T:2 * T], rows_h[0:1, T:2 * T],
                                 AF.Ln, bias=eps5[0:1, :]),
          (rows_k, "cpool"), (rows_k,), tag="l3n2ln")
    b.act(lambda s: s.activation(rows_h[0:1, T:2 * T], rows_h[0:1, T:2 * T],
                                 AF.Exp, scale=-0.5), (rows_k,), (rows_k,),
          tag="l3n2exp")
    f1h, f1k = P.fb[1]
    for t in range(NT):
        bcast_row(b, f"l3cbm{t}", (rows_h, rows_k, slice(t * TT, (t + 1) * TT)),
                  P.pb[2], (P.oner[0], P.oner[1]))
        bcast_row(b, f"l3cbr{t}", (rows_h, rows_k,
                                   slice(T + t * TT, T + (t + 1) * TT)),
                  P.pb[3], (P.oner[0], P.oner[1]))
        for j in range(KC):
            sh_, sk_ = spre[j]
            b.dma("SP", f1h[:, t * TT:(t + 1) * TT] if False else
                  f1h[:, 0:1], f1h[:, 0:1], (), (), tag="nop") if False else None
            b.dve(lambda v, sh_=sh_, t=t: v.tensor_sub(
                f2h[:, t * TT:(t + 1) * TT], sh_[:, t * TT:(t + 1) * TT],
                P.pb[2][0][:, :TT]),
                (sk_, P.pb[2][1]), (f2k,), tag=f"l3c1_{j}_{t}")
            b.dve(lambda v, j=j, t=t: v.scalar_tensor_tensor(
                f2h[:, t * TT:(t + 1) * TT], f2h[:, t * TT:(t + 1) * TT],
                ln2w[0][:, j:j + 1], P.pb[3][0][:, :TT], ALU.mult, ALU.mult),
                (f2k, ln2w[1], P.pb[3][1]), (f2k,), tag=f"l3c2_{j}_{t}")
            b.dve(lambda v, j=j, t=t: v.tensor_scalar_add(
                f2h[:, t * TT:(t + 1) * TT], f2h[:, t * TT:(t + 1) * TT],
                ln2b[0][:, j:j + 1]),
                (f2k, ln2b[1]), (f2k,), tag=f"l3c3_{j}_{t}")
            b.dve(lambda v, j=j, t=t: v.tensor_mul(
                f2h[:, t * TT:(t + 1) * TT], f2h[:, t * TT:(t + 1) * TT],
                P.gb[j][0][:, t * TT:(t + 1) * TT]),
                (f2k, P.gb[j][1]), (f2k,), tag=f"l3c4_{j}_{t}")
            b.dma("SP", f1h[:, t * TT:(t + 1) * TT],
                  xsrc[j * 128:(j + 1) * 128, t * TT:(t + 1) * TT],
                  (() if src_keys is None else (src_keys[j],)), (f1k,),
                  tag=f"l3cx{j}_{t}")
            b.dve(lambda v, t=t: v.tensor_add(
                f2h[:, t * TT:(t + 1) * TT], f2h[:, t * TT:(t + 1) * TT],
                f1h[:, t * TT:(t + 1) * TT]),
                (f2k, f1k), (f2k,), tag=f"l3c5_{j}_{t}")
            b.dma("SP", xdst[j * 128:(j + 1) * 128, t * TT:(t + 1) * TT],
                  f2h[:, t * TT:(t + 1) * TT], (f2k,),
                  (dst_keys[j] + f"_{t}" if False else dst_keys[j],),
                  tag=f"l3cw{j}_{t}")


def prep_l3(ins, d):
    C = np.ascontiguousarray
    d["l3_router_w"] = C(np.asarray(ins["L3_ffn_router_w"]).T)
    d["l3_up"] = np.asarray(ins["L3_ffn_up_proj"])
    d["l3_dn"] = np.asarray(ins["L3_ffn_down_proj"])
    d["l3_ln1_w2d"] = _col2d(np.asarray(ins["L3_attn_ln1_w"]))
    d["l3_ln1_b2d"] = _col2d(np.asarray(ins["L3_attn_ln1_b"]))
    d["l3_ln2_w2d"] = _col2d(np.asarray(ins["L3_attn_ln2_w"]))
    d["l3_ln2_b2d"] = _col2d(np.asarray(ins["L3_attn_ln2_b"]))
    d["l3_wqkT"] = C(np.asarray(ins["L3_attn_Wqk"]).T)
    d["l3_bqk2d"] = C(np.asarray(ins["L3_attn_bqk"]).reshape(2, 128).T)
    d["l3_wvT"] = C(np.asarray(ins["L3_attn_Wv"]).T)
    d["l3_bv_row"] = C(np.asarray(ins["L3_attn_bv"]).reshape(1, 2048))
    d["l3_wgT"] = C(np.asarray(ins["L3_attn_Wg"]).T)
    d["l3_bg2d"] = C(np.asarray(ins["L3_attn_bg"]).reshape(16, 128).T)
    d["l3_woT"] = C(np.asarray(ins["L3_attn_Wo"]).T)
    d["l3_bo2d"] = _col2d(np.asarray(ins["L3_attn_bo"]))
    gm = np.asarray(ins["L3_attn_gamma"]).reshape(4, 2, 128)
    bt = np.asarray(ins["L3_attn_beta"]).reshape(4, 2, 128)
    d["l3_gamma2d"] = C(gm.transpose(2, 0, 1).reshape(128, 8))
    d["l3_beta2d"] = C(bt.transpose(2, 0, 1).reshape(128, 8))
    idx = np.arange(64)
    dist = np.clip(np.abs(idx[:, None] - idx[None, :]), 0, 32)
    d["l3_bias64"] = C(np.asarray(ins["L3_attn_pos_emb"])[dist].astype(np.float32))
    pos = np.arange(T, dtype=np.float64)
    invf = 10000.0 ** (-np.arange(128, dtype=np.float64) / 128)
    ang = invf[:, None] * pos[None, :]
    d["l3_sin"] = C(np.sin(ang).astype(np.float32))
    d["l3_cos"] = C(np.cos(ang).astype(np.float32))


L3_DEVINS = [
    ("l3_router_w", [1024, 8]), ("l3_up", [8, 1024, 2048]),
    ("l3_dn", [8, 1024, 1024]),
    ("l3_ln1_w2d", [128, 8]), ("l3_ln1_b2d", [128, 8]),
    ("l3_ln2_w2d", [128, 8]), ("l3_ln2_b2d", [128, 8]),
    ("l3_wqkT", [1024, 256]), ("l3_bqk2d", [128, 2]),
    ("l3_wvT", [1024, 2048]), ("l3_bv_row", [1, 2048]),
    ("l3_wgT", [1024, 2048]), ("l3_bg2d", [128, 16]),
    ("l3_woT", [2048, 1024]), ("l3_bo2d", [128, 8]),
    ("l3_gamma2d", [128, 8]), ("l3_beta2d", [128, 8]),
    ("l3_bias64", [64, 64]), ("l3_sin", [128, 2048]), ("l3_cos", [128, 2048]),
]
LAYER_DEVINS[3] = L3_DEVINS
LAYER_PREP[3] = prep_l3
LAYER_EMIT[3] = emit_layer3


def emit_layer0(P, xsrc, xdst, src_keys=None, dst_keys=None):
    b = P.b
    ins = P.ins
    emit_outer_ln(P, xsrc, "l0o", src_keys)

    n1w = load_smallvec(P, "l0n1w", "l0_n1w2d", (128, 8))
    n2w = load_smallvec(P, "l0n2w", "l0_n2w2d", (128, 8))
    qn2 = load_smallvec(P, "l0qn", "l0_qn2d", (128, 1))
    kn2 = load_smallvec(P, "l0kn", "l0_kn2d", (128, 1))
    EPS0 = 1.1920929e-07
    eps0c = b.const(EPS0)
    one_c = b.const(1.0)
    h1 = [P.bb[k] for k in range(8)]
    emit_rms(P, "l0n1", P.xn, h1, n1w, EPS0)

    rows_h, rows_k = P.rows
    wq_r = ins["l0_wqT"].rearrange("(k p) n -> p k n", p=128)
    wk_r = ins["l0_wkT"].rearrange("(k p) n -> p k n", p=128)
    wv_r = ins["l0_wvT"].rearrange("(k p) n -> p k n", p=128)
    wqsl = [(P.wbA[0][:, s * 1024:(s + 1) * 1024], P.wbA[1] + f":{s}")
            for s in range(2)]
    wksl = [(P.wbB[0][:, s * 1024:(s + 1) * 1024], P.wbB[1] + f":{s}")
            for s in range(2)]
    wvsl = [(P.wfull[0][:, s * 1024:(s + 1) * 1024], P.wfull[1] + f":{s}")
            for s in range(8)]
    ytiles = [(fbi_bf(P.fb[0][0], 0, T), P.fb[0][1]),
              (fbi_bf(P.fb[0][0], T, 2 * T), P.fb[0][1]),
              (fbi_bf(P.fb[1][0], 0, T), P.fb[1][1]),
              (fbi_bf(P.fb[1][0], T, 2 * T), P.fb[1][1]),
              (fbi_bf(P.fb[2][0], 0, T), P.fb[2][1]),
              (fbi_bf(P.fb[2][0], T, 2 * T), P.fb[2][1]),
              P.bb[13], P.bb[14]]

    def dma_qk(j):
        wh, wk = wqsl[j % 2]
        b.dma("POOL", wh.rearrange("p (k c) -> p k c", c=128),
              wq_r[:, :, j * 128:(j + 1) * 128], (), (wk,), tag=f"l0wq{j}")
        wh2, wk2 = wksl[j % 2]
        b.dma("POOL", wh2.rearrange("p (k c) -> p k c", c=128),
              wk_r[:, :, j * 128:(j + 1) * 128], (), (wk2,), tag=f"l0wk{j}")
        wh3, wk3 = wvsl[j % 8]
        b.dma("POOL", wh3.rearrange("p (k c) -> p k c", c=128),
              wv_r[:, :, j * 128:(j + 1) * 128], (), (wk3,), tag=f"l0wv{j}")
    dma_qk(0)
    for j in range(8):
        if j + 1 < 8:
            dma_qk(j + 1)
        for (slots, dstt, nw2) in ((wqsl, P.bb[8], qn2), (wksl, P.bb[9], kn2)):
            wh, wk = slots[j % 2]
            dh, dk = dstt
            for t in range(NT):
                ps, psk = P.pb[t]
                for k in range(KC):
                    b.pe(lambda e, ps=ps, wh=wh, k=k, t=t: e.matmul(
                        ps[:, :TT], wh[:, k * 128:(k + 1) * 128],
                        h1[k][0][:, t * TT:(t + 1) * TT],
                        start=(k == 0), stop=(k == KC - 1)),
                        (wk, h1[k][1]), (psk,), tag=f"l0qk{j}_{t}_{k}")
                sh, sk = P.sq
                b.act(lambda s, ps=ps, sh=sh: s.activation(
                    sh[:, 0:TT], ps[:, :TT], AF.Square),
                    (psk,), (sk,), tag=f"l0qs{j}_{t}")
                b.act(lambda s, ps=ps, dh=dh, t=t: s.activation(
                    dh[:, t * TT:(t + 1) * TT], ps[:, :TT], AF.Copy),
                    (psk,), (dk,), tag=f"l0qcp{j}_{t}")
                for hh in range(2):
                    bs = hh * 64
                    pr, prk = P.pb[6 if hh == 0 else 7]
                    b.pe(lambda e, pr=pr, sh=sh, bs=bs: e.matmul(
                        pr[0:1, :TT], P.onec[0][bs:bs + 64, 0:1],
                        sh[bs:bs + 64, 0:TT], start=True, stop=True,
                        tile_position=(bs, 0)),
                        (P.onec[1], sk), (prk,), tag=f"l0qr{j}_{t}_{hh}")
                    b.dve(lambda v, pr=pr, t=t, hh=hh: v.tensor_scalar_mul(
                        rows_h[0:1, hh * T + t * TT:hh * T + (t + 1) * TT],
                        pr[0:1, :TT], 1.0 / 64),
                        (prk,), (rows_k,), tag=f"l0qm{j}_{t}_{hh}")
            for hh in range(2):
                b.act(lambda s, hh=hh: s.activation(
                    rows_h[0:1, hh * T:(hh + 1) * T],
                    rows_h[0:1, hh * T:(hh + 1) * T], AF.Ln,
                    bias=eps0c[0:1, :]),
                    (rows_k, "cpool"), (rows_k,), tag=f"l0ql{j}_{hh}")
                b.act(lambda s, hh=hh: s.activation(
                    rows_h[0:1, hh * T:(hh + 1) * T],
                    rows_h[0:1, hh * T:(hh + 1) * T], AF.Exp, scale=-0.5),
                    (rows_k,), (rows_k,), tag=f"l0qe{j}_{hh}")
            for t in range(NT):
                bcast_row(b, f"l0qb{j}_{t}_0",
                          (rows_h, rows_k, slice(t * TT, (t + 1) * TT)),
                          P.pb[4], (P.oner[0], P.oner[1]))
                bcast_row(b, f"l0qb{j}_{t}_1",
                          (rows_h, rows_k, slice(T + t * TT, T + (t + 1) * TT)),
                          P.pb[5], (P.oner[0], P.oner[1]))
                for hh in range(2):
                    bs = hh * 64
                    pbb = P.pb[4 + hh]
                    b.dve(lambda v, dh=dh, t=t, bs=bs, pbb=pbb, nw2=nw2:
                          v.scalar_tensor_tensor(
                              dh[bs:bs + 64, t * TT:(t + 1) * TT],
                              dh[bs:bs + 64, t * TT:(t + 1) * TT],
                              nw2[0][bs:bs + 64, 0:1],
                              pbb[0][bs:bs + 64, :TT], ALU.mult, ALU.mult),
                          (dk, nw2[1], pbb[1]), (dk,),
                          tag=f"l0qn{j}_{t}_{hh}")
        wh3, wk3 = wvsl[j % 8]
        for ti in range(16):
            ps, psk = P.pb[2 + (ti % 2)]
            for k in range(KC):
                b.pe(lambda e, ps=ps, wh3=wh3, k=k, ti=ti: e.matmul(
                    ps[:, 0:128], h1[k][0][:, ti * 128:(ti + 1) * 128],
                    wh3[:, k * 128:(k + 1) * 128],
                    start=(k == 0), stop=(k == KC - 1)),
                    (h1[k][1], wk3), (psk,), tag=f"l0v{j}_{ti}_{k}")
            b.act(lambda s, ps=ps, ti=ti: s.activation(
                P.bb[10][0][:, ti * 128:(ti + 1) * 128], ps[:, 0:128],
                AF.Copy), (psk,), (P.bb[10][1],), tag=f"l0ve{j}_{ti}")

        def qspec(hh):
            return [(P.bb[8][0][hh * 64:(hh + 1) * 64, :], P.bb[8][1],
                     hh * 64, 64)]

        def kspec(hh):
            return [(P.bb[9][0][hh * 64:(hh + 1) * 64, :], P.bb[9][1],
                     hh * 64, 64)]

        def out_cb(hh, qi, pv, j=j):
            pvh, pvk = pv
            yh, yk = ytiles[j]
            b.dve(lambda v, pvh=pvh, yh=yh, hh=hh, qi=qi: v.tensor_copy(
                yh[hh * 64:(hh + 1) * 64, qi * 128:(qi + 1) * 128],
                pvh[hh * 64:(hh + 1) * 64, 0:128]),
                (pvk,), (yk,), tag=f"l0y{j}_{hh}_{qi}")

        emit_attn(P, f"l0a{j}", qspec, kspec,
                  (P.bb[10][0], P.bb[10][1]), 128, lambda hh: hh * 64,
                  out_cb, 0.125, (P.bb[11][0], P.bb[11][1]),
                  (P.bb[12][0], P.bb[12][1]), window=False, sink=None)

    def epi_wo(j, t, ph, pk):
        dh, dk = P.bb[j]
        b.dve(lambda v, ph=ph, dh=dh, t=t, j=j: v.tensor_add(
            dh[:, t * TT:(t + 1) * TT], ph[:, :TT],
            P.xn[j][0][:, t * TT:(t + 1) * TT]),
            (pk, P.xn[j][1]), (dk,), tag=f"l0x2_{j}_{t}")
    mm_dense(b, "l0wo", ins["l0_woT"], D, D, ytiles, P.pb[:6],
             [(P.wbA[0][:, s * 1024:(s + 1) * 1024], P.wbA[1] + f":{s}")
              for s in range(2)], epi_wo, js=list(range(2, 8)))
    for j in range(2):
        dh, dk = P.bb[j]
        b.act(lambda s, dh=dh, j=j: s.activation(
            dh[:, :], P.xn[j][0][:, :], AF.Copy),
            (P.xn[j][1],), (dk,), tag=f"l0x2c{j}")
    x2 = [P.bb[j] for j in range(8)]
    for j in range(8):
        b.dma("SP", P.ins["x2scr"][j * 128:(j + 1) * 128, :], x2[j][0][:, :],
              (x2[j][1],), (f"x2s{j}",), tag=f"l0x2s{j}")
    xf = [P.bb[8 + j] for j in range(8)]
    emit_rms(P, "l0n2", x2, xf, n2w, EPS0)

    def epi_g0(j, t, ph, pk):
        b.dve(lambda v, ph=ph, t=t: v.tensor_copy(
            P.fb[2][0][0:4, t * TT:(t + 1) * TT], ph[0:4, :TT]),
            (pk,), (P.fb[2][1],), tag=f"l0gl{t}")
    mm_dense(b, "l0g", ins["l0_gateT"], D, 4, xf, P.pb[:6],
             [(P.wbB[0][:, 0:32], P.wbB[1] + ":0")], epi_g0, blk_cols=4)
    emit_topk_probs(P, "l0tk", P.fb[2], P.fb[2], 4, 2, eps=1e-8)

    cebc_h, cebc_k = P.fb[1]
    first = [True]

    def load_ce(e):
        b.dma("SP", rows_h[0:1, 0:T], P.ins["cescr"][e:e + 1, :],
              ("cescr",), (rows_k,), tag=f"l0ce{e}_{np.random.randint(1 << 30)}")
        for t in range(NT):
            bcast_row(b, f"l0cb{e}_{t}_{len(b.q['PE'])}",
                      (rows_h, rows_k, slice(t * TT, (t + 1) * TT)),
                      P.pb[6], (P.oner[0], P.oner[1]))
            b.act(lambda s, t=t: s.activation(
                cebc_h[:, t * TT:(t + 1) * TT], P.pb[6][0][:, :TT], AF.Copy),
                (P.pb[6][1],), (cebc_k,), tag=f"l0cbe{e}_{t}_{len(b.q['ACT'])}")

    def up_pass(pref, w_hbm, din, srcs, dst, act, ce, extra_mul=None,
                js=None, colbase=0):
        w_r = w_hbm.rearrange("(k p) n -> p k n", p=128)
        kcd = din // 128
        jlist = list(range(8)) if js is None else js
        slots = [(P.wbA[0][:, s * 1024:(s + 1) * 1024], P.wbA[1] + f":{s}")
                 for s in range(2)]

        def dmab(ji):
            wh, wk = slots[ji % 2]
            j = jlist[ji]
            b.dma("POOL", wh.rearrange("p (k c) -> p k c", c=128),
                  w_r[:, :, colbase + j * 128:colbase + (j + 1) * 128],
                  (), (wk,), tag=f"{pref}w{j}")
        for ji in range(min(2, len(jlist))):
            dmab(ji)
        for ji, j in enumerate(jlist):
            wh, wk = slots[ji % 2]
            for t in range(NT):
                ps, psk = P.pb[(ji * NT + t) % 6]
                for k in range(kcd):
                    b.pe(lambda e2, ps=ps, wh=wh, k=k, t=t: e2.matmul(
                        ps[:, :TT], wh[:, k * 128:(k + 1) * 128],
                        srcs[k][0][:, t * TT:(t + 1) * TT],
                        start=(k == 0), stop=(k == kcd - 1)),
                        (wk, srcs[k][1]), (psk,), tag=f"{pref}m{j}_{t}_{k}")
                dh, dk = dst[ji]
                if act is not None:
                    b.act(lambda s, ps=ps, dh=dh, t=t, act=act: s.activation(
                        dh[:, t * TT:(t + 1) * TT], ps[:, :TT], act),
                        (psk,), (dk,), tag=f"{pref}a{j}_{t}")
                else:
                    b.dve(lambda v, ps=ps, dh=dh, t=t: v.tensor_copy(
                        dh[:, t * TT:(t + 1) * TT], ps[:, :TT]),
                        (psk,), (dk,), tag=f"{pref}a{j}_{t}")
                if extra_mul is not None:
                    eh, ek = extra_mul[ji]
                    b.dve(lambda v, dh=dh, eh=eh, t=t: v.tensor_mul(
                        dh[:, t * TT:(t + 1) * TT], dh[:, t * TT:(t + 1) * TT],
                        eh[:, t * TT:(t + 1) * TT]),
                        (dk, ek), (dk,), tag=f"{pref}x{j}_{t}")
                if ce:
                    b.dve(lambda v, dh=dh, t=t: v.tensor_mul(
                        dh[:, t * TT:(t + 1) * TT], dh[:, t * TT:(t + 1) * TT],
                        cebc_h[:, t * TT:(t + 1) * TT]),
                        (dk, cebc_k), (dk,), tag=f"{pref}c{j}_{t}")
            if ji + 2 < len(jlist):
                dmab(ji + 2)

    def down_pass(pref, w_hbm, din, hidden, colbase=0):
        w_r = w_hbm.rearrange("(k p) n -> p k n", p=128)
        kcd = din // 128
        slots = [(P.wbB[0][:, s * 1024:(s + 1) * 1024], P.wbB[1] + f":{s}")
                 for s in range(2)]

        def dmab(ji, j):
            wh, wk = slots[ji % 2]
            b.dma("POOL", wh.rearrange("p (k c) -> p k c", c=128),
                  w_r[:, :, j * 128:(j + 1) * 128], (), (wk,),
                  tag=f"{pref}w{j}")
        js = list(range(2, 8))
        for ji in range(min(2, 6)):
            dmab(ji, js[ji])
        for ji, j in enumerate(js):
            wh, wk = slots[ji % 2]
            for t in range(NT):
                ps, psk = P.pb[(ji * NT + t) % 6]
                for k in range(kcd):
                    b.pe(lambda e2, ps=ps, wh=wh, k=k, t=t: e2.matmul(
                        ps[:, :TT], wh[:, k * 128:(k + 1) * 128],
                        hidden[k][0][:, t * TT:(t + 1) * TT],
                        start=(k == 0), stop=(k == kcd - 1)),
                        (wk, hidden[k][1]), (psk,), tag=f"{pref}m{j}_{t}_{k}")
                gh, gk = P.gb[j]
                if first[0]:
                    b.act(lambda s, ps=ps, gh=gh, t=t: s.activation(
                        gh[:, t * TT:(t + 1) * TT], ps[:, :TT], AF.Copy),
                        (psk,), (gk,), tag=f"{pref}i{j}_{t}")
                else:
                    b.dve(lambda v, ps=ps, gh=gh, t=t: v.tensor_add(
                        gh[:, t * TT:(t + 1) * TT],
                        gh[:, t * TT:(t + 1) * TT], ps[:, :TT]),
                        (psk, gk), (gk,), tag=f"{pref}d{j}_{t}")
            if ji + 2 < 6:
                dmab(ji + 2, js[ji + 2])
        first[0] = False

    rot = [(fbi_bf(P.fb[0][0], 0, T), P.fb[0][1]),
           (fbi_bf(P.fb[0][0], T, 2 * T), P.fb[0][1]),
           (fbi_bf(P.fb[2][0], 0, T), P.fb[2][1]),
           (fbi_bf(P.fb[2][0], T, 2 * T), P.fb[2][1])]

    def mm_res(pref, w_hbm, srcs, js_out, act, ce, scrname, pair_hbm=None,
               colbase=0):
        w_r = w_hbm.rearrange("(k p) n -> p k n", p=128)
        kcd = len(srcs)
        slots = [(P.wbA[0][:, s * 1024:(s + 1) * 1024], P.wbA[1] + f":{s}")
                 for s in range(2)]
        slots2 = [(P.wbB[0][:, s * 1024:(s + 1) * 1024], P.wbB[1] + f":{s}")
                  for s in range(2)]
        if pair_hbm is not None:
            w2_r = pair_hbm.rearrange("(k p) n -> p k n", p=128)

        def dmab(ji):
            j = js_out[ji]
            wh, wk = slots[ji % 2]
            b.dma("POOL", wh.rearrange("p (k c) -> p k c", c=128),
                  w_r[:, :, colbase + j * 128:colbase + (j + 1) * 128],
                  (), (wk,), tag=f"{pref}w{j}")
            if pair_hbm is not None:
                wh2, wk2 = slots2[ji % 2]
                b.dma("POOL", wh2.rearrange("p (k c) -> p k c", c=128),
                      w2_r[:, :, colbase + j * 128:colbase + (j + 1) * 128],
                      (), (wk2,), tag=f"{pref}w2{j}")
        for ji in range(min(2, len(js_out))):
            dmab(ji)
        for ji, j in enumerate(js_out):
            wh, wk = slots[ji % 2]
            dh, dk = rot[ji % 2]
            for t in range(NT):
                ps, psk = P.pb[t % 2]
                for k in range(kcd):
                    b.pe(lambda e2, ps=ps, wh=wh, k=k, t=t: e2.matmul(
                        ps[:, :TT], wh[:, k * 128:(k + 1) * 128],
                        srcs[k][0][:, t * TT:(t + 1) * TT],
                        start=(k == 0), stop=(k == kcd - 1)),
                        (wk, srcs[k][1]), (psk,), tag=f"{pref}m{j}_{t}_{k}")
                if pair_hbm is not None:
                    wh2, wk2 = slots2[ji % 2]
                    ps2, psk2 = P.pb[2 + t % 2]
                    for k in range(kcd):
                        b.pe(lambda e2, ps2=ps2, wh2=wh2, k=k, t=t: e2.matmul(
                            ps2[:, :TT], wh2[:, k * 128:(k + 1) * 128],
                            srcs[k][0][:, t * TT:(t + 1) * TT],
                            start=(k == 0), stop=(k == kcd - 1)),
                            (wk2, srcs[k][1]), (psk2,),
                            tag=f"{pref}m2{j}_{t}_{k}")
                    sh, sk = P.sq
                    ss = (t % 2) * TT
                    b.act(lambda s, ps=ps, sh=sh, ss=ss: s.activation(
                        sh[:, ss:ss + TT], ps[:, :TT], AF.Silu),
                        (psk,), (sk,), tag=f"{pref}s{j}_{t}")
                    b.dve(lambda v, ps2=ps2, dh=dh, sh=sh, ss=ss, t=t:
                          v.tensor_mul(dh[:, t * TT:(t + 1) * TT],
                                       ps2[:, :TT], sh[:, ss:ss + TT]),
                          (psk2, sk), (dk,), tag=f"{pref}p{j}_{t}")
                elif act is not None:
                    b.act(lambda s, ps=ps, dh=dh, t=t, act=act: s.activation(
                        dh[:, t * TT:(t + 1) * TT], ps[:, :TT], act),
                        (psk,), (dk,), tag=f"{pref}a{j}_{t}")
                else:
                    b.dve(lambda v, ps=ps, dh=dh, t=t: v.tensor_copy(
                        dh[:, t * TT:(t + 1) * TT], ps[:, :TT]),
                        (psk,), (dk,), tag=f"{pref}a{j}_{t}")
                if ce:
                    b.dve(lambda v, dh=dh, t=t: v.tensor_mul(
                        dh[:, t * TT:(t + 1) * TT],
                        dh[:, t * TT:(t + 1) * TT],
                        cebc_h[:, t * TT:(t + 1) * TT]),
                        (dk, cebc_k), (dk,), tag=f"{pref}c{j}_{t}")
            if ji + 2 < len(js_out):
                dmab(ji + 2)
            b.dma("POOL", P.ins[scrname][j * 128:(j + 1) * 128, :], dh[:, :],
                  (dk,), (f"{scrname}{j}",), tag=f"{pref}o{j}")

    def mm_scr(pref, w_hbm, nk, scrname, js_out, to_gb, act_out=None,
               dstscr=None, ce=False):
        w_r = w_hbm.rearrange("(k p) n -> p k n", p=128)
        jp_list = [js_out[i:i + 2] for i in range(0, len(js_out), 2)]
        nsl = max(1, nk // 8)
        for jp in jp_list:
            wks = []
            for jj, j in enumerate(jp):
                s0 = jj * 4
                whj = P.wfull[0][:, s0 * 1024:s0 * 1024 + nk * 128]
                wkj = tuple(P.wfull[1] + f":{s0 + i}" for i in range(nsl))
                b.dma("POOL", whj.rearrange("p (k c) -> p k c", c=128),
                      w_r[:, :, j * 128:(j + 1) * 128], (), wkj,
                      tag=f"{pref}w{j}")
                wks.append((whj, wkj))
            for k in range(nk):
                rh, rk = rot[2 + k % 2]
                b.dma("POOL", rh[:, :],
                      P.ins[scrname][k * 128:(k + 1) * 128, :],
                      (f"{scrname}{k}",), (rk + "_s",), tag=f"{pref}r{jp[0]}_{k}")
                for jj, j in enumerate(jp):
                    whj, wkj = wks[jj]
                    for t in range(NT):
                        ps, psk = P.pb[jj * 4 + t]
                        b.pe(lambda e2, ps=ps, whj=whj, k=k, t=t, rh=rh:
                             e2.matmul(ps[:, :TT],
                                       whj[:, k * 128:(k + 1) * 128],
                                       rh[:, t * TT:(t + 1) * TT],
                                       start=(k == 0), stop=(k == nk - 1)),
                             (wkj[k // 8], rk + "_s"), (psk,),
                             tag=f"{pref}m{j}_{k}_{t}")
            for jj, j in enumerate(jp):
                for t in range(NT):
                    ps, psk = P.pb[jj * 4 + t]
                    if to_gb:
                        gh, gk = P.gb[j]
                        if first[0]:
                            b.act(lambda s, ps=ps, gh=gh, t=t: s.activation(
                                gh[:, t * TT:(t + 1) * TT], ps[:, :TT],
                                AF.Copy), (psk,), (gk,), tag=f"{pref}i{j}_{t}")
                        else:
                            b.dve(lambda v, ps=ps, gh=gh, t=t: v.tensor_add(
                                gh[:, t * TT:(t + 1) * TT],
                                gh[:, t * TT:(t + 1) * TT], ps[:, :TT]),
                                (psk, gk), (gk,), tag=f"{pref}d{j}_{t}")
                    else:
                        dh, dk = rot[jj % 2]
                        b.act(lambda s, ps=ps, dh=dh, t=t: s.activation(
                            dh[:, t * TT:(t + 1) * TT], ps[:, :TT], act_out),
                            (psk,), (dk,), tag=f"{pref}e{j}_{t}")
                        if ce:
                            b.dve(lambda v, dh=dh, t=t: v.tensor_mul(
                                dh[:, t * TT:(t + 1) * TT],
                                dh[:, t * TT:(t + 1) * TT],
                                cebc_h[:, t * TT:(t + 1) * TT]),
                                (dk, cebc_k), (dk,), tag=f"{pref}q{j}_{t}")
                if not to_gb:
                    dh, dk = rot[jj % 2]
                    b.dma("POOL", P.ins[dstscr][j * 128:(j + 1) * 128, :],
                          dh[:, :], (dk,), (f"{dstscr}{j}",),
                          tag=f"{pref}o{j}")
        if to_gb:
            first[0] = False

    load_ce(0)
    mm_res("l0e0u", ins["l0_e0upT"], xf, list(range(16)), AF.Gelu, True,
           "hscr")
    mm_scr("l0e0d", ins["l0_e0dnT"], 16, "hscr", list(range(2, 8)), True)
    load_ce(1)
    mm_res("l0e1u", ins["l0_e1w1T"], xf, list(range(16)), None, True, "hscr",
           pair_hbm=ins["l0_e1w2T"])
    mm_scr("l0e1d", ins["l0_e1dnT"], 16, "hscr", list(range(2, 8)), True)
    load_ce(3)
    mm_res("l0e3a", ins["l0_e3dT"], xf, list(range(8)), AF.Gelu, False,
           "hscr2")
    hscr2_res = []
    d1 = [P.bb[k] for k in range(8)]
    for k in range(8):
        dh, dk = d1[k]
        b.dma("POOL", dh[:, :], P.ins["hscr2"][k * 128:(k + 1) * 128, :],
              (f"hscr2{k}",), (dk,), tag=f"l0d1r{k}")
    mm_res("l0e3u", ins["l0_e3uT"], d1, list(range(16)), AF.Gelu, True,
           "hscr")
    mm_scr("l0e3o", ins["l0_e3oT"], 16, "hscr", list(range(2, 8)), True)
    load_ce(2)
    mm_res("l0e2a", ins["l0_e2l1T"], xf, list(range(16)), AF.Gelu, False,
           "hscr")
    mm_scr("l0e2b", ins["l0_e2l2T"], 16, "hscr", list(range(16)), False,
           act_out=AF.Gelu, dstscr="hscr2")
    mm_scr("l0e2c", ins["l0_e2l3T"], 16, "hscr2", list(range(16)), False,
           act_out=AF.Gelu, dstscr="hscr", ce=True)
    mm_scr("l0e2d", ins["l0_e2l4T"], 16, "hscr", list(range(2, 8)), True)

    for j in range(8):
        rh, rk = P.bb[8 + (j % 2)]
        b.dma("POOL", rh[:, :], P.ins["x2scr"][j * 128:(j + 1) * 128, :],
              (f"x2s{j}",), (rk,), tag=f"l0x2r{j}")
        gh, gk = P.gb[j]
        if j >= 2:
            b.dve(lambda v, rh=rh, gh=gh: v.tensor_add(gh[:, :], gh[:, :],
                                                       rh[:, :]),
                  (rk, gk), (gk,), tag=f"l0ga{j}")
            b.act(lambda s, gh=gh: s.activation(gh[:, :], gh[:, :], AF.Silu),
                  (gk,), (gk,), tag=f"l0gs{j}")
        else:
            b.act(lambda s, gh=gh, rh=rh: s.activation(gh[:, :], rh[:, :],
                                                       AF.Silu),
                  (rk,), (gk,), tag=f"l0gs{j}")

    if DEBUG_DUMP:
        for j in range(8):
            b.dma("POOL", P.ins["dbg_g"][j * 128:(j + 1) * 128, :],
                  P.gb[j][0][:, :], (P.gb[j][1],), (f"dbgg{j}",),
                  tag=f"dbgg{j}")
            b.dma("POOL", P.ins["dbg_xn"][j * 128:(j + 1) * 128, :],
                  P.xn[j][0][:, :], (P.xn[j][1],), (f"dbgxn{j}",),
                  tag=f"dbgxn{j}")

    nsc = load_smallvec(P, "l0nsc", "l0_nsc2d", (128, 8))
    nsb = load_smallvec(P, "l0nsb", "l0_nsb2d", (128, 8))
    alog = load_smallvec(P, "l0alog", "l0_alog2d", (128, 1))
    dtb = load_smallvec(P, "l0dtb", "l0_dtb2d", (128, 1))
    D2d = load_smallvec(P, "l0D", "l0_D2d", (128, 16))
    bcs = load_smallvec(P, "l0bcs", "l0_bcs2d", (128, 1))
    bcb = load_smallvec(P, "l0bcb", "l0_bcb2d", (128, 1))
    bcm = load_smallvec(P, "l0bcm", "l0_bcmask2d", (128, 2))
    ut01 = load_smallvec(P, "l0ut", "c_ut01", (128, 128))
    ls01 = load_smallvec(P, "l0ls", "c_ls01", (128, 128))
    radt = P.small("l0radt", [128, 128], F32)
    w2t = P.small("l0w2", [64, 128], F32)
    p2t = P.small("l0p2", [64, 128], F32)
    rows_h, rows_k = P.rows
    f2h2, f2k2 = P.fb[2]
    eps6 = b.const(1e-6)

    for t in range(NT):
        for k in range(KC):
            xh, xk = P.xn[k]
            b.pe(lambda e, xh=xh, k=k, t=t: e.matmul(
                P.pb[0][0][0:1, :TT], P.onecb[0][:, 0:1],
                xh[:, t * TT:(t + 1) * TT],
                start=(k == 0), stop=(k == KC - 1)),
                (P.onecb[1], xk), (P.pb[0][1],), tag=f"l0isx{k}_{t}")
            sh, sk = P.sq
            b.act(lambda s, xh=xh, sh=sh, t=t: s.activation(
                sh[:, :TT], xh[:, t * TT:(t + 1) * TT], AF.Square),
                (xk,), (sk,), tag=f"l0isq{k}_{t}")
            b.pe(lambda e, sh=sh, k=k: e.matmul(
                P.pb[1][0][0:1, :TT], P.onec[0][:, 0:1], sh[:, :TT],
                start=(k == 0), stop=(k == KC - 1)),
                (P.onec[1], sk), (P.pb[1][1],), tag=f"l0isq2{k}_{t}")
        b.dve(lambda v, t=t: v.tensor_scalar_mul(
            rows_h[0:1, t * TT:(t + 1) * TT], P.pb[0][0][0:1, :TT], 1.0 / D),
            (P.pb[0][1],), (rows_k,), tag=f"l0imu{t}")
        b.dve(lambda v, t=t: v.tensor_scalar_mul(
            f2h2[0:1, t * TT:(t + 1) * TT], P.pb[1][0][0:1, :TT], 1.0 / D),
            (P.pb[1][1],), (f2k2,), tag=f"l0ims{t}")
    b.act(lambda s: s.activation(rows_h[0:1, T:2 * T], rows_h[0:1, 0:T],
                                 AF.Square), (rows_k,), (rows_k,), tag="l0im2")
    b.dve(lambda v: v.tensor_sub(rows_h[0:1, T:2 * T], f2h2[0:1, 0:T],
                                 rows_h[0:1, T:2 * T]),
          (rows_k, f2k2), (rows_k,), tag="l0ivar")
    b.act(lambda s: s.activation(rows_h[0:1, T:2 * T], rows_h[0:1, T:2 * T],
                                 AF.Ln, bias=eps6[0:1, :]),
          (rows_k, "cpool"), (rows_k,), tag="l0iln")
    b.act(lambda s: s.activation(rows_h[0:1, T:2 * T], rows_h[0:1, T:2 * T],
                                 AF.Exp, scale=-0.5), (rows_k,), (rows_k,),
          tag="l0iexp")
    for t in range(NT):
        bcast_row(b, f"l0ibm{t}", (rows_h, rows_k, slice(t * TT, (t + 1) * TT)),
                  P.pb[2], (P.oner[0], P.oner[1]))
        bcast_row(b, f"l0ibr{t}", (rows_h, rows_k,
                                   slice(T + t * TT, T + (t + 1) * TT)),
                  P.pb[3], (P.oner[0], P.oner[1]))
        for k in range(KC):
            xh, xk = P.xn[k]
            dh, dk = P.bb[k]
            b.dve(lambda v, xh=xh, t=t: v.tensor_sub(
                f2h2[:, t * TT:(t + 1) * TT], xh[:, t * TT:(t + 1) * TT],
                P.pb[2][0][:, :TT]),
                (xk, P.pb[2][1]), (f2k2,), tag=f"l0i1{k}_{t}")
            b.dve(lambda v, k=k, t=t: v.scalar_tensor_tensor(
                f2h2[:, t * TT:(t + 1) * TT], f2h2[:, t * TT:(t + 1) * TT],
                nsc[0][:, k:k + 1], P.pb[3][0][:, :TT], ALU.mult, ALU.mult),
                (f2k2, nsc[1], P.pb[3][1]), (f2k2,), tag=f"l0i2{k}_{t}")
            b.dve(lambda v, dh=dh, k=k, t=t: v.tensor_scalar_add(
                dh[:, t * TT:(t + 1) * TT], f2h2[:, t * TT:(t + 1) * TT],
                nsb[0][:, k:k + 1]),
                (f2k2, nsb[1]), (dk,), tag=f"l0i3{k}_{t}")
    xi = [P.bb[k] for k in range(KC)]

    def epi_bcdl(j, t, ph, pk):
        if j == 0:
            b.act(lambda s, ph=ph, t=t: s.activation(
                f2h2[0:64, t * TT:(t + 1) * TT], ph[0:64, :TT], AF.Copy),
                (pk,), (f2k2,), tag=f"l0bc0{t}")
        elif j == 1:
            b.act(lambda s, ph=ph, t=t: s.activation(
                f2h2[64:128, t * TT:(t + 1) * TT], ph[64:128, :TT], AF.Copy),
                (pk,), (f2k2,), tag=f"l0bc1{t}")
        elif j == 2:
            b.dve(lambda v, ph=ph, t=t: v.tensor_copy(
                P.fb[0][0][0:16, t * TT:(t + 1) * TT], ph[0:16, :TT]),
                (pk,), (P.fb[0][1],), tag=f"l0dt{t}")
        else:
            b.dve(lambda v, ph=ph, t=t: v.tensor_copy(
                P.fb[1][0][0:16, t * TT:(t + 1) * TT], ph[0:16, :TT]),
                (pk,), (P.fb[1][1],), tag=f"l0lam{t}")
    mm_dense(b, "l0bcdl", ins["l0_wbcdl"], D, 512, xi, P.pb[:6],
             [(P.wbA[0][:, s * 1024:(s + 1) * 1024], P.wbA[1] + f":{s}")
              for s in range(2)], epi_bcdl)

    sqh, sqk = P.sq
    for t in range(NT):
        b.act(lambda s, t=t: s.activation(sqh[:, 0:TT],
                                          f2h2[:, t * TT:(t + 1) * TT],
                                          AF.Square),
              (f2k2,), (sqk,), tag=f"l0bsq{t}")
        for ridx in range(2):
            pr, prk = P.pb[6]
            b.pe(lambda e, pr=pr, ridx=ridx, t=t: e.matmul(
                pr[0:1, :TT], bcm[0][:, ridx:ridx + 1],
                sqh[:, 0:TT], start=True, stop=True),
                (bcm[1], sqk), (prk,), tag=f"l0br{ridx}_{t}_a")
            b.dve(lambda v, pr=pr, ridx=ridx, t=t: v.tensor_scalar_mul(
                rows_h[0:1, ridx * T + t * TT:ridx * T + (t + 1) * TT],
                pr[0:1, :TT], 1.0 / 64),
                (prk,), (rows_k,), tag=f"l0brm{ridx}_{t}")
    for ridx in range(2):
        b.act(lambda s, ridx=ridx: s.activation(
            rows_h[0:1, ridx * T:(ridx + 1) * T],
            rows_h[0:1, ridx * T:(ridx + 1) * T], AF.Ln, bias=eps6[0:1, :]),
            (rows_k, "cpool"), (rows_k,), tag=f"l0brl{ridx}")
        b.act(lambda s, ridx=ridx: s.activation(
            rows_h[0:1, ridx * T:(ridx + 1) * T],
            rows_h[0:1, ridx * T:(ridx + 1) * T], AF.Exp, scale=-0.5),
            (rows_k,), (rows_k,), tag=f"l0bre{ridx}")
    for t in range(NT):
        bcast_row(b, f"l0bbB{t}", (rows_h, rows_k, slice(t * TT, (t + 1) * TT)),
                  P.pb[4], (P.oner[0], P.oner[1]))
        bcast_row(b, f"l0bbC{t}", (rows_h, rows_k,
                                   slice(T + t * TT, T + (t + 1) * TT)),
                  P.pb[5], (P.oner[0], P.oner[1]))
        for (bandlo, pbi) in ((0, 4), (32, 5), (64, 4), (96, 5)):
            b.dve(lambda v, bandlo=bandlo, pbi=pbi, t=t:
                  v.scalar_tensor_tensor(
                      f2h2[bandlo:bandlo + 32, t * TT:(t + 1) * TT],
                      f2h2[bandlo:bandlo + 32, t * TT:(t + 1) * TT],
                      bcs[0][bandlo:bandlo + 32, 0:1],
                      P.pb[pbi][0][bandlo:bandlo + 32, :TT],
                      ALU.mult, ALU.mult),
                  (f2k2, bcs[1], P.pb[pbi][1]), (f2k2,),
                  tag=f"l0bsc{bandlo}_{t}")
            b.dve(lambda v, bandlo=bandlo, t=t: v.tensor_scalar_add(
                f2h2[bandlo:bandlo + 32, t * TT:(t + 1) * TT],
                f2h2[bandlo:bandlo + 32, t * TT:(t + 1) * TT],
                bcb[0][bandlo:bandlo + 32, 0:1]),
                (f2k2, bcb[1]), (f2k2,), tag=f"l0bsb{bandlo}_{t}")

    DTh, DTk = P.fb[0][0], P.fb[0][1]
    LMh, LMk = P.fb[1][0], P.fb[1][1]
    b.dve(lambda v: v.tensor_scalar_add(DTh[0:16, :], DTh[0:16, :],
                                        dtb[0][0:16, 0:1]),
          (DTk, dtb[1]), (DTk,), tag="l0sdtb")
    for th in range(2):
        c = slice(th * 1024, (th + 1) * 1024)
        b.act(lambda s, c=c: s.activation(sqh[0:16, 0:1024], DTh[0:16, c],
                                          AF.Abs), (DTk,), (sqk,),
              tag=f"l0sab{th}")
        b.act(lambda s: s.activation(sqh[0:16, 0:1024], sqh[0:16, 0:1024],
                                     AF.Exp, scale=-1.0), (sqk,), (sqk,),
              tag=f"l0sex{th}")
        b.act(lambda s: s.activation(sqh[0:16, 0:1024], sqh[0:16, 0:1024],
                                     AF.Ln, bias=one_c[0:16, :]),
              (sqk, "cpool"), (sqk,), tag=f"l0sln{th}")
        b.dve(lambda v, c=c: v.tensor_scalar_max(DTh[0:16, c], DTh[0:16, c],
                                                 0.0),
              (DTk,), (DTk,), tag=f"l0smx{th}")
        b.dve(lambda v, c=c: v.tensor_add(DTh[0:16, c], DTh[0:16, c],
                                          sqh[0:16, 0:1024]),
              (DTk, sqk), (DTk,), tag=f"l0sad{th}")
    b.act(lambda s: s.activation(LMh[0:16, :], LMh[0:16, :], AF.Sigmoid),
          (LMk,), (LMk,), tag="l0slam")
    for t in range(NT):
        pr, prk = P.pb[6]
        b.pe(lambda e, pr=pr, t=t: e.matmul(
            pr[0:1, :TT], P.onec[0][0:16, 0:1],
            DTh[0:16, t * TT:(t + 1) * TT], start=True, stop=True),
            (P.onec[1], DTk), (prk,), tag=f"l0dm{t}")
        b.dve(lambda v, pr=pr, t=t: v.tensor_scalar_mul(
            rows_h[0:1, t * TT:(t + 1) * TT], pr[0:1, :TT], 1.0 / 16),
            (prk,), (rows_k,), tag=f"l0dme{t}")
    b.dve(lambda v: v.tensor_mul(LMh[0:16, :], DTh[0:16, :], LMh[0:16, :]),
          (DTk, LMk), (LMk,), tag="l0c1")
    for th in range(2):
        c = slice(th * 1024, (th + 1) * 1024)
        b.dve(lambda v, c=c: v.tensor_sub(sqh[0:16, 0:1024], DTh[0:16, c],
                                          LMh[0:16, c]),
              (DTk, LMk), (sqk,), tag=f"l0f{th}")
        b.dma("SP", P.ins["rowscr"][0:16, th * 1024:(th + 1) * 1024],
              sqh[0:16, 0:1024], (sqk,), (f"l0f{th}",), tag=f"l0fd{th}")
    for th in range(2):
        lo = th * 1024
        n = 1023 if th == 1 else 1024
        b.dve(lambda v: v.memset(sqh[0:16, 0:1024], 0.0), (), (sqk,),
              tag=f"l0z{th}")
        b.dma("SP", sqh[0:16, 0:n], P.ins["rowscr"][0:16, lo + 1:lo + 1 + n],
              (f"l0f0", f"l0f1"), (sqk,), tag=f"l0fr{th}")
        b.dve(lambda v, lo=lo: v.tensor_add(sqh[0:16, 0:1024],
                                            sqh[0:16, 0:1024],
                                            LMh[0:16, lo:lo + 1024]),
              (sqk, LMk), (sqk,), tag=f"l0et{th}")
        b.act(lambda s: s.activation(sqh[0:16, 0:1024], sqh[0:16, 0:1024],
                                     AF.Ln), (sqk,), (sqk,), tag=f"l0el{th}")
        b.dma("SP", P.ins["rowscr"][16:32, lo:lo + 1024], sqh[0:16, 0:1024],
              (sqk,), (f"l0le{th}",), tag=f"l0led{th}")
    b.act(lambda s: s.activation(LMh[0:16, :], LMh[0:16, :], AF.Ln),
          (LMk,), (LMk,), tag="l0lc1")
    b.act(lambda s: s.activation(P.stat[0][0:16, 0:1], alog[0][0:16, 0:1],
                                 AF.Exp), (alog[1],), (P.stat[1],),
          tag="l0aexp")
    b.dve(lambda v: v.tensor_scalar_mul(DTh[0:16, :], DTh[0:16, :],
                                        P.stat[0][0:16, 0:1]),
          (DTk, P.stat[1]), (DTk,), tag="l0adt1")
    b.dve(lambda v: v.tensor_scalar_mul(DTh[0:16, :], DTh[0:16, :], -1.0),
          (DTk,), (DTk,), tag="l0adt2")
    for c in range(16):
        ph, pk = P.pb[6]
        b.pe(lambda e, c=c, ph=ph: e.transpose(
            ph[0:128, 0:16], DTh[0:16, c * 128:(c + 1) * 128],
            P.idf[0][0:16, 0:16]), (DTk, P.idf[1]), (pk,), tag=f"l0ta{c}")
        b.dve(lambda v, c=c, ph=ph: v.tensor_copy(
            sqh[:, c * 16:(c + 1) * 16], ph[0:128, 0:16]),
            (pk,), (sqk,), tag=f"l0tae{c}")
    for c in range(16):
        b.dma("SP", sqh[0:16, 768:896], P.ins["rowscr"][16:32, c * 128:(c + 1) * 128],
              ("l0le0", "l0le1"), (sqk,), tag=f"l0lr{c}")
        ph, pk = P.pb[6]
        b.pe(lambda e, ph=ph: e.transpose(
            ph[0:128, 0:16], sqh[0:16, 768:896], P.idf[0][0:16, 0:16]),
            (sqk, P.idf[1]), (pk,), tag=f"l0tl{c}")
        b.dve(lambda v, c=c, ph=ph: v.tensor_copy(
            sqh[:, 256 + c * 16:256 + (c + 1) * 16], ph[0:128, 0:16]),
            (pk,), (sqk,), tag=f"l0tle{c}")
        ph2, pk2 = P.pb[6]
        b.pe(lambda e, c=c, ph2=ph2: e.transpose(
            ph2[0:128, 16:32], LMh[0:16, c * 128:(c + 1) * 128],
            P.idf[0][0:16, 0:16]), (LMk, P.idf[1]), (pk2,), tag=f"l0td{c}")
        b.dve(lambda v, c=c, ph2=ph2: v.tensor_sub(
            sqh[:, 512 + c * 16:512 + (c + 1) * 16], ph2[0:128, 16:32],
            sqh[:, 256 + c * 16:256 + (c + 1) * 16]),
            (pk2, sqk), (sqk,), tag=f"l0tde{c}")

    thsl = (P.wbB[0][:, 0:512], P.wbB[1] + ":0")
    b.dma("POOL", thsl[0].rearrange("p (k c) -> p k c", c=64),
          ins["l0_thdup"].rearrange("(k p) n -> p k n", p=128),
          (), (thsl[1],), tag="l0thw")
    for t in range(NT):
        ps, psk = P.pb[0]
        for k in range(KC):
            b.pe(lambda e, ps=ps, k=k, t=t: e.matmul(
                ps[64:128, :TT], thsl[0][:, k * 64:(k + 1) * 64],
                P.xn[k][0][:, t * TT:(t + 1) * TT],
                start=(k == 0), stop=(k == KC - 1), tile_position=(0, 64)),
                (thsl[1], P.xn[k][1]), (psk,), tag=f"l0th{t}_{k}")
        bcast_row(b, f"l0dbc{t}", (rows_h, rows_k, slice(t * TT, (t + 1) * TT)),
                  P.pb[1], (P.oner[0], P.oner[1]))
        b.act(lambda s, ps=ps, t=t: s.activation(
            P.fb[0][0][64:128, t * TT:(t + 1) * TT], ps[64:128, :TT],
            AF.Copy), (psk,), (P.fb[0][1],), tag=f"l0ze{t}")
        b.dve(lambda v, t=t: v.tensor_mul(
            P.fb[0][0][64:128, t * TT:(t + 1) * TT],
            P.fb[0][0][64:128, t * TT:(t + 1) * TT],
            P.pb[1][0][64:128, :TT]),
            (P.fb[0][1], P.pb[1][1]), (P.fb[0][1],), tag=f"l0zz{t}")
    b.dve(lambda v: v.tensor_tensor_scan(
        P.fb[0][0][64:128, :], P.onec[0][64:128, 0:1].broadcast_to((64, T)),
        P.fb[0][0][64:128, :], 0.0, ALU.mult, ALU.add),
        (P.fb[0][1], P.onec[1]), (P.fb[0][1],), tag="l0phis")
    TWO_PI = float(2 * np.pi)
    for c in range(16):
        ph, pk = P.pb[6]
        b.pe(lambda e, c=c, ph=ph: e.transpose(
            ph[0:128, 0:64], P.fb[0][0][64:128, c * 128:(c + 1) * 128],
            P.idf[0][64:128, 64:128], tile_position=(64, 0)),
            (P.fb[0][1], P.idf[1]), (pk,), tag=f"l0pt{c}")
        b.dve(lambda v, c=c, ph=ph: v.tensor_scalar_mul(
            sqh[:, 768 + 0:768 + 64] if False else sqh[:, 768:832],
            ph[0:128, 0:64], 1.0 / TWO_PI),
            (pk,), (sqk,), tag=f"l0pr{c}")
        b.dve(lambda v: v.tensor_copy(
            sqh[:, :].bitcast(mybir.dt.int32)[:, 960:1024], sqh[:, 768:832]),
              (sqk,), (sqk,), tag=f"l0pi{c}")
        b.dve(lambda v: v.tensor_copy(
            sqh[:, 832:896], sqh[:, :].bitcast(mybir.dt.int32)[:, 960:1024]),
              (sqk,), (sqk,), tag=f"l0pj{c}")
        b.dve(lambda v: v.tensor_sub(sqh[:, 768:832], sqh[:, 768:832],
                                     sqh[:, 832:896]),
              (sqk,), (sqk,), tag=f"l0pk{c}")
        b.dve(lambda v: v.tensor_scalar_mul(sqh[:, 768:832], sqh[:, 768:832],
                                            TWO_PI),
              (sqk,), (sqk,), tag=f"l0pl{c}")
        b.act(lambda s, c=c: s.activation(
            P.bb[14][0][:, c * 64:(c + 1) * 64], sqh[:, 768:832], AF.Sin),
            (sqk,), (P.bb[14][1],), tag=f"l0psn{c}")
        b.dve(lambda v: v.tensor_scalar(sqh[:, 896:960], sqh[:, 768:832],
                                        float(-np.pi / 2), -TWO_PI,
                                        ALU.is_lt, ALU.mult),
              (sqk,), (sqk,), tag=f"l0pc1{c}")
        b.dve(lambda v: v.tensor_scalar(sqh[:, 832:896], sqh[:, 768:832],
                                        -1.0, float(np.pi / 2),
                                        ALU.mult, ALU.add),
              (sqk,), (sqk,), tag=f"l0pc2{c}")
        b.dve(lambda v: v.tensor_add(sqh[:, 832:896], sqh[:, 832:896],
                                     sqh[:, 896:960]),
              (sqk,), (sqk,), tag=f"l0pc3{c}")
        b.act(lambda s, c=c: s.activation(
            P.bb[14][0][:, 1024 + c * 64:1024 + (c + 1) * 64],
            sqh[:, 832:896], AF.Sin),
            (sqk,), (P.bb[14][1],), tag=f"l0pcs{c}")

    for c in range(16):
        ph, pk = P.pb[6]
        b.pe(lambda e, c=c, ph=ph: e.transpose(
            ph[0:128, 0:64], f2h2[0:64, c * 128:(c + 1) * 128],
            P.idf[0][0:64, 0:64]),
            (f2k2, P.idf[1]), (pk,), tag=f"l0bt{c}")
        b.dve(lambda v, c=c, ph=ph: v.tensor_copy(
            P.bb[13][0][:, c * 64:(c + 1) * 64], ph[0:128, 0:64]),
            (pk,), (P.bb[13][1],), tag=f"l0bte{c}")
        ph2, pk2 = P.pb[7]
        b.pe(lambda e, c=c, ph2=ph2: e.transpose(
            ph2[0:128, 0:64], f2h2[64:128, c * 128:(c + 1) * 128],
            P.idf[0][64:128, 64:128], tile_position=(64, 0)),
            (f2k2, P.idf[1]), (pk2,), tag=f"l0bt2{c}")
        b.dve(lambda v, c=c, ph2=ph2: v.tensor_copy(
            P.bb[13][0][:, 1024 + c * 64:1024 + (c + 1) * 64],
            ph2[0:128, 0:64]),
            (pk2,), (P.bb[13][1],), tag=f"l0bt2e{c}")
    snT = P.bb[14]
    for c in range(16):
        for src_off, dst in ((0, 0), (32, 1024)):
            re = slice(c * 64 + src_off, c * 64 + src_off + 32)
            im = slice(1024 + c * 64 + src_off, 1024 + c * 64 + src_off + 32)
            sn = slice(c * 64, c * 64 + 32)
            cs = slice(1024 + c * 64, 1024 + c * 64 + 32)
            orr = slice(dst + c * 64, dst + c * 64 + 32)
            oii = slice(dst + c * 64 + 32, dst + c * 64 + 64)
            b.dve(lambda v, re=re, cs=cs: v.tensor_mul(
                sqh[:, 768:800], P.bb[13][0][:, re], snT[0][:, cs]),
                (P.bb[13][1], snT[1]), (sqk,), tag=f"l0r1_{c}_{src_off}")
            b.dve(lambda v, im=im, sn=sn: v.tensor_mul(
                sqh[:, 800:832], P.bb[13][0][:, im], snT[0][:, sn]),
                (P.bb[13][1], snT[1]), (sqk,), tag=f"l0r2_{c}_{src_off}")
            b.dve(lambda v, orr=orr: v.tensor_sub(
                P.bb[15][0][:, orr], sqh[:, 768:800], sqh[:, 800:832]),
                (sqk,), (P.bb[15][1],), tag=f"l0r3_{c}_{src_off}")
            b.dve(lambda v, re=re, sn=sn: v.tensor_mul(
                sqh[:, 768:800], P.bb[13][0][:, re], snT[0][:, sn]),
                (P.bb[13][1], snT[1]), (sqk,), tag=f"l0r4_{c}_{src_off}")
            b.dve(lambda v, im=im, cs=cs: v.tensor_mul(
                sqh[:, 800:832], P.bb[13][0][:, im], snT[0][:, cs]),
                (P.bb[13][1], snT[1]), (sqk,), tag=f"l0r5_{c}_{src_off}")
            b.dve(lambda v, oii=oii: v.tensor_add(
                P.bb[15][0][:, oii], sqh[:, 768:800], sqh[:, 800:832]),
                (sqk,), (P.bb[15][1],), tag=f"l0r6_{c}_{src_off}")

    wx_r = ins["l0_wx"].rearrange("(k p) n -> p k n", p=128)
    wz_r = ins["l0_wz"].rearrange("(k p) n -> p k n", p=128)
    wo_r = ins["l0_outp"].rearrange("(h p) n -> p h n", p=128)
    sacc = [(fbi_bf(P.fb[0][0], 0, T), P.fb[0][1]),
            (fbi_bf(P.fb[0][0], T, 2 * T), P.fb[0][1]),
            (fbi_bf(P.fb[1][0], 0, T), P.fb[1][1]),
            (fbi_bf(P.fb[1][0], T, 2 * T), P.fb[1][1]),
            (fbi_bf(P.fb[2][0], 0, T), P.fb[2][1]),
            (fbi_bf(P.fb[2][0], T, 2 * T), P.fb[2][1]),
            P.bb[13], P.bb[14]]
    wxsl = [(P.wbA[0][:, s * 1024:(s + 1) * 1024], P.wbA[1] + f":{s}")
            for s in range(2)]
    wzsl = [(P.wbB[0][:, s * 1024:(s + 1) * 1024], P.wbB[1] + f":{s}")
            for s in range(2)]
    wosl = [(P.wfull[0][:, s * 1024:(s + 1) * 1024], P.wfull[1] + f":{s}")
            for s in range(8)]

    def dma_head(h):
        wh, wk = wxsl[h % 2]
        b.dma("POOL", wh.rearrange("p (k c) -> p k c", c=128),
              wx_r[:, :, h * 128:(h + 1) * 128], (), (wk,), tag=f"l0hwx{h}")
        wh2, wk2 = wzsl[h % 2]
        b.dma("POOL", wh2.rearrange("p (k c) -> p k c", c=128),
              wz_r[:, :, h * 128:(h + 1) * 128], (), (wk2,), tag=f"l0hwz{h}")
        wh3, wk3 = wosl[h % 8]
        b.dma("POOL", wh3.rearrange("p (j c) -> p j c", c=128),
              wo_r[:, h, :].rearrange("p (j c) -> p j c", c=128),
              (), (wk3,), tag=f"l0hwo{h}")
    dma_head(0)
    for h in range(16):
        if h + 1 < 16:
            dma_head(h + 1)
        for (slots, dstt) in ((wxsl, P.bb[8]), (wzsl, P.bb[10])):
            wh, wk = slots[h % 2]
            dh, dk = dstt
            for t in range(NT):
                ps, psk = P.pb[t % 2]
                for k in range(KC):
                    b.pe(lambda e, ps=ps, wh=wh, k=k, t=t: e.matmul(
                        ps[:, :TT], wh[:, k * 128:(k + 1) * 128],
                        xi[k][0][:, t * TT:(t + 1) * TT],
                        start=(k == 0), stop=(k == KC - 1)),
                        (wk, xi[k][1]), (psk,), tag=f"l0hx{h}_{t}_{k}")
                b.act(lambda s, ps=ps, dh=dh, t=t: s.activation(
                    dh[:, t * TT:(t + 1) * TT], ps[:, :TT], AF.Silu),
                    (psk,), (dk,), tag=f"l0hxe{h}_{t}")
        for c in range(16):
            tph, tpk = P.pb[6 + (c % 2)]
            b.pe(lambda e, c=c, tph=tph: e.transpose(
                tph[:, :].bitcast(BF16)[:, 0:128],
                P.bb[8][0][:, c * 128:(c + 1) * 128], P.idb[0][:, :]),
                (P.bb[8][1], P.idb[1]), (tpk,), tag=f"l0ht{h}_{c}")
            b.act(lambda s, c=c, tph=tph: s.activation(
                P.bb[9][0][:, c * 128:(c + 1) * 128],
                tph[:, :].bitcast(BF16)[:, 0:128], AF.Copy),
                (tpk,), (P.bb[9][1],), tag=f"l0hte{h}_{c}")
        b.dve(lambda v: v.memset(sqh[0:64, 768:896], 0.0), (), (sqk,),
              tag=f"l0hz{h}")
        for c in range(16):
            b.dve(lambda v, c=c, h=h: v.tensor_scalar_mul(
                radt[0][:, :], ut01[0][:, :],
                sqh[:, c * 16 + h:c * 16 + h + 1]),
                (ut01[1], sqk), (radt[1],), tag=f"l0cr{h}_{c}")
            psW, psWk = P.pb[0]
            b.pe(lambda e, psW=psW: e.matmul(psW[:, 0:128], ls01[0][:, :],
                                             radt[0][:, :], start=True,
                                             stop=False),
                 (ls01[1], radt[1]), (psWk,), tag=f"l0cE{h}_{c}")
            ps2, ps2k = P.pb[2]
            b.pe(lambda e, ps2=ps2: e.matmul(ps2[0:1, 0:128],
                                             P.onec[0][:, 0:1],
                                             radt[0][:, :], start=True,
                                             stop=True),
                 (P.onec[1], radt[1]), (ps2k,), tag=f"l0c2{h}_{c}")
            b.dve(lambda v, ps2=ps2: v.tensor_copy(sqh[0:1, 896:1024],
                                                   ps2[0:1, 0:128]),
                  (ps2k,), (sqk,), tag=f"l0c2r{h}_{c}")
            b.dve(lambda v: v.tensor_scalar_mul(radt[0][:, :], ls01[0][:, :],
                                                NEG_BIG),
                  (ls01[1],), (radt[1],), tag=f"l0cmb{h}_{c}")
            b.pe(lambda e, psW=psW: e.matmul(psW[:, 0:128], P.idf[0][:, :],
                                             radt[0][:, :], start=False,
                                             stop=False),
                 (P.idf[1], radt[1]), (psWk,), tag=f"l0cM{h}_{c}")
            psww, pswwk = P.pb[3 + ((c + 1) % 2)]
            b.pe(lambda e, c=c, h=h, psww=psww: e.matmul(
                psww[:, 0:1], ls01[0][:, :],
                sqh[:, c * 16 + h:c * 16 + h + 1],
                start=True, stop=True),
                (ls01[1], sqk), (pswwk,), tag=f"l0cWm{h}_{c}")
            b.act(lambda s, c=c, h=h, psww=psww: s.activation(
                P.stat[0][:, 3:4], psww[:, 0:1], AF.Exp,
                bias=sqh[:, 256 + c * 16 + h:256 + c * 16 + h + 1], scale=1.0),
                (pswwk, sqk, "cpool"), (P.stat[1],), tag=f"l0cWc{h}_{c}")
            b.dve(lambda v, c=c, h=h: v.tensor_scalar_mul(
                radt[0][:, :], P.idf[0][:, :],
                sqh[:, 512 + c * 16 + h:512 + c * 16 + h + 1]),
                (P.idf[1], sqk), (radt[1],), tag=f"l0cd{h}_{c}")
            b.pe(lambda e, psW=psW: e.matmul(psW[:, 0:128], P.idf[0][:, :],
                                             radt[0][:, :], start=False,
                                             stop=True),
                 (P.idf[1], radt[1]), (psWk,), tag=f"l0cD{h}_{c}")
            b.act(lambda s, c=c, h=h, psW=psW: s.activation(
                P.bb[12][0][:, 0:128], psW[:, 0:128], AF.Exp,
                bias=sqh[:, 256 + c * 16 + h:256 + c * 16 + h + 1], scale=1.0),
                (psWk, sqk, "cpool"), (P.bb[12][1],), tag=f"l0cW{h}_{c}")
            tph, tpk = P.pb[6]
            b.pe(lambda e, c=c, tph=tph: e.transpose(
                tph[:, :].bitcast(BF16)[0:64, 0:128],
                P.bb[15][0][:, c * 64:(c + 1) * 64], P.idb[0][:, :]),
                (P.bb[15][1], P.idb[1]), (tpk,), tag=f"l0cbt{h}_{c}")
            b.act(lambda s, tph=tph: s.activation(
                P.bb[12][0][0:64, 256:384], tph[:, :].bitcast(BF16)[0:64, 0:128],
                AF.Copy), (tpk,), (P.bb[12][1],), tag=f"l0cbe{h}_{c}")
            b.pe(lambda e, c=c, tph=tph: e.transpose(
                tph[:, :].bitcast(BF16)[0:64, 128:256],
                P.bb[15][0][:, 1024 + c * 64:1024 + (c + 1) * 64],
                P.idb[0][:, :]),
                (P.bb[15][1], P.idb[1]), (tpk,), tag=f"l0cct{h}_{c}")
            b.act(lambda s, tph=tph: s.activation(
                P.bb[12][0][0:64, 384:512], tph[:, :].bitcast(BF16)[0:64, 128:256],
                AF.Copy), (tpk,), (P.bb[12][1],), tag=f"l0cce{h}_{c}")
            psG, psGk = P.pb[1]
            b.pe(lambda e, psG=psG: e.matmul(psG[:, 0:128],
                                             P.bb[12][0][0:64, 256:384],
                                             P.bb[12][0][0:64, 384:512],
                                             start=True, stop=True),
                 (P.bb[12][1],), (psGk,), tag=f"l0cG{h}_{c}")
            b.dve(lambda v, psG=psG: v.tensor_mul(
                P.bb[12][0][:, 128:256], psG[:, 0:128],
                P.bb[12][0][:, 0:128]),
                (psGk, P.bb[12][1]), (P.bb[12][1],), tag=f"l0cP{h}_{c}")
            ps2, ps2k = P.pb[2]
            b.pe(lambda e, ps2=ps2: e.matmul(ps2[0:64, 0:128],
                                             P.oner[0][0:1, 0:64],
                                             sqh[0:1, 896:1024], start=True,
                                             stop=True),
                 (P.oner[1], sqk), (ps2k,), tag=f"l0c2b{h}_{c}")
            b.act(lambda s, ps2=ps2: s.activation(w2t[0][:, :],
                                                  ps2[0:64, 0:128], AF.Exp),
                  (ps2k,), (w2t[1],), tag=f"l0c2e{h}_{c}")
            b.dve(lambda v: v.tensor_mul(p2t[0][:, :],
                                         P.bb[12][0][0:64, 384:512],
                                         w2t[0][:, :]),
                  (P.bb[12][1], w2t[1]), (p2t[1],), tag=f"l0cp2{h}_{c}")
            b.act(lambda s: s.activation(P.stat[0][0:64, 2:3],
                                         w2t[0][:, 127:128], AF.Copy),
                  (w2t[1],), (P.stat[1],), tag=f"l0cg{h}_{c}")
            psY, psYk = P.pb[3 + (c % 2)]
            b.pe(lambda e, c=c, psY=psY: e.matmul(
                psY[:, 0:128], P.bb[9][0][:, c * 128:(c + 1) * 128],
                P.bb[12][0][:, 128:256], start=True, stop=False),
                (P.bb[9][1], P.bb[12][1]), (psYk,), tag=f"l0cy1{h}_{c}")
            b.pe(lambda e, psY=psY: e.matmul(
                psY[:, 0:128], sqh[0:64, 768:896], p2t[0][:, :],
                start=False, stop=True),
                (sqk, p2t[1]), (psYk,), tag=f"l0cy2{h}_{c}")
            b.dve(lambda v, c=c, psY=psY, h=h: v.scalar_tensor_tensor(
                P.bb[11][0][:, c * 128:(c + 1) * 128],
                P.bb[8][0][:, c * 128:(c + 1) * 128], D2d[0][:, h:h + 1],
                psY[:, 0:128], ALU.mult, ALU.add),
                (P.bb[8][1], D2d[1], psYk), (P.bb[11][1],),
                tag=f"l0cyd{h}_{c}")
            b.dve(lambda v, c=c: v.tensor_mul(
                P.bb[11][0][:, c * 128:(c + 1) * 128],
                P.bb[11][0][:, c * 128:(c + 1) * 128],
                P.bb[10][0][:, c * 128:(c + 1) * 128]),
                (P.bb[11][1], P.bb[10][1]), (P.bb[11][1],),
                tag=f"l0cyz{h}_{c}")
            b.dve(lambda v, c=c: v.tensor_scalar_mul(
                P.bb[12][0][:, 512:576], P.bb[15][0][:, c * 64:(c + 1) * 64],
                P.stat[0][:, 3:4]),
                (P.bb[15][1], P.stat[1]), (P.bb[12][1],), tag=f"l0cbw{h}_{c}")
            psH, psHk = P.pb[5]
            b.pe(lambda e, c=c, psH=psH: e.matmul(
                psH[0:64, 0:128], P.bb[12][0][:, 512:576],
                P.bb[9][0][:, c * 128:(c + 1) * 128], start=True, stop=True),
                (P.bb[12][1], P.bb[9][1]), (psHk,), tag=f"l0cdh{h}_{c}")
            b.dve(lambda v, psH=psH: v.scalar_tensor_tensor(
                sqh[0:64, 768:896], sqh[0:64, 768:896], P.stat[0][0:64, 2:3],
                psH[0:64, 0:128], ALU.mult, ALU.add),
                (sqk, P.stat[1], psHk), (sqk,), tag=f"l0chu{h}_{c}")
        if DEBUG_DUMP and h == 0:
            for nm, t_ in (("dbg_t0", P.bb[8]), ("dbg_t1", P.bb[10]),
                           ("dbg_t2", P.bb[11]), ("dbg_t3", P.bb[15])):
                b.dma("POOL", P.ins[nm], t_[0][:, :], (t_[1],), (nm,),
                      tag=nm)
            b.dma("SP", P.ins["dbg_t4"], P.sq[0][:, :1024] if False else
                  P.sq[0][:, 0:1024], (P.sq[1],), ("dbg_t4",), tag="dbg_t4x")
        wh3, wk3 = wosl[h % 8]
        for j in range(8):
            for t in range(NT):
                ps, psk = P.pb[(j % 2) * 2 + (t % 2)]
                b.pe(lambda e, ps=ps, wh3=wh3, j=j, t=t: e.matmul(
                    ps[:, :TT], wh3[:, j * 128:(j + 1) * 128],
                    P.bb[11][0][:, t * TT:(t + 1) * TT],
                    start=True, stop=True),
                    (wk3, P.bb[11][1]), (psk,), tag=f"l0so{h}_{j}_{t}")
                sh_, sk_ = sacc[j]
                if h == 0:
                    b.act(lambda s, ps=ps, sh_=sh_, t=t: s.activation(
                        sh_[:, t * TT:(t + 1) * TT], ps[:, :TT], AF.Copy),
                        (psk,), (sk_,), tag=f"l0si{h}_{j}_{t}")
                else:
                    b.dve(lambda v, ps=ps, sh_=sh_, t=t: v.tensor_add(
                        sh_[:, t * TT:(t + 1) * TT],
                        sh_[:, t * TT:(t + 1) * TT], ps[:, :TT]),
                        (psk, sk_), (sk_,), tag=f"l0sa{h}_{j}_{t}")

    if DEBUG_DUMP:
        for j in range(8):
            sh_, sk_ = sacc[j]
            b.dma("POOL", P.ins["dbg_s"][j * 128:(j + 1) * 128, :],
                  sh_[:, :], (sk_,), (f"dbgs{j}",), tag=f"dbgs{j}")

    for j in range(8):
        sh_, sk_ = sacc[j]
        for th in range(2):
            b.dma("SP", sqh[:, 0:1024],
                  xsrc[j * 128:(j + 1) * 128, th * 1024:(th + 1) * 1024],
                  (() if src_keys is None else (src_keys[j],)), (sqk,),
                  tag=f"l0fx{j}_{th}")
            for tt in range(2):
                t = th * 2 + tt
                pc, pck = P.pb[6]
                b.dve(lambda v, sh_=sh_, j=j, t=t, pc=pc: v.tensor_add(
                    pc[:, :TT], sh_[:, t * TT:(t + 1) * TT],
                    P.xn[j][0][:, t * TT:(t + 1) * TT]),
                    (sk_, P.xn[j][1]), (pck,), tag=f"l0f1{j}_{t}")
                b.dve(lambda v, j=j, t=t, pc=pc: v.tensor_mul(
                    pc[:, :TT], pc[:, :TT],
                    P.gb[j][0][:, t * TT:(t + 1) * TT]),
                    (pck, P.gb[j][1]), (pck,), tag=f"l0f2{j}_{t}")
                b.dve(lambda v, tt=tt, pc=pc: v.tensor_add(
                    sqh[:, tt * TT:(tt + 1) * TT],
                    sqh[:, tt * TT:(tt + 1) * TT], pc[:, :TT]),
                    (pck, sqk), (sqk,), tag=f"l0f3{j}_{t}")
            b.dma("SP", xdst[j * 128:(j + 1) * 128, th * 1024:(th + 1) * 1024],
                  sqh[:, 0:1024], (sqk,), (dst_keys[j],), tag=f"l0fw{j}_{th}")


def prep_l0(ins, d):
    C = np.ascontiguousarray
    d["l0_n1w2d"] = _col2d(np.asarray(ins["L0_ffn_n1_w"]))
    d["l0_n2w2d"] = _col2d(np.asarray(ins["L0_ffn_n2_w"]))
    d["l0_qn2d"] = C(np.tile(np.asarray(ins["L0_ffn_qn_w"]), 2).reshape(128, 1))
    d["l0_kn2d"] = C(np.tile(np.asarray(ins["L0_ffn_kn_w"]), 2).reshape(128, 1))
    d["l0_wqT"] = C(np.asarray(ins["L0_ffn_wq"]).T)
    d["l0_wkT"] = C(np.asarray(ins["L0_ffn_wk"]).T)
    d["l0_wvT"] = C(np.asarray(ins["L0_ffn_wv"]).T)
    d["l0_woT"] = C(np.asarray(ins["L0_ffn_wo"]).T)
    d["l0_gateT"] = C(np.asarray(ins["L0_ffn_gate_w"]).T)
    d["l0_e0upT"] = C(np.asarray(ins["L0_ffn_e0_up"]).T)
    d["l0_e0dnT"] = C(np.asarray(ins["L0_ffn_e0_down"]).T)
    d["l0_e1w1T"] = C(np.asarray(ins["L0_ffn_e1_w1"]).T)
    d["l0_e1w2T"] = C(np.asarray(ins["L0_ffn_e1_w2"]).T)
    d["l0_e1dnT"] = C(np.asarray(ins["L0_ffn_e1_down"]).T)
    d["l0_e2l1T"] = C(np.asarray(ins["L0_ffn_e2_l1"]).T)
    d["l0_e2l2T"] = C(np.asarray(ins["L0_ffn_e2_l2"]).T)
    d["l0_e2l3T"] = C(np.asarray(ins["L0_ffn_e2_l3"]).T)
    d["l0_e2l4T"] = C(np.asarray(ins["L0_ffn_e2_l4"]).T)
    d["l0_e3dT"] = C(np.asarray(ins["L0_ffn_e3_down"]).T)
    d["l0_e3uT"] = C(np.asarray(ins["L0_ffn_e3_up"]).T)
    d["l0_e3oT"] = C(np.asarray(ins["L0_ffn_e3_out"]).T)
    d["l0_nsc2d"] = _col2d(np.asarray(ins["L0_attn_norm_scale"]))
    d["l0_nsb2d"] = _col2d(np.asarray(ins["L0_attn_norm_bias"]))
    ipw = np.asarray(ins["L0_attn_in_proj_w"])
    d["l0_wx"] = C(ipw[:, 0:2048])
    d["l0_wz"] = C(ipw[:, 2048:4096])
    B = ipw[:, 4096:4160]
    Cm = ipw[:, 4160:4224]
    dt = ipw[:, 4224:4240]
    lam = ipw[:, 4240:4256]
    z32 = np.zeros((1024, 32), np.float32)
    z112 = np.zeros((1024, 112), np.float32)
    blk0 = np.concatenate([B[:, 0:32], Cm[:, 0:32], z32, z32], 1)
    blk1 = np.concatenate([z32, z32, B[:, 32:64], Cm[:, 32:64]], 1)
    blk2 = np.concatenate([dt, z112], 1)
    blk3 = np.concatenate([lam, z112], 1)
    d["l0_wbcdl"] = C(np.concatenate([blk0, blk1, blk2, blk3], 1))
    al = np.zeros((128, 1), np.float32)
    al[0:16, 0] = np.asarray(ins["L0_attn_A_log"])
    d["l0_alog2d"] = al
    db_ = np.zeros((128, 1), np.float32)
    db_[0:16, 0] = np.asarray(ins["L0_attn_dt_bias"])
    d["l0_dtb2d"] = db_
    d["l0_D2d"] = C(np.broadcast_to(np.asarray(ins["L0_attn_D"]),
                                    (128, 16)).astype(np.float32))
    bs_ = np.zeros((128, 1), np.float32)
    bb_ = np.zeros((128, 1), np.float32)
    bs_[0:32, 0] = np.asarray(ins["L0_attn_bcB_scale"])[0:32]
    bs_[32:64, 0] = np.asarray(ins["L0_attn_bcC_scale"])[0:32]
    bs_[64:96, 0] = np.asarray(ins["L0_attn_bcB_scale"])[32:64]
    bs_[96:128, 0] = np.asarray(ins["L0_attn_bcC_scale"])[32:64]
    bb_[0:32, 0] = np.asarray(ins["L0_attn_B_bias"])[0:32]
    bb_[32:64, 0] = np.asarray(ins["L0_attn_C_bias"])[0:32]
    bb_[64:96, 0] = np.asarray(ins["L0_attn_B_bias"])[32:64]
    bb_[96:128, 0] = np.asarray(ins["L0_attn_C_bias"])[32:64]
    d["l0_bcs2d"] = bs_
    d["l0_bcb2d"] = bb_
    m_ = np.zeros((128, 2), np.float32)
    m_[0:32, 0] = 1.0
    m_[64:96, 0] = 1.0
    m_[32:64, 1] = 1.0
    m_[96:128, 1] = 1.0
    d["l0_bcmask2d"] = m_
    th = np.asarray(ins["L0_attn_theta_w"])
    d["l0_thdup"] = C(np.concatenate([th, th], 1))
    d["l0_outp"] = C(np.asarray(ins["L0_attn_out_proj_w"]))
    r = np.arange(128)
    d["c_ut01"] = C((r[:, None] <= r[None, :]).astype(np.float32))
    d["c_ls01"] = C((r[:, None] > r[None, :]).astype(np.float32))


L0_DEVINS = [
    ("l0_n1w2d", [128, 8]), ("l0_n2w2d", [128, 8]),
    ("l0_qn2d", [128, 1]), ("l0_kn2d", [128, 1]),
    ("l0_wqT", [1024, 1024]), ("l0_wkT", [1024, 1024]),
    ("l0_wvT", [1024, 1024]), ("l0_woT", [1024, 1024]),
    ("l0_gateT", [1024, 4]),
    ("l0_e0upT", [1024, 2048]), ("l0_e0dnT", [2048, 1024]),
    ("l0_e1w1T", [1024, 2048]), ("l0_e1w2T", [1024, 2048]),
    ("l0_e1dnT", [2048, 1024]),
    ("l0_e2l1T", [1024, 2048]), ("l0_e2l2T", [2048, 2048]),
    ("l0_e2l3T", [2048, 2048]), ("l0_e2l4T", [2048, 1024]),
    ("l0_e3dT", [1024, 1024]), ("l0_e3uT", [1024, 2048]),
    ("l0_e3oT", [2048, 1024]),
    ("l0_nsc2d", [128, 8]), ("l0_nsb2d", [128, 8]),
    ("l0_wx", [1024, 2048]), ("l0_wz", [1024, 2048]),
    ("l0_wbcdl", [1024, 512]),
    ("l0_alog2d", [128, 1]), ("l0_dtb2d", [128, 1]), ("l0_D2d", [128, 16]),
    ("l0_bcs2d", [128, 1]), ("l0_bcb2d", [128, 1]),
    ("l0_bcmask2d", [128, 2]),
    ("l0_thdup", [1024, 64]), ("l0_outp", [2048, 1024]),
    ("c_ut01", [128, 128]), ("c_ls01", [128, 128]),
]
LAYER_DEVINS[0] = L0_DEVINS
LAYER_PREP[0] = prep_l0
LAYER_EMIT[0] = emit_layer0


def emit_layer1(P, xsrc, xdst, src_keys=None, dst_keys=None):
    b = P.b
    ins = P.ins
    emit_outer_ln(P, xsrc, "l1o", src_keys)

    rows_h, rows_k = P.rows
    sqh, sqk = P.sq
    sgw = P.small("l1sgw", [128, 8], BF16)
    b.dma("POOL", sgw[0][:, :], ins["l1_sgwT2d"], (), (sgw[1],), tag="l1sgw")
    for t in range(NT):
        pr, prk = P.pb[6]
        for k in range(KC):
            b.pe(lambda e, pr=pr, k=k, t=t: e.matmul(
                pr[0:1, :TT], sgw[0][:, k:k + 1],
                P.xn[k][0][:, t * TT:(t + 1) * TT],
                start=(k == 0), stop=(k == KC - 1)),
                (sgw[1], P.xn[k][1]), (prk,), tag=f"l1sg{t}_{k}")
        b.act(lambda s, pr=pr, t=t: s.activation(
            rows_h[0:1, t * TT:(t + 1) * TT], pr[0:1, :TT], AF.Sigmoid),
            (prk,), (rows_k,), tag=f"l1sge{t}")
    wg_r = ins["l1_shgT"].rearrange("(k p) n -> p k n", p=128)
    wu_r = ins["l1_shuT"].rearrange("(k p) n -> p k n", p=128)
    slotsA = [(P.wbA[0][:, s * 1024:(s + 1) * 1024], P.wbA[1] + f":{s}")
              for s in range(2)]
    slotsB = [(P.wbB[0][:, s * 1024:(s + 1) * 1024], P.wbB[1] + f":{s}")
              for s in range(2)]
    rot = [(fbi_bf(P.fb[0][0], 0, T), P.fb[0][1]),
           (fbi_bf(P.fb[0][0], T, 2 * T), P.fb[0][1]),
           (fbi_bf(P.fb[2][0], 0, T), P.fb[2][1]),
           (fbi_bf(P.fb[2][0], T, 2 * T), P.fb[2][1])]

    def dma_sh(ji):
        wh, wk = slotsA[ji % 2]
        b.dma("POOL", wh.rearrange("p (k c) -> p k c", c=128),
              wg_r[:, :, ji * 128:(ji + 1) * 128], (), (wk,), tag=f"l1shg{ji}")
        wh2, wk2 = slotsB[ji % 2]
        b.dma("POOL", wh2.rearrange("p (k c) -> p k c", c=128),
              wu_r[:, :, ji * 128:(ji + 1) * 128], (), (wk2,), tag=f"l1shu{ji}")
    for ji in range(min(2, 22)):
        dma_sh(ji)
    for ji in range(22):
        wh, wk = slotsA[ji % 2]
        wh2, wk2 = slotsB[ji % 2]
        dh, dk = rot[ji % 2]
        for t in range(NT):
            ps, psk = P.pb[t % 2]
            ps2, psk2 = P.pb[2 + t % 2]
            for k in range(KC):
                b.pe(lambda e, ps=ps, wh=wh, k=k, t=t: e.matmul(
                    ps[:, :TT], wh[:, k * 128:(k + 1) * 128],
                    P.xn[k][0][:, t * TT:(t + 1) * TT],
                    start=(k == 0), stop=(k == KC - 1)),
                    (wk, P.xn[k][1]), (psk,), tag=f"l1sm{ji}_{t}_{k}")
                b.pe(lambda e, ps2=ps2, wh2=wh2, k=k, t=t: e.matmul(
                    ps2[:, :TT], wh2[:, k * 128:(k + 1) * 128],
                    P.xn[k][0][:, t * TT:(t + 1) * TT],
                    start=(k == 0), stop=(k == KC - 1)),
                    (wk2, P.xn[k][1]), (psk2,), tag=f"l1sm2{ji}_{t}_{k}")
            ss = (t % 2) * TT
            b.act(lambda s, ps=ps, ss=ss: s.activation(
                sqh[:, ss:ss + TT], ps[:, :TT], AF.Silu),
                (psk,), (sqk,), tag=f"l1ss{ji}_{t}")
            b.dve(lambda v, ps2=ps2, dh=dh, ss=ss, t=t: v.tensor_mul(
                dh[:, t * TT:(t + 1) * TT], ps2[:, :TT], sqh[:, ss:ss + TT]),
                (psk2, sqk), (dk,), tag=f"l1sh{ji}_{t}")
        if ji + 2 < 22:
            dma_sh(ji + 2)
        b.dma("POOL", P.ins["hscr"][ji * 128:(ji + 1) * 128, :], dh[:, :],
              (dk,), (f"hscr{ji}",), tag=f"l1so{ji}")
    shd_r = ins["l1_shdT"].rearrange("(k p) n -> p k n", p=128)
    first = [True]
    for jp in ([0, 1], [2, 3], [4, 5], [6, 7]):
        wks = []
        for jj, j in enumerate(jp):
            s0 = jj * 4
            whj = P.wfull[0][:, s0 * 1024:s0 * 1024 + 22 * 128]
            wkj = tuple(P.wfull[1] + f":{s0 + i}" for i in range(3))
            b.dma("POOL", whj.rearrange("p (k c) -> p k c", c=128),
                  shd_r[:, :, j * 128:(j + 1) * 128], (), wkj,
                  tag=f"l1dw{j}")
            wks.append((whj, wkj))
        for k in range(22):
            rh, rk = rot[2 + k % 2]
            b.dma("POOL", rh[:, :], P.ins["hscr"][k * 128:(k + 1) * 128, :],
                  (f"hscr{k}",), (rk + "_s",), tag=f"l1dr{jp[0]}_{k}")
            for jj, j in enumerate(jp):
                whj, wkj = wks[jj]
                for t in range(NT):
                    ps, psk = P.pb[jj * 4 + t]
                    b.pe(lambda e2, ps=ps, whj=whj, k=k, t=t, rh=rh:
                         e2.matmul(ps[:, :TT], whj[:, k * 128:(k + 1) * 128],
                                   rh[:, t * TT:(t + 1) * TT],
                                   start=(k == 0), stop=(k == 21)),
                         (wkj[k // 8], rk + "_s"), (psk,),
                         tag=f"l1dm{j}_{k}_{t}")
        for jj, j in enumerate(jp):
            gh, gk = P.gb[j]
            for t in range(NT):
                ps, psk = P.pb[jj * 4 + t]
                ss = (t % 2) * TT
                b.act(lambda s, ps=ps, ss=ss: s.activation(
                    sqh[:, ss:ss + TT], ps[:, :TT], AF.Copy),
                    (psk,), (sqk,), tag=f"l1dc{j}_{t}")
                bcast_row(b, f"l1sb{j}_{t}",
                          (rows_h, rows_k, slice(t * TT, (t + 1) * TT)),
                          (ps, psk), (P.oner[0], P.oner[1]))
                b.dve(lambda v, ps=ps, gh=gh, ss=ss, t=t: v.tensor_mul(
                    gh[:, t * TT:(t + 1) * TT], sqh[:, ss:ss + TT],
                    ps[:, :TT]),
                    (sqk, psk, gk), (gk,), tag=f"l1de{j}_{t}")
    def epi_g1(j, t, ph, pk):
        b.dve(lambda v, ph=ph, t=t: v.tensor_copy(
            P.fb[1][0][0:16, t * TT:(t + 1) * TT], ph[0:16, :TT]),
            (pk,), (P.fb[1][1],), tag=f"l1gl{t}")
    mm_dense(b, "l1g", ins["l1_gateT"], D, 16, P.xn, P.pb[:6],
             [(P.wbB[0][:, 0:128], P.wbB[1] + ":0")], epi_g1, blk_cols=16)
    emit_topk_probs(P, "l1tk", P.fb[1], P.fb[1], 16, 2, eps=0.0)
    upslots = [(P.wbA[0][:, s * 1024:(s + 1) * 1024], P.wbA[1] + f":{s}")
               for s in range(2)] + \
              [(P.wbB[0][:, s * 1024:(s + 1) * 1024], P.wbB[1] + f":{s}")
               for s in range(2)]
    dnslots = [(P.wfull[0][:, s * 1024:(s + 1) * 1024], P.wfull[1] + f":{s}")
               for s in range(8)]
    wup_r = ins["l1_wgu"].rearrange("e (k p) n -> e p k n", p=128)
    wdn_r = ins["l1_wdn"].rearrange("e (k p) n -> e p k n", p=128)
    emit_moe_std(P, "l1m", 16, 512, wup_r, wdn_r, P.xn,
                 [P.bb[8 + k] for k in range(4)], P.gb, upslots, dnslots,
                 acc_init_first=False)
    for j in range(8):
        gh, gk = P.gb[j]
        b.act(lambda s, gh=gh: s.activation(gh[:, :], gh[:, :], AF.Silu),
              (gk,), (gk,), tag=f"l1gs{j}")

    adn = load_smallvec(P, "l1adn", "l1_adn2d", (128, 3))
    alog = (adn[0][:, 0:1], adn[1])
    dtb = (adn[0][:, 1:2], adn[1])
    nw = (adn[0][:, 2:3], adn[1])
    cvq = load_smallvec(P, "l1cvq", "l1_convq2d", (128, 32), F32, "POOL")
    cvk = load_smallvec(P, "l1cvk", "l1_convk2d", (128, 32), F32, "POOL")
    cvv = load_smallvec(P, "l1cvv", "l1_convv2d", (128, 64), F32, "POOL")
    ut01 = P.smalls["l0ut"] if "l0ut" in P.smalls else \
        load_smallvec(P, "l0ut", "c_ut01", (128, 128))
    ls01 = P.smalls["l0ls"] if "l0ls" in P.smalls else \
        load_smallvec(P, "l0ls", "c_ls01", (128, 128))
    radt = P.small("l0radt", [128, 128], F32)
    sst = P.small("l1s", [64, 128], BF16)
    eps6b = b.const(1e-6)
    eps5b = b.const(1e-5)
    one_cb = b.const(1.0)

    def epi_ba(j, t, ph, pk):
        if j == 0:
            b.act(lambda s, ph=ph, t=t: s.activation(
                P.fb[0][0][0:16, t * TT:(t + 1) * TT], ph[0:16, :TT],
                AF.Sigmoid), (pk,), (P.fb[0][1],), tag=f"l1be{t}")
        else:
            b.dve(lambda v, ph=ph, t=t: v.tensor_copy(
                P.fb[1][0][0:16, t * TT:(t + 1) * TT], ph[0:16, :TT]),
                (pk,), (P.fb[1][1],), tag=f"l1ae{t}")
    mm_dense(b, "l1ba", ins["l1_wba"], D, 256, P.xn, P.pb[:6],
             [(P.wbA[0][:, s * 1024:(s + 1) * 1024], P.wbA[1] + f":{s}")
              for s in range(2)], epi_ba)
    BEh, BEk = P.fb[0][0], P.fb[0][1]
    GAh, GAk = P.fb[1][0], P.fb[1][1]
    b.dve(lambda v: v.tensor_scalar_mul(BEh[0:16, :], BEh[0:16, :], 2.0),
          (BEk,), (BEk,), tag="l1b2")
    b.dve(lambda v: v.tensor_scalar_add(GAh[0:16, :], GAh[0:16, :],
                                        dtb[0][0:16, :]),
          (GAk, dtb[1]), (GAk,), tag="l1sdtb")
    for th in range(2):
        c = slice(th * 1024, (th + 1) * 1024)
        b.act(lambda s, c=c: s.activation(sqh[0:16, 0:1024], GAh[0:16, c],
                                          AF.Abs), (GAk,), (sqk,),
              tag=f"l1sab{th}")
        b.act(lambda s: s.activation(sqh[0:16, 0:1024], sqh[0:16, 0:1024],
                                     AF.Exp, scale=-1.0), (sqk,), (sqk,),
              tag=f"l1sex{th}")
        b.act(lambda s: s.activation(sqh[0:16, 0:1024], sqh[0:16, 0:1024],
                                     AF.Ln, bias=one_cb[0:16, :]),
              (sqk, "cpool"), (sqk,), tag=f"l1sln{th}")
        b.dve(lambda v, c=c: v.tensor_scalar_max(GAh[0:16, c], GAh[0:16, c],
                                                 0.0),
              (GAk,), (GAk,), tag=f"l1smx{th}")
        b.dve(lambda v, c=c: v.tensor_add(GAh[0:16, c], GAh[0:16, c],
                                          sqh[0:16, 0:1024]),
              (GAk, sqk), (GAk,), tag=f"l1sad{th}")
    b.act(lambda s: s.activation(P.stat[0][0:16, 0:1], alog[0][0:16, :],
                                 AF.Exp), (alog[1],), (P.stat[1],),
          tag="l1aexp")
    b.dve(lambda v: v.tensor_scalar_mul(GAh[0:16, :], GAh[0:16, :],
                                        P.stat[0][0:16, 0:1]),
          (GAk, P.stat[1]), (GAk,), tag="l1ga1")
    b.dve(lambda v: v.tensor_scalar_mul(GAh[0:16, :], GAh[0:16, :], -1.0),
          (GAk,), (GAk,), tag="l1ga2")
    for c in range(16):
        ph, pk = P.pb[6]
        b.pe(lambda e, c=c, ph=ph: e.transpose(
            ph[0:128, 0:16], GAh[0:16, c * 128:(c + 1) * 128],
            P.idf[0][0:16, 0:16]), (GAk, P.idf[1]), (pk,), tag=f"l1tg{c}")
        b.dve(lambda v, c=c, ph=ph: v.tensor_copy(
            sqh[:, c * 16:(c + 1) * 16], ph[0:128, 0:16]),
            (pk,), (sqk,), tag=f"l1tge{c}")
        ph2, pk2 = P.pb[7]
        b.pe(lambda e, c=c, ph2=ph2: e.transpose(
            ph2[0:128, 0:16], BEh[0:16, c * 128:(c + 1) * 128],
            P.idf[0][0:16, 0:16]), (BEk, P.idf[1]), (pk2,), tag=f"l1tb{c}")
        b.dve(lambda v, c=c, ph2=ph2: v.tensor_copy(
            sqh[:, 256 + c * 16:256 + (c + 1) * 16], ph2[0:128, 0:16]),
            (pk2,), (sqk,), tag=f"l1tbe{c}")

    wq_r = ins["l1_wq"].rearrange("(k p) n -> p k n", p=128)
    wk2_r = ins["l1_wk"].rearrange("(k p) n -> p k n", p=128)
    wv_r = ins["l1_wv"].rearrange("(k p) n -> p k n", p=128)
    wgg_r = ins["l1_wg"].rearrange("(k p) n -> p k n", p=128)
    wo_r = ins["l1_wout"].rearrange("(h p) n -> p h n", p=128)
    sacc = [(fbi_bf(P.fb[0][0], 0, T), P.fb[0][1]),
            (fbi_bf(P.fb[0][0], T, 2 * T), P.fb[0][1]),
            (fbi_bf(P.fb[1][0], 0, T), P.fb[1][1]),
            (fbi_bf(P.fb[1][0], T, 2 * T), P.fb[1][1]),
            (fbi_bf(P.fb[2][0], 0, T), P.fb[2][1]),
            (fbi_bf(P.fb[2][0], T, 2 * T), P.fb[2][1]),
            P.bb[13], P.bb[14]]
    wqsl = [(P.wbA[0][:, s * 1024:(s + 1) * 1024], P.wbA[1] + f":{s}")
            for s in range(2)]
    wksl = [(P.wbB[0][:, s * 1024:(s + 1) * 1024], P.wbB[1] + f":{s}")
            for s in range(2)]
    wvsl = [(P.wfull[0][:, s * 1024:(s + 1) * 1024], P.wfull[1] + f":{s}")
            for s in range(8)]

    def dma_h1(h):
        wh, wk = wqsl[h % 2]
        b.dma("POOL", wh[:, 0:512].rearrange("p (k c) -> p k c", c=64),
              wq_r[:, :, h * 64:(h + 1) * 64], (), (wk,), tag=f"l1wq{h}")
        b.dma("POOL", wh[:, 512:1024].rearrange("p (k c) -> p k c", c=64),
              wk2_r[:, :, h * 64:(h + 1) * 64], (), (wk,), tag=f"l1wk{h}")
        wh2, wk2 = wksl[h % 2]
        b.dma("POOL", wh2[:, 0:1024].rearrange("p (k c) -> p k c", c=128),
              wv_r[:, :, h * 128:(h + 1) * 128], (), (wk2,), tag=f"l1wv{h}")
        wh3, wk3 = wvsl[h % 4]
        b.dma("POOL", wh3.rearrange("p (k c) -> p k c", c=128),
              wgg_r[:, :, h * 128:(h + 1) * 128], (), (wk3,), tag=f"l1wg{h}")
        wh4, wk4 = wvsl[4 + h % 4]
        b.dma("POOL", wh4.rearrange("p (k c) -> p k c", c=128),
              wo_r[:, h, :].rearrange("p (j c) -> p j c", c=128),
              (), (wk4,), tag=f"l1wo{h}")
    dma_h1(0)
    for h in range(16):
        if h + 1 < 16:
            dma_h1(h + 1)
        wh, wk = wqsl[h % 2]
        wh2, wk2 = wksl[h % 2]
        wh3, wk3 = wvsl[h % 4]
        for (wsl_, wke_, cols, dstt, dbase, nrow) in (
                (wh[:, 0:512], wk, 64, P.bb[0], 0, 64),
                (wh[:, 512:1024], wk, 64, P.bb[7], 0, 64),
                (wh2[:, 0:1024], wk2, 128, P.bb[4], 0, 128)):
            for t in range(NT):
                ps, psk = P.pb[t % 2]
                for k in range(KC):
                    b.pe(lambda e, ps=ps, wsl_=wsl_, k=k, t=t, cols=cols:
                         e.matmul(ps[0:cols, :TT],
                                  wsl_[:, k * cols:(k + 1) * cols],
                                  P.xn[k][0][:, t * TT:(t + 1) * TT],
                                  start=(k == 0), stop=(k == KC - 1)),
                         (wke_, P.xn[k][1]), (psk,), tag=f"l1p{h}_{id(wsl_)}_{t}_{k}")
                dh, dk = dstt
                b.act(lambda s, ps=ps, dh=dh, t=t, cols=cols: s.activation(
                    dh[0:cols, t * TT:(t + 1) * TT], ps[0:cols, :TT],
                    AF.Copy), (psk,), (dk,), tag=f"l1pe{h}_{id(wsl_)}_{t}")
        for t in range(NT):
            ps, psk = P.pb[t % 2]
            for k in range(KC):
                b.pe(lambda e, ps=ps, wh3=wh3, k=k, t=t: e.matmul(
                    ps[:, :TT], wh3[:, k * 128:(k + 1) * 128],
                    P.xn[k][0][:, t * TT:(t + 1) * TT],
                    start=(k == 0), stop=(k == KC - 1)),
                    (wk3, P.xn[k][1]), (psk,), tag=f"l1pg{h}_{t}_{k}")
            b.act(lambda s, ps=ps, t=t: s.activation(
                P.bb[3][0][:, t * TT:(t + 1) * TT], ps[:, :TT], AF.Silu),
                (psk,), (P.bb[3][1],), tag=f"l1pge{h}_{t}")
        for (src_, sk_, cw, nr, dst_, dk_) in (
                (P.bb[0], P.bb[0][1], cvq, 64, P.bb[1], P.bb[1][1]),
                (P.bb[7], P.bb[7][1], cvk, 64, P.bb[2], P.bb[2][1]),
                (P.bb[4], P.bb[4][1], cvv, 128, P.bb[5], P.bb[5][1])):
            sh_, _ = src_
            dh_, _ = dst_
            ccol = lambda j, hh=h: cw[0][0:nr, (hh % (32 // 4 if nr == 64 else 16)) * 0 + 0:1]
            coff = (h % 2) * 0
            b.dve(lambda v, sh_=sh_, dh_=dh_, cw=cw, nr=nr, h=h: v.tensor_scalar_mul(
                dh_[0:nr, :], sh_[0:nr, :],
                cw[0][0:nr, conv_col(h, nr, 3):conv_col(h, nr, 3) + 1]),
                (sk_, cw[1]), (dk_,), tag=f"l1cv3{h}_{id(sh_)}")
            for j in (2, 1, 0):
                a = 3 - j
                b.dve(lambda v, sh_=sh_, dh_=dh_, cw=cw, nr=nr, a=a, j=j, h=h:
                      v.scalar_tensor_tensor(
                          dh_[0:nr, a:T], sh_[0:nr, 0:T - a],
                          cw[0][0:nr, conv_col(h, nr, j):conv_col(h, nr, j) + 1],
                          dh_[0:nr, a:T], ALU.mult, ALU.add),
                      (sk_, cw[1], dk_), (dk_,), tag=f"l1cv{j}_{h}_{id(sh_)}")
            b.act(lambda s, dh_=dh_, nr=nr: s.activation(
                dh_[0:nr, :], dh_[0:nr, :], AF.Silu),
                (dk_,), (dk_,), tag=f"l1cvs{h}_{id(sh_)}")
        for (th_, tk_) in ((P.bb[1][0], P.bb[1][1]), (P.bb[2][0], P.bb[2][1])):
            for t in range(NT):
                b.act(lambda s, th_=th_, t=t: s.activation(
                    sqh[:, 512:512 + TT][0:64, :], th_[0:64, t * TT:(t + 1) * TT],
                    AF.Square), (tk_,), (sqk,), tag=f"l1nq{h}_{t}_{id(th_)}")
                pr, prk = P.pb[6]
                b.pe(lambda e, pr=pr: e.matmul(
                    pr[0:1, :TT], P.onec[0][0:64, 0:1], sqh[0:64, 512:512 + TT],
                    start=True, stop=True),
                    (P.onec[1], sqk), (prk,), tag=f"l1nm{h}_{t}_{id(th_)}")
                b.dve(lambda v, pr=pr, t=t: v.tensor_copy(
                    rows_h[0:1, t * TT:(t + 1) * TT], pr[0:1, :TT]),
                    (prk,), (rows_k,), tag=f"l1nr{h}_{t}_{id(th_)}")
            b.act(lambda s: s.activation(rows_h[0:1, 0:T], rows_h[0:1, 0:T],
                                         AF.Ln, bias=eps6b[0:1, :]),
                  (rows_k, "cpool"), (rows_k,), tag=f"l1nl{h}_{id(th_)}")
            b.act(lambda s: s.activation(rows_h[0:1, 0:T], rows_h[0:1, 0:T],
                                         AF.Exp, scale=-0.5),
                  (rows_k,), (rows_k,), tag=f"l1ne{h}_{id(th_)}")
            for t in range(NT):
                bcast_row(b, f"l1nb{h}_{t}_{id(th_)}",
                          (rows_h, rows_k, slice(t * TT, (t + 1) * TT)),
                          P.pb[6], (P.oner[0], P.oner[1]))
                b.dve(lambda v, th_=th_, t=t: v.tensor_mul(
                    th_[0:64, t * TT:(t + 1) * TT],
                    th_[0:64, t * TT:(t + 1) * TT],
                    P.pb[6][0][0:64, :TT]),
                    (tk_, P.pb[6][1]), (tk_,), tag=f"l1nn{h}_{t}_{id(th_)}")
        for c in range(16):
            tph, tpk = P.pb[6 + (c % 2)]
            b.pe(lambda e, c=c, tph=tph: e.transpose(
                tph[:, :].bitcast(BF16)[:, 0:128],
                P.bb[5][0][:, c * 128:(c + 1) * 128], P.idb[0][:, :]),
                (P.bb[5][1], P.idb[1]), (tpk,), tag=f"l1vt{h}_{c}")
            b.act(lambda s, c=c, tph=tph: s.activation(
                P.bb[6][0][:, c * 128:(c + 1) * 128],
                tph[:, :].bitcast(BF16)[:, 0:128], AF.Copy),
                (tpk,), (P.bb[6][1],), tag=f"l1vte{h}_{c}")
            b.pe(lambda e, c=c, tph=tph: e.transpose(
                tph[:, :].bitcast(BF16)[:, 128:192],
                P.bb[2][0][0:64, c * 128:(c + 1) * 128], P.idb[0][0:64, 0:64]),
                (P.bb[2][1], P.idb[1]), (tpk,), tag=f"l1kt{h}_{c}")
            b.act(lambda s, c=c, tph=tph: s.activation(
                P.bb[8][0][:, c * 64:(c + 1) * 64],
                tph[:, :].bitcast(BF16)[:, 128:192], AF.Copy),
                (tpk,), (P.bb[8][1],), tag=f"l1kte{h}_{c}")
            b.pe(lambda e, c=c, tph=tph: e.transpose(
                tph[:, :].bitcast(BF16)[:, 192:256],
                P.bb[1][0][0:64, c * 128:(c + 1) * 128], P.idb[0][0:64, 0:64]),
                (P.bb[1][1], P.idb[1]), (tpk,), tag=f"l1qt{h}_{c}")
            b.act(lambda s, c=c, tph=tph: s.activation(
                P.bb[8][0][:, 1024 + c * 64:1024 + (c + 1) * 64],
                tph[:, :].bitcast(BF16)[:, 192:256], AF.Copy),
                (tpk,), (P.bb[8][1],), tag=f"l1qte{h}_{c}")
        b.dve(lambda v: v.memset(sst[0][:, :], 0.0), (), (sst[1],),
              tag=f"l1sz{h}")
        for c in range(16):
            gcol = sqh[:, c * 16 + h:c * 16 + h + 1]
            bcol = sqh[:, 256 + c * 16 + h:256 + c * 16 + h + 1]
            b.dve(lambda v, gcol=gcol: v.tensor_scalar_mul(
                radt[0][:, :], ut01[0][:, :], gcol),
                (ut01[1], sqk), (radt[1],), tag=f"l1cr{h}_{c}")
            psc, psck = P.pb[2]
            b.pe(lambda e, psc=psc: e.matmul(
                psc[:, 0:1], radt[0][:, :], P.onec[0][:, 0:1],
                start=True, stop=True),
                (radt[1], P.onec[1]), (psck,), tag=f"l1cc{h}_{c}")
            b.act(lambda s, psc=psc: s.activation(P.stat[0][:, 4:5],
                                                  psc[:, 0:1], AF.Exp),
                  (psck,), (P.stat[1],), tag=f"l1cce{h}_{c}")
            b.pe(lambda e, psc=psc, gcol=gcol: e.matmul(
                psc[:, 1:2], ls01[0][:, :], gcol, start=True, stop=True),
                (ls01[1], sqk), (psck,), tag=f"l1cw3{h}_{c}")
            b.act(lambda s, psc=psc: s.activation(P.stat[0][:, 5:6],
                                                  psc[:, 1:2], AF.Exp),
                  (psck,), (P.stat[1],), tag=f"l1cw3e{h}_{c}")
            b.pe(lambda e, psc=psc, gcol=gcol: e.matmul(
                psc[0:1, 2:3], P.onec[0][:, 0:1], gcol, start=True, stop=True),
                (P.onec[1], sqk), (psck,), tag=f"l1gs{h}_{c}")
            b.dve(lambda v, psc=psc: v.tensor_copy(P.stat[0][0:1, 6:7],
                                                   psc[0:1, 2:3]),
                  (psck,), (P.stat[1],), tag=f"l1gse{h}_{c}")
            b.pe(lambda e, psc=psc: e.matmul(
                psc[0:64, 3:4], P.oner[0][0:1, 0:64], P.stat[0][0:1, 6:7],
                start=True, stop=True),
                (P.oner[1], P.stat[1]), (psck,), tag=f"l1gb{h}_{c}")
            b.act(lambda s, psc=psc: s.activation(P.stat[0][0:64, 7:8],
                                                  psc[0:64, 3:4], AF.Exp),
                  (psck,), (P.stat[1],), tag=f"l1gbe{h}_{c}")
            psA, psAk = P.pb[0]
            b.pe(lambda e, psA=psA: e.matmul(psA[:, 0:128], radt[0][:, :],
                                             ls01[0][:, :], start=True,
                                             stop=False),
                 (radt[1], ls01[1]), (psAk,), tag=f"l1eA{h}_{c}")
            b.dve(lambda v: v.tensor_scalar_mul(radt[0][:, :],
                                                ut01[0][:, :], NEG_BIG),
                  (ut01[1],), (radt[1],), tag=f"l1mA{h}_{c}")
            b.pe(lambda e, psA=psA: e.matmul(psA[:, 0:128], P.idf[0][:, :],
                                             radt[0][:, :], start=False,
                                             stop=True),
                 (P.idf[1], radt[1]), (psAk,), tag=f"l1mA2{h}_{c}")
            b.act(lambda s, psA=psA: s.activation(
                P.bb[7][0][:, 1024:1152], psA[:, 0:128], AF.Exp),
                (psAk,), (P.bb[7][1],), tag=f"l1eAe{h}_{c}")
            b.dve(lambda v, gcol=gcol: v.tensor_scalar_mul(
                radt[0][:, :], ut01[0][:, :], gcol),
                (ut01[1], sqk), (radt[1],), tag=f"l1cr2{h}_{c}")
            psW, psWk = P.pb[1]
            b.pe(lambda e, psW=psW: e.matmul(psW[:, 0:128], ls01[0][:, :],
                                             radt[0][:, :], start=True,
                                             stop=False),
                 (ls01[1], radt[1]), (psWk,), tag=f"l1eW{h}_{c}")
            b.dve(lambda v: v.tensor_scalar_mul(radt[0][:, :],
                                                ls01[0][:, :], NEG_BIG),
                  (ls01[1],), (radt[1],), tag=f"l1mW{h}_{c}")
            b.pe(lambda e, psW=psW: e.matmul(psW[:, 0:128], P.idf[0][:, :],
                                             radt[0][:, :], start=False,
                                             stop=True),
                 (P.idf[1], radt[1]), (psWk,), tag=f"l1mW2{h}_{c}")
            b.act(lambda s, psW=psW: s.activation(
                P.bb[7][0][:, 1152:1280], psW[:, 0:128], AF.Exp),
                (psWk,), (P.bb[7][1],), tag=f"l1eWe{h}_{c}")
            psK, psKk = P.pb[3]
            b.pe(lambda e, psK=psK, c=c: e.matmul(
                psK[:, 0:128], P.bb[2][0][0:64, c * 128:(c + 1) * 128],
                P.bb[2][0][0:64, c * 128:(c + 1) * 128],
                start=True, stop=True),
                (P.bb[2][1],), (psKk,), tag=f"l1kk{h}_{c}")
            b.dve(lambda v, psK=psK, bcol=bcol: v.scalar_tensor_tensor(
                P.bb[7][0][:, 256:384], psK[:, 0:128], bcol,
                P.bb[7][0][:, 1024:1152], ALU.mult, ALU.mult),
                (psKk, sqk, P.bb[7][1]), (P.bb[7][1],), tag=f"l1A{h}_{c}")
            psQ, psQk = P.pb[4]
            b.pe(lambda e, psQ=psQ, c=c: e.matmul(
                psQ[:, 0:128], P.bb[2][0][0:64, c * 128:(c + 1) * 128],
                P.bb[1][0][0:64, c * 128:(c + 1) * 128],
                start=True, stop=True),
                (P.bb[2][1], P.bb[1][1]), (psQk,), tag=f"l1kq{h}_{c}")
            b.dve(lambda v, psQ=psQ: v.tensor_mul(
                P.bb[7][0][:, 896:1024], psQ[:, 0:128],
                P.bb[7][0][:, 1152:1280]),
                (psQk, P.bb[7][1]), (P.bb[7][1],), tag=f"l1pp{h}_{c}")
            psR, psRk = P.pb[5]
            b.pe(lambda e, psR=psR, c=c: e.matmul(
                psR[:, 0:128], P.bb[2][0][0:64, c * 128:(c + 1) * 128],
                sst[0][:, :], start=True, stop=True),
                (P.bb[2][1], sst[1]), (psRk,), tag=f"l1ks{h}_{c}")
            b.dve(lambda v, c=c, bcol=bcol: v.tensor_scalar_mul(
                P.bb[7][0][:, 768:896], P.bb[6][0][:, c * 128:(c + 1) * 128],
                bcol),
                (P.bb[6][1], sqk), (P.bb[7][1],), tag=f"l1bv{h}_{c}")
            b.dve(lambda v, bcol=bcol: v.tensor_mul(
                P.stat[0][:, 8:9], P.stat[0][:, 4:5], bcol),
                (P.stat[1], sqk), (P.stat[1],), tag=f"l1bcg{h}_{c}")
            b.dve(lambda v: v.tensor_scalar_mul(P.stat[0][:, 8:9],
                                                P.stat[0][:, 8:9], -1.0),
                  (P.stat[1],), (P.stat[1],), tag=f"l1bcg2{h}_{c}")
            b.dve(lambda v, psR=psR: v.scalar_tensor_tensor(
                P.bb[7][0][:, 640:768], psR[:, 0:128], P.stat[0][:, 8:9],
                P.bb[7][0][:, 768:896], ALU.mult, ALU.add),
                (psRk, P.stat[1], P.bb[7][1]), (P.bb[7][1],),
                tag=f"l1R{h}_{c}")
            tph, tpk = P.pb[6]
            b.pe(lambda e, tph=tph: e.transpose(
                tph[:, :].bitcast(BF16)[:, 0:128], P.bb[7][0][:, 256:384],
                P.idb[0][:, :]),
                (P.bb[7][1], P.idb[1]), (tpk,), tag=f"l1at{h}_{c}")
            b.act(lambda s, tph=tph: s.activation(
                P.bb[7][0][:, 512:640], tph[:, :].bitcast(BF16)[:, 0:128],
                AF.Copy), (tpk,), (P.bb[7][1],), tag=f"l1ate{h}_{c}")
            psX, psXk = P.pb[2]
            b.pe(lambda e, psX=psX: e.matmul(
                psX[:, 0:128], P.bb[7][0][:, 512:640],
                P.bb[7][0][:, 640:768], start=True, stop=True),
                (P.bb[7][1],), (psXk,), tag=f"l1x0{h}_{c}")
            b.dve(lambda v, psX=psX: v.tensor_sub(
                P.bb[7][0][:, 640:768], P.bb[7][0][:, 640:768],
                psX[:, 0:128]),
                (psXk, P.bb[7][1]), (P.bb[7][1],), tag=f"l1x0e{h}_{c}")
            for lev in range(6):
                psN, psNk = P.pb[2]
                nsrc = slice(256, 384) if lev == 0 else slice(384, 512)
                b.pe(lambda e, psN=psN, nsrc=nsrc: e.matmul(
                    psN[:, 0:128], P.bb[7][0][:, 512:640],
                    P.bb[7][0][:, nsrc], start=True, stop=True),
                    (P.bb[7][1],), (psNk,), tag=f"l1n{h}_{c}_{lev}")
                b.dve(lambda v, psN=psN: v.tensor_copy(
                    P.bb[7][0][:, 384:512], psN[:, 0:128]),
                    (psNk, P.bb[7][1]), (P.bb[7][1],), tag=f"l1ne{h}_{c}_{lev}")
                tph2, tpk2 = P.pb[6]
                b.pe(lambda e, tph2=tph2: e.transpose(
                    tph2[:, :].bitcast(BF16)[:, 128:256],
                    P.bb[7][0][:, 384:512], P.idb[0][:, :]),
                    (P.bb[7][1], P.idb[1]), (tpk2,), tag=f"l1nt{h}_{c}_{lev}")
                b.act(lambda s, tph2=tph2: s.activation(
                    P.bb[7][0][:, 512:640],
                    tph2[:, :].bitcast(BF16)[:, 128:256], AF.Copy),
                    (tpk2,), (P.bb[7][1],), tag=f"l1nte{h}_{c}_{lev}")
                psX2, psX2k = P.pb[2]
                b.pe(lambda e, psX2=psX2: e.matmul(
                    psX2[:, 0:128], P.bb[7][0][:, 512:640],
                    P.bb[7][0][:, 640:768], start=True, stop=True),
                    (P.bb[7][1],), (psX2k,), tag=f"l1xa{h}_{c}_{lev}")
                b.dve(lambda v, psX2=psX2: v.tensor_add(
                    P.bb[7][0][:, 640:768], P.bb[7][0][:, 640:768],
                    psX2[:, 0:128]),
                    (psX2k, P.bb[7][1]), (P.bb[7][1],),
                    tag=f"l1xae{h}_{c}_{lev}")
            b.dve(lambda v, c=c: v.tensor_scalar_mul(
                P.bb[7][0][:, 0:64], P.bb[8][0][:, 1024 + c * 64:1024 + (c + 1) * 64],
                P.stat[0][:, 4:5]),
                (P.bb[8][1], P.stat[1]), (P.bb[7][1],), tag=f"l1ex{h}_{c}")
            tph3, tpk3 = P.pb[6]
            b.pe(lambda e, tph3=tph3: e.transpose(
                tph3[:, :].bitcast(BF16)[0:64, 256:384],
                P.bb[7][0][:, 0:64], P.idb[0][:, :]),
                (P.bb[7][1], P.idb[1]), (tpk3,), tag=f"l1ext{h}_{c}")
            b.act(lambda s, tph3=tph3: s.activation(
                P.bb[7][0][0:64, 64:192],
                tph3[:, :].bitcast(BF16)[0:64, 256:384], AF.Copy),
                (tpk3,), (P.bb[7][1],), tag=f"l1exte{h}_{c}")
            psO, psOk = P.pb[3]
            b.pe(lambda e, psO=psO: e.matmul(
                psO[:, 0:128], P.bb[7][0][:, 640:768],
                P.bb[7][0][:, 896:1024], start=True, stop=False),
                (P.bb[7][1],), (psOk,), tag=f"l1o1{h}_{c}")
            b.pe(lambda e, psO=psO: e.matmul(
                psO[:, 0:128], sst[0][:, :], P.bb[7][0][0:64, 64:192],
                start=False, stop=True),
                (sst[1], P.bb[7][1]), (psOk,), tag=f"l1o2{h}_{c}")
            b.dve(lambda v, psO=psO, c=c: v.tensor_scalar_mul(
                P.bb[9][0][:, c * 128:(c + 1) * 128], psO[:, 0:128], 0.125),
                (psOk,), (P.bb[9][1],), tag=f"l1oe{h}_{c}")
            b.dve(lambda v, c=c: v.tensor_scalar_mul(
                P.bb[7][0][:, 192:256], P.bb[8][0][:, c * 64:(c + 1) * 64],
                P.stat[0][:, 5:6]),
                (P.bb[8][1], P.stat[1]), (P.bb[7][1],), tag=f"l1kw{h}_{c}")
            psS, psSk = P.pb[5]
            b.pe(lambda e, psS=psS: e.matmul(
                psS[0:64, 0:128], P.bb[7][0][:, 192:256],
                P.bb[7][0][:, 640:768], start=True, stop=True),
                (P.bb[7][1],), (psSk,), tag=f"l1ds{h}_{c}")
            b.dve(lambda v, psS=psS: v.scalar_tensor_tensor(
                sst[0][:, :], sst[0][:, :], P.stat[0][0:64, 7:8],
                psS[0:64, 0:128], ALU.mult, ALU.add),
                (sst[1], P.stat[1], psSk), (sst[1],), tag=f"l1su{h}_{c}")
        for t in range(NT):
            b.act(lambda s, t=t: s.activation(
                sqh[:, 512:512 + TT], P.bb[9][0][:, t * TT:(t + 1) * TT],
                AF.Square), (P.bb[9][1],), (sqk,), tag=f"l1or{h}_{t}")
            pr, prk = P.pb[6]
            b.pe(lambda e, pr=pr: e.matmul(
                pr[0:1, :TT], P.onec[0][:, 0:1], sqh[:, 512:512 + TT],
                start=True, stop=True),
                (P.onec[1], sqk), (prk,), tag=f"l1orm{h}_{t}")
            b.dve(lambda v, pr=pr, t=t: v.tensor_scalar_mul(
                rows_h[0:1, t * TT:(t + 1) * TT], pr[0:1, :TT], 1.0 / 128),
                (prk,), (rows_k,), tag=f"l1orr{h}_{t}")
        b.act(lambda s: s.activation(rows_h[0:1, 0:T], rows_h[0:1, 0:T],
                                     AF.Ln, bias=eps5b[0:1, :]),
              (rows_k, "cpool"), (rows_k,), tag=f"l1orl{h}")
        b.act(lambda s: s.activation(rows_h[0:1, 0:T], rows_h[0:1, 0:T],
                                     AF.Exp, scale=-0.5),
              (rows_k,), (rows_k,), tag=f"l1ore{h}")
        for t in range(NT):
            bcast_row(b, f"l1orb{h}_{t}",
                      (rows_h, rows_k, slice(t * TT, (t + 1) * TT)),
                      P.pb[6], (P.oner[0], P.oner[1]))
            b.dve(lambda v, t=t: v.scalar_tensor_tensor(
                P.bb[5][0][:, t * TT:(t + 1) * TT],
                P.bb[9][0][:, t * TT:(t + 1) * TT], nw[0][:, 0:1],
                P.pb[6][0][:, :TT], ALU.mult, ALU.mult),
                (P.bb[9][1], nw[1], P.pb[6][1]), (P.bb[5][1],),
                tag=f"l1on{h}_{t}")
            b.dve(lambda v, t=t: v.tensor_mul(
                P.bb[5][0][:, t * TT:(t + 1) * TT],
                P.bb[5][0][:, t * TT:(t + 1) * TT],
                P.bb[3][0][:, t * TT:(t + 1) * TT]),
                (P.bb[5][1], P.bb[3][1]), (P.bb[5][1],), tag=f"l1og{h}_{t}")
        wh4, wk4 = wvsl[4 + h % 4]
        for j in range(8):
            for t in range(NT):
                ps, psk = P.pb[(j % 2) * 2 + (t % 2)]
                b.pe(lambda e, ps=ps, wh4=wh4, j=j, t=t: e.matmul(
                    ps[:, :TT], wh4[:, j * 128:(j + 1) * 128],
                    P.bb[5][0][:, t * TT:(t + 1) * TT],
                    start=True, stop=True),
                    (wk4, P.bb[5][1]), (psk,), tag=f"l1so{h}_{j}_{t}")
                sh_, sk_ = sacc[j]
                if h == 0:
                    b.act(lambda s, ps=ps, sh_=sh_, t=t: s.activation(
                        sh_[:, t * TT:(t + 1) * TT], ps[:, :TT], AF.Copy),
                        (psk,), (sk_,), tag=f"l1si{h}_{j}_{t}")
                else:
                    b.dve(lambda v, ps=ps, sh_=sh_, t=t: v.tensor_add(
                        sh_[:, t * TT:(t + 1) * TT],
                        sh_[:, t * TT:(t + 1) * TT], ps[:, :TT]),
                        (psk, sk_), (sk_,), tag=f"l1sa{h}_{j}_{t}")

    for j in range(8):
        sh_, sk_ = sacc[j]
        for th in range(2):
            b.dma("SP", sqh[:, 0:1024],
                  xsrc[j * 128:(j + 1) * 128, th * 1024:(th + 1) * 1024],
                  (() if src_keys is None else (src_keys[j],)), (sqk,),
                  tag=f"l1fx{j}_{th}")
            for tt in range(2):
                t = th * 2 + tt
                pc, pck = P.pb[6]
                b.dve(lambda v, sh_=sh_, j=j, t=t, pc=pc: v.tensor_mul(
                    pc[:, :TT], sh_[:, t * TT:(t + 1) * TT],
                    P.gb[j][0][:, t * TT:(t + 1) * TT]),
                    (sk_, P.gb[j][1]), (pck,), tag=f"l1f1{j}_{t}")
                b.dve(lambda v, tt=tt, pc=pc: v.tensor_add(
                    sqh[:, tt * TT:(tt + 1) * TT],
                    sqh[:, tt * TT:(tt + 1) * TT], pc[:, :TT]),
                    (pck, sqk), (sqk,), tag=f"l1f2{j}_{t}")
            b.dma("SP", xdst[j * 128:(j + 1) * 128, th * 1024:(th + 1) * 1024],
                  sqh[:, 0:1024], (sqk,), (dst_keys[j],), tag=f"l1fw{j}_{th}")


def conv_col(h, nrows, j):
    if nrows == 64:
        return (h // 2) * 4 + j
    return h * 4 + j


def prep_l1(ins, d):
    C = np.ascontiguousarray
    d["l1_sgwT2d"] = _col2d(np.asarray(ins["L1_ffn_shared_gate_w"])[0])
    d["l1_shgT"] = C(np.asarray(ins["L1_ffn_shared_w_gate"]).T)
    d["l1_shuT"] = C(np.asarray(ins["L1_ffn_shared_w_up"]).T)
    d["l1_shdT"] = C(np.asarray(ins["L1_ffn_shared_w_down"]).T)
    d["l1_gateT"] = C(np.asarray(ins["L1_ffn_gate_w"]).T)
    d["l1_wgu"] = np.asarray(ins["L1_ffn_w_gate_up"])
    d["l1_wdn"] = np.asarray(ins["L1_ffn_w_down"])
    d["l1_wq"] = np.asarray(ins["L1_attn_w_q"])
    d["l1_wk"] = np.asarray(ins["L1_attn_w_k"])
    d["l1_wv"] = np.asarray(ins["L1_attn_w_v"])
    d["l1_wg"] = np.asarray(ins["L1_attn_w_g"])
    d["l1_wout"] = np.asarray(ins["L1_attn_w_out"])
    wbA = np.asarray(ins["L1_attn_w_b"])
    waA = np.asarray(ins["L1_attn_w_a"])
    z112 = np.zeros((1024, 112), np.float32)
    d["l1_wba"] = C(np.concatenate([wbA, z112, waA, z112], 1))
    adn = np.zeros((128, 3), np.float32)
    adn[0:16, 0] = np.asarray(ins["L1_attn_A_log"])
    adn[0:16, 1] = np.asarray(ins["L1_attn_dt_bias"])
    adn[:, 2] = np.asarray(ins["L1_attn_norm_w"])
    d["l1_adn2d"] = adn
    cq = np.asarray(ins["L1_attn_conv_q"]).reshape(8, 128, 4)
    ck = np.asarray(ins["L1_attn_conv_k"]).reshape(8, 128, 4)
    cv = np.asarray(ins["L1_attn_conv_v"]).reshape(16, 128, 4)
    d["l1_convq2d"] = C(cq.transpose(1, 0, 2).reshape(128, 32))
    d["l1_convk2d"] = C(ck.transpose(1, 0, 2).reshape(128, 32))
    d["l1_convv2d"] = C(cv.transpose(1, 0, 2).reshape(128, 64))
    r = np.arange(128)
    d["c_ut01"] = C((r[:, None] <= r[None, :]).astype(np.float32))
    d["c_ls01"] = C((r[:, None] > r[None, :]).astype(np.float32))


L1_DEVINS = [
    ("l1_sgwT2d", [128, 8]), ("l1_shgT", [1024, 2816]),
    ("l1_shuT", [1024, 2816]), ("l1_shdT", [2816, 1024]),
    ("l1_gateT", [1024, 16]),
    ("l1_wgu", [16, 1024, 1024]), ("l1_wdn", [16, 512, 1024]),
    ("l1_wq", [1024, 1024]), ("l1_wk", [1024, 1024]),
    ("l1_wv", [1024, 2048]), ("l1_wg", [1024, 2048]),
    ("l1_wout", [2048, 1024]), ("l1_wba", [1024, 256]),
    ("l1_adn2d", [128, 3]),
    ("l1_convq2d", [128, 32]), ("l1_convk2d", [128, 32]),
    ("l1_convv2d", [128, 64]),
    ("c_ut01", [128, 128]), ("c_ls01", [128, 128]),
]
LAYER_DEVINS[1] = L1_DEVINS
LAYER_PREP[1] = prep_l1
LAYER_EMIT[1] = emit_layer1


# version 05 (final)
# speedup vs baseline: 1.0191x; 1.0191x over previous
import numpy as np
from contextlib import ExitStack

import concourse.bass as bass
import concourse.mybir as mybir
from concourse.bass_utils import run_bass_kernel_spmd

F32 = mybir.dt.float32
BF16 = mybir.dt.bfloat16
AF = mybir.ActivationFunctionType
ALU = mybir.AluOpType
AX = mybir.AxisListType

T = 2048
NT = 4
TT = 512
D = 1024
KC = 8

ENGINES = ("PE", "DVE", "ACT", "POOL", "SP")
DMA_QUEUES = ("SP", "ACT", "POOL")


class Op:
    __slots__ = ("eng", "fn", "reads", "writes", "kind", "inc_needed",
                 "deps", "seq", "inc_no", "dma_no", "tag")

    def __init__(self, eng, fn, reads, writes, kind, tag=""):
        self.eng = eng
        self.fn = fn
        self.reads = reads
        self.writes = writes
        self.kind = kind
        self.inc_needed = kind == "dma"
        self.deps = []
        self.seq = None
        self.inc_no = None
        self.dma_no = None
        self.tag = tag


class B:

    def __init__(self, nc, st):
        self.nc = nc
        self.st = st
        self.q = {e: [] for e in ENGINES}
        self.last_writer = {}
        self.readers = {}
        self.esem = {e: st.enter_context(nc.semaphore(f"sem_{e}")) for e in ENGINES}
        self.esem2 = {e: [self.esem[e]] + [
            st.enter_context(nc.semaphore(f"sem_{e}b{i}")) for i in range(1, 3)]
            for e in ENGINES}
        self.NDSEM = 6
        self.dsem = {e: [st.enter_context(nc.semaphore(f"dsem_{e}{i}"))
                         for i in range(self.NDSEM)] for e in DMA_QUEUES}
        self.n_ops = 0
        self._consts = {}
        self.cpool = st.enter_context(nc.sbuf_tensor("cpool", [128, 16], F32))
        self._cpool_n = 0

    def const(self, val):
        if val not in self._consts:
            idx = self._cpool_n
            assert idx < 16
            self._cpool_n += 1
            ap = self.cpool[:, idx:idx + 1]
            self.nc.const_aps.register(F32, float(val), ap)
            self.dve(lambda v, ap=ap, val=val: v.memset(ap, float(val)),
                     reads=(), writes=("cpool",), tag="const")
            self._consts[val] = ap
        return self._consts[val]

    def _op(self, eng, fn, reads, writes, kind, tag=""):
        op = Op(eng, fn, tuple(reads), tuple(writes), kind, tag)
        op.seq = len(self.q[eng])
        self.q[eng].append(op)
        self.n_ops += 1
        deps = {}

        def add_dep(p):
            if p is None or p is op:
                return
            if p.eng == op.eng and p.eng == "PE" and p.kind == "compute" and kind == "compute":
                return
            deps[id(p)] = p

        for k in reads:
            add_dep(self.last_writer.get(k))
        for k in writes:
            add_dep(self.last_writer.get(k))
            for r in self.readers.get(k, ()):
                add_dep(r)
        op.deps = list(deps.values())
        for k in writes:
            self.last_writer[k] = op
            self.readers[k] = []
        for k in reads:
            self.readers.setdefault(k, []).append(op)
        return op

    def pe(self, fn, reads, writes, tag=""):
        return self._op("PE", fn, reads, writes, "compute", tag)

    def dve(self, fn, reads, writes, tag=""):
        return self._op("DVE", fn, reads, writes, "compute", tag)

    def act(self, fn, reads, writes, tag=""):
        return self._op("ACT", fn, reads, writes, "compute", tag)

    def pool(self, fn, reads, writes, tag=""):
        return self._op("POOL", fn, reads, writes, "compute", tag)

    def sp(self, fn, reads, writes, tag=""):
        return self._op("SP", fn, reads, writes, "compute", tag)

    def dma(self, queue, dst, src, reads, writes, tag=""):
        fn = lambda e: e.dma_start(dst, src)
        return self._op(queue, fn, reads, writes, "dma", tag)

    def barrier_on(self, eng, producers, tag="barrier"):
        op = self._op(eng, None, (), (), "wait", tag)
        for p in producers:
            if p is not None:
                op.deps.append(p)
        return op

    def finalize(self):
        for e in ENGINES:
            for op in self.q[e]:
                for p in op.deps:
                    if p.kind == "compute":
                        p.inc_needed = True
        for e in ENGINES:
            n_inc = 0
            n_dma = 0
            for op in self.q[e]:
                if op.kind == "dma":
                    n_dma += 1
                    op.dma_no = n_dma
                elif op.inc_needed:
                    n_inc += 1
                    op.inc_no = n_inc

        SEM_BAND = 50000

        def sem_val(p):
            if p.kind == "dma":
                slot = (p.dma_no - 1) % self.NDSEM
                s = (p.dma_no - 1) // self.NDSEM
                return (self.dsem[p.eng][slot], 16 * (s + 1))
            band, val = divmod(p.inc_no - 1, SEM_BAND)
            return (self.esem2[p.eng][band], val + 1)

        nc = self.nc
        nc.all_engine_barrier()
        for ss in self.esem2.values():
            for s in ss:
                nc.gpsimd.sem_clear(s)
        for ss in self.dsem.values():
            for s in ss:
                nc.gpsimd.sem_clear(s)
        nc.all_engine_barrier()
        blk = self.st.enter_context(nc.Block())
        builder = self

        def emit_engine(proxy, e):
            hw = {}
            for op in builder.q[e]:
                waits = {}
                for p in op.deps:
                    s, v = sem_val(p)
                    key = id(s)
                    if hw.get(key, 0) >= v:
                        continue
                    if key not in waits or waits[key][1] < v:
                        waits[key] = (s, v)
                if op.kind == "dma":
                    slot = (op.dma_no - 1) % builder.NDSEM
                    s_ = (op.dma_no - 1) // builder.NDSEM
                    if s_ > 0:
                        gsem = builder.dsem[e][slot]
                        gval = 16 * s_
                        if hw.get(id(gsem), 0) < gval:
                            waits[id(gsem)] = (gsem, gval)
                for s, v in waits.values():
                    proxy.wait_ge(s, v)
                    hw[id(s)] = v
                if op.kind == "wait":
                    continue
                inst = op.fn(proxy)
                if op.kind == "dma":
                    slot = (op.dma_no - 1) % builder.NDSEM
                    inst.then_inc(builder.dsem[e][slot], 16)
                elif op.inc_needed:
                    inst.then_inc(builder.esem2[e][(op.inc_no - 1) // 50000], 1)

        if self.q["PE"]:
            @blk.tensor
            def _(t):
                emit_engine(t, "PE")
        if self.q["DVE"]:
            @blk.vector
            def _(v):
                emit_engine(v, "DVE")
        if self.q["ACT"]:
            @blk.scalar
            def _(s):
                emit_engine(s, "ACT")
        if self.q["POOL"]:
            @blk.gpsimd
            def _(g):
                emit_engine(g, "POOL")
        if self.q["SP"]:
            @blk.sync
            def _(sy):
                emit_engine(sy, "SP")


class Tiles:

    def __init__(self, nc, st):
        self.nc = nc
        self.st = st
        self.t = {}

    def sb(self, key, shape, dtype):
        assert key not in self.t
        h = self.st.enter_context(self.nc.sbuf_tensor(key, shape, dtype))
        self.t[key] = h
        return h

    def ps(self, key, shape, dtype=F32):
        assert key not in self.t
        h = self.st.enter_context(self.nc.psum_tensor(key, shape, dtype))
        self.t[key] = h
        return h

    def __getitem__(self, key):
        return self.t[key]




def as_ap(x):
    return x if isinstance(x, bass.AP) else x[:, :]


def wtsrc(ap, kcd, bc=128):
    return ap.rearrange("p (j m) -> p j m", m=kcd * bc)


def wtsrc_e(ap, kcd, bc=128):
    return ap.rearrange("e p (j m) -> e p j m", m=kcd * bc)


def mm_dense(b, name, w_hbm, din, dout, rhs, psum_pool, wbufs, epilogue,
             dq="POOL", nt=NT, tt=TT, blk_cols=128, js=None):
    kcd = din // 128
    njall = dout // blk_cols
    js = list(range(njall)) if js is None else js
    src = wtsrc(w_hbm, kcd, blk_cols)
    depth = len(wbufs)
    wb = [(as_ap(h), k) for h, k in wbufs]

    def dma_block(ji):
        j = js[ji]
        wh, wk = wb[ji % depth]
        b.dma(dq, wh[:, 0:kcd * blk_cols], src[:, j, :],
              reads=(), writes=(wk,), tag=f"{name}dma{j}")

    nj = len(js)
    for ji in range(min(depth, nj)):
        dma_block(ji)
    for ji in range(nj):
        j = js[ji]
        wh, wk = wb[ji % depth]
        banks = [psum_pool[(ji * nt + t) % len(psum_pool)] for t in range(nt)]
        for k in range(kcd):
            lhsT = wh[:, k * blk_cols:(k + 1) * blk_cols]
            rh, rk = rhs[k]
            for t in range(nt):
                ph, pk = banks[t]
                b.pe(lambda e, ph=ph, lhsT=lhsT, rh=rh, t=t, k=k:
                     e.matmul(ph[0:blk_cols, :tt], lhsT,
                              rh[:, t * tt:(t + 1) * tt],
                              start=(k == 0), stop=(k == kcd - 1)),
                     reads=(wk, rk), writes=(pk,), tag=f"{name}mm{j}_{k}_{t}")
        if ji + depth < nj:
            dma_block(ji + depth)
        for t in range(nt):
            ph, pk = banks[t]
            epilogue(j, t, ph, pk)


def reduce_channels(b, name, srcs, ps, t_idx, op_engine="pe", tt=TT,
                    ones=None, scale=None):
    oh, ok = ones
    ph, pk = ps
    n = len(srcs)
    for i, (sh, sk) in enumerate(srcs):
        b.pe(lambda e, sh=sh, i=i: e.matmul(
            ph[0:1, :tt], oh[:, 0:1], sh[:, t_idx * tt:(t_idx + 1) * tt],
            start=(i == 0), stop=(i == n - 1)),
            reads=(ok, sk), writes=(pk,), tag=f"{name}rc{i}")


def bcast_row(b, name, row, ps, ones_row, tt=TT):
    rh, rk, csl = row
    ph, pk = ps
    oh, ok = ones_row
    b.pe(lambda e: e.matmul(ph[:, :tt], oh[0:1, :], rh[0:1, csl],
                            start=True, stop=True),
         reads=(ok, rk), writes=(pk,), tag=f"{name}bc")


NEG_BIG = -30000.0


class Prog:

    def __init__(self, nc, st, ins):
        self.nc = nc
        self.st = st
        self.ins = ins
        self.b = B(nc, st)
        self.tl = Tiles(nc, st)
        t = self.tl
        self.xn = [(t.sb(f"xn{k}", [128, T], BF16), f"xn{k}") for k in range(KC)]
        self.gb = [(t.sb(f"gbt{k}", [128, T], BF16), f"gbt{k}") for k in range(KC)]
        self.bb = [(t.sb(f"bb{k}", [128, T], BF16), f"bb{k}") for k in range(16)]
        self.fb = [(t.sb(f"fbt{k}", [128, T], F32), f"fbt{k}") for k in range(3)]
        self.sq = (t.sb("sqt", [128, 2 * TT], F32), "sqt")
        self.rows = (t.sb("rowst", [1, 2 * T], F32), "rowst")
        self.stat = (t.sb("statt", [128, 64], F32), "statt")
        self.onec = (t.sb("onect", [128, 1], F32), "onect")
        self.onecb = (t.sb("onecbt", [128, 1], BF16), "onecbt")
        self.oner = (t.sb("onert", [1, 128], F32), "onert")
        self.idf = (t.sb("idft", [128, 128], F32), "idft")
        self.idb = (t.sb("idbt", [128, 128], BF16), "idbt")
        self.nbI = (t.sb("nbIt", [128, 128], BF16), "nbIt")
        self.u01 = (t.sb("u01t", [128, 1280], BF16), "u01t")
        self.u01off = (0, 512, 896, 1152)
        self.lm = (t.sb("lmt", [128, 128], BF16), "lmt")
        self.wbA = (t.sb("wbAt", [128, 2048], BF16), "wbAt")
        self.wbB = (t.sb("wbBt", [128, 2048], BF16), "wbBt")
        self.wfull = (t.sb("wfullt", [128, 8192], BF16), "wfullt")
        self.pb = [(t.ps(f"pbk{i}", [128, TT], F32), f"pbk{i}") for i in range(8)]
        self.smalls = {}

    def small(self, name, shape, dtype=F32):
        if name not in self.smalls:
            self.smalls[name] = (self.tl.sb(name, shape, dtype), name)
        return self.smalls[name]

    def fbb(self, i):
        h, k = self.fb[i]
        return h, k

    def pbt(self, i):
        h, k = self.pb[i]
        return h, k


def fbi_bf(h, c0, c1):
    return h[:, :].bitcast(BF16)[:, c0:c1]


def emit_prologue(P):
    b = P.b
    b.dve(lambda v: v.memset(P.onec[0][:, :], 1.0), (), (P.onec[1],))
    b.dve(lambda v: v.memset(P.onecb[0][:, :], 1.0), (), (P.onecb[1],))
    b.dve(lambda v: v.memset(P.oner[0][:, :], 1.0), (), (P.oner[1],))
    for th, tk in (P.idf, P.idb), (P.nbI, P.nbI):
        pass
    b.pool(lambda g: g.memset(P.idf[0][:, :], 1.0), (), (P.idf[1],))
    b.pool(lambda g: g.affine_select(P.idf[0][:, :], P.idf[0][:, :],
                                     pattern=[[-1, 128]], base=0,
                                     channel_multiplier=1,
                                     compare_op=ALU.is_equal, fill=0.0),
           (P.idf[1],), (P.idf[1],))
    b.pool(lambda g: g.memset(P.idb[0][:, :], 1.0), (), (P.idb[1],))
    b.pool(lambda g: g.affine_select(P.idb[0][:, :], P.idb[0][:, :],
                                     pattern=[[-1, 128]], base=0,
                                     channel_multiplier=1,
                                     compare_op=ALU.is_equal, fill=0.0),
           (P.idb[1],), (P.idb[1],))
    b.pool(lambda g: g.memset(P.nbI[0][:, :], NEG_BIG), (), (P.nbI[1],))
    b.pool(lambda g: g.affine_select(P.nbI[0][:, :], P.nbI[0][:, :],
                                     pattern=[[-1, 128]], base=0,
                                     channel_multiplier=1,
                                     compare_op=ALU.is_equal, fill=0.0),
           (P.nbI[1],), (P.nbI[1],))
    for v in range(4):
        off = P.u01off[v]
        w = 512 - 128 * v
        b.pool(lambda g, off=off, w=w: g.memset(P.u01[0][:, off:off + w], 0.0),
               (), (P.u01[1],))
        b.pool(lambda g, off=off, w=w: g.affine_select(
            P.u01[0][:, off:off + w], P.u01[0][:, off:off + w],
            pattern=[[-1, w]], base=0, channel_multiplier=1,
            compare_op=ALU.is_ge, fill=1.0), (P.u01[1],), (P.u01[1],))
    b.pool(lambda g: g.memset(P.lm[0][:, :], 0.0), (), (P.lm[1],))
    b.pool(lambda g: g.affine_select(P.lm[0][:, :], P.lm[0][:, :],
                                     pattern=[[1, 128]], base=0,
                                     channel_multiplier=-1,
                                     compare_op=ALU.is_ge, fill=1.0),
           (P.lm[1],), (P.lm[1],))


def load_smallvec(P, name, hbm_name, shape, dtype=F32, dq="SP"):
    tile = P.small(name, list(shape), dtype)
    b = P.b
    b.dma(dq, tile[0][:, :], P.ins[hbm_name], (), (tile[1],), tag=f"ld_{name}")
    return tile


def emit_outer_ln(P, src, pref, src_keys=None):
    b = P.b
    eps = b.const(1e-5)
    xb = []
    for k in range(KC):
        fh, fk = P.fb[k % 2]
        b.dma("SP", fh[:, :], src[k * 128:(k + 1) * 128, :],
              (() if src_keys is None else (src_keys[k],)), (fk,),
              tag=f"{pref}lnx{k}")
        bh, bk = P.bb[8 + k]
        b.act(lambda s, fh=fh, bh=bh: s.activation(bh[:, :], fh[:, :], AF.Copy),
              (fk,), (bk,), tag=f"{pref}lncast{k}")
        xb.append((bh, bk))
    rows_h, rows_k = P.rows
    for t in range(NT):
        for k in range(KC):
            bh, bk = xb[k]
            b.pe(lambda e, bh=bh, k=k, t=t: e.matmul(
                P.pb[0][0][0:1, :TT], P.onecb[0][:, 0:1],
                bh[:, t * TT:(t + 1) * TT],
                start=(k == 0), stop=(k == KC - 1)),
                (P.onecb[1], bk), (P.pb[0][1],), tag=f"{pref}lnsx{k}_{t}")
            sh, sk = P.sq
            b.act(lambda s, bh=bh, t=t, sh=sh: s.activation(
                sh[:, :TT], bh[:, t * TT:(t + 1) * TT], AF.Square),
                (bk,), (sk,), tag=f"{pref}lnsq{k}_{t}")
            b.pe(lambda e, sh=sh, k=k: e.matmul(
                P.pb[1][0][0:1, :TT], P.onec[0][:, 0:1], sh[:, :TT],
                start=(k == 0), stop=(k == KC - 1)),
                (P.onec[1], sk), (P.pb[1][1],), tag=f"{pref}lnsq2{k}_{t}")
        b.dve(lambda v, t=t: v.tensor_scalar_mul(
            rows_h[0:1, t * TT:(t + 1) * TT], P.pb[0][0][0:1, :TT], 1.0 / D),
            (P.pb[0][1],), (rows_k,), tag=f"{pref}lnmu{t}")
        f2h, f2k = P.fb[2]
        b.dve(lambda v, t=t, f2h=f2h: v.tensor_scalar_mul(
            f2h[0:1, t * TT:(t + 1) * TT],
            P.pb[1][0][0:1, :TT], 1.0 / D),
            (P.pb[1][1],), (f2k,), tag=f"{pref}lnms{t}")
    f2h, f2k = P.fb[2]
    b.act(lambda s: s.activation(rows_h[0:1, T:2 * T], rows_h[0:1, 0:T],
                                 AF.Square), (rows_k,), (rows_k,),
          tag=f"{pref}lnmu2")
    b.dve(lambda v: v.tensor_sub(rows_h[0:1, T:2 * T], f2h[0:1, 0:T],
                                 rows_h[0:1, T:2 * T]),
          (rows_k, f2k), (rows_k,), tag=f"{pref}lnvar")
    b.act(lambda s: s.activation(rows_h[0:1, T:2 * T],
                                 rows_h[0:1, T:2 * T], AF.Ln,
                                 bias=eps[0:1, :]),
          (rows_k, "cpool"), (rows_k,), tag=f"{pref}lnln")
    b.act(lambda s: s.activation(rows_h[0:1, T:2 * T],
                                 rows_h[0:1, T:2 * T], AF.Exp, scale=-0.5),
          (rows_k,), (rows_k,), tag=f"{pref}lnexp")
    for t in range(NT):
        bcast_row(b, f"{pref}lnbmu{t}", (rows_h, rows_k, slice(t * TT, (t + 1) * TT)),
                  P.pb[2], (P.oner[0], P.oner[1]))
        bcast_row(b, f"{pref}lnbr{t}", (rows_h, rows_k, slice(T + t * TT, T + (t + 1) * TT)),
                  P.pb[3], (P.oner[0], P.oner[1]))
        for k in range(KC):
            bh, bk = xb[k]
            xh, xk = P.xn[k]
            fh, fk = P.fb[2]
            b.dve(lambda v, bh=bh, fh=fh, t=t: v.tensor_sub(
                fh[:, t * TT:(t + 1) * TT], bh[:, t * TT:(t + 1) * TT],
                P.pb[2][0][:, :TT]),
                (bk, P.pb[2][1]), (fk,), tag=f"{pref}lnsub{k}_{t}")
            b.dve(lambda v, fh=fh, xh=xh, t=t: v.tensor_mul(
                xh[:, t * TT:(t + 1) * TT], fh[:, t * TT:(t + 1) * TT],
                P.pb[3][0][:, :TT]),
                (fk, P.pb[3][1]), (xk,), tag=f"{pref}lnmul{k}_{t}")


def emit_rms(P, pref, src, dst, w2d, eps_val, kcin=KC):
    b = P.b
    eps = b.const(eps_val)
    rows_h, rows_k = P.rows
    for t in range(NT):
        for k in range(kcin):
            bh, bk = src[k]
            sh, sk = P.sq
            b.act(lambda s, bh=bh, sh=sh, t=t: s.activation(
                sh[:, :TT], bh[:, t * TT:(t + 1) * TT], AF.Square),
                (bk,), (sk,), tag=f"{pref}sq{k}_{t}")
            b.pe(lambda e, sh=sh, k=k: e.matmul(
                P.pb[0][0][0:1, :TT], P.onec[0][:, 0:1], sh[:, :TT],
                start=(k == 0), stop=(k == kcin - 1)),
                (P.onec[1], sk), (P.pb[0][1],), tag=f"{pref}rs{k}_{t}")
        b.dve(lambda v, t=t: v.tensor_scalar_mul(
            rows_h[0:1, t * TT:(t + 1) * TT], P.pb[0][0][0:1, :TT],
            1.0 / (128 * kcin)),
            (P.pb[0][1],), (rows_k,), tag=f"{pref}ms{t}")
    b.act(lambda s: s.activation(rows_h[0:1, 0:T], rows_h[0:1, 0:T], AF.Ln,
                                 bias=eps[0:1, :]),
          (rows_k, "cpool"), (rows_k,), tag=f"{pref}ln")
    b.act(lambda s: s.activation(rows_h[0:1, 0:T], rows_h[0:1, 0:T], AF.Exp,
                                 scale=-0.5), (rows_k,), (rows_k,),
          tag=f"{pref}exp")
    for t in range(NT):
        bcast_row(b, f"{pref}br{t}", (rows_h, rows_k, slice(t * TT, (t + 1) * TT)),
                  P.pb[2], (P.oner[0], P.oner[1]))
        for k in range(kcin):
            bh, bk = src[k]
            dh, dk = dst[k]
            if w2d is None:
                b.dve(lambda v, bh=bh, dh=dh, t=t: v.tensor_mul(
                    dh[:, t * TT:(t + 1) * TT], bh[:, t * TT:(t + 1) * TT],
                    P.pb[2][0][:, :TT]),
                    (bk, P.pb[2][1]), (dk,), tag=f"{pref}n{k}_{t}")
            else:
                wh, wk = w2d
                b.dve(lambda v, bh=bh, dh=dh, wh=wh, t=t, k=k:
                      v.scalar_tensor_tensor(
                          dh[:, t * TT:(t + 1) * TT],
                          bh[:, t * TT:(t + 1) * TT], wh[:, k:k + 1],
                          P.pb[2][0][:, :TT], ALU.mult, ALU.mult),
                      (bk, wk, P.pb[2][1]), (dk,), tag=f"{pref}n{k}_{t}")


def emit_attn(P, pref, qspec, kspec, vt, vstride, vch, out_cb, scale, ptile,
              pts, window=False, sink=None):
    b = P.b
    st_h, st_k = P.stat
    sc = b.const(scale)
    vth, vtk = vt
    pth, ptk = ptile
    psh, psk = pts
    for hh in (0, 1):
        qs = qspec(hh)
        ks = kspec(hh)
        for qi in range(16):
            if window:
                kts = [qi - 1, qi] if qi > 0 else [0]
                ktw = 128
            else:
                kts = list(range(qi // 4 + 1))
                ktw = TT
            nk = len(kts)
            for i, kt in enumerate(kts):
                ph, pk = P.pb[i]
                npc = len(qs)
                if window:
                    mask = (P.u01[0][:, 0:128], P.u01[1], 0) if kt == qi else \
                           (P.lm[0][:, 0:128], P.lm[1], 0)
                else:
                    v4 = qi % 4
                    mask = ((P.u01[0][:, P.u01off[v4]:P.u01off[v4] + 512 - 128 * v4],
                             P.u01[1], 128 * v4) if kt == kts[-1] else None)
                for ip in range(npc):
                    qh, qk, qb, cw = qs[ip]
                    kh, kk, kb, _ = ks[ip]
                    is_last = (ip == npc - 1) and mask is None
                    b.pe(lambda e, ph=ph, qh=qh, kh=kh, qb=qb, kb=kb, qi=qi,
                         kt=kt, ip=ip, is_last=is_last, ktw=ktw:
                         e.matmul(ph[:, :ktw],
                                  qh[:, qi * 128:(qi + 1) * 128],
                                  kh[:, kt * ktw:kt * ktw + ktw],
                                  start=(ip == 0), stop=is_last,
                                  tile_position=(qb, 0)),
                         (qk, kk), (pk,), tag=f"{pref}sc{hh}_{qi}_{i}_{ip}")
                if mask is not None:
                    map_, mk, moff = mask
                    b.pe(lambda e, ph=ph, map_=map_, ktw=ktw, moff=moff:
                         e.matmul(ph[:, moff:ktw], P.nbI[0][:, :], map_,
                                  start=False, stop=True),
                        (P.nbI[1], mk), (pk,), tag=f"{pref}msk{hh}_{qi}_{i}")
            for i in range(nk):
                ph, pk = P.pb[i]
                w = ktw
                b.dve(lambda v, ph=ph, i=i, w=w: v.reduce_max(
                    st_h[:, (4 if i == 0 else 5):(5 if i == 0 else 6)],
                    ph[:, :w], axis=AX.X),
                    (pk,), (st_k,), tag=f"{pref}mx{hh}_{qi}_{i}")
                if i > 0:
                    b.dve(lambda v: v.tensor_max(st_h[:, 4:5], st_h[:, 4:5],
                                                 st_h[:, 5:6]),
                          (st_k,), (st_k,), tag=f"{pref}mxc{hh}_{qi}_{i}")
            b.dve(lambda v: v.tensor_scalar_mul(st_h[:, 6:7], st_h[:, 4:5],
                                                -scale),
                  (st_k,), (st_k,), tag=f"{pref}nm{hh}_{qi}")
            for i in range(nk):
                ph, pk = P.pb[i]
                b.act(lambda s, ph=ph, i=i, w=ktw: s.activation(
                    pth[:, i * TT:i * TT + w], ph[:, :w], AF.Exp,
                    bias=st_h[:, 6:7], scale=sc[:, :],
                    accum_out=st_h[:, 8 + i:9 + i]),
                    (pk, st_k, "cpool"), (ptk, st_k),
                    tag=f"{pref}ex{hh}_{qi}_{i}")
            for i in range(1, nk):
                b.dve(lambda v, i=i: v.tensor_add(st_h[:, 8:9], st_h[:, 8:9],
                                                  st_h[:, 8 + i:9 + i]),
                      (st_k,), (st_k,), tag=f"{pref}sm{hh}_{qi}_{i}")
            if sink is not None:
                skh, skk, hidx = sink(hh)
                b.act(lambda s, skh=skh, hidx=hidx: s.activation(
                    st_h[:, 9:10], st_h[:, 4:5], AF.Exp,
                    bias=skh[:, hidx:hidx + 1], scale=-1.0),
                    (st_k, skk, "cpool"), (st_k,), tag=f"{pref}snk{hh}_{qi}")
                b.dve(lambda v: v.tensor_add(st_h[:, 8:9], st_h[:, 8:9],
                                             st_h[:, 9:10]),
                      (st_k,), (st_k,), tag=f"{pref}snka{hh}_{qi}")
            b.dve(lambda v: v.reciprocal(st_h[:, 10:11], st_h[:, 8:9]),
                  (st_k,), (st_k,), tag=f"{pref}rcp{hh}_{qi}")
            for i in range(nk):
                b.dve(lambda v, i=i, w=ktw: v.tensor_scalar_mul(
                    pth[:, i * TT:i * TT + w], pth[:, i * TT:i * TT + w],
                    st_h[:, 10:11]),
                    (ptk, st_k), (ptk,), tag=f"{pref}nn{hh}_{qi}_{i}")
            sub128 = []
            if window:
                for i, kt in enumerate(kts):
                    sub128.append((i * TT, kt))
            else:
                for s in range(qi + 1):
                    sub128.append(((s // 4) * TT + (s % 4) * 128, s))
            pvh, pvk = P.pb[4 if (qi % 2 == 0) else 5]
            n128 = len(sub128)
            for s, (src_col, kt128) in enumerate(sub128):
                tph, tpk = P.pb[6 + (s % 2)]
                slot = (s // 2) % 4
                b.pe(lambda e, src_col=src_col, slot=slot, tph=tph:
                     e.transpose(
                         tph[:, :].bitcast(BF16)[:, slot * 128:(slot + 1) * 128],
                         pth[:, src_col:src_col + 128], P.idb[0][:, :]),
                     (ptk, P.idb[1]), (tpk,), tag=f"{pref}tp{hh}_{qi}_{s}")
                b.act(lambda sx, slot=slot, s=s, tph=tph: sx.activation(
                    psh[:, s * 128:(s + 1) * 128],
                    tph[:, :].bitcast(BF16)[:, slot * 128:(slot + 1) * 128],
                    AF.Copy),
                    (tpk,), (psk,), tag=f"{pref}tpe{hh}_{qi}_{s}")
            for s, (src_col, kt128) in enumerate(sub128):
                b.pe(lambda e, kt128=kt128, s=s, pvh=pvh, hh=hh, n128=n128:
                     e.matmul(pvh[hh * 64:hh * 64 + 64, 0:128],
                              vth[:, kt128 * vstride + vch(hh):
                                  kt128 * vstride + vch(hh) + 64],
                              psh[:, s * 128:(s + 1) * 128],
                              start=(s == 0), stop=(s == n128 - 1),
                              tile_position=(0, hh * 64)),
                     (vtk, psk), (pvk,), tag=f"{pref}pv{hh}_{qi}_{s}")
            out_cb(hh, qi, P.pb[4 if (qi % 2 == 0) else 5])


def emit_layer2(P, xsrc, xdst, src_keys=None, dst_keys=None):
    b = P.b
    ins = P.ins
    emit_outer_ln(P, xsrc, "l2o", src_keys)
    anw = load_smallvec(P, "l2anw", "l2_attn_norm_w", (128, 8))
    fnw = load_smallvec(P, "l2fnw", "l2_ffn_norm_w", (128, 8))
    qkb = load_smallvec(P, "l2qkb", "l2_qkv_b2d", (128, 10))
    pjb = load_smallvec(P, "l2pjb", "l2_proj_b2d", (128, 8))
    snk = load_smallvec(P, "l2snk", "l2_sink_bc", (128, 16))
    gbv = load_smallvec(P, "l2gbv", "l2_gate_b2d", (32, 1))
    bqv = load_smallvec(P, "l2bq", "l2_bq2d", (128, 8))
    bkv = load_smallvec(P, "l2bk", "l2_bk2d", (128, 8))

    h = [P.bb[k] for k in range(8)]
    emit_rms(P, "l2h", P.xn, h, anw, 1e-5)

    def epi_qkv(j, t, ph, pk):
        if j < 8:
            dh, dk = P.bb[8 + j]
        else:
            dh, dk = (fbi_bf(P.fb[0][0], (j - 8) * T, (j - 7) * T), P.fb[0][1])
        b.act(lambda s, ph=ph, dh=dh, t=t, j=j: s.activation(
            dh[:, t * TT:(t + 1) * TT], ph[:, :TT], AF.Identity,
            bias=qkb[0][:, j:j + 1], scale=1.0),
            (pk, qkb[1], "cpool"), (dk,), tag=f"l2qkv_e{j}_{t}")
    mm_dense(b, "l2qkv", ins["l2_qkv_w"], D, 1536, h, P.pb[:6],
             [(P.wbA[0][:, s * 1024:(s + 1) * 1024], P.wbA[1] + f":{s}")
              for s in range(2)], epi_qkv, js=list(range(10)))

    b.dma("POOL", P.wbB[0][:, 0:2048].rearrange("p (k c) -> p k c", c=256),
          ins["l2_wv_cols"].rearrange("(k p) n -> p k n", p=128),
          (), (P.wbB[1] + ":0", P.wbB[1] + ":1"), tag="l2wv")
    b.dma("SP", P.rows[0][0:1, T:T + 256], ins["l2_bv_row"],
          (), (P.rows[1],), tag="l2bvr")
    vt = (fbi_bf(P.fb[2][0], 0, 2 * T), P.fb[2][1])
    for ti in range(16):
        ph, pk = P.pb[ti % 2]
        for k in range(KC):
            b.pe(lambda e, ph=ph, k=k, ti=ti: e.matmul(
                ph[:, 0:256], P.xn[k][0][:, ti * 128:(ti + 1) * 128],
                P.wbB[0][:, k * 256:(k + 1) * 256],
                start=(k == 0), stop=False),
                (P.xn[k][1], P.wbB[1] + f":{k // 4}"), (pk,), tag=f"l2v{ti}_{k}")
        b.pe(lambda e, ph=ph: e.matmul(
            ph[:, 0:256], P.oner[0][0:1, :], P.rows[0][0:1, T:T + 256],
            start=False, stop=True),
            (P.oner[1], P.rows[1]), (pk,), tag=f"l2vb{ti}")
        vth, vtk = vt
        b.act(lambda s, ph=ph, ti=ti, vth=vth: s.activation(
            vth[:, ti * 256:(ti + 1) * 256], ph[:, 0:256], AF.Copy),
            (pk,), (vtk,), tag=f"l2ve{ti}")

    sinT = P.bb[0]
    cosT = P.bb[1]
    b.dma("POOL", sinT[0][:, :], ins["l2_rope_sin"], (), (sinT[1],), tag="l2sin")
    b.dma("POOL", cosT[0][:, :], ins["l2_rope_cos"], (), (cosT[1],), tag="l2cos")
    pairs = [(P.bb[8 + j], P.bb[12 + j]) for j in range(4)]
    pairs.append(((fbi_bf(P.fb[0][0], 0, T), P.fb[0][1]),
                  (fbi_bf(P.fb[0][0], T, 2 * T), P.fb[0][1])))
    f1h, f1k = P.fb[1]
    for pi, ((q1h, q1k), (q2h, q2k)) in enumerate(pairs):
        for th in range(2):
            c = slice(th * 1024, (th + 1) * 1024)
            sA = f1h[:, 0:1024]
            sB = f1h[:, 1024:2048]
            b.dve(lambda v, q1h=q1h, c=c, sA=sA: v.tensor_mul(
                sA, q1h[:, c], sinT[0][:, c]),
                (q1k, sinT[1]), (f1k,), tag=f"l2r1_{pi}_{th}")
            b.dve(lambda v, q1h=q1h, c=c, sB=sB: v.tensor_mul(
                sB, q1h[:, c], cosT[0][:, c]),
                (q1k, cosT[1]), (f1k,), tag=f"l2r2_{pi}_{th}")
            b.dve(lambda v, q1h=q1h, q2h=q2h, c=c: v.tensor_mul(
                q1h[:, c], q2h[:, c], sinT[0][:, c]),
                (q2k, sinT[1]), (q1k,), tag=f"l2r3_{pi}_{th}")
            b.dve(lambda v, q1h=q1h, c=c, sB=sB: v.tensor_sub(
                q1h[:, c], sB, q1h[:, c]),
                (f1k,), (q1k,), tag=f"l2r4_{pi}_{th}")
            b.dve(lambda v, q2h=q2h, c=c: v.tensor_mul(
                q2h[:, c], q2h[:, c], cosT[0][:, c]),
                (q2k, cosT[1]), (q2k,), tag=f"l2r5_{pi}_{th}")
            b.dve(lambda v, q2h=q2h, c=c, sA=sA: v.tensor_add(
                q2h[:, c], q2h[:, c], sA),
                (f1k,), (q2k,), tag=f"l2r6_{pi}_{th}")

    ytiles = [P.bb[2 + j] for j in range(6)]
    ytiles.append((fbi_bf(P.fb[1][0], 0, T), P.fb[1][1]))
    ytiles.append((fbi_bf(P.fb[1][0], T, 2 * T), P.fb[1][1]))
    kx1 = fbi_bf(P.fb[0][0], 0, T)
    kx2 = fbi_bf(P.fb[0][0], T, 2 * T)
    ptile = (P.bb[0][0], P.bb[0][1])
    pts = (P.bb[1][0], P.bb[1][1])
    for j in range(8):
        def qspec(hh, j=j):
            hsel = 2 * j + hh
            base = (hsel // 4) * 32
            q1h, q1k = P.bb[8 + hsel % 4]
            q2h, q2k = P.bb[12 + hsel % 4]
            return [(q1h[base:base + 32, :], q1k, base, 32),
                    (q2h[base:base + 32, :], q2k, base, 32)]

        def kspec(hh, j=j):
            gq = (2 * j + hh) // 4
            base = gq * 32
            return [(kx1[base:base + 32, :], P.fb[0][1], base, 32),
                    (kx2[base:base + 32, :], P.fb[0][1], base, 32)]

        def out_cb(hh, qi, pv, j=j):
            pvh, pvk = pv
            yh, yk = ytiles[j]
            b.dve(lambda v, pvh=pvh, yh=yh, hh=hh, qi=qi: v.tensor_copy(
                yh[hh * 64:(hh + 1) * 64, qi * 128:(qi + 1) * 128],
                pvh[hh * 64:(hh + 1) * 64, 0:128]),
                (pvk,), (yk,), tag=f"l2y{j}_{hh}_{qi}")

        emit_attn(P, f"l2w{j}", qspec, kspec, vt, 256,
                  lambda hh, j=j: ((2 * j + hh) // 4) * 64, out_cb, 0.125,
                  ptile, pts, window=True,
                  sink=lambda hh, j=j: (snk[0], snk[1], 2 * j + hh))

    def epi_proj(j, t, ph, pk):
        sh, sk = P.sq
        b.act(lambda s, ph=ph, sh=sh, j=j: s.activation(
            sh[:, 0:TT], ph[:, :TT], AF.Identity, bias=pjb[0][:, j:j + 1],
            scale=1.0), (pk, pjb[1], "cpool"), (sk,), tag=f"l2pj_{j}_{t}")
        dh, dk = P.bb[8 + j]
        b.dve(lambda v, sh=sh, dh=dh, t=t, j=j: v.tensor_add(
            dh[:, t * TT:(t + 1) * TT], sh[:, 0:TT],
            P.xn[j][0][:, t * TT:(t + 1) * TT]),
            (sk, P.xn[j][1]), (dk,), tag=f"l2x2_{j}_{t}")
    mm_dense(b, "l2pj", ins["l2_proj_w"], D, D, ytiles, P.pb[:6],
             [(P.wbA[0][:, s * 1024:(s + 1) * 1024], P.wbA[1] + f":{s}")
              for s in range(2)], epi_proj)

    x2 = [P.bb[8 + j] for j in range(8)]
    if DEBUG_DUMP:
        for j in range(8):
            b.dma("POOL", P.ins["dbg_x2"][j * 128:(j + 1) * 128, :],
                  x2[j][0][:, :], (x2[j][1],), (f"dbgx2_{j}",),
                  tag=f"dbgx2_{j}")
    emit_rms(P, "l2h2", x2, [P.bb[j] for j in range(8)], fnw, 1e-5)
    for j in range(8):
        b.dma("SP", P.ins["x2scr"][j * 128:(j + 1) * 128, :], x2[j][0][:, :],
              (x2[j][1],), (f"x2s{j}",), tag=f"l2x2s{j}")
    h2 = [P.bb[j] for j in range(8)]

    def epi_gate(j, t, ph, pk):
        f1h, f1k = P.fb[2]
        b.dve(lambda v, ph=ph, t=t: v.tensor_scalar_add(
            f1h[0:32, t * TT:(t + 1) * TT], ph[0:32, :TT], gbv[0][0:32, 0:1]),
            (pk, gbv[1]), (f1k,), tag=f"l2gl{t}")
    mm_dense(b, "l2g", ins["l2_gate_w"], D, 32, h2, P.pb[:6],
             [(P.wbB[0][:, 0:256], P.wbB[1] + ":0")], epi_gate, blk_cols=32)
    f1h, f1k = P.fb[2]
    f3h, f3k = P.fb[1]
    st_h, st_k = P.stat
    tkh, tkk = P.small("tktile", [128, 64], F32)
    for ti in range(16):
        ph, pk = P.pb[6]
        b.pe(lambda e, ti=ti, ph=ph: e.transpose(
            ph[0:128, 0:32], f1h[0:32, ti * 128:(ti + 1) * 128],
            P.idf[0][0:32, 0:32]), (f1k, P.idf[1]), (pk,), tag=f"l2gt{ti}")
        b.dve(lambda v, ti=ti, ph=ph: v.tensor_copy(
            f3h[:, ti * 32:(ti + 1) * 32], ph[0:128, 0:32]),
            (pk,), (f3k,), tag=f"l2gte{ti}")
    if DEBUG_DUMP:
        b.dma("SP", P.ins["dbg_gl"], f1h[0:32, :], (f1k,), ("dbggl",),
              tag="dbggl")
        b.dma("SP", P.ins["dbg_glt"], f3h[:, 0:512], (f3k,), ("dbgglt",),
              tag="dbgglt")
    for ti in range(16):
        gsl = lambda: f3h[:, ti * 32:(ti + 1) * 32]
        b.dve(lambda v, ti=ti: v.tensor_copy(tkh[:, 0:32],
                                             f3h[:, ti * 32:(ti + 1) * 32]),
              (f3k,), (tkk,), tag=f"l2tk0_{ti}")
        for r in range(4):
            b.dve(lambda v, r=r: v.reduce_max(st_h[:, 12 + r:13 + r],
                                              tkh[:, 0:32], axis=AX.X),
                  (tkk,), (st_k,), tag=f"l2tkm{ti}_{r}")
            if r < 3:
                b.dve(lambda v, r=r: v.tensor_scalar(
                    tkh[:, 32:64], tkh[:, 0:32], st_h[:, 12 + r:13 + r],
                    NEG_BIG, ALU.is_ge, ALU.mult),
                    (tkk, st_k), (tkk,), tag=f"l2tks{ti}_{r}")
                b.dve(lambda v: v.tensor_add(tkh[:, 0:32], tkh[:, 0:32],
                                             tkh[:, 32:64]),
                      (tkk,), (tkk,), tag=f"l2tka{ti}_{r}")
        b.dve(lambda v: v.tensor_scalar_mul(st_h[:, 16:17], st_h[:, 12:13],
                                            -1.0), (st_k,), (st_k,),
              tag=f"l2tkn{ti}")
        b.act(lambda s, ti=ti: s.activation(tkh[:, 0:32],
                                            f3h[:, ti * 32:(ti + 1) * 32],
                                            AF.Exp, bias=st_h[:, 16:17],
                                            scale=1.0),
              (f3k, st_k, "cpool"), (tkk,), tag=f"l2tke{ti}")
        b.dve(lambda v, ti=ti: v.tensor_scalar(
            tkh[:, 32:64], f3h[:, ti * 32:(ti + 1) * 32], st_h[:, 15:16],
            None, ALU.is_ge), (f3k, st_k), (tkk,), tag=f"l2tkg{ti}")
        b.dve(lambda v: v.tensor_mul(tkh[:, 0:32], tkh[:, 0:32],
                                     tkh[:, 32:64]),
              (tkk,), (tkk,), tag=f"l2tkw{ti}")
        b.dve(lambda v: v.reduce_sum(st_h[:, 17:18], tkh[:, 0:32], axis=AX.X),
              (tkk,), (st_k,), tag=f"l2tkss{ti}")
        b.dve(lambda v: v.reciprocal(st_h[:, 18:19], st_h[:, 17:18]),
              (st_k,), (st_k,), tag=f"l2tkr{ti}")
        b.dve(lambda v: v.tensor_scalar_mul(tkh[:, 0:32], tkh[:, 0:32],
                                            st_h[:, 18:19]),
              (tkk, st_k), (tkk,), tag=f"l2tkf{ti}")
        ph, pk = P.pb[6]
        b.pe(lambda e, ph=ph: e.transpose(ph[0:32, 0:128], tkh[:, 0:32],
                                          P.idf[0][:, :]),
             (tkk, P.idf[1]), (pk,), tag=f"l2ct{ti}")
        f0h, f0k = P.fb[0]
        b.dve(lambda v, ti=ti, ph=ph: v.tensor_copy(
            f0h[0:32, ti * 128:(ti + 1) * 128], ph[0:32, 0:128]),
            (pk,), (f0k,), tag=f"l2cte{ti}")
    b.dma("SP", P.ins["cescr"], P.fb[0][0][0:32, :], (P.fb[0][1],), ("cescr",),
          tag="l2ce")
    if DEBUG_DUMP:
        b.dma("SP", P.ins["dbg_ce"], P.fb[0][0][0:32, :], (P.fb[0][1],),
              ("dbgce",), tag="dbgce")

    hid = [P.bb[8 + k] for k in range(8)]
    cebc_h, cebc_k = P.fb[1]
    upslots = [(P.wbA[0][:, s * 1024:(s + 1) * 1024], P.wbA[1] + f":{s}")
               for s in range(2)] + \
              [(P.wbB[0][:, s * 1024:(s + 1) * 1024], P.wbB[1] + f":{s}")
               for s in range(2)]
    dnslots = [(P.wfull[0][:, s * 1024:(s + 1) * 1024], P.wfull[1] + f":{s}")
               for s in range(8)]
    wup_r = wtsrc_e(ins["l2_wup"], 8)
    wdn_r = wtsrc_e(ins["l2_wdn"], 8)
    for e in range(32):
        b.dma("SP", P.rows[0][0:1, 0:T], P.ins["cescr"][e:e + 1, :],
              ("cescr",), (P.rows[1],), tag=f"l2cer{e}")
        for t in range(NT):
            bcast_row(b, f"l2ceb{e}_{t}",
                      (P.rows[0], P.rows[1], slice(t * TT, (t + 1) * TT)),
                      P.pb[6], (P.oner[0], P.oner[1]))
            b.act(lambda s, t=t: s.activation(
                cebc_h[:, t * TT:(t + 1) * TT], P.pb[6][0][:, :TT], AF.Copy),
                (P.pb[6][1],), (cebc_k,), tag=f"l2cee{e}_{t}")
        hid_h = [h_ for h_, _ in hid]
        seq = []
        for jj in range(8):
            seq.append(jj * 128)
            seq.append(1024 + jj * 128)
        nseq = len(seq)

        def dma_up(si, e=e):
            col0 = seq[si]
            wh, wk = upslots[si % 4]
            b.dma("POOL", wh[:, 0:KC * 128], wup_r[e, :, col0 // 128, :],
                  (), (wk,), tag=f"l2up{e}_{si}")
        for si in range(min(4, nseq)):
            dma_up(si)
        for jj in range(8):
            for half in (0, 1):
                si = jj * 2 + half
                wh, wk = upslots[si % 4]
                for k in range(KC):
                    for t in range(NT):
                        ps, psk = P.pb[half * 4 + t]
                        b.pe(lambda e2, ps=ps, wh=wh, k=k, t=t: e2.matmul(
                            ps[:, :TT], wh[:, k * 128:(k + 1) * 128],
                            h2[k][0][:, t * TT:(t + 1) * TT],
                            start=(k == 0), stop=(k == KC - 1)),
                            (wk, h2[k][1]), (psk,), tag=f"l2um{e}_{si}_{k}_{t}")
                if si + 4 < nseq:
                    dma_up(si + 4)
            hh_, hk_ = hid[jj]
            for t in range(NT):
                sh, sk = P.sq
                ss = ((jj * 4 + t) % 2) * TT
                b.act(lambda s, t=t, sh=sh, ss=ss: s.activation(
                    sh[:, ss:ss + TT], P.pb[4 + t][0][:, :TT], AF.Silu),
                    (P.pb[4 + t][1],), (sk,), tag=f"l2sg{e}_{jj}_{t}")
                b.dve(lambda v, t=t, hh_=hh_, sh=sh, ss=ss: v.tensor_mul(
                    hh_[:, t * TT:(t + 1) * TT], P.pb[t][0][:, :TT],
                    sh[:, ss:ss + TT]),
                    (P.pb[t][1], sk), (hk_,), tag=f"l2hu{e}_{jj}_{t}")
                b.dve(lambda v, t=t, hh_=hh_: v.tensor_mul(
                    hh_[:, t * TT:(t + 1) * TT], hh_[:, t * TT:(t + 1) * TT],
                    cebc_h[:, t * TT:(t + 1) * TT]),
                    (hk_, cebc_k), (hk_,), tag=f"l2hc{e}_{jj}_{t}")
        def dma_dn(j, e=e):
            wh, wk = dnslots[j % 8]
            b.dma("POOL", wh[:, 0:KC * 128], wdn_r[e, :, j, :], (), (wk,),
                  tag=f"l2dn{e}_{j}")
        for j in range(min(8, 8)):
            dma_dn(j)
        for j in range(8):
            wh, wk = dnslots[j % 8]
            for k in range(KC):
                for t in range(NT):
                    ps, psk = P.pb[(j % 2) * 4 + t]
                    hh_, hk_ = hid[k]
                    b.pe(lambda e2, ps=ps, wh=wh, k=k, t=t, hh_=hh_:
                         e2.matmul(ps[:, :TT], wh[:, k * 128:(k + 1) * 128],
                                   hh_[:, t * TT:(t + 1) * TT],
                                   start=(k == 0), stop=(k == KC - 1)),
                         (wk, hk_), (psk,), tag=f"l2dm{e}_{j}_{k}_{t}")
            gh, gk = P.gb[j]
            for t in range(NT):
                ps, psk = P.pb[(j % 2) * 4 + t]
                if e == 0:
                    b.act(lambda s, ps=ps, gh=gh, t=t: s.activation(
                        gh[:, t * TT:(t + 1) * TT], ps[:, :TT], AF.Copy),
                        (psk,), (gk,), tag=f"l2y2i{j}_{t}")
                else:
                    b.dve(lambda v, ps=ps, gh=gh, t=t: v.tensor_add(
                        gh[:, t * TT:(t + 1) * TT],
                        gh[:, t * TT:(t + 1) * TT], ps[:, :TT]),
                        (psk, gk), (gk,), tag=f"l2y2a{e}_{j}_{t}")

    if DEBUG_DUMP:
        for j in range(8):
            b.dma("POOL", P.ins["dbg_y2"][j * 128:(j + 1) * 128, :],
                  P.gb[j][0][:, :], (P.gb[j][1],), (f"dbgy2_{j}",),
                  tag=f"dbgy2_{j}")
    for j in range(8):
        rh, rk = P.bb[8 + (j % 2)]
        b.dma("ACT", rh[:, :], P.ins["x2scr"][j * 128:(j + 1) * 128, :],
              (f"x2s{j}",), (rk,), tag=f"l2x2r{j}")
        gh, gk = P.gb[j]
        b.dve(lambda v, rh=rh, gh=gh: v.tensor_add(gh[:, :], gh[:, :],
                                                   rh[:, :]),
              (rk, gk), (gk,), tag=f"l2gadd{j}")
        b.act(lambda s, gh=gh: s.activation(gh[:, :], gh[:, :], AF.Silu),
              (gk,), (gk,), tag=f"l2gsilu{j}")
        if DEBUG_DUMP:
            b.dma("POOL", P.ins["dbg_g"][j * 128:(j + 1) * 128, :], gh[:, :],
                  (gk,), (f"dbgg{j}",), tag=f"dbgg{j}")
            b.dma("POOL", P.ins["dbg_xn"][j * 128:(j + 1) * 128, :],
                  P.xn[j][0][:, :], (P.xn[j][1],), (f"dbgxn{j}",),
                  tag=f"dbgxn{j}")

    b.dma("POOL", P.wfull[0][:, :].rearrange("p (k c) -> p k c", c=1024),
          ins["l2_wvT"].rearrange("(k p) n -> p k n", p=128),
          (), tuple(P.wfull[1] + f":{s}" for s in range(8)), tag="l2wvT")
    b.dma("SP", P.rows[0][0:1, T:T + 1024], ins["l2_bv_row2"],
          (), (P.rows[1],), tag="l2bvr2")
    wqslots = [(P.wbA[0][:, s * 1024:(s + 1) * 1024], P.wbA[1] + f":{s}")
               for s in range(2)]
    wkslots = [(P.wbB[0][:, s * 1024:(s + 1) * 1024], P.wbB[1] + f":{s}")
               for s in range(2)]
    wq_r = wtsrc(ins["l2_wqT"], 8)
    wk_r = wtsrc(ins["l2_wkT"], 8)

    def dma_qk(j):
        wh, wk = wqslots[j % 2]
        b.dma("POOL", wh, wq_r[:, j, :], (), (wk,), tag=f"l2wq{j}")
        wh2, wk2 = wkslots[j % 2]
        b.dma("POOL", wh2, wk_r[:, j, :], (), (wk2,), tag=f"l2wk{j}")
    dma_qk(0)
    for j in range(8):
        if j + 1 < 8:
            dma_qk(j + 1)
        for (slots, dstt, bias2d) in ((wqslots, P.bb[0], bqv),
                                      (wkslots, P.bb[1], bkv)):
            wh, wk = slots[j % 2]
            dh, dk = dstt
            for t in range(NT):
                ps, psk = P.pb[t % 2]
                for k in range(KC):
                    b.pe(lambda e, ps=ps, wh=wh, k=k, t=t: e.matmul(
                        ps[:, :TT], wh[:, k * 128:(k + 1) * 128],
                        P.xn[k][0][:, t * TT:(t + 1) * TT],
                        start=(k == 0), stop=(k == KC - 1)),
                        (wk, P.xn[k][1]), (psk,), tag=f"l2qk{j}_{t}_{k}")
                b.act(lambda s, ps=ps, dh=dh, t=t, j=j, bias2d=bias2d:
                      s.activation(dh[:, t * TT:(t + 1) * TT], ps[:, :TT],
                                   AF.Identity, bias=bias2d[0][:, j:j + 1],
                                   scale=1.0),
                      (psk, bias2d[1], "cpool"), (dk,), tag=f"l2qke{j}_{t}")
        for ti in range(16):
            ps, psk = P.pb[2 + (ti % 2)]
            for k in range(KC):
                b.pe(lambda e, ps=ps, k=k, ti=ti, j=j: e.matmul(
                    ps[:, 0:128], P.xn[k][0][:, ti * 128:(ti + 1) * 128],
                    P.wfull[0][:, k * 1024 + j * 128:k * 1024 + (j + 1) * 128],
                    start=(k == 0), stop=False),
                    (P.xn[k][1], P.wfull[1] + f":{k}"), (psk,),
                    tag=f"l2sv{j}_{ti}_{k}")
            b.pe(lambda e, ps=ps, j=j: e.matmul(
                ps[:, 0:128], P.oner[0][0:1, :],
                P.rows[0][0:1, T + j * 128:T + (j + 1) * 128],
                start=False, stop=True),
                (P.oner[1], P.rows[1]), (psk,), tag=f"l2svb{j}_{ti}")
            b.act(lambda s, ps=ps, ti=ti: s.activation(
                P.bb[2][0][:, ti * 128:(ti + 1) * 128], ps[:, 0:128],
                AF.Copy), (psk,), (P.bb[2][1],), tag=f"l2sve{j}_{ti}")

        if DEBUG_DUMP and j == 0:
            for nm, tt_ in (("dbg_q", P.bb[0]), ("dbg_k", P.bb[1]),
                            ("dbg_v", P.bb[2])):
                b.dma("POOL", P.ins[nm], tt_[0][:, :], (tt_[1],),
                      (nm,), tag=nm)

        def qspec(hh):
            return [(P.bb[0][0][hh * 64:(hh + 1) * 64, :], P.bb[0][1],
                     hh * 64, 64)]

        def kspec(hh):
            return [(P.bb[1][0][hh * 64:(hh + 1) * 64, :], P.bb[1][1],
                     hh * 64, 64)]

        def out_cb(hh, qi, pv, j=j):
            pvh, pvk = pv
            b.dve(lambda v, pvh=pvh, hh=hh, qi=qi: v.tensor_copy(
                P.fb[2][0][hh * 64:(hh + 1) * 64, qi * 128:(qi + 1) * 128],
                pvh[hh * 64:(hh + 1) * 64, 0:128]),
                (pvk,), (P.fb[2][1],), tag=f"l2s{j}_{hh}_{qi}")

        emit_attn(P, f"l2a{j}", qspec, kspec,
                  (P.bb[2][0], P.bb[2][1]), 128, lambda hh: hh * 64,
                  out_cb, 0.125, (P.bb[3][0], P.bb[3][1]),
                  (P.bb[4][0], P.bb[4][1]), window=False, sink=None)
        f0h, f0k = P.fb[0]
        b.dma("SP", f0h[:, :], xsrc[j * 128:(j + 1) * 128, :],
              (() if src_keys is None else (src_keys[j],)), (f0k,),
              tag=f"l2cx{j}")
        if DEBUG_DUMP:
            b.dma("SP", P.ins["dbg_s"][j * 128:(j + 1) * 128, :],
                  P.fb[2][0][:, :], (P.fb[2][1],), (f"dbgs{j}",),
                  tag=f"dbgs{j}")
        b.dve(lambda v, j=j: v.tensor_mul(P.fb[2][0][:, :], P.fb[2][0][:, :],
                                          P.gb[j][0][:, :]),
              (P.fb[2][1], P.gb[j][1]), (P.fb[2][1],), tag=f"l2cm{j}")
        b.dve(lambda v: v.tensor_add(P.fb[2][0][:, :], P.fb[2][0][:, :],
                                     f0h[:, :]),
              (P.fb[2][1], f0k), (P.fb[2][1],), tag=f"l2ca{j}")
        b.dma("SP", xdst[j * 128:(j + 1) * 128, :], P.fb[2][0][:, :],
              (P.fb[2][1],), (dst_keys[j],), tag=f"l2cw{j}")


def _perm_half(n_heads, hd):
    idx = []
    for h in range(n_heads):
        idx.extend(h * hd + 2 * np.arange(hd // 2))
    for h in range(n_heads):
        idx.extend(h * hd + 2 * np.arange(hd // 2) + 1)
    return np.array(idx)


def _wt(a, bc=128):
    a = np.asarray(a)
    din, dout = a.shape
    kcd, nj = din // 128, dout // bc
    return np.ascontiguousarray(
        a.reshape(kcd, 128, nj, bc).transpose(1, 2, 0, 3)
        .reshape(128, nj * kcd * bc))


def _wte(a, bc=128):
    a = np.asarray(a)
    E, din, dout = a.shape
    kcd, nj = din // 128, dout // bc
    return np.ascontiguousarray(
        a.reshape(E, kcd, 128, nj, bc).transpose(0, 2, 3, 1, 4)
        .reshape(E, 128, nj * kcd * bc))


def _col2d(v):
    return np.ascontiguousarray(v.reshape(-1, 128).T)


def prep_l2(ins, d):
    C = np.ascontiguousarray
    d["l2_attn_norm_w"] = _col2d(ins["L2_ffn_attn_norm_w"])
    d["l2_ffn_norm_w"] = _col2d(ins["L2_ffn_ffn_norm_w"])
    qkv_w = np.asarray(ins["L2_ffn_qkv_w"])
    qkv_b = np.asarray(ins["L2_ffn_qkv_b"])
    qp_x1, qp_x2 = [], []
    for j in range(4):
        for g in range(4):
            h = g * 4 + j
            qp_x1.extend(h * 64 + 2 * np.arange(32))
            qp_x2.extend(h * 64 + 2 * np.arange(32) + 1)
    qp = np.array(qp_x1 + qp_x2)
    kp = 1024 + _perm_half(4, 64)
    cols = np.concatenate([qp, kp])
    d["l2_qkv_w"] = _wt(qkv_w[:, cols])
    d["l2_qkv_b2d"] = C(qkv_b[cols].reshape(10, 128).T)
    d["l2_wv_cols"] = C(qkv_w[:, 1280:1536])
    d["l2_bv_row"] = C(qkv_b[1280:1536].reshape(1, 256))
    d["l2_proj_w"] = _wt(np.asarray(ins["L2_ffn_proj_w"]))
    d["l2_proj_b2d"] = _col2d(np.asarray(ins["L2_ffn_proj_b"]))
    d["l2_sink_bc"] = C(np.broadcast_to(np.asarray(ins["L2_ffn_attn_sink"]),
                                        (128, 16)))
    d["l2_gate_w"] = _wt(np.asarray(ins["L2_ffn_gate_w"]), 32)
    d["l2_gate_b2d"] = C(np.asarray(ins["L2_ffn_gate_b"]).reshape(32, 1))
    d["l2_wup"] = _wte(np.asarray(ins["L2_ffn_W_up"]))
    d["l2_wdn"] = _wte(np.asarray(ins["L2_ffn_W_down"]))
    DH = 64
    pos = np.arange(T, dtype=np.float64) / 32.0
    invf = 1.0 / (150000.0 ** (np.arange(0, DH, 2, dtype=np.float64) / DH))
    ang = pos[None, :] * invf[:, None]
    sin = np.sin(ang).astype(np.float32)
    cos = np.cos(ang).astype(np.float32)
    d["l2_rope_sin"] = C(np.tile(sin, (4, 1)))
    d["l2_rope_cos"] = C(np.tile(cos, (4, 1)))
    d["l2_wqT"] = _wt(np.asarray(ins["L2_attn_Wq"]).T)
    d["l2_wkT"] = _wt(np.asarray(ins["L2_attn_Wk"]).T)
    d["l2_wvT"] = C(np.asarray(ins["L2_attn_Wv"]).T)
    d["l2_bq2d"] = _col2d(np.asarray(ins["L2_attn_bq"]))
    d["l2_bk2d"] = _col2d(np.asarray(ins["L2_attn_bk"]))
    d["l2_bv_row2"] = C(np.asarray(ins["L2_attn_bv"]).reshape(1, 1024))


L2_DEVINS = [
    ("l2_attn_norm_w", [128, 8]), ("l2_ffn_norm_w", [128, 8]),
    ("l2_qkv_w", [128, 10240]), ("l2_qkv_b2d", [128, 10]),
    ("l2_wv_cols", [1024, 256]), ("l2_bv_row", [1, 256]),
    ("l2_proj_w", [128, 8192]), ("l2_proj_b2d", [128, 8]),
    ("l2_sink_bc", [128, 16]), ("l2_gate_w", [128, 256]),
    ("l2_gate_b2d", [32, 1]), ("l2_wup", [32, 128, 16384]),
    ("l2_wdn", [32, 128, 8192]), ("l2_rope_sin", [128, 2048]),
    ("l2_rope_cos", [128, 2048]), ("l2_wqT", [128, 8192]),
    ("l2_wkT", [128, 8192]), ("l2_wvT", [1024, 1024]),
    ("l2_bq2d", [128, 8]), ("l2_bk2d", [128, 8]), ("l2_bv_row2", [1, 1024]),
]

LAYER_DEVINS = {2: L2_DEVINS}
LAYER_PREP = {2: prep_l2}
LAYER_EMIT = {}


def build_program(layer_ids):
    nc = bass.Bass("TRN2", target_bir_lowering=False)
    ins = {}
    ins["x_in"] = nc.dram_tensor("x_in", [D, T], F32, kind="ExternalInput").ap()
    ins["x_out"] = nc.dram_tensor("x_out", [D, T], F32, kind="ExternalOutput").ap()
    ins["xres"] = nc.dram_tensor("xres", [D, T], F32).ap()
    ins["x2scr"] = nc.dram_tensor("x2scr", [D, T], BF16).ap()
    ins["cescr"] = nc.dram_tensor("cescr", [32, T], F32).ap()
    ins["rowscr"] = nc.dram_tensor("rowscr", [64, T], F32).ap()
    ins["vscr"] = nc.dram_tensor("vscr", [T, 2048], BF16).ap()
    ins["outscr"] = nc.dram_tensor("outscr", [2048, T], BF16).ap()
    ins["hscr"] = nc.dram_tensor("hscr", [2816, T], BF16).ap()
    ins["hscr2"] = nc.dram_tensor("hscr2", [2048, T], BF16).ap()
    ins["xfscr"] = nc.dram_tensor("xfscr", [D, T], BF16).ap()
    if DEBUG_DUMP:
        for nm in ("dbg_xn", "dbg_g", "dbg_s"):
            ins[nm] = nc.dram_tensor(nm, [D, T], F32,
                                     kind="ExternalOutput").ap()
        for nm in ("dbg_q", "dbg_k", "dbg_v"):
            ins[nm] = nc.dram_tensor(nm, [128, T], F32,
                                     kind="ExternalOutput").ap()
        for nm in ("dbg_t0", "dbg_t1", "dbg_t2", "dbg_t3"):
            ins[nm] = nc.dram_tensor(nm, [128, T], F32,
                                     kind="ExternalOutput").ap()
        ins["dbg_t4"] = nc.dram_tensor("dbg_t4", [128, 1024], F32,
                                       kind="ExternalOutput").ap()
        for nm in ("dbg_x2", "dbg_y2"):
            ins[nm] = nc.dram_tensor(nm, [D, T], F32,
                                     kind="ExternalOutput").ap()
        ins["dbg_ce"] = nc.dram_tensor("dbg_ce", [32, T], F32,
                                       kind="ExternalOutput").ap()
        ins["dbg_gl"] = nc.dram_tensor("dbg_gl", [32, T], F32,
                                       kind="ExternalOutput").ap()
        ins["dbg_glt"] = nc.dram_tensor("dbg_glt", [128, 512], F32,
                                        kind="ExternalOutput").ap()
    for l in layer_ids:
        for name, shape in LAYER_DEVINS[l]:
            if name in ins:
                continue
            ins[name] = nc.dram_tensor(name, shape, F32,
                                       kind="ExternalInput").ap()
    with ExitStack() as st:
        P = Prog(nc, st, ins)
        emit_prologue(P)
        nl = len(layer_ids)
        for i, l in enumerate(layer_ids):
            if i == 0:
                xsrc, src_keys = ins["x_in"], None
            else:
                xsrc, src_keys = ins["xres"], [f"xres{j}" for j in range(8)]
            if i == nl - 1:
                xdst, dst_keys = ins["x_out"], [f"xout{j}" for j in range(8)]
            else:
                xdst, dst_keys = ins["xres"], [f"xres{j}" for j in range(8)]
            LAYER_EMIT[l](P, xsrc, xdst, src_keys, dst_keys)
        b = P.b
        outs = [b.last_writer[f"xout{j}"] for j in range(8)]
        if DEBUG_DUMP:
            for j in range(8):
                for pre in ("dbgg", "dbgxn", "dbgs"):
                    if f"{pre}{j}" in b.last_writer:
                        outs.append(b.last_writer[f"{pre}{j}"])
            for nm in ("dbg_q", "dbg_k", "dbg_v", "dbgce", "dbggl", "dbgglt"):
                if nm in b.last_writer:
                    outs.append(b.last_writer[nm])
            for j in range(8):
                for pre in ("dbgx2_", "dbgy2_"):
                    if f"{pre}{j}" in b.last_writer:
                        outs.append(b.last_writer[f"{pre}{j}"])
        b.barrier_on("SP", outs, tag="final")
        b.finalize()
    return nc


FUSED_GROUPS = [[0, 1, 2, 3]]
DEBUG_DUMP = False


def kernel(**inputs):
    d = {}
    for l in set(sum(FUSED_GROUPS, [])):
        LAYER_PREP[l](inputs, d)
    x = np.asarray(inputs["L0_attn_u"])
    ncores = 8
    xcur = [np.ascontiguousarray(x[c].T) for c in range(ncores)]
    for grp in FUSED_GROUPS:
        nc = build_program(grp)
        names = {n for n, _ in sum((LAYER_DEVINS[l] for l in grp), [])}
        in_maps = []
        for c in range(ncores):
            m = {n: d[n] for n in names}
            m["x_in"] = xcur[c]
            in_maps.append(m)
        res = run_bass_kernel_spmd(nc, in_maps, core_ids=list(range(ncores)))
        xcur = [r["x_out"] for r in res.results]
    out = np.stack([xc.T for xc in xcur]).astype(np.float32)
    return np.ascontiguousarray(out)


LAYER_EMIT[2] = emit_layer2


def emit_topk_probs(P, pref, glsrc, cedst, E, K, eps=0.0):
    b = P.b
    st_h, st_k = P.stat
    glh, glk = glsrc
    ceh, cek = cedst
    tkh, tkk = P.small("tktile", [128, 64], F32)
    glth, gltk = P.stat[0][:, 32:64], P.stat[1]
    for ti in range(16):
        ph, pk = P.pb[6]
        b.pe(lambda e, ti=ti, ph=ph: e.transpose(
            ph[0:128, 0:E], glh[0:E, ti * 128:(ti + 1) * 128],
            P.idf[0][0:E, 0:E]), (glk, P.idf[1]), (pk,), tag=f"{pref}t{ti}")
        b.dve(lambda v, ph=ph: v.tensor_copy(glth[:, 0:E], ph[0:128, 0:E]),
              (pk,), (gltk,), tag=f"{pref}te{ti}")
        b.dve(lambda v: v.tensor_copy(tkh[:, 0:E], glth[:, 0:E]),
              (gltk,), (tkk,), tag=f"{pref}c{ti}")
        for r in range(K):
            b.dve(lambda v, r=r: v.reduce_max(st_h[:, 12 + r:13 + r],
                                              tkh[:, 0:E], axis=AX.X),
                  (tkk,), (st_k,), tag=f"{pref}m{ti}_{r}")
            if r < K - 1:
                b.dve(lambda v, r=r: v.tensor_scalar(
                    tkh[:, 32:32 + E], tkh[:, 0:E], st_h[:, 12 + r:13 + r],
                    NEG_BIG, ALU.is_ge, ALU.mult),
                    (tkk, st_k), (tkk,), tag=f"{pref}s{ti}_{r}")
                b.dve(lambda v: v.tensor_add(tkh[:, 0:E], tkh[:, 0:E],
                                             tkh[:, 32:32 + E]),
                      (tkk,), (tkk,), tag=f"{pref}a{ti}_{r}")
        b.dve(lambda v: v.tensor_scalar_mul(st_h[:, 16:17], st_h[:, 12:13],
                                            -1.0), (st_k,), (st_k,),
              tag=f"{pref}n{ti}")
        b.act(lambda s: s.activation(tkh[:, 0:E], glth[:, 0:E], AF.Exp,
                                     bias=st_h[:, 16:17], scale=1.0,
                                     accum_out=st_h[:, 19:20]),
              (gltk, st_k, "cpool"), (tkk, st_k), tag=f"{pref}e{ti}")
        b.dve(lambda v: v.tensor_scalar(
            tkh[:, 32:32 + E], glth[:, 0:E], st_h[:, 12 + K - 1:13 + K - 1],
            None, ALU.is_ge), (gltk, st_k), (tkk,), tag=f"{pref}g{ti}")
        b.dve(lambda v: v.tensor_mul(tkh[:, 0:E], tkh[:, 0:E],
                                     tkh[:, 32:32 + E]),
              (tkk,), (tkk,), tag=f"{pref}w{ti}")
        b.dve(lambda v: v.reduce_sum(st_h[:, 17:18], tkh[:, 0:E], axis=AX.X),
              (tkk,), (st_k,), tag=f"{pref}ss{ti}")
        if eps:
            b.dve(lambda v, eps=eps: v.tensor_scalar(
                st_h[:, 20:21], st_h[:, 19:20], eps, None, ALU.mult),
                (st_k,), (st_k,), tag=f"{pref}ep{ti}")
            b.dve(lambda v: v.tensor_add(st_h[:, 17:18], st_h[:, 17:18],
                                         st_h[:, 20:21]),
                  (st_k,), (st_k,), tag=f"{pref}ea{ti}")
        b.dve(lambda v: v.reciprocal(st_h[:, 18:19], st_h[:, 17:18]),
              (st_k,), (st_k,), tag=f"{pref}r{ti}")
        b.dve(lambda v: v.tensor_scalar_mul(tkh[:, 0:E], tkh[:, 0:E],
                                            st_h[:, 18:19]),
              (tkk, st_k), (tkk,), tag=f"{pref}f{ti}")
        ph2, pk2 = P.pb[6]
        b.pe(lambda e, ph2=ph2: e.transpose(ph2[0:E, 0:128], tkh[:, 0:E],
                                            P.idf[0][:, :]),
             (tkk, P.idf[1]), (pk2,), tag=f"{pref}tb{ti}")
        b.dve(lambda v, ti=ti, ph2=ph2: v.tensor_copy(
            ceh[0:E, ti * 128:(ti + 1) * 128], ph2[0:E, 0:128]),
            (pk2,), (cek,), tag=f"{pref}ce{ti}")
    b.dma("SP", P.ins["cescr"][0:E, :], ceh[0:E, :], (cek,), ("cescr",),
          tag=f"{pref}dma")


def emit_moe_std(P, pref, nE, dff, wup_r, wdn_r, src, hid, acc, upslots,
                 dnslots, acc_init_first=True):
    b = P.b
    kd = dff // 128
    cebc_h, cebc_k = P.fb[1]
    nus = len(upslots)
    for e in range(nE):
        b.dma("SP", P.rows[0][0:1, 0:T], P.ins["cescr"][e:e + 1, :],
              ("cescr",), (P.rows[1],), tag=f"{pref}cer{e}")
        for t in range(NT):
            bcast_row(b, f"{pref}ceb{e}_{t}",
                      (P.rows[0], P.rows[1], slice(t * TT, (t + 1) * TT)),
                      P.pb[6], (P.oner[0], P.oner[1]))
            b.act(lambda s, t=t: s.activation(
                cebc_h[:, t * TT:(t + 1) * TT], P.pb[6][0][:, :TT], AF.Copy),
                (P.pb[6][1],), (cebc_k,), tag=f"{pref}cee{e}_{t}")
        seq = []
        for jj in range(kd):
            seq.append(jj * 128)
            seq.append(dff + jj * 128)
        nseq = len(seq)

        def dma_up(si, e=e):
            col0 = seq[si]
            wh, wk = upslots[si % nus]
            b.dma("POOL", wh[:, 0:KC * 128], wup_r[e, :, col0 // 128, :],
                  (), (wk,), tag=f"{pref}up{e}_{si}")
        for si in range(min(nus, nseq)):
            dma_up(si)
        for jj in range(kd):
            for half in (0, 1):
                si = jj * 2 + half
                wh, wk = upslots[si % nus]
                for k in range(KC):
                    for t in range(NT):
                        ps, psk = P.pb[half * 4 + t] if half == 0 else P.pb[4 + t]
                        ps, psk = P.pb[half * 4 + t]
                        b.pe(lambda e2, ps=ps, wh=wh, k=k, t=t: e2.matmul(
                            ps[:, :TT], wh[:, k * 128:(k + 1) * 128],
                            src[k][0][:, t * TT:(t + 1) * TT],
                            start=(k == 0), stop=(k == KC - 1)),
                            (wk, src[k][1]), (psk,),
                            tag=f"{pref}um{e}_{si}_{k}_{t}")
                if si + nus < nseq:
                    dma_up(si + nus)
            hh_, hk_ = hid[jj]
            for t in range(NT):
                sh, sk = P.sq
                ss = ((jj * 4 + t) % 2) * TT
                b.act(lambda s, t=t, sh=sh, ss=ss: s.activation(
                    sh[:, ss:ss + TT], P.pb[t][0][:, :TT], AF.Silu),
                    (P.pb[t][1],), (sk,), tag=f"{pref}sg{e}_{jj}_{t}")
                b.dve(lambda v, t=t, hh_=hh_, sh=sh, ss=ss: v.tensor_mul(
                    hh_[:, t * TT:(t + 1) * TT], P.pb[4 + t][0][:, :TT],
                    sh[:, ss:ss + TT]),
                    (P.pb[4 + t][1], sk), (hk_,), tag=f"{pref}hu{e}_{jj}_{t}")
                b.dve(lambda v, t=t, hh_=hh_: v.tensor_mul(
                    hh_[:, t * TT:(t + 1) * TT], hh_[:, t * TT:(t + 1) * TT],
                    cebc_h[:, t * TT:(t + 1) * TT]),
                    (hk_, cebc_k), (hk_,), tag=f"{pref}hc{e}_{jj}_{t}")
        nds = len(dnslots)

        def dma_dn(j, e=e):
            wh, wk = dnslots[j % nds]
            b.dma("POOL", wh[:, 0:kd * 128], wdn_r[e, :, j, :], (), (wk,),
                  tag=f"{pref}dn{e}_{j}")
        for j in range(min(nds, 8)):
            dma_dn(j)
        for j in range(8):
            wh, wk = dnslots[j % nds]
            for k in range(kd):
                for t in range(NT):
                    ps, psk = P.pb[(j % 2) * 4 + t]
                    hh_, hk_ = hid[k]
                    b.pe(lambda e2, ps=ps, wh=wh, k=k, t=t, hh_=hh_:
                         e2.matmul(ps[:, :TT], wh[:, k * 128:(k + 1) * 128],
                                   hh_[:, t * TT:(t + 1) * TT],
                                   start=(k == 0), stop=(k == kd - 1)),
                         (wk, hk_), (psk,), tag=f"{pref}dm{e}_{j}_{k}_{t}")
            if j + nds < 8:
                dma_dn(j + nds)
            gh, gk = acc[j]
            for t in range(NT):
                ps, psk = P.pb[(j % 2) * 4 + t]
                if e == 0 and acc_init_first:
                    b.act(lambda s, ps=ps, gh=gh, t=t: s.activation(
                        gh[:, t * TT:(t + 1) * TT], ps[:, :TT], AF.Copy),
                        (psk,), (gk,), tag=f"{pref}y2i{j}_{t}")
                else:
                    b.dve(lambda v, ps=ps, gh=gh, t=t: v.tensor_add(
                        gh[:, t * TT:(t + 1) * TT],
                        gh[:, t * TT:(t + 1) * TT], ps[:, :TT]),
                        (psk, gk), (gk,), tag=f"{pref}y2a{e}_{j}_{t}")


def emit_layer3(P, xsrc, xdst, src_keys=None, dst_keys=None):
    b = P.b
    ins = P.ins
    emit_outer_ln(P, xsrc, "l3o", src_keys)

    rl = P.fb[2]
    def epi_rt(j, t, ph, pk):
        b.dve(lambda v, ph=ph, t=t: v.tensor_copy(
            rl[0][0:8, t * TT:(t + 1) * TT], ph[0:8, :TT]),
            (pk,), (rl[1],), tag=f"l3rt{t}")
    mm_dense(b, "l3r", ins["l3_router_w"], D, 8, P.xn, P.pb[:6],
             [(P.wbA[0][:, 0:64], P.wbA[1] + ":0")], epi_rt, blk_cols=8)
    emit_topk_probs(P, "l3tk", rl, rl, 8, 2, eps=0.0)
    upslots = [(P.wbA[0][:, s * 1024:(s + 1) * 1024], P.wbA[1] + f":{s}")
               for s in range(2)] + \
              [(P.wbB[0][:, s * 1024:(s + 1) * 1024], P.wbB[1] + f":{s}")
               for s in range(2)]
    dnslots = [(P.wfull[0][:, s * 1024:(s + 1) * 1024], P.wfull[1] + f":{s}")
               for s in range(8)]
    wup_r = wtsrc_e(ins["l3_up"], 8)
    wdn_r = wtsrc_e(ins["l3_dn"], 8)
    emit_moe_std(P, "l3m", 8, 1024, wup_r, wdn_r, P.xn,
                 [P.bb[8 + k] for k in range(8)], P.gb, upslots, dnslots)
    for j in range(8):
        gh, gk = P.gb[j]
        b.act(lambda s, gh=gh: s.activation(gh[:, :], gh[:, :], AF.Silu),
              (gk,), (gk,), tag=f"l3gs{j}")

    ln1w = load_smallvec(P, "l3ln1w", "l3_ln1_w2d", (128, 8))
    ln1b = load_smallvec(P, "l3ln1b", "l3_ln1_b2d", (128, 8))
    ln2w = load_smallvec(P, "l3ln2w", "l3_ln2_w2d", (128, 8))
    ln2b = load_smallvec(P, "l3ln2b", "l3_ln2_b2d", (128, 8))
    bqk2 = load_smallvec(P, "l3bqk", "l3_bqk2d", (128, 2))
    gam = load_smallvec(P, "l3gam", "l3_gamma2d", (128, 8))
    bet = load_smallvec(P, "l3bet", "l3_beta2d", (128, 8))
    bg2 = load_smallvec(P, "l3bg", "l3_bg2d", (128, 16))
    bo2 = load_smallvec(P, "l3bo", "l3_bo2d", (128, 8))
    b64 = P.small("l3b64", [64, 64], BF16)
    b.dma("POOL", b64[0][:, :], ins["l3_bias64"], (), (b64[1],), tag="l3b64")

    rows_h, rows_k = P.rows
    f2h, f2k = P.fb[2]
    eps5 = b.const(1e-5)
    for t in range(NT):
        for k in range(KC):
            xh, xk = P.xn[k]
            b.pe(lambda e, xh=xh, k=k, t=t: e.matmul(
                P.pb[0][0][0:1, :TT], P.onecb[0][:, 0:1],
                xh[:, t * TT:(t + 1) * TT],
                start=(k == 0), stop=(k == KC - 1)),
                (P.onecb[1], xk), (P.pb[0][1],), tag=f"l3lsx{k}_{t}")
            sh, sk = P.sq
            b.act(lambda s, xh=xh, sh=sh, t=t: s.activation(
                sh[:, :TT], xh[:, t * TT:(t + 1) * TT], AF.Square),
                (xk,), (sk,), tag=f"l3lsq{k}_{t}")
            b.pe(lambda e, sh=sh, k=k: e.matmul(
                P.pb[1][0][0:1, :TT], P.onec[0][:, 0:1], sh[:, :TT],
                start=(k == 0), stop=(k == KC - 1)),
                (P.onec[1], sk), (P.pb[1][1],), tag=f"l3lsq2{k}_{t}")
        b.dve(lambda v, t=t: v.tensor_scalar_mul(
            rows_h[0:1, t * TT:(t + 1) * TT], P.pb[0][0][0:1, :TT], 1.0 / D),
            (P.pb[0][1],), (rows_k,), tag=f"l3lmu{t}")
        b.dve(lambda v, t=t: v.tensor_scalar_mul(
            f2h[0:1, t * TT:(t + 1) * TT], P.pb[1][0][0:1, :TT], 1.0 / D),
            (P.pb[1][1],), (f2k,), tag=f"l3lms{t}")
    b.act(lambda s: s.activation(rows_h[0:1, T:2 * T], rows_h[0:1, 0:T],
                                 AF.Square), (rows_k,), (rows_k,), tag="l3lm2")
    b.dve(lambda v: v.tensor_sub(rows_h[0:1, T:2 * T], f2h[0:1, 0:T],
                                 rows_h[0:1, T:2 * T]),
          (rows_k, f2k), (rows_k,), tag="l3lvar")
    b.act(lambda s: s.activation(rows_h[0:1, T:2 * T], rows_h[0:1, T:2 * T],
                                 AF.Ln, bias=eps5[0:1, :]),
          (rows_k, "cpool"), (rows_k,), tag="l3lln")
    b.act(lambda s: s.activation(rows_h[0:1, T:2 * T], rows_h[0:1, T:2 * T],
                                 AF.Exp, scale=-0.5), (rows_k,), (rows_k,),
          tag="l3lexp")
    for t in range(NT):
        bcast_row(b, f"l3bm{t}", (rows_h, rows_k, slice(t * TT, (t + 1) * TT)),
                  P.pb[2], (P.oner[0], P.oner[1]))
        bcast_row(b, f"l3br{t}", (rows_h, rows_k, slice(T + t * TT, T + (t + 1) * TT)),
                  P.pb[3], (P.oner[0], P.oner[1]))
        for k in range(KC):
            xh, xk = P.xn[k]
            dh, dk = P.bb[k]
            b.dve(lambda v, xh=xh, t=t: v.tensor_sub(
                f2h[:, t * TT:(t + 1) * TT], xh[:, t * TT:(t + 1) * TT],
                P.pb[2][0][:, :TT]),
                (xk, P.pb[2][1]), (f2k,), tag=f"l3ls{k}_{t}")
            b.dve(lambda v, dh=dh, k=k, t=t: v.scalar_tensor_tensor(
                f2h[:, t * TT:(t + 1) * TT], f2h[:, t * TT:(t + 1) * TT],
                ln1w[0][:, k:k + 1], P.pb[3][0][:, :TT], ALU.mult, ALU.mult),
                (f2k, ln1w[1], P.pb[3][1]), (f2k,), tag=f"l3lm{k}_{t}")
            b.dve(lambda v, dh=dh, k=k, t=t: v.tensor_scalar_add(
                dh[:, t * TT:(t + 1) * TT], f2h[:, t * TT:(t + 1) * TT],
                ln1b[0][:, k:k + 1]),
                (f2k, ln1b[1]), (dk,), tag=f"l3lb{k}_{t}")
    xl = [P.bb[k] for k in range(KC)]

    def epi_qk(j, t, ph, pk):
        dh, dk = P.bb[8 + j]
        b.act(lambda s, ph=ph, dh=dh, t=t, j=j: s.activation(
            dh[:, t * TT:(t + 1) * TT], ph[:, :TT], AF.Silu,
            bias=bqk2[0][:, j:j + 1], scale=1.0),
            (pk, bqk2[1], "cpool"), (dk,), tag=f"l3qk{j}_{t}")
    mm_dense(b, "l3qk", ins["l3_wqkT"], D, 256, xl, P.pb[:6],
             [(P.wbA[0][:, s * 1024:(s + 1) * 1024], P.wbA[1] + f":{s}")
              for s in range(2)], epi_qk)

    sinT = P.fb[0]
    cosT = P.fb[1]
    b.dma("SP", sinT[0][:, :], ins["l3_sin"], (), (sinT[1],), tag="l3sin")
    b.dma("SP", cosT[0][:, :], ins["l3_cos"], (), (cosT[1],), tag="l3cos")
    gbs = P.small("l3gbs", [128, 4], F32)
    for c in range(2):
        b.dve(lambda v, c=c: v.tensor_scalar_mul(
            gbs[0][:, c:c + 1], gam[0][:, c:c + 1], 1.0 / 64),
            (gam[1],), (gbs[1],), tag=f"l3gs{c}")
        b.dve(lambda v, c=c: v.tensor_scalar_mul(
            gbs[0][:, 2 + c:3 + c], bet[0][:, c:c + 1], 1.0 / 64),
            (bet[1],), (gbs[1],), tag=f"l3bs{c}")

    def build_mat(i, dsts, gcols, bcols, gsrc, bsrc, pref):
        pre = []
        for c in range(2):
            qh, qkey = P.bb[8 + c]
            dh, dk = dsts[c]
            b.dve(lambda v, qh=qh, dh=dh, c=c: v.tensor_scalar(
                dh[:, :], qh[:, :], gsrc[0][:, gcols[c]:gcols[c] + 1],
                bsrc[0][:, bcols[c]:bcols[c] + 1], ALU.mult, ALU.add),
                (qkey, gsrc[1], bsrc[1]), (dk,), tag=f"{pref}p{c}")
        (d0h, d0k), (d1h, d1k) = dsts
        for th in range(2):
            cs = slice(th * 1024, (th + 1) * 1024)
            sA = f2h[:, 0:1024]
            sB = f2h[:, 1024:2048]
            b.dve(lambda v, cs=cs, sA=sA: v.tensor_mul(
                sA, d0h[:, cs], sinT[0][:, cs]),
                (d0k, sinT[1]), (f2k,), tag=f"{pref}r1_{th}")
            b.dve(lambda v, cs=cs, sB=sB: v.tensor_mul(
                sB, d0h[:, cs], cosT[0][:, cs]),
                (d0k, cosT[1]), (f2k,), tag=f"{pref}r2_{th}")
            b.dve(lambda v, cs=cs: v.tensor_mul(
                d0h[:, cs], d1h[:, cs], sinT[0][:, cs]),
                (d1k, sinT[1]), (d0k,), tag=f"{pref}r3_{th}")
            b.dve(lambda v, cs=cs, sB=sB: v.tensor_sub(
                d0h[:, cs], sB, d0h[:, cs]),
                (f2k,), (d0k,), tag=f"{pref}r4_{th}")
            b.dve(lambda v, cs=cs: v.tensor_mul(
                d1h[:, cs], d1h[:, cs], cosT[0][:, cs]),
                (d1k, cosT[1]), (d1k,), tag=f"{pref}r5_{th}")
            b.dve(lambda v, cs=cs, sA=sA: v.tensor_add(
                d1h[:, cs], d1h[:, cs], sA),
                (f2k,), (d1k,), tag=f"{pref}r6_{th}")

    build_mat(0, [P.bb[10], P.bb[11]], [0, 1], [0, 1], gbs, gbs, "l3qq")
    build_mat(1, [P.bb[14], P.bb[15]], [2, 3], [2, 3], gam, bet, "l3lq")
    build_mat(2, [P.bb[12], P.bb[13]], [4, 5], [4, 5], gam, bet, "l3qkk")
    build_mat(3, [P.bb[8], P.bb[9]], [6, 7], [6, 7], gam, bet, "l3lk")

    att = (fbi_bf(P.fb[2][0], 0, T), P.fb[2][1])
    for g in range(32):
        ph, pk = P.pb[g % 2]
        ob = (g % 2) * 64
        for c in range(2):
            b.pe(lambda e, ph=ph, g=g, c=c, ob=ob: e.matmul(
                ph[ob:ob + 64, 0:64], P.bb[12 + c][0][:, g * 64:(g + 1) * 64],
                P.bb[10 + c][0][:, g * 64:(g + 1) * 64],
                start=(c == 0), stop=False, tile_position=(0, ob)),
                (P.bb[12 + c][1], P.bb[10 + c][1]), (pk,), tag=f"l3sc{g}_{c}")
        b.pe(lambda e, ph=ph, ob=ob: e.matmul(
            ph[ob:ob + 64, 0:64], P.idb[0][0:64, 0:64], b64[0][:, :],
            start=False, stop=True, tile_position=(0, ob)),
            (P.idb[1], b64[1]), (pk,), tag=f"l3scb{g}")
        sh, sk = P.sq
        b.dve(lambda v, ph=ph, ob=ob: v.tensor_scalar_max(
            sh[ob:ob + 64, 0:64], ph[ob:ob + 64, 0:64], 0.0),
              (pk,), (sk,), tag=f"l3rl{g}")
        ah, ak = att
        b.act(lambda s, g=g, ah=ah, sh=sh, ob=ob: s.activation(
            ah[ob:ob + 64, g * 64:(g + 1) * 64], sh[ob:ob + 64, 0:64],
            AF.Square), (sk,), (ak,), tag=f"l3r2{g}")

    lkt = [(fbi_bf(P.fb[0][0], 0, T), P.fb[0][1]),
           (fbi_bf(P.fb[0][0], T, 2 * T), P.fb[0][1])]
    for c in range(2):
        for ti in range(16):
            ph, pk = P.pb[6]
            b.pe(lambda e, c=c, ti=ti, ph=ph: e.transpose(
                ph[:, :].bitcast(BF16)[:, 0:128],
                P.bb[8 + c][0][:, ti * 128:(ti + 1) * 128], P.idb[0][:, :]),
                (P.bb[8 + c][1], P.idb[1]), (pk,), tag=f"l3lt{c}_{ti}")
            dh, dk = lkt[c]
            b.act(lambda s, dh=dh, ti=ti, ph=ph: s.activation(
                dh[:, ti * 128:(ti + 1) * 128],
                ph[:, :].bitcast(BF16)[:, 0:128], AF.Copy),
                (pk,), (dk,), tag=f"l3lte{c}_{ti}")

    b.dma("SP", rows_h[0:1, 0:T], ins["l3_bv_row"], (), (rows_k,),
          tag="l3bvr")
    linkv = [P.bb[8], P.bb[9]]
    wv_r = ins["l3_wvT"].rearrange("(k p) n -> p k n", p=128)
    vth, vtk = P.bb[13][0][:, 0:TT], P.bb[13][1]
    for vb in range(4):
        s0 = (vb % 2) * 4
        wh = P.wfull[0][:, s0 * 1024:s0 * 1024 + 4096]
        wks = tuple(P.wfull[1] + f":{s0 + i}" for i in range(4))
        wk = wks[0]
        b.dma("POOL", wh.rearrange("p (k c) -> p k c", c=TT),
              wv_r[:, :, vb * TT:(vb + 1) * TT], (), wks, tag=f"l3wv{vb}")
        for ti in range(16):
            ps, psk = P.pb[2 + (ti % 2)]
            for k in range(KC):
                b.pe(lambda e, ps=ps, wh=wh, k=k, ti=ti: e.matmul(
                    ps[:, :TT], xl[k][0][:, ti * 128:(ti + 1) * 128],
                    wh[:, k * TT:(k + 1) * TT], start=(k == 0), stop=False),
                    (xl[k][1], wks[k // 2]), (psk,), tag=f"l3v{vb}_{ti}_{k}")
            b.pe(lambda e, ps=ps, vb=vb: e.matmul(
                ps[:, :TT], P.oner[0][0:1, :],
                rows_h[0:1, vb * TT:(vb + 1) * TT],
                start=False, stop=True),
                (P.oner[1], rows_k), (psk,), tag=f"l3vb{vb}_{ti}")
            b.act(lambda s, ps=ps, vth=vth: s.activation(
                vth[:, :], ps[:, :TT], AF.Silu), (psk,), (vtk,),
                tag=f"l3ve{vb}_{ti}")
            for c in range(2):
                ps2, psk2 = P.pb[4 + c]
                b.pe(lambda e, ps2=ps2, c=c, ti=ti, vth=vth: e.matmul(
                    ps2[:, :TT], lkt[c][0][:, ti * 128:(ti + 1) * 128],
                    vth[:, :], start=(ti == 0), stop=(ti == 15)),
                    (lkt[c][1], vtk), (psk2,), tag=f"l3kv{vb}_{ti}_{c}")
            b.dma("ACT", P.ins["vscr"][ti * 128:(ti + 1) * 128,
                                         vb * TT:(vb + 1) * TT],
                  vth[:, :], (vtk,), (f"vscr_{vb}_{ti}",), tag=f"l3vs{vb}_{ti}")
        for c in range(2):
            ps2, psk2 = P.pb[4 + c]
            dh, dk = linkv[c]
            b.dve(lambda v, ps2=ps2, dh=dh, vb=vb: v.tensor_scalar_mul(
                dh[:, vb * TT:(vb + 1) * TT], ps2[:, :TT], 1.0 / T),
                (psk2,), (dk,), tag=f"l3kve{vb}_{c}")


    wg_r = wtsrc(ins["l3_wgT"], 8)
    gt = P.bb[10]
    oc = P.bb[11]
    vte = P.bb[12]
    vscr_keys = tuple(f"vscr_{vb}_{ti}" for vb in range(4) for ti in range(16))
    for ec in range(16):
        wh, wk = ((P.wbA[0][:, 0:1024], P.wbA[1] + ":0") if ec % 2 == 0 else
                  (P.wbA[0][:, 1024:2048], P.wbA[1] + ":1"))
        b.dma("POOL", wh, wg_r[:, ec, :], (), (wk,),
              tag=f"l3wg{ec}")
        for t in range(NT):
            ps, psk = P.pb[t % 2]
            for k in range(KC):
                b.pe(lambda e, ps=ps, wh=wh, k=k, t=t: e.matmul(
                    ps[:, :TT], wh[:, k * 128:(k + 1) * 128],
                    xl[k][0][:, t * TT:(t + 1) * TT],
                    start=(k == 0), stop=(k == KC - 1)),
                    (wk, xl[k][1]), (psk,), tag=f"l3g{ec}_{t}_{k}")
            b.act(lambda s, ps=ps, t=t, ec=ec: s.activation(
                gt[0][:, t * TT:(t + 1) * TT], ps[:, :TT], AF.Silu,
                bias=bg2[0][:, ec:ec + 1], scale=1.0),
                (psk, bg2[1], "cpool"), (gt[1],), tag=f"l3ge{ec}_{t}")
        b.dma("ACT", vte[0][:, :],
              P.ins["vscr"].rearrange("(ti p) c -> p ti c", p=128)
              [:, :, ec * 128:(ec + 1) * 128],
              vscr_keys, (f"vte{ec}",), tag=f"l3vl{ec}")
        ah, ak = att
        for g in range(32):
            ps, psk = P.pb[2 + (g % 2)]
            ob = (g % 2) * 64
            b.pe(lambda e, ps=ps, g=g, ob=ob: e.matmul(
                ps[:, 0:64],
                vte[0][ob:ob + 64, (g // 2) * 128:(g // 2) * 128 + 128],
                ah[ob:ob + 64, g * 64:(g + 1) * 64],
                start=True, stop=False, tile_position=(ob, 0)),
                ("vte" + str(ec) if False else f"vte{ec}", ak), (psk,),
                tag=f"l3q{ec}_{g}")
            for c in range(2):
                dh, dk = linkv[c]
                b.pe(lambda e, ps=ps, dh=dh, c=c, g=g, ec=ec: e.matmul(
                    ps[:, 0:64], dh[:, ec * 128:(ec + 1) * 128],
                    P.bb[14 + c][0][:, g * 64:(g + 1) * 64],
                    start=False, stop=(c == 1)),
                    (dk, P.bb[14 + c][1]), (psk,), tag=f"l3l{ec}_{g}_{c}")
            b.dve(lambda v, ps=ps, g=g: v.tensor_mul(
                oc[0][:, g * 64:(g + 1) * 64], ps[:, 0:64],
                gt[0][:, g * 64:(g + 1) * 64]),
                (psk, gt[1]), (oc[1],), tag=f"l3o{ec}_{g}")
        b.dma("ACT", P.ins["outscr"][ec * 128:(ec + 1) * 128, :],
              oc[0][:, :], (oc[1],), (f"outscr{ec}",), tag=f"l3os{ec}")

    wo_r = wtsrc(ins["l3_woT"], 16)
    spre = [P.bb[10 + i] for i in range(6)] +            [(fbi_bf(P.fb[0][0], 0, T), P.fb[0][1]),
            (fbi_bf(P.fb[0][0], T, 2 * T), P.fb[0][1])]
    rhrot = [P.bb[8], P.bb[9]]
    for jp in range(4):
        wks = []
        for jj in range(2):
            j = jp * 2 + jj
            s0 = jj * 4 if jp % 2 == 0 else 4 * jj
            s0 = jj * 4
            whj = P.wfull[0][:, s0 * 1024:s0 * 1024 + 2048]
            wkj = tuple(P.wfull[1] + f":{s0 + i}" for i in range(2))
            b.dma("POOL", whj, wo_r[:, j, :], (), wkj,
                  tag=f"l3wo{j}")
            wks.append((whj, wkj))
        for k in range(16):
            rh, rk = rhrot[k % 2]
            b.dma("ACT", rh[:, :],
                  P.ins["outscr"][k * 128:(k + 1) * 128, :],
                  (f"outscr{k}",), (rk,), tag=f"l3or{jp}_{k}")
            for jj in range(2):
                whj, wkj = wks[jj]
                for t in range(NT):
                    ps, psk = P.pb[jj * 4 + t]
                    b.pe(lambda e, ps=ps, whj=whj, k=k, t=t, rh=rh: e.matmul(
                        ps[:, :TT], whj[:, k * 128:(k + 1) * 128],
                        rh[:, t * TT:(t + 1) * TT],
                        start=(k == 0), stop=(k == 15)),
                        (wkj[k // 8], rk), (psk,), tag=f"l3wm{jp}_{jj}_{k}_{t}")
        for jj in range(2):
            j = jp * 2 + jj
            dh, dk = spre[j]
            for t in range(NT):
                ps, psk = P.pb[jj * 4 + t]
                b.act(lambda s, ps=ps, dh=dh, t=t, j=j: s.activation(
                    dh[:, t * TT:(t + 1) * TT], ps[:, :TT], AF.Identity,
                    bias=bo2[0][:, j:j + 1], scale=1.0),
                    (psk, bo2[1], "cpool"), (dk,), tag=f"l3sp{j}_{t}")

    for t in range(NT):
        for k in range(KC):
            sh_, sk_ = spre[k]
            b.pe(lambda e, sh_=sh_, k=k, t=t: e.matmul(
                P.pb[0][0][0:1, :TT], P.onecb[0][:, 0:1],
                sh_[:, t * TT:(t + 1) * TT],
                start=(k == 0), stop=(k == KC - 1)),
                (P.onecb[1], sk_), (P.pb[0][1],), tag=f"l3n2x{k}_{t}")
            sqh, sqk = P.sq
            b.act(lambda s, sh_=sh_, sqh=sqh, t=t: s.activation(
                sqh[:, :TT], sh_[:, t * TT:(t + 1) * TT], AF.Square),
                (sk_,), (sqk,), tag=f"l3n2q{k}_{t}")
            b.pe(lambda e, sqh=sqh, k=k: e.matmul(
                P.pb[1][0][0:1, :TT], P.onec[0][:, 0:1], sqh[:, :TT],
                start=(k == 0), stop=(k == KC - 1)),
                (P.onec[1], sqk), (P.pb[1][1],), tag=f"l3n2q2{k}_{t}")
        b.dve(lambda v, t=t: v.tensor_scalar_mul(
            rows_h[0:1, t * TT:(t + 1) * TT], P.pb[0][0][0:1, :TT], 1.0 / D),
            (P.pb[0][1],), (rows_k,), tag=f"l3n2mu{t}")
        b.dve(lambda v, t=t: v.tensor_scalar_mul(
            f2h[0:1, t * TT:(t + 1) * TT], P.pb[1][0][0:1, :TT], 1.0 / D),
            (P.pb[1][1],), (f2k,), tag=f"l3n2ms{t}")
    b.act(lambda s: s.activation(rows_h[0:1, T:2 * T], rows_h[0:1, 0:T],
                                 AF.Square), (rows_k,), (rows_k,), tag="l3n2m2")
    b.dve(lambda v: v.tensor_sub(rows_h[0:1, T:2 * T], f2h[0:1, 0:T],
                                 rows_h[0:1, T:2 * T]),
          (rows_k, f2k), (rows_k,), tag="l3n2var")
    b.act(lambda s: s.activation(rows_h[0:1, T:2 * T], rows_h[0:1, T:2 * T],
                                 AF.Ln, bias=eps5[0:1, :]),
          (rows_k, "cpool"), (rows_k,), tag="l3n2ln")
    b.act(lambda s: s.activation(rows_h[0:1, T:2 * T], rows_h[0:1, T:2 * T],
                                 AF.Exp, scale=-0.5), (rows_k,), (rows_k,),
          tag="l3n2exp")
    f1h, f1k = P.fb[1]
    for t in range(NT):
        bcast_row(b, f"l3cbm{t}", (rows_h, rows_k, slice(t * TT, (t + 1) * TT)),
                  P.pb[2], (P.oner[0], P.oner[1]))
        bcast_row(b, f"l3cbr{t}", (rows_h, rows_k,
                                   slice(T + t * TT, T + (t + 1) * TT)),
                  P.pb[3], (P.oner[0], P.oner[1]))
        for j in range(KC):
            sh_, sk_ = spre[j]
            b.dma("SP", f1h[:, t * TT:(t + 1) * TT] if False else
                  f1h[:, 0:1], f1h[:, 0:1], (), (), tag="nop") if False else None
            b.dve(lambda v, sh_=sh_, t=t: v.tensor_sub(
                f2h[:, t * TT:(t + 1) * TT], sh_[:, t * TT:(t + 1) * TT],
                P.pb[2][0][:, :TT]),
                (sk_, P.pb[2][1]), (f2k,), tag=f"l3c1_{j}_{t}")
            b.dve(lambda v, j=j, t=t: v.scalar_tensor_tensor(
                f2h[:, t * TT:(t + 1) * TT], f2h[:, t * TT:(t + 1) * TT],
                ln2w[0][:, j:j + 1], P.pb[3][0][:, :TT], ALU.mult, ALU.mult),
                (f2k, ln2w[1], P.pb[3][1]), (f2k,), tag=f"l3c2_{j}_{t}")
            b.dve(lambda v, j=j, t=t: v.tensor_scalar_add(
                f2h[:, t * TT:(t + 1) * TT], f2h[:, t * TT:(t + 1) * TT],
                ln2b[0][:, j:j + 1]),
                (f2k, ln2b[1]), (f2k,), tag=f"l3c3_{j}_{t}")
            b.dve(lambda v, j=j, t=t: v.tensor_mul(
                f2h[:, t * TT:(t + 1) * TT], f2h[:, t * TT:(t + 1) * TT],
                P.gb[j][0][:, t * TT:(t + 1) * TT]),
                (f2k, P.gb[j][1]), (f2k,), tag=f"l3c4_{j}_{t}")
            b.dma("SP", f1h[:, t * TT:(t + 1) * TT],
                  xsrc[j * 128:(j + 1) * 128, t * TT:(t + 1) * TT],
                  (() if src_keys is None else (src_keys[j],)), (f1k,),
                  tag=f"l3cx{j}_{t}")
            b.dve(lambda v, t=t: v.tensor_add(
                f2h[:, t * TT:(t + 1) * TT], f2h[:, t * TT:(t + 1) * TT],
                f1h[:, t * TT:(t + 1) * TT]),
                (f2k, f1k), (f2k,), tag=f"l3c5_{j}_{t}")
            b.dma("SP", xdst[j * 128:(j + 1) * 128, t * TT:(t + 1) * TT],
                  f2h[:, t * TT:(t + 1) * TT], (f2k,),
                  (dst_keys[j] + f"_{t}" if False else dst_keys[j],),
                  tag=f"l3cw{j}_{t}")


def prep_l3(ins, d):
    C = np.ascontiguousarray
    d["l3_router_w"] = _wt(np.asarray(ins["L3_ffn_router_w"]).T, 8)
    d["l3_up"] = _wte(np.asarray(ins["L3_ffn_up_proj"]))
    d["l3_dn"] = _wte(np.asarray(ins["L3_ffn_down_proj"]))
    d["l3_ln1_w2d"] = _col2d(np.asarray(ins["L3_attn_ln1_w"]))
    d["l3_ln1_b2d"] = _col2d(np.asarray(ins["L3_attn_ln1_b"]))
    d["l3_ln2_w2d"] = _col2d(np.asarray(ins["L3_attn_ln2_w"]))
    d["l3_ln2_b2d"] = _col2d(np.asarray(ins["L3_attn_ln2_b"]))
    d["l3_wqkT"] = _wt(np.asarray(ins["L3_attn_Wqk"]).T)
    d["l3_bqk2d"] = C(np.asarray(ins["L3_attn_bqk"]).reshape(2, 128).T)
    d["l3_wvT"] = C(np.asarray(ins["L3_attn_Wv"]).T)
    d["l3_bv_row"] = C(np.asarray(ins["L3_attn_bv"]).reshape(1, 2048))
    d["l3_wgT"] = _wt(np.asarray(ins["L3_attn_Wg"]).T)
    d["l3_bg2d"] = C(np.asarray(ins["L3_attn_bg"]).reshape(16, 128).T)
    d["l3_woT"] = _wt(np.asarray(ins["L3_attn_Wo"]).T)
    d["l3_bo2d"] = _col2d(np.asarray(ins["L3_attn_bo"]))
    gm = np.asarray(ins["L3_attn_gamma"]).reshape(4, 2, 128)
    bt = np.asarray(ins["L3_attn_beta"]).reshape(4, 2, 128)
    d["l3_gamma2d"] = C(gm.transpose(2, 0, 1).reshape(128, 8))
    d["l3_beta2d"] = C(bt.transpose(2, 0, 1).reshape(128, 8))
    idx = np.arange(64)
    dist = np.clip(np.abs(idx[:, None] - idx[None, :]), 0, 32)
    d["l3_bias64"] = C(np.asarray(ins["L3_attn_pos_emb"])[dist].astype(np.float32))
    pos = np.arange(T, dtype=np.float64)
    invf = 10000.0 ** (-np.arange(128, dtype=np.float64) / 128)
    ang = invf[:, None] * pos[None, :]
    d["l3_sin"] = C(np.sin(ang).astype(np.float32))
    d["l3_cos"] = C(np.cos(ang).astype(np.float32))


L3_DEVINS = [
    ("l3_router_w", [128, 64]), ("l3_up", [8, 128, 16384]),
    ("l3_dn", [8, 128, 8192]),
    ("l3_ln1_w2d", [128, 8]), ("l3_ln1_b2d", [128, 8]),
    ("l3_ln2_w2d", [128, 8]), ("l3_ln2_b2d", [128, 8]),
    ("l3_wqkT", [128, 2048]), ("l3_bqk2d", [128, 2]),
    ("l3_wvT", [1024, 2048]), ("l3_bv_row", [1, 2048]),
    ("l3_wgT", [128, 16384]), ("l3_bg2d", [128, 16]),
    ("l3_woT", [128, 16384]), ("l3_bo2d", [128, 8]),
    ("l3_gamma2d", [128, 8]), ("l3_beta2d", [128, 8]),
    ("l3_bias64", [64, 64]), ("l3_sin", [128, 2048]), ("l3_cos", [128, 2048]),
]
LAYER_DEVINS[3] = L3_DEVINS
LAYER_PREP[3] = prep_l3
LAYER_EMIT[3] = emit_layer3


def emit_layer0(P, xsrc, xdst, src_keys=None, dst_keys=None):
    b = P.b
    ins = P.ins
    emit_outer_ln(P, xsrc, "l0o", src_keys)

    n1w = load_smallvec(P, "l0n1w", "l0_n1w2d", (128, 8))
    n2w = load_smallvec(P, "l0n2w", "l0_n2w2d", (128, 8))
    qn2 = load_smallvec(P, "l0qn", "l0_qn2d", (128, 1))
    kn2 = load_smallvec(P, "l0kn", "l0_kn2d", (128, 1))
    EPS0 = 1.1920929e-07
    eps0c = b.const(EPS0)
    one_c = b.const(1.0)
    h1 = [P.bb[k] for k in range(8)]
    emit_rms(P, "l0n1", P.xn, h1, n1w, EPS0)

    rows_h, rows_k = P.rows
    wq_r = wtsrc(ins["l0_wqT"], 8)
    wk_r = wtsrc(ins["l0_wkT"], 8)
    wv_r = wtsrc(ins["l0_wvT"], 8)
    wqsl = [(P.wbA[0][:, s * 1024:(s + 1) * 1024], P.wbA[1] + f":{s}")
            for s in range(2)]
    wksl = [(P.wbB[0][:, s * 1024:(s + 1) * 1024], P.wbB[1] + f":{s}")
            for s in range(2)]
    wvsl = [(P.wfull[0][:, s * 1024:(s + 1) * 1024], P.wfull[1] + f":{s}")
            for s in range(8)]
    ytiles = [(fbi_bf(P.fb[0][0], 0, T), P.fb[0][1]),
              (fbi_bf(P.fb[0][0], T, 2 * T), P.fb[0][1]),
              (fbi_bf(P.fb[1][0], 0, T), P.fb[1][1]),
              (fbi_bf(P.fb[1][0], T, 2 * T), P.fb[1][1]),
              (fbi_bf(P.fb[2][0], 0, T), P.fb[2][1]),
              (fbi_bf(P.fb[2][0], T, 2 * T), P.fb[2][1]),
              P.bb[13], P.bb[14]]

    def dma_qk(j):
        wh, wk = wqsl[j % 2]
        b.dma("POOL", wh, wq_r[:, j, :], (), (wk,), tag=f"l0wq{j}")
        wh2, wk2 = wksl[j % 2]
        b.dma("POOL", wh2, wk_r[:, j, :], (), (wk2,), tag=f"l0wk{j}")
        wh3, wk3 = wvsl[j % 8]
        b.dma("POOL", wh3, wv_r[:, j, :], (), (wk3,), tag=f"l0wv{j}")
    dma_qk(0)
    for j in range(8):
        if j + 1 < 8:
            dma_qk(j + 1)
        for (slots, dstt, nw2) in ((wqsl, P.bb[8], qn2), (wksl, P.bb[9], kn2)):
            wh, wk = slots[j % 2]
            dh, dk = dstt
            for t in range(NT):
                ps, psk = P.pb[t]
                for k in range(KC):
                    b.pe(lambda e, ps=ps, wh=wh, k=k, t=t: e.matmul(
                        ps[:, :TT], wh[:, k * 128:(k + 1) * 128],
                        h1[k][0][:, t * TT:(t + 1) * TT],
                        start=(k == 0), stop=(k == KC - 1)),
                        (wk, h1[k][1]), (psk,), tag=f"l0qk{j}_{t}_{k}")
                sh, sk = P.sq
                b.act(lambda s, ps=ps, sh=sh: s.activation(
                    sh[:, 0:TT], ps[:, :TT], AF.Square),
                    (psk,), (sk,), tag=f"l0qs{j}_{t}")
                b.act(lambda s, ps=ps, dh=dh, t=t: s.activation(
                    dh[:, t * TT:(t + 1) * TT], ps[:, :TT], AF.Copy),
                    (psk,), (dk,), tag=f"l0qcp{j}_{t}")
                for hh in range(2):
                    bs = hh * 64
                    pr, prk = P.pb[6 if hh == 0 else 7]
                    b.pe(lambda e, pr=pr, sh=sh, bs=bs: e.matmul(
                        pr[0:1, :TT], P.onec[0][bs:bs + 64, 0:1],
                        sh[bs:bs + 64, 0:TT], start=True, stop=True,
                        tile_position=(bs, 0)),
                        (P.onec[1], sk), (prk,), tag=f"l0qr{j}_{t}_{hh}")
                    b.dve(lambda v, pr=pr, t=t, hh=hh: v.tensor_scalar_mul(
                        rows_h[0:1, hh * T + t * TT:hh * T + (t + 1) * TT],
                        pr[0:1, :TT], 1.0 / 64),
                        (prk,), (rows_k,), tag=f"l0qm{j}_{t}_{hh}")
            for hh in range(2):
                b.act(lambda s, hh=hh: s.activation(
                    rows_h[0:1, hh * T:(hh + 1) * T],
                    rows_h[0:1, hh * T:(hh + 1) * T], AF.Ln,
                    bias=eps0c[0:1, :]),
                    (rows_k, "cpool"), (rows_k,), tag=f"l0ql{j}_{hh}")
                b.act(lambda s, hh=hh: s.activation(
                    rows_h[0:1, hh * T:(hh + 1) * T],
                    rows_h[0:1, hh * T:(hh + 1) * T], AF.Exp, scale=-0.5),
                    (rows_k,), (rows_k,), tag=f"l0qe{j}_{hh}")
            for t in range(NT):
                bcast_row(b, f"l0qb{j}_{t}_0",
                          (rows_h, rows_k, slice(t * TT, (t + 1) * TT)),
                          P.pb[4], (P.oner[0], P.oner[1]))
                bcast_row(b, f"l0qb{j}_{t}_1",
                          (rows_h, rows_k, slice(T + t * TT, T + (t + 1) * TT)),
                          P.pb[5], (P.oner[0], P.oner[1]))
                for hh in range(2):
                    bs = hh * 64
                    pbb = P.pb[4 + hh]
                    b.dve(lambda v, dh=dh, t=t, bs=bs, pbb=pbb, nw2=nw2:
                          v.scalar_tensor_tensor(
                              dh[bs:bs + 64, t * TT:(t + 1) * TT],
                              dh[bs:bs + 64, t * TT:(t + 1) * TT],
                              nw2[0][bs:bs + 64, 0:1],
                              pbb[0][bs:bs + 64, :TT], ALU.mult, ALU.mult),
                          (dk, nw2[1], pbb[1]), (dk,),
                          tag=f"l0qn{j}_{t}_{hh}")
        wh3, wk3 = wvsl[j % 8]
        for ti in range(16):
            ps, psk = P.pb[2 + (ti % 2)]
            for k in range(KC):
                b.pe(lambda e, ps=ps, wh3=wh3, k=k, ti=ti: e.matmul(
                    ps[:, 0:128], h1[k][0][:, ti * 128:(ti + 1) * 128],
                    wh3[:, k * 128:(k + 1) * 128],
                    start=(k == 0), stop=(k == KC - 1)),
                    (h1[k][1], wk3), (psk,), tag=f"l0v{j}_{ti}_{k}")
            b.act(lambda s, ps=ps, ti=ti: s.activation(
                P.bb[10][0][:, ti * 128:(ti + 1) * 128], ps[:, 0:128],
                AF.Copy), (psk,), (P.bb[10][1],), tag=f"l0ve{j}_{ti}")

        def qspec(hh):
            return [(P.bb[8][0][hh * 64:(hh + 1) * 64, :], P.bb[8][1],
                     hh * 64, 64)]

        def kspec(hh):
            return [(P.bb[9][0][hh * 64:(hh + 1) * 64, :], P.bb[9][1],
                     hh * 64, 64)]

        def out_cb(hh, qi, pv, j=j):
            pvh, pvk = pv
            yh, yk = ytiles[j]
            b.dve(lambda v, pvh=pvh, yh=yh, hh=hh, qi=qi: v.tensor_copy(
                yh[hh * 64:(hh + 1) * 64, qi * 128:(qi + 1) * 128],
                pvh[hh * 64:(hh + 1) * 64, 0:128]),
                (pvk,), (yk,), tag=f"l0y{j}_{hh}_{qi}")

        emit_attn(P, f"l0a{j}", qspec, kspec,
                  (P.bb[10][0], P.bb[10][1]), 128, lambda hh: hh * 64,
                  out_cb, 0.125, (P.bb[11][0], P.bb[11][1]),
                  (P.bb[12][0], P.bb[12][1]), window=False, sink=None)

    def epi_wo(j, t, ph, pk):
        dh, dk = P.bb[j]
        b.dve(lambda v, ph=ph, dh=dh, t=t, j=j: v.tensor_add(
            dh[:, t * TT:(t + 1) * TT], ph[:, :TT],
            P.xn[j][0][:, t * TT:(t + 1) * TT]),
            (pk, P.xn[j][1]), (dk,), tag=f"l0x2_{j}_{t}")
    mm_dense(b, "l0wo", ins["l0_woT"], D, D, ytiles, P.pb[:6],
             [(P.wbA[0][:, s * 1024:(s + 1) * 1024], P.wbA[1] + f":{s}")
              for s in range(2)], epi_wo, js=list(range(2, 8)))
    for j in range(2):
        dh, dk = P.bb[j]
        b.act(lambda s, dh=dh, j=j: s.activation(
            dh[:, :], P.xn[j][0][:, :], AF.Copy),
            (P.xn[j][1],), (dk,), tag=f"l0x2c{j}")
    x2 = [P.bb[j] for j in range(8)]
    for j in range(8):
        b.dma("SP", P.ins["x2scr"][j * 128:(j + 1) * 128, :], x2[j][0][:, :],
              (x2[j][1],), (f"x2s{j}",), tag=f"l0x2s{j}")
    xf = [P.bb[8 + j] for j in range(8)]
    emit_rms(P, "l0n2", x2, xf, n2w, EPS0)

    def epi_g0(j, t, ph, pk):
        b.dve(lambda v, ph=ph, t=t: v.tensor_copy(
            P.fb[2][0][0:4, t * TT:(t + 1) * TT], ph[0:4, :TT]),
            (pk,), (P.fb[2][1],), tag=f"l0gl{t}")
    mm_dense(b, "l0g", ins["l0_gateT"], D, 4, xf, P.pb[:6],
             [(P.wbB[0][:, 0:32], P.wbB[1] + ":0")], epi_g0, blk_cols=4)
    emit_topk_probs(P, "l0tk", P.fb[2], P.fb[2], 4, 2, eps=1e-8)

    cebc_h, cebc_k = P.fb[1]
    first = [True]

    def load_ce(e):
        b.dma("SP", rows_h[0:1, 0:T], P.ins["cescr"][e:e + 1, :],
              ("cescr",), (rows_k,), tag=f"l0ce{e}_{np.random.randint(1 << 30)}")
        for t in range(NT):
            bcast_row(b, f"l0cb{e}_{t}_{len(b.q['PE'])}",
                      (rows_h, rows_k, slice(t * TT, (t + 1) * TT)),
                      P.pb[6], (P.oner[0], P.oner[1]))
            b.act(lambda s, t=t: s.activation(
                cebc_h[:, t * TT:(t + 1) * TT], P.pb[6][0][:, :TT], AF.Copy),
                (P.pb[6][1],), (cebc_k,), tag=f"l0cbe{e}_{t}_{len(b.q['ACT'])}")

    def up_pass(pref, w_hbm, din, srcs, dst, act, ce, extra_mul=None,
                js=None, colbase=0):
        kcd = din // 128
        w_r = wtsrc(w_hbm, kcd)
        jlist = list(range(8)) if js is None else js
        slots = [(P.wbA[0][:, s * 1024:(s + 1) * 1024], P.wbA[1] + f":{s}")
                 for s in range(2)]

        def dmab(ji):
            wh, wk = slots[ji % 2]
            j = jlist[ji]
            b.dma("POOL", wh[:, 0:kcd * 128], w_r[:, colbase // 128 + j, :],
                  (), (wk,), tag=f"{pref}w{j}")
        for ji in range(min(2, len(jlist))):
            dmab(ji)
        for ji, j in enumerate(jlist):
            wh, wk = slots[ji % 2]
            for t in range(NT):
                ps, psk = P.pb[(ji * NT + t) % 6]
                for k in range(kcd):
                    b.pe(lambda e2, ps=ps, wh=wh, k=k, t=t: e2.matmul(
                        ps[:, :TT], wh[:, k * 128:(k + 1) * 128],
                        srcs[k][0][:, t * TT:(t + 1) * TT],
                        start=(k == 0), stop=(k == kcd - 1)),
                        (wk, srcs[k][1]), (psk,), tag=f"{pref}m{j}_{t}_{k}")
                dh, dk = dst[ji]
                if act is not None:
                    b.act(lambda s, ps=ps, dh=dh, t=t, act=act: s.activation(
                        dh[:, t * TT:(t + 1) * TT], ps[:, :TT], act),
                        (psk,), (dk,), tag=f"{pref}a{j}_{t}")
                else:
                    b.dve(lambda v, ps=ps, dh=dh, t=t: v.tensor_copy(
                        dh[:, t * TT:(t + 1) * TT], ps[:, :TT]),
                        (psk,), (dk,), tag=f"{pref}a{j}_{t}")
                if extra_mul is not None:
                    eh, ek = extra_mul[ji]
                    b.dve(lambda v, dh=dh, eh=eh, t=t: v.tensor_mul(
                        dh[:, t * TT:(t + 1) * TT], dh[:, t * TT:(t + 1) * TT],
                        eh[:, t * TT:(t + 1) * TT]),
                        (dk, ek), (dk,), tag=f"{pref}x{j}_{t}")
                if ce:
                    b.dve(lambda v, dh=dh, t=t: v.tensor_mul(
                        dh[:, t * TT:(t + 1) * TT], dh[:, t * TT:(t + 1) * TT],
                        cebc_h[:, t * TT:(t + 1) * TT]),
                        (dk, cebc_k), (dk,), tag=f"{pref}c{j}_{t}")
            if ji + 2 < len(jlist):
                dmab(ji + 2)

    def down_pass(pref, w_hbm, din, hidden, colbase=0):
        kcd = din // 128
        w_r = wtsrc(w_hbm, kcd)
        slots = [(P.wbB[0][:, s * 1024:(s + 1) * 1024], P.wbB[1] + f":{s}")
                 for s in range(2)]

        def dmab(ji, j):
            wh, wk = slots[ji % 2]
            b.dma("POOL", wh[:, 0:kcd * 128], w_r[:, j, :], (), (wk,),
                  tag=f"{pref}w{j}")
        js = list(range(2, 8))
        for ji in range(min(2, 6)):
            dmab(ji, js[ji])
        for ji, j in enumerate(js):
            wh, wk = slots[ji % 2]
            for t in range(NT):
                ps, psk = P.pb[(ji * NT + t) % 6]
                for k in range(kcd):
                    b.pe(lambda e2, ps=ps, wh=wh, k=k, t=t: e2.matmul(
                        ps[:, :TT], wh[:, k * 128:(k + 1) * 128],
                        hidden[k][0][:, t * TT:(t + 1) * TT],
                        start=(k == 0), stop=(k == kcd - 1)),
                        (wk, hidden[k][1]), (psk,), tag=f"{pref}m{j}_{t}_{k}")
                gh, gk = P.gb[j]
                if first[0]:
                    b.act(lambda s, ps=ps, gh=gh, t=t: s.activation(
                        gh[:, t * TT:(t + 1) * TT], ps[:, :TT], AF.Copy),
                        (psk,), (gk,), tag=f"{pref}i{j}_{t}")
                else:
                    b.dve(lambda v, ps=ps, gh=gh, t=t: v.tensor_add(
                        gh[:, t * TT:(t + 1) * TT],
                        gh[:, t * TT:(t + 1) * TT], ps[:, :TT]),
                        (psk, gk), (gk,), tag=f"{pref}d{j}_{t}")
            if ji + 2 < 6:
                dmab(ji + 2, js[ji + 2])
        first[0] = False

    rot = [(fbi_bf(P.fb[0][0], 0, T), P.fb[0][1]),
           (fbi_bf(P.fb[0][0], T, 2 * T), P.fb[0][1]),
           (fbi_bf(P.fb[2][0], 0, T), P.fb[2][1]),
           (fbi_bf(P.fb[2][0], T, 2 * T), P.fb[2][1])]

    def mm_res(pref, w_hbm, srcs, js_out, act, ce, scrname, pair_hbm=None,
               colbase=0):
        kcd = len(srcs)
        w_r = wtsrc(w_hbm, kcd)
        slots = [(P.wbA[0][:, s * 1024:(s + 1) * 1024], P.wbA[1] + f":{s}")
                 for s in range(2)]
        slots2 = [(P.wbB[0][:, s * 1024:(s + 1) * 1024], P.wbB[1] + f":{s}")
                  for s in range(2)]
        if pair_hbm is not None:
            w2_r = wtsrc(pair_hbm, kcd)

        def dmab(ji):
            j = js_out[ji]
            wh, wk = slots[ji % 2]
            b.dma("POOL", wh[:, 0:kcd * 128], w_r[:, colbase // 128 + j, :],
                  (), (wk,), tag=f"{pref}w{j}")
            if pair_hbm is not None:
                wh2, wk2 = slots2[ji % 2]
                b.dma("POOL", wh2[:, 0:kcd * 128],
                      w2_r[:, colbase // 128 + j, :],
                      (), (wk2,), tag=f"{pref}w2{j}")
        for ji in range(min(2, len(js_out))):
            dmab(ji)
        for ji, j in enumerate(js_out):
            wh, wk = slots[ji % 2]
            dh, dk = rot[ji % 2]
            for t in range(NT):
                ps, psk = P.pb[t % 2]
                for k in range(kcd):
                    b.pe(lambda e2, ps=ps, wh=wh, k=k, t=t: e2.matmul(
                        ps[:, :TT], wh[:, k * 128:(k + 1) * 128],
                        srcs[k][0][:, t * TT:(t + 1) * TT],
                        start=(k == 0), stop=(k == kcd - 1)),
                        (wk, srcs[k][1]), (psk,), tag=f"{pref}m{j}_{t}_{k}")
                if pair_hbm is not None:
                    wh2, wk2 = slots2[ji % 2]
                    ps2, psk2 = P.pb[2 + t % 2]
                    for k in range(kcd):
                        b.pe(lambda e2, ps2=ps2, wh2=wh2, k=k, t=t: e2.matmul(
                            ps2[:, :TT], wh2[:, k * 128:(k + 1) * 128],
                            srcs[k][0][:, t * TT:(t + 1) * TT],
                            start=(k == 0), stop=(k == kcd - 1)),
                            (wk2, srcs[k][1]), (psk2,),
                            tag=f"{pref}m2{j}_{t}_{k}")
                    sh, sk = P.sq
                    ss = (t % 2) * TT
                    b.act(lambda s, ps=ps, sh=sh, ss=ss: s.activation(
                        sh[:, ss:ss + TT], ps[:, :TT], AF.Silu),
                        (psk,), (sk,), tag=f"{pref}s{j}_{t}")
                    b.dve(lambda v, ps2=ps2, dh=dh, sh=sh, ss=ss, t=t:
                          v.tensor_mul(dh[:, t * TT:(t + 1) * TT],
                                       ps2[:, :TT], sh[:, ss:ss + TT]),
                          (psk2, sk), (dk,), tag=f"{pref}p{j}_{t}")
                elif act is not None:
                    b.act(lambda s, ps=ps, dh=dh, t=t, act=act: s.activation(
                        dh[:, t * TT:(t + 1) * TT], ps[:, :TT], act),
                        (psk,), (dk,), tag=f"{pref}a{j}_{t}")
                else:
                    b.dve(lambda v, ps=ps, dh=dh, t=t: v.tensor_copy(
                        dh[:, t * TT:(t + 1) * TT], ps[:, :TT]),
                        (psk,), (dk,), tag=f"{pref}a{j}_{t}")
                if ce:
                    b.dve(lambda v, dh=dh, t=t: v.tensor_mul(
                        dh[:, t * TT:(t + 1) * TT],
                        dh[:, t * TT:(t + 1) * TT],
                        cebc_h[:, t * TT:(t + 1) * TT]),
                        (dk, cebc_k), (dk,), tag=f"{pref}c{j}_{t}")
            if ji + 2 < len(js_out):
                dmab(ji + 2)
            b.dma("ACT", P.ins[scrname][j * 128:(j + 1) * 128, :], dh[:, :],
                  (dk,), (f"{scrname}{j}",), tag=f"{pref}o{j}")

    def mm_scr(pref, w_hbm, nk, scrname, js_out, to_gb, act_out=None,
               dstscr=None, ce=False):
        w_r = wtsrc(w_hbm, nk)
        jp_list = [js_out[i:i + 2] for i in range(0, len(js_out), 2)]
        nsl = max(1, nk // 8)
        for jp in jp_list:
            wks = []
            for jj, j in enumerate(jp):
                s0 = jj * 4
                whj = P.wfull[0][:, s0 * 1024:s0 * 1024 + nk * 128]
                wkj = tuple(P.wfull[1] + f":{s0 + i}" for i in range(nsl))
                b.dma("POOL", whj, w_r[:, j, :], (), wkj,
                      tag=f"{pref}w{j}")
                wks.append((whj, wkj))
            for k in range(nk):
                rh, rk = rot[2 + k % 2]
                b.dma("ACT", rh[:, :],
                      P.ins[scrname][k * 128:(k + 1) * 128, :],
                      (f"{scrname}{k}",), (rk + "_s",), tag=f"{pref}r{jp[0]}_{k}")
                for jj, j in enumerate(jp):
                    whj, wkj = wks[jj]
                    for t in range(NT):
                        ps, psk = P.pb[jj * 4 + t]
                        b.pe(lambda e2, ps=ps, whj=whj, k=k, t=t, rh=rh:
                             e2.matmul(ps[:, :TT],
                                       whj[:, k * 128:(k + 1) * 128],
                                       rh[:, t * TT:(t + 1) * TT],
                                       start=(k == 0), stop=(k == nk - 1)),
                             (wkj[k // 8], rk + "_s"), (psk,),
                             tag=f"{pref}m{j}_{k}_{t}")
            for jj, j in enumerate(jp):
                for t in range(NT):
                    ps, psk = P.pb[jj * 4 + t]
                    if to_gb:
                        gh, gk = P.gb[j]
                        if first[0]:
                            b.act(lambda s, ps=ps, gh=gh, t=t: s.activation(
                                gh[:, t * TT:(t + 1) * TT], ps[:, :TT],
                                AF.Copy), (psk,), (gk,), tag=f"{pref}i{j}_{t}")
                        else:
                            b.dve(lambda v, ps=ps, gh=gh, t=t: v.tensor_add(
                                gh[:, t * TT:(t + 1) * TT],
                                gh[:, t * TT:(t + 1) * TT], ps[:, :TT]),
                                (psk, gk), (gk,), tag=f"{pref}d{j}_{t}")
                    else:
                        dh, dk = rot[jj % 2]
                        b.act(lambda s, ps=ps, dh=dh, t=t: s.activation(
                            dh[:, t * TT:(t + 1) * TT], ps[:, :TT], act_out),
                            (psk,), (dk,), tag=f"{pref}e{j}_{t}")
                        if ce:
                            b.dve(lambda v, dh=dh, t=t: v.tensor_mul(
                                dh[:, t * TT:(t + 1) * TT],
                                dh[:, t * TT:(t + 1) * TT],
                                cebc_h[:, t * TT:(t + 1) * TT]),
                                (dk, cebc_k), (dk,), tag=f"{pref}q{j}_{t}")
                if not to_gb:
                    dh, dk = rot[jj % 2]
                    b.dma("ACT", P.ins[dstscr][j * 128:(j + 1) * 128, :],
                          dh[:, :], (dk,), (f"{dstscr}{j}",),
                          tag=f"{pref}o{j}")
        if to_gb:
            first[0] = False

    load_ce(0)
    mm_res("l0e0u", ins["l0_e0upT"], xf, list(range(16)), AF.Gelu, True,
           "hscr")
    mm_scr("l0e0d", ins["l0_e0dnT"], 16, "hscr", list(range(2, 8)), True)
    load_ce(1)
    mm_res("l0e1u", ins["l0_e1w1T"], xf, list(range(16)), None, True, "hscr",
           pair_hbm=ins["l0_e1w2T"])
    mm_scr("l0e1d", ins["l0_e1dnT"], 16, "hscr", list(range(2, 8)), True)
    load_ce(3)
    mm_res("l0e3a", ins["l0_e3dT"], xf, list(range(8)), AF.Gelu, False,
           "hscr2")
    hscr2_res = []
    d1 = [P.bb[k] for k in range(8)]
    for k in range(8):
        dh, dk = d1[k]
        b.dma("ACT", dh[:, :], P.ins["hscr2"][k * 128:(k + 1) * 128, :],
              (f"hscr2{k}",), (dk,), tag=f"l0d1r{k}")
    mm_res("l0e3u", ins["l0_e3uT"], d1, list(range(16)), AF.Gelu, True,
           "hscr")
    mm_scr("l0e3o", ins["l0_e3oT"], 16, "hscr", list(range(2, 8)), True)
    load_ce(2)
    mm_res("l0e2a", ins["l0_e2l1T"], xf, list(range(16)), AF.Gelu, False,
           "hscr")
    mm_scr("l0e2b", ins["l0_e2l2T"], 16, "hscr", list(range(16)), False,
           act_out=AF.Gelu, dstscr="hscr2")
    mm_scr("l0e2c", ins["l0_e2l3T"], 16, "hscr2", list(range(16)), False,
           act_out=AF.Gelu, dstscr="hscr", ce=True)
    mm_scr("l0e2d", ins["l0_e2l4T"], 16, "hscr", list(range(2, 8)), True)

    for j in range(8):
        rh, rk = P.bb[8 + (j % 2)]
        b.dma("ACT", rh[:, :], P.ins["x2scr"][j * 128:(j + 1) * 128, :],
              (f"x2s{j}",), (rk,), tag=f"l0x2r{j}")
        gh, gk = P.gb[j]
        if j >= 2:
            b.dve(lambda v, rh=rh, gh=gh: v.tensor_add(gh[:, :], gh[:, :],
                                                       rh[:, :]),
                  (rk, gk), (gk,), tag=f"l0ga{j}")
            b.act(lambda s, gh=gh: s.activation(gh[:, :], gh[:, :], AF.Silu),
                  (gk,), (gk,), tag=f"l0gs{j}")
        else:
            b.act(lambda s, gh=gh, rh=rh: s.activation(gh[:, :], rh[:, :],
                                                       AF.Silu),
                  (rk,), (gk,), tag=f"l0gs{j}")

    if DEBUG_DUMP:
        for j in range(8):
            b.dma("POOL", P.ins["dbg_g"][j * 128:(j + 1) * 128, :],
                  P.gb[j][0][:, :], (P.gb[j][1],), (f"dbgg{j}",),
                  tag=f"dbgg{j}")
            b.dma("POOL", P.ins["dbg_xn"][j * 128:(j + 1) * 128, :],
                  P.xn[j][0][:, :], (P.xn[j][1],), (f"dbgxn{j}",),
                  tag=f"dbgxn{j}")

    nsc = load_smallvec(P, "l0nsc", "l0_nsc2d", (128, 8))
    nsb = load_smallvec(P, "l0nsb", "l0_nsb2d", (128, 8))
    alog = load_smallvec(P, "l0alog", "l0_alog2d", (128, 1))
    dtb = load_smallvec(P, "l0dtb", "l0_dtb2d", (128, 1))
    D2d = load_smallvec(P, "l0D", "l0_D2d", (128, 16))
    bcs = load_smallvec(P, "l0bcs", "l0_bcs2d", (128, 1))
    bcb = load_smallvec(P, "l0bcb", "l0_bcb2d", (128, 1))
    bcm = load_smallvec(P, "l0bcm", "l0_bcmask2d", (128, 2))
    ut01 = load_smallvec(P, "l0ut", "c_ut01", (128, 128))
    ls01 = load_smallvec(P, "l0ls", "c_ls01", (128, 128))
    radt = P.small("l0radt", [128, 128], F32)
    w2t = P.small("l0w2", [64, 128], F32)
    p2t = P.small("l0p2", [64, 128], F32)
    rows_h, rows_k = P.rows
    f2h2, f2k2 = P.fb[2]
    eps6 = b.const(1e-6)

    for t in range(NT):
        for k in range(KC):
            xh, xk = P.xn[k]
            b.pe(lambda e, xh=xh, k=k, t=t: e.matmul(
                P.pb[0][0][0:1, :TT], P.onecb[0][:, 0:1],
                xh[:, t * TT:(t + 1) * TT],
                start=(k == 0), stop=(k == KC - 1)),
                (P.onecb[1], xk), (P.pb[0][1],), tag=f"l0isx{k}_{t}")
            sh, sk = P.sq
            b.act(lambda s, xh=xh, sh=sh, t=t: s.activation(
                sh[:, :TT], xh[:, t * TT:(t + 1) * TT], AF.Square),
                (xk,), (sk,), tag=f"l0isq{k}_{t}")
            b.pe(lambda e, sh=sh, k=k: e.matmul(
                P.pb[1][0][0:1, :TT], P.onec[0][:, 0:1], sh[:, :TT],
                start=(k == 0), stop=(k == KC - 1)),
                (P.onec[1], sk), (P.pb[1][1],), tag=f"l0isq2{k}_{t}")
        b.dve(lambda v, t=t: v.tensor_scalar_mul(
            rows_h[0:1, t * TT:(t + 1) * TT], P.pb[0][0][0:1, :TT], 1.0 / D),
            (P.pb[0][1],), (rows_k,), tag=f"l0imu{t}")
        b.dve(lambda v, t=t: v.tensor_scalar_mul(
            f2h2[0:1, t * TT:(t + 1) * TT], P.pb[1][0][0:1, :TT], 1.0 / D),
            (P.pb[1][1],), (f2k2,), tag=f"l0ims{t}")
    b.act(lambda s: s.activation(rows_h[0:1, T:2 * T], rows_h[0:1, 0:T],
                                 AF.Square), (rows_k,), (rows_k,), tag="l0im2")
    b.dve(lambda v: v.tensor_sub(rows_h[0:1, T:2 * T], f2h2[0:1, 0:T],
                                 rows_h[0:1, T:2 * T]),
          (rows_k, f2k2), (rows_k,), tag="l0ivar")
    b.act(lambda s: s.activation(rows_h[0:1, T:2 * T], rows_h[0:1, T:2 * T],
                                 AF.Ln, bias=eps6[0:1, :]),
          (rows_k, "cpool"), (rows_k,), tag="l0iln")
    b.act(lambda s: s.activation(rows_h[0:1, T:2 * T], rows_h[0:1, T:2 * T],
                                 AF.Exp, scale=-0.5), (rows_k,), (rows_k,),
          tag="l0iexp")
    for t in range(NT):
        bcast_row(b, f"l0ibm{t}", (rows_h, rows_k, slice(t * TT, (t + 1) * TT)),
                  P.pb[2], (P.oner[0], P.oner[1]))
        bcast_row(b, f"l0ibr{t}", (rows_h, rows_k,
                                   slice(T + t * TT, T + (t + 1) * TT)),
                  P.pb[3], (P.oner[0], P.oner[1]))
        for k in range(KC):
            xh, xk = P.xn[k]
            dh, dk = P.bb[k]
            b.dve(lambda v, xh=xh, t=t: v.tensor_sub(
                f2h2[:, t * TT:(t + 1) * TT], xh[:, t * TT:(t + 1) * TT],
                P.pb[2][0][:, :TT]),
                (xk, P.pb[2][1]), (f2k2,), tag=f"l0i1{k}_{t}")
            b.dve(lambda v, k=k, t=t: v.scalar_tensor_tensor(
                f2h2[:, t * TT:(t + 1) * TT], f2h2[:, t * TT:(t + 1) * TT],
                nsc[0][:, k:k + 1], P.pb[3][0][:, :TT], ALU.mult, ALU.mult),
                (f2k2, nsc[1], P.pb[3][1]), (f2k2,), tag=f"l0i2{k}_{t}")
            b.dve(lambda v, dh=dh, k=k, t=t: v.tensor_scalar_add(
                dh[:, t * TT:(t + 1) * TT], f2h2[:, t * TT:(t + 1) * TT],
                nsb[0][:, k:k + 1]),
                (f2k2, nsb[1]), (dk,), tag=f"l0i3{k}_{t}")
    xi = [P.bb[k] for k in range(KC)]

    def epi_bcdl(j, t, ph, pk):
        if j == 0:
            b.act(lambda s, ph=ph, t=t: s.activation(
                f2h2[0:64, t * TT:(t + 1) * TT], ph[0:64, :TT], AF.Copy),
                (pk,), (f2k2,), tag=f"l0bc0{t}")
        elif j == 1:
            b.act(lambda s, ph=ph, t=t: s.activation(
                f2h2[64:128, t * TT:(t + 1) * TT], ph[64:128, :TT], AF.Copy),
                (pk,), (f2k2,), tag=f"l0bc1{t}")
        elif j == 2:
            b.dve(lambda v, ph=ph, t=t: v.tensor_copy(
                P.fb[0][0][0:16, t * TT:(t + 1) * TT], ph[0:16, :TT]),
                (pk,), (P.fb[0][1],), tag=f"l0dt{t}")
        else:
            b.dve(lambda v, ph=ph, t=t: v.tensor_copy(
                P.fb[1][0][0:16, t * TT:(t + 1) * TT], ph[0:16, :TT]),
                (pk,), (P.fb[1][1],), tag=f"l0lam{t}")
    mm_dense(b, "l0bcdl", ins["l0_wbcdl"], D, 512, xi, P.pb[:6],
             [(P.wbA[0][:, s * 1024:(s + 1) * 1024], P.wbA[1] + f":{s}")
              for s in range(2)], epi_bcdl)

    sqh, sqk = P.sq
    for t in range(NT):
        b.act(lambda s, t=t: s.activation(sqh[:, 0:TT],
                                          f2h2[:, t * TT:(t + 1) * TT],
                                          AF.Square),
              (f2k2,), (sqk,), tag=f"l0bsq{t}")
        for ridx in range(2):
            pr, prk = P.pb[6]
            b.pe(lambda e, pr=pr, ridx=ridx, t=t: e.matmul(
                pr[0:1, :TT], bcm[0][:, ridx:ridx + 1],
                sqh[:, 0:TT], start=True, stop=True),
                (bcm[1], sqk), (prk,), tag=f"l0br{ridx}_{t}_a")
            b.dve(lambda v, pr=pr, ridx=ridx, t=t: v.tensor_scalar_mul(
                rows_h[0:1, ridx * T + t * TT:ridx * T + (t + 1) * TT],
                pr[0:1, :TT], 1.0 / 64),
                (prk,), (rows_k,), tag=f"l0brm{ridx}_{t}")
    for ridx in range(2):
        b.act(lambda s, ridx=ridx: s.activation(
            rows_h[0:1, ridx * T:(ridx + 1) * T],
            rows_h[0:1, ridx * T:(ridx + 1) * T], AF.Ln, bias=eps6[0:1, :]),
            (rows_k, "cpool"), (rows_k,), tag=f"l0brl{ridx}")
        b.act(lambda s, ridx=ridx: s.activation(
            rows_h[0:1, ridx * T:(ridx + 1) * T],
            rows_h[0:1, ridx * T:(ridx + 1) * T], AF.Exp, scale=-0.5),
            (rows_k,), (rows_k,), tag=f"l0bre{ridx}")
    for t in range(NT):
        bcast_row(b, f"l0bbB{t}", (rows_h, rows_k, slice(t * TT, (t + 1) * TT)),
                  P.pb[4], (P.oner[0], P.oner[1]))
        bcast_row(b, f"l0bbC{t}", (rows_h, rows_k,
                                   slice(T + t * TT, T + (t + 1) * TT)),
                  P.pb[5], (P.oner[0], P.oner[1]))
        for (bandlo, pbi) in ((0, 4), (32, 5), (64, 4), (96, 5)):
            b.dve(lambda v, bandlo=bandlo, pbi=pbi, t=t:
                  v.scalar_tensor_tensor(
                      f2h2[bandlo:bandlo + 32, t * TT:(t + 1) * TT],
                      f2h2[bandlo:bandlo + 32, t * TT:(t + 1) * TT],
                      bcs[0][bandlo:bandlo + 32, 0:1],
                      P.pb[pbi][0][bandlo:bandlo + 32, :TT],
                      ALU.mult, ALU.mult),
                  (f2k2, bcs[1], P.pb[pbi][1]), (f2k2,),
                  tag=f"l0bsc{bandlo}_{t}")
            b.dve(lambda v, bandlo=bandlo, t=t: v.tensor_scalar_add(
                f2h2[bandlo:bandlo + 32, t * TT:(t + 1) * TT],
                f2h2[bandlo:bandlo + 32, t * TT:(t + 1) * TT],
                bcb[0][bandlo:bandlo + 32, 0:1]),
                (f2k2, bcb[1]), (f2k2,), tag=f"l0bsb{bandlo}_{t}")

    DTh, DTk = P.fb[0][0], P.fb[0][1]
    LMh, LMk = P.fb[1][0], P.fb[1][1]
    b.dve(lambda v: v.tensor_scalar_add(DTh[0:16, :], DTh[0:16, :],
                                        dtb[0][0:16, 0:1]),
          (DTk, dtb[1]), (DTk,), tag="l0sdtb")
    for th in range(2):
        c = slice(th * 1024, (th + 1) * 1024)
        b.act(lambda s, c=c: s.activation(sqh[0:16, 0:1024], DTh[0:16, c],
                                          AF.Abs), (DTk,), (sqk,),
              tag=f"l0sab{th}")
        b.act(lambda s: s.activation(sqh[0:16, 0:1024], sqh[0:16, 0:1024],
                                     AF.Exp, scale=-1.0), (sqk,), (sqk,),
              tag=f"l0sex{th}")
        b.act(lambda s: s.activation(sqh[0:16, 0:1024], sqh[0:16, 0:1024],
                                     AF.Ln, bias=one_c[0:16, :]),
              (sqk, "cpool"), (sqk,), tag=f"l0sln{th}")
        b.dve(lambda v, c=c: v.tensor_scalar_max(DTh[0:16, c], DTh[0:16, c],
                                                 0.0),
              (DTk,), (DTk,), tag=f"l0smx{th}")
        b.dve(lambda v, c=c: v.tensor_add(DTh[0:16, c], DTh[0:16, c],
                                          sqh[0:16, 0:1024]),
              (DTk, sqk), (DTk,), tag=f"l0sad{th}")
    b.act(lambda s: s.activation(LMh[0:16, :], LMh[0:16, :], AF.Sigmoid),
          (LMk,), (LMk,), tag="l0slam")
    for t in range(NT):
        pr, prk = P.pb[6]
        b.pe(lambda e, pr=pr, t=t: e.matmul(
            pr[0:1, :TT], P.onec[0][0:16, 0:1],
            DTh[0:16, t * TT:(t + 1) * TT], start=True, stop=True),
            (P.onec[1], DTk), (prk,), tag=f"l0dm{t}")
        b.dve(lambda v, pr=pr, t=t: v.tensor_scalar_mul(
            rows_h[0:1, t * TT:(t + 1) * TT], pr[0:1, :TT], 1.0 / 16),
            (prk,), (rows_k,), tag=f"l0dme{t}")
    b.dve(lambda v: v.tensor_mul(LMh[0:16, :], DTh[0:16, :], LMh[0:16, :]),
          (DTk, LMk), (LMk,), tag="l0c1")
    for th in range(2):
        c = slice(th * 1024, (th + 1) * 1024)
        b.dve(lambda v, c=c: v.tensor_sub(sqh[0:16, 0:1024], DTh[0:16, c],
                                          LMh[0:16, c]),
              (DTk, LMk), (sqk,), tag=f"l0f{th}")
        b.dma("SP", P.ins["rowscr"][0:16, th * 1024:(th + 1) * 1024],
              sqh[0:16, 0:1024], (sqk,), (f"l0f{th}",), tag=f"l0fd{th}")
    for th in range(2):
        lo = th * 1024
        n = 1023 if th == 1 else 1024
        b.dve(lambda v: v.memset(sqh[0:16, 0:1024], 0.0), (), (sqk,),
              tag=f"l0z{th}")
        b.dma("SP", sqh[0:16, 0:n], P.ins["rowscr"][0:16, lo + 1:lo + 1 + n],
              (f"l0f0", f"l0f1"), (sqk,), tag=f"l0fr{th}")
        b.dve(lambda v, lo=lo: v.tensor_add(sqh[0:16, 0:1024],
                                            sqh[0:16, 0:1024],
                                            LMh[0:16, lo:lo + 1024]),
              (sqk, LMk), (sqk,), tag=f"l0et{th}")
        b.act(lambda s: s.activation(sqh[0:16, 0:1024], sqh[0:16, 0:1024],
                                     AF.Ln), (sqk,), (sqk,), tag=f"l0el{th}")
        b.dma("SP", P.ins["rowscr"][16:32, lo:lo + 1024], sqh[0:16, 0:1024],
              (sqk,), (f"l0le{th}",), tag=f"l0led{th}")
    b.act(lambda s: s.activation(LMh[0:16, :], LMh[0:16, :], AF.Ln),
          (LMk,), (LMk,), tag="l0lc1")
    b.act(lambda s: s.activation(P.stat[0][0:16, 0:1], alog[0][0:16, 0:1],
                                 AF.Exp), (alog[1],), (P.stat[1],),
          tag="l0aexp")
    b.dve(lambda v: v.tensor_scalar_mul(DTh[0:16, :], DTh[0:16, :],
                                        P.stat[0][0:16, 0:1]),
          (DTk, P.stat[1]), (DTk,), tag="l0adt1")
    b.dve(lambda v: v.tensor_scalar_mul(DTh[0:16, :], DTh[0:16, :], -1.0),
          (DTk,), (DTk,), tag="l0adt2")
    for c in range(16):
        ph, pk = P.pb[6]
        b.pe(lambda e, c=c, ph=ph: e.transpose(
            ph[0:128, 0:16], DTh[0:16, c * 128:(c + 1) * 128],
            P.idf[0][0:16, 0:16]), (DTk, P.idf[1]), (pk,), tag=f"l0ta{c}")
        b.dve(lambda v, c=c, ph=ph: v.tensor_copy(
            sqh[:, c * 16:(c + 1) * 16], ph[0:128, 0:16]),
            (pk,), (sqk,), tag=f"l0tae{c}")
    for c in range(16):
        b.dma("SP", sqh[0:16, 768:896], P.ins["rowscr"][16:32, c * 128:(c + 1) * 128],
              ("l0le0", "l0le1"), (sqk,), tag=f"l0lr{c}")
        ph, pk = P.pb[6]
        b.pe(lambda e, ph=ph: e.transpose(
            ph[0:128, 0:16], sqh[0:16, 768:896], P.idf[0][0:16, 0:16]),
            (sqk, P.idf[1]), (pk,), tag=f"l0tl{c}")
        b.dve(lambda v, c=c, ph=ph: v.tensor_copy(
            sqh[:, 256 + c * 16:256 + (c + 1) * 16], ph[0:128, 0:16]),
            (pk,), (sqk,), tag=f"l0tle{c}")
        ph2, pk2 = P.pb[6]
        b.pe(lambda e, c=c, ph2=ph2: e.transpose(
            ph2[0:128, 16:32], LMh[0:16, c * 128:(c + 1) * 128],
            P.idf[0][0:16, 0:16]), (LMk, P.idf[1]), (pk2,), tag=f"l0td{c}")
        b.dve(lambda v, c=c, ph2=ph2: v.tensor_sub(
            sqh[:, 512 + c * 16:512 + (c + 1) * 16], ph2[0:128, 16:32],
            sqh[:, 256 + c * 16:256 + (c + 1) * 16]),
            (pk2, sqk), (sqk,), tag=f"l0tde{c}")

    thsl = (P.wbB[0][:, 0:512], P.wbB[1] + ":0")
    b.dma("POOL", thsl[0].rearrange("p (k c) -> p k c", c=64),
          ins["l0_thdup"].rearrange("(k p) n -> p k n", p=128),
          (), (thsl[1],), tag="l0thw")
    for t in range(NT):
        ps, psk = P.pb[0]
        for k in range(KC):
            b.pe(lambda e, ps=ps, k=k, t=t: e.matmul(
                ps[64:128, :TT], thsl[0][:, k * 64:(k + 1) * 64],
                P.xn[k][0][:, t * TT:(t + 1) * TT],
                start=(k == 0), stop=(k == KC - 1), tile_position=(0, 64)),
                (thsl[1], P.xn[k][1]), (psk,), tag=f"l0th{t}_{k}")
        bcast_row(b, f"l0dbc{t}", (rows_h, rows_k, slice(t * TT, (t + 1) * TT)),
                  P.pb[1], (P.oner[0], P.oner[1]))
        b.act(lambda s, ps=ps, t=t: s.activation(
            P.fb[0][0][64:128, t * TT:(t + 1) * TT], ps[64:128, :TT],
            AF.Copy), (psk,), (P.fb[0][1],), tag=f"l0ze{t}")
        b.dve(lambda v, t=t: v.tensor_mul(
            P.fb[0][0][64:128, t * TT:(t + 1) * TT],
            P.fb[0][0][64:128, t * TT:(t + 1) * TT],
            P.pb[1][0][64:128, :TT]),
            (P.fb[0][1], P.pb[1][1]), (P.fb[0][1],), tag=f"l0zz{t}")
    b.dve(lambda v: v.tensor_tensor_scan(
        P.fb[0][0][64:128, :], P.onec[0][64:128, 0:1].broadcast_to((64, T)),
        P.fb[0][0][64:128, :], 0.0, ALU.mult, ALU.add),
        (P.fb[0][1], P.onec[1]), (P.fb[0][1],), tag="l0phis")
    TWO_PI = float(2 * np.pi)
    for c in range(16):
        ph, pk = P.pb[6]
        b.pe(lambda e, c=c, ph=ph: e.transpose(
            ph[0:128, 0:64], P.fb[0][0][64:128, c * 128:(c + 1) * 128],
            P.idf[0][64:128, 64:128], tile_position=(64, 0)),
            (P.fb[0][1], P.idf[1]), (pk,), tag=f"l0pt{c}")
        b.dve(lambda v, c=c, ph=ph: v.tensor_scalar_mul(
            sqh[:, 768 + 0:768 + 64] if False else sqh[:, 768:832],
            ph[0:128, 0:64], 1.0 / TWO_PI),
            (pk,), (sqk,), tag=f"l0pr{c}")
        b.dve(lambda v: v.tensor_copy(
            sqh[:, :].bitcast(mybir.dt.int32)[:, 960:1024], sqh[:, 768:832]),
              (sqk,), (sqk,), tag=f"l0pi{c}")
        b.dve(lambda v: v.tensor_copy(
            sqh[:, 832:896], sqh[:, :].bitcast(mybir.dt.int32)[:, 960:1024]),
              (sqk,), (sqk,), tag=f"l0pj{c}")
        b.dve(lambda v: v.tensor_sub(sqh[:, 768:832], sqh[:, 768:832],
                                     sqh[:, 832:896]),
              (sqk,), (sqk,), tag=f"l0pk{c}")
        b.dve(lambda v: v.tensor_scalar_mul(sqh[:, 768:832], sqh[:, 768:832],
                                            TWO_PI),
              (sqk,), (sqk,), tag=f"l0pl{c}")
        b.act(lambda s, c=c: s.activation(
            P.bb[14][0][:, c * 64:(c + 1) * 64], sqh[:, 768:832], AF.Sin),
            (sqk,), (P.bb[14][1],), tag=f"l0psn{c}")
        b.dve(lambda v: v.tensor_scalar(sqh[:, 896:960], sqh[:, 768:832],
                                        float(-np.pi / 2), -TWO_PI,
                                        ALU.is_lt, ALU.mult),
              (sqk,), (sqk,), tag=f"l0pc1{c}")
        b.dve(lambda v: v.tensor_scalar(sqh[:, 832:896], sqh[:, 768:832],
                                        -1.0, float(np.pi / 2),
                                        ALU.mult, ALU.add),
              (sqk,), (sqk,), tag=f"l0pc2{c}")
        b.dve(lambda v: v.tensor_add(sqh[:, 832:896], sqh[:, 832:896],
                                     sqh[:, 896:960]),
              (sqk,), (sqk,), tag=f"l0pc3{c}")
        b.act(lambda s, c=c: s.activation(
            P.bb[14][0][:, 1024 + c * 64:1024 + (c + 1) * 64],
            sqh[:, 832:896], AF.Sin),
            (sqk,), (P.bb[14][1],), tag=f"l0pcs{c}")

    for c in range(16):
        ph, pk = P.pb[6]
        b.pe(lambda e, c=c, ph=ph: e.transpose(
            ph[0:128, 0:64], f2h2[0:64, c * 128:(c + 1) * 128],
            P.idf[0][0:64, 0:64]),
            (f2k2, P.idf[1]), (pk,), tag=f"l0bt{c}")
        b.dve(lambda v, c=c, ph=ph: v.tensor_copy(
            P.bb[13][0][:, c * 64:(c + 1) * 64], ph[0:128, 0:64]),
            (pk,), (P.bb[13][1],), tag=f"l0bte{c}")
        ph2, pk2 = P.pb[7]
        b.pe(lambda e, c=c, ph2=ph2: e.transpose(
            ph2[0:128, 0:64], f2h2[64:128, c * 128:(c + 1) * 128],
            P.idf[0][64:128, 64:128], tile_position=(64, 0)),
            (f2k2, P.idf[1]), (pk2,), tag=f"l0bt2{c}")
        b.dve(lambda v, c=c, ph2=ph2: v.tensor_copy(
            P.bb[13][0][:, 1024 + c * 64:1024 + (c + 1) * 64],
            ph2[0:128, 0:64]),
            (pk2,), (P.bb[13][1],), tag=f"l0bt2e{c}")
    snT = P.bb[14]
    for c in range(16):
        for src_off, dst in ((0, 0), (32, 1024)):
            re = slice(c * 64 + src_off, c * 64 + src_off + 32)
            im = slice(1024 + c * 64 + src_off, 1024 + c * 64 + src_off + 32)
            sn = slice(c * 64, c * 64 + 32)
            cs = slice(1024 + c * 64, 1024 + c * 64 + 32)
            orr = slice(dst + c * 64, dst + c * 64 + 32)
            oii = slice(dst + c * 64 + 32, dst + c * 64 + 64)
            b.dve(lambda v, re=re, cs=cs: v.tensor_mul(
                sqh[:, 768:800], P.bb[13][0][:, re], snT[0][:, cs]),
                (P.bb[13][1], snT[1]), (sqk,), tag=f"l0r1_{c}_{src_off}")
            b.dve(lambda v, im=im, sn=sn: v.tensor_mul(
                sqh[:, 800:832], P.bb[13][0][:, im], snT[0][:, sn]),
                (P.bb[13][1], snT[1]), (sqk,), tag=f"l0r2_{c}_{src_off}")
            b.dve(lambda v, orr=orr: v.tensor_sub(
                P.bb[15][0][:, orr], sqh[:, 768:800], sqh[:, 800:832]),
                (sqk,), (P.bb[15][1],), tag=f"l0r3_{c}_{src_off}")
            b.dve(lambda v, re=re, sn=sn: v.tensor_mul(
                sqh[:, 768:800], P.bb[13][0][:, re], snT[0][:, sn]),
                (P.bb[13][1], snT[1]), (sqk,), tag=f"l0r4_{c}_{src_off}")
            b.dve(lambda v, im=im, cs=cs: v.tensor_mul(
                sqh[:, 800:832], P.bb[13][0][:, im], snT[0][:, cs]),
                (P.bb[13][1], snT[1]), (sqk,), tag=f"l0r5_{c}_{src_off}")
            b.dve(lambda v, oii=oii: v.tensor_add(
                P.bb[15][0][:, oii], sqh[:, 768:800], sqh[:, 800:832]),
                (sqk,), (P.bb[15][1],), tag=f"l0r6_{c}_{src_off}")

    wx_r = wtsrc(ins["l0_wx"], 8)
    wz_r = wtsrc(ins["l0_wz"], 8)
    wo_r = ins["l0_outp"].rearrange("(h p) n -> p h n", p=128)
    sacc = [(fbi_bf(P.fb[0][0], 0, T), P.fb[0][1]),
            (fbi_bf(P.fb[0][0], T, 2 * T), P.fb[0][1]),
            (fbi_bf(P.fb[1][0], 0, T), P.fb[1][1]),
            (fbi_bf(P.fb[1][0], T, 2 * T), P.fb[1][1]),
            (fbi_bf(P.fb[2][0], 0, T), P.fb[2][1]),
            (fbi_bf(P.fb[2][0], T, 2 * T), P.fb[2][1]),
            P.bb[13], P.bb[14]]
    wxsl = [(P.wbA[0][:, s * 1024:(s + 1) * 1024], P.wbA[1] + f":{s}")
            for s in range(2)]
    wzsl = [(P.wbB[0][:, s * 1024:(s + 1) * 1024], P.wbB[1] + f":{s}")
            for s in range(2)]
    wosl = [(P.wfull[0][:, s * 1024:(s + 1) * 1024], P.wfull[1] + f":{s}")
            for s in range(8)]

    def dma_head(h):
        wh, wk = wxsl[h % 2]
        b.dma("POOL", wh, wx_r[:, h, :], (), (wk,), tag=f"l0hwx{h}")
        wh2, wk2 = wzsl[h % 2]
        b.dma("POOL", wh2, wz_r[:, h, :], (), (wk2,), tag=f"l0hwz{h}")
        wh3, wk3 = wosl[h % 8]
        b.dma("POOL", wh3, wo_r[:, h, :], (), (wk3,), tag=f"l0hwo{h}")
    dma_head(0)
    for h in range(16):
        if h + 1 < 16:
            dma_head(h + 1)
        for (slots, dstt) in ((wxsl, P.bb[8]), (wzsl, P.bb[10])):
            wh, wk = slots[h % 2]
            dh, dk = dstt
            for t in range(NT):
                ps, psk = P.pb[t % 2]
                for k in range(KC):
                    b.pe(lambda e, ps=ps, wh=wh, k=k, t=t: e.matmul(
                        ps[:, :TT], wh[:, k * 128:(k + 1) * 128],
                        xi[k][0][:, t * TT:(t + 1) * TT],
                        start=(k == 0), stop=(k == KC - 1)),
                        (wk, xi[k][1]), (psk,), tag=f"l0hx{h}_{t}_{k}")
                b.act(lambda s, ps=ps, dh=dh, t=t: s.activation(
                    dh[:, t * TT:(t + 1) * TT], ps[:, :TT], AF.Silu),
                    (psk,), (dk,), tag=f"l0hxe{h}_{t}")
        for c in range(16):
            tph, tpk = P.pb[6 + (c % 2)]
            b.pe(lambda e, c=c, tph=tph: e.transpose(
                tph[:, :].bitcast(BF16)[:, 0:128],
                P.bb[8][0][:, c * 128:(c + 1) * 128], P.idb[0][:, :]),
                (P.bb[8][1], P.idb[1]), (tpk,), tag=f"l0ht{h}_{c}")
            b.act(lambda s, c=c, tph=tph: s.activation(
                P.bb[9][0][:, c * 128:(c + 1) * 128],
                tph[:, :].bitcast(BF16)[:, 0:128], AF.Copy),
                (tpk,), (P.bb[9][1],), tag=f"l0hte{h}_{c}")
        b.dve(lambda v: v.memset(sqh[0:64, 768:896], 0.0), (), (sqk,),
              tag=f"l0hz{h}")
        for c in range(16):
            b.dve(lambda v, c=c, h=h: v.tensor_scalar_mul(
                radt[0][:, :], ut01[0][:, :],
                sqh[:, c * 16 + h:c * 16 + h + 1]),
                (ut01[1], sqk), (radt[1],), tag=f"l0cr{h}_{c}")
            psW, psWk = P.pb[0]
            b.pe(lambda e, psW=psW: e.matmul(psW[:, 0:128], ls01[0][:, :],
                                             radt[0][:, :], start=True,
                                             stop=False),
                 (ls01[1], radt[1]), (psWk,), tag=f"l0cE{h}_{c}")
            ps2, ps2k = P.pb[2]
            b.pe(lambda e, ps2=ps2: e.matmul(ps2[0:1, 0:128],
                                             P.onec[0][:, 0:1],
                                             radt[0][:, :], start=True,
                                             stop=True),
                 (P.onec[1], radt[1]), (ps2k,), tag=f"l0c2{h}_{c}")
            b.dve(lambda v, ps2=ps2: v.tensor_copy(sqh[0:1, 896:1024],
                                                   ps2[0:1, 0:128]),
                  (ps2k,), (sqk,), tag=f"l0c2r{h}_{c}")
            b.dve(lambda v: v.tensor_scalar_mul(radt[0][:, :], ls01[0][:, :],
                                                NEG_BIG),
                  (ls01[1],), (radt[1],), tag=f"l0cmb{h}_{c}")
            b.pe(lambda e, psW=psW: e.matmul(psW[:, 0:128], P.idf[0][:, :],
                                             radt[0][:, :], start=False,
                                             stop=False),
                 (P.idf[1], radt[1]), (psWk,), tag=f"l0cM{h}_{c}")
            psww, pswwk = P.pb[3 + ((c + 1) % 2)]
            b.pe(lambda e, c=c, h=h, psww=psww: e.matmul(
                psww[:, 0:1], ls01[0][:, :],
                sqh[:, c * 16 + h:c * 16 + h + 1],
                start=True, stop=True),
                (ls01[1], sqk), (pswwk,), tag=f"l0cWm{h}_{c}")
            b.act(lambda s, c=c, h=h, psww=psww: s.activation(
                P.stat[0][:, 3:4], psww[:, 0:1], AF.Exp,
                bias=sqh[:, 256 + c * 16 + h:256 + c * 16 + h + 1], scale=1.0),
                (pswwk, sqk, "cpool"), (P.stat[1],), tag=f"l0cWc{h}_{c}")
            b.dve(lambda v, c=c, h=h: v.tensor_scalar_mul(
                radt[0][:, :], P.idf[0][:, :],
                sqh[:, 512 + c * 16 + h:512 + c * 16 + h + 1]),
                (P.idf[1], sqk), (radt[1],), tag=f"l0cd{h}_{c}")
            b.pe(lambda e, psW=psW: e.matmul(psW[:, 0:128], P.idf[0][:, :],
                                             radt[0][:, :], start=False,
                                             stop=True),
                 (P.idf[1], radt[1]), (psWk,), tag=f"l0cD{h}_{c}")
            b.act(lambda s, c=c, h=h, psW=psW: s.activation(
                P.bb[12][0][:, 0:128], psW[:, 0:128], AF.Exp,
                bias=sqh[:, 256 + c * 16 + h:256 + c * 16 + h + 1], scale=1.0),
                (psWk, sqk, "cpool"), (P.bb[12][1],), tag=f"l0cW{h}_{c}")
            tph, tpk = P.pb[6]
            b.pe(lambda e, c=c, tph=tph: e.transpose(
                tph[:, :].bitcast(BF16)[0:64, 0:128],
                P.bb[15][0][:, c * 64:(c + 1) * 64], P.idb[0][:, :]),
                (P.bb[15][1], P.idb[1]), (tpk,), tag=f"l0cbt{h}_{c}")
            b.act(lambda s, tph=tph: s.activation(
                P.bb[12][0][0:64, 256:384], tph[:, :].bitcast(BF16)[0:64, 0:128],
                AF.Copy), (tpk,), (P.bb[12][1],), tag=f"l0cbe{h}_{c}")
            b.pe(lambda e, c=c, tph=tph: e.transpose(
                tph[:, :].bitcast(BF16)[0:64, 128:256],
                P.bb[15][0][:, 1024 + c * 64:1024 + (c + 1) * 64],
                P.idb[0][:, :]),
                (P.bb[15][1], P.idb[1]), (tpk,), tag=f"l0cct{h}_{c}")
            b.act(lambda s, tph=tph: s.activation(
                P.bb[12][0][0:64, 384:512], tph[:, :].bitcast(BF16)[0:64, 128:256],
                AF.Copy), (tpk,), (P.bb[12][1],), tag=f"l0cce{h}_{c}")
            psG, psGk = P.pb[1]
            b.pe(lambda e, psG=psG: e.matmul(psG[:, 0:128],
                                             P.bb[12][0][0:64, 256:384],
                                             P.bb[12][0][0:64, 384:512],
                                             start=True, stop=True),
                 (P.bb[12][1],), (psGk,), tag=f"l0cG{h}_{c}")
            b.dve(lambda v, psG=psG: v.tensor_mul(
                P.bb[12][0][:, 128:256], psG[:, 0:128],
                P.bb[12][0][:, 0:128]),
                (psGk, P.bb[12][1]), (P.bb[12][1],), tag=f"l0cP{h}_{c}")
            ps2, ps2k = P.pb[2]
            b.pe(lambda e, ps2=ps2: e.matmul(ps2[0:64, 0:128],
                                             P.oner[0][0:1, 0:64],
                                             sqh[0:1, 896:1024], start=True,
                                             stop=True),
                 (P.oner[1], sqk), (ps2k,), tag=f"l0c2b{h}_{c}")
            b.act(lambda s, ps2=ps2: s.activation(w2t[0][:, :],
                                                  ps2[0:64, 0:128], AF.Exp),
                  (ps2k,), (w2t[1],), tag=f"l0c2e{h}_{c}")
            b.dve(lambda v: v.tensor_mul(p2t[0][:, :],
                                         P.bb[12][0][0:64, 384:512],
                                         w2t[0][:, :]),
                  (P.bb[12][1], w2t[1]), (p2t[1],), tag=f"l0cp2{h}_{c}")
            b.act(lambda s: s.activation(P.stat[0][0:64, 2:3],
                                         w2t[0][:, 127:128], AF.Copy),
                  (w2t[1],), (P.stat[1],), tag=f"l0cg{h}_{c}")
            psY, psYk = P.pb[3 + (c % 2)]
            b.pe(lambda e, c=c, psY=psY: e.matmul(
                psY[:, 0:128], P.bb[9][0][:, c * 128:(c + 1) * 128],
                P.bb[12][0][:, 128:256], start=True, stop=False),
                (P.bb[9][1], P.bb[12][1]), (psYk,), tag=f"l0cy1{h}_{c}")
            b.pe(lambda e, psY=psY: e.matmul(
                psY[:, 0:128], sqh[0:64, 768:896], p2t[0][:, :],
                start=False, stop=True),
                (sqk, p2t[1]), (psYk,), tag=f"l0cy2{h}_{c}")
            b.dve(lambda v, c=c, psY=psY, h=h: v.scalar_tensor_tensor(
                P.bb[11][0][:, c * 128:(c + 1) * 128],
                P.bb[8][0][:, c * 128:(c + 1) * 128], D2d[0][:, h:h + 1],
                psY[:, 0:128], ALU.mult, ALU.add),
                (P.bb[8][1], D2d[1], psYk), (P.bb[11][1],),
                tag=f"l0cyd{h}_{c}")
            b.dve(lambda v, c=c: v.tensor_mul(
                P.bb[11][0][:, c * 128:(c + 1) * 128],
                P.bb[11][0][:, c * 128:(c + 1) * 128],
                P.bb[10][0][:, c * 128:(c + 1) * 128]),
                (P.bb[11][1], P.bb[10][1]), (P.bb[11][1],),
                tag=f"l0cyz{h}_{c}")
            b.dve(lambda v, c=c: v.tensor_scalar_mul(
                P.bb[12][0][:, 512:576], P.bb[15][0][:, c * 64:(c + 1) * 64],
                P.stat[0][:, 3:4]),
                (P.bb[15][1], P.stat[1]), (P.bb[12][1],), tag=f"l0cbw{h}_{c}")
            psH, psHk = P.pb[5]
            b.pe(lambda e, c=c, psH=psH: e.matmul(
                psH[0:64, 0:128], P.bb[12][0][:, 512:576],
                P.bb[9][0][:, c * 128:(c + 1) * 128], start=True, stop=True),
                (P.bb[12][1], P.bb[9][1]), (psHk,), tag=f"l0cdh{h}_{c}")
            b.dve(lambda v, psH=psH: v.scalar_tensor_tensor(
                sqh[0:64, 768:896], sqh[0:64, 768:896], P.stat[0][0:64, 2:3],
                psH[0:64, 0:128], ALU.mult, ALU.add),
                (sqk, P.stat[1], psHk), (sqk,), tag=f"l0chu{h}_{c}")
        if DEBUG_DUMP and h == 0:
            for nm, t_ in (("dbg_t0", P.bb[8]), ("dbg_t1", P.bb[10]),
                           ("dbg_t2", P.bb[11]), ("dbg_t3", P.bb[15])):
                b.dma("POOL", P.ins[nm], t_[0][:, :], (t_[1],), (nm,),
                      tag=nm)
            b.dma("SP", P.ins["dbg_t4"], P.sq[0][:, :1024] if False else
                  P.sq[0][:, 0:1024], (P.sq[1],), ("dbg_t4",), tag="dbg_t4x")
        wh3, wk3 = wosl[h % 8]
        for j in range(8):
            for t in range(NT):
                ps, psk = P.pb[(j % 2) * 2 + (t % 2)]
                b.pe(lambda e, ps=ps, wh3=wh3, j=j, t=t: e.matmul(
                    ps[:, :TT], wh3[:, j * 128:(j + 1) * 128],
                    P.bb[11][0][:, t * TT:(t + 1) * TT],
                    start=True, stop=True),
                    (wk3, P.bb[11][1]), (psk,), tag=f"l0so{h}_{j}_{t}")
                sh_, sk_ = sacc[j]
                if h == 0:
                    b.act(lambda s, ps=ps, sh_=sh_, t=t: s.activation(
                        sh_[:, t * TT:(t + 1) * TT], ps[:, :TT], AF.Copy),
                        (psk,), (sk_,), tag=f"l0si{h}_{j}_{t}")
                else:
                    b.dve(lambda v, ps=ps, sh_=sh_, t=t: v.tensor_add(
                        sh_[:, t * TT:(t + 1) * TT],
                        sh_[:, t * TT:(t + 1) * TT], ps[:, :TT]),
                        (psk, sk_), (sk_,), tag=f"l0sa{h}_{j}_{t}")

    if DEBUG_DUMP:
        for j in range(8):
            sh_, sk_ = sacc[j]
            b.dma("POOL", P.ins["dbg_s"][j * 128:(j + 1) * 128, :],
                  sh_[:, :], (sk_,), (f"dbgs{j}",), tag=f"dbgs{j}")

    for j in range(8):
        sh_, sk_ = sacc[j]
        for th in range(2):
            b.dma("SP", sqh[:, 0:1024],
                  xsrc[j * 128:(j + 1) * 128, th * 1024:(th + 1) * 1024],
                  (() if src_keys is None else (src_keys[j],)), (sqk,),
                  tag=f"l0fx{j}_{th}")
            for tt in range(2):
                t = th * 2 + tt
                pc, pck = P.pb[6]
                b.dve(lambda v, sh_=sh_, j=j, t=t, pc=pc: v.tensor_add(
                    pc[:, :TT], sh_[:, t * TT:(t + 1) * TT],
                    P.xn[j][0][:, t * TT:(t + 1) * TT]),
                    (sk_, P.xn[j][1]), (pck,), tag=f"l0f1{j}_{t}")
                b.dve(lambda v, j=j, t=t, pc=pc: v.tensor_mul(
                    pc[:, :TT], pc[:, :TT],
                    P.gb[j][0][:, t * TT:(t + 1) * TT]),
                    (pck, P.gb[j][1]), (pck,), tag=f"l0f2{j}_{t}")
                b.dve(lambda v, tt=tt, pc=pc: v.tensor_add(
                    sqh[:, tt * TT:(tt + 1) * TT],
                    sqh[:, tt * TT:(tt + 1) * TT], pc[:, :TT]),
                    (pck, sqk), (sqk,), tag=f"l0f3{j}_{t}")
            b.dma("SP", xdst[j * 128:(j + 1) * 128, th * 1024:(th + 1) * 1024],
                  sqh[:, 0:1024], (sqk,), (dst_keys[j],), tag=f"l0fw{j}_{th}")


def prep_l0(ins, d):
    C = np.ascontiguousarray
    d["l0_n1w2d"] = _col2d(np.asarray(ins["L0_ffn_n1_w"]))
    d["l0_n2w2d"] = _col2d(np.asarray(ins["L0_ffn_n2_w"]))
    d["l0_qn2d"] = C(np.tile(np.asarray(ins["L0_ffn_qn_w"]), 2).reshape(128, 1))
    d["l0_kn2d"] = C(np.tile(np.asarray(ins["L0_ffn_kn_w"]), 2).reshape(128, 1))
    d["l0_wqT"] = _wt(np.asarray(ins["L0_ffn_wq"]).T)
    d["l0_wkT"] = _wt(np.asarray(ins["L0_ffn_wk"]).T)
    d["l0_wvT"] = _wt(np.asarray(ins["L0_ffn_wv"]).T)
    d["l0_woT"] = _wt(np.asarray(ins["L0_ffn_wo"]).T)
    d["l0_gateT"] = _wt(np.asarray(ins["L0_ffn_gate_w"]).T, 4)
    d["l0_e0upT"] = _wt(np.asarray(ins["L0_ffn_e0_up"]).T)
    d["l0_e0dnT"] = _wt(np.asarray(ins["L0_ffn_e0_down"]).T)
    d["l0_e1w1T"] = _wt(np.asarray(ins["L0_ffn_e1_w1"]).T)
    d["l0_e1w2T"] = _wt(np.asarray(ins["L0_ffn_e1_w2"]).T)
    d["l0_e1dnT"] = _wt(np.asarray(ins["L0_ffn_e1_down"]).T)
    d["l0_e2l1T"] = _wt(np.asarray(ins["L0_ffn_e2_l1"]).T)
    d["l0_e2l2T"] = _wt(np.asarray(ins["L0_ffn_e2_l2"]).T)
    d["l0_e2l3T"] = _wt(np.asarray(ins["L0_ffn_e2_l3"]).T)
    d["l0_e2l4T"] = _wt(np.asarray(ins["L0_ffn_e2_l4"]).T)
    d["l0_e3dT"] = _wt(np.asarray(ins["L0_ffn_e3_down"]).T)
    d["l0_e3uT"] = _wt(np.asarray(ins["L0_ffn_e3_up"]).T)
    d["l0_e3oT"] = _wt(np.asarray(ins["L0_ffn_e3_out"]).T)
    d["l0_nsc2d"] = _col2d(np.asarray(ins["L0_attn_norm_scale"]))
    d["l0_nsb2d"] = _col2d(np.asarray(ins["L0_attn_norm_bias"]))
    ipw = np.asarray(ins["L0_attn_in_proj_w"])
    d["l0_wx"] = _wt(ipw[:, 0:2048])
    d["l0_wz"] = _wt(ipw[:, 2048:4096])
    B = ipw[:, 4096:4160]
    Cm = ipw[:, 4160:4224]
    dt = ipw[:, 4224:4240]
    lam = ipw[:, 4240:4256]
    z32 = np.zeros((1024, 32), np.float32)
    z112 = np.zeros((1024, 112), np.float32)
    blk0 = np.concatenate([B[:, 0:32], Cm[:, 0:32], z32, z32], 1)
    blk1 = np.concatenate([z32, z32, B[:, 32:64], Cm[:, 32:64]], 1)
    blk2 = np.concatenate([dt, z112], 1)
    blk3 = np.concatenate([lam, z112], 1)
    d["l0_wbcdl"] = _wt(np.concatenate([blk0, blk1, blk2, blk3], 1))
    al = np.zeros((128, 1), np.float32)
    al[0:16, 0] = np.asarray(ins["L0_attn_A_log"])
    d["l0_alog2d"] = al
    db_ = np.zeros((128, 1), np.float32)
    db_[0:16, 0] = np.asarray(ins["L0_attn_dt_bias"])
    d["l0_dtb2d"] = db_
    d["l0_D2d"] = C(np.broadcast_to(np.asarray(ins["L0_attn_D"]),
                                    (128, 16)).astype(np.float32))
    bs_ = np.zeros((128, 1), np.float32)
    bb_ = np.zeros((128, 1), np.float32)
    bs_[0:32, 0] = np.asarray(ins["L0_attn_bcB_scale"])[0:32]
    bs_[32:64, 0] = np.asarray(ins["L0_attn_bcC_scale"])[0:32]
    bs_[64:96, 0] = np.asarray(ins["L0_attn_bcB_scale"])[32:64]
    bs_[96:128, 0] = np.asarray(ins["L0_attn_bcC_scale"])[32:64]
    bb_[0:32, 0] = np.asarray(ins["L0_attn_B_bias"])[0:32]
    bb_[32:64, 0] = np.asarray(ins["L0_attn_C_bias"])[0:32]
    bb_[64:96, 0] = np.asarray(ins["L0_attn_B_bias"])[32:64]
    bb_[96:128, 0] = np.asarray(ins["L0_attn_C_bias"])[32:64]
    d["l0_bcs2d"] = bs_
    d["l0_bcb2d"] = bb_
    m_ = np.zeros((128, 2), np.float32)
    m_[0:32, 0] = 1.0
    m_[64:96, 0] = 1.0
    m_[32:64, 1] = 1.0
    m_[96:128, 1] = 1.0
    d["l0_bcmask2d"] = m_
    th = np.asarray(ins["L0_attn_theta_w"])
    d["l0_thdup"] = C(np.concatenate([th, th], 1))
    d["l0_outp"] = C(np.asarray(ins["L0_attn_out_proj_w"]))
    r = np.arange(128)
    d["c_ut01"] = C((r[:, None] <= r[None, :]).astype(np.float32))
    d["c_ls01"] = C((r[:, None] > r[None, :]).astype(np.float32))


L0_DEVINS = [
    ("l0_n1w2d", [128, 8]), ("l0_n2w2d", [128, 8]),
    ("l0_qn2d", [128, 1]), ("l0_kn2d", [128, 1]),
    ("l0_wqT", [128, 8192]), ("l0_wkT", [128, 8192]),
    ("l0_wvT", [128, 8192]), ("l0_woT", [128, 8192]),
    ("l0_gateT", [128, 32]),
    ("l0_e0upT", [128, 16384]), ("l0_e0dnT", [128, 16384]),
    ("l0_e1w1T", [128, 16384]), ("l0_e1w2T", [128, 16384]),
    ("l0_e1dnT", [128, 16384]),
    ("l0_e2l1T", [128, 16384]), ("l0_e2l2T", [128, 32768]),
    ("l0_e2l3T", [128, 32768]), ("l0_e2l4T", [128, 16384]),
    ("l0_e3dT", [128, 8192]), ("l0_e3uT", [128, 16384]),
    ("l0_e3oT", [128, 16384]),
    ("l0_nsc2d", [128, 8]), ("l0_nsb2d", [128, 8]),
    ("l0_wx", [128, 16384]), ("l0_wz", [128, 16384]),
    ("l0_wbcdl", [128, 4096]),
    ("l0_alog2d", [128, 1]), ("l0_dtb2d", [128, 1]), ("l0_D2d", [128, 16]),
    ("l0_bcs2d", [128, 1]), ("l0_bcb2d", [128, 1]),
    ("l0_bcmask2d", [128, 2]),
    ("l0_thdup", [1024, 64]), ("l0_outp", [2048, 1024]),
    ("c_ut01", [128, 128]), ("c_ls01", [128, 128]),
]
LAYER_DEVINS[0] = L0_DEVINS
LAYER_PREP[0] = prep_l0
LAYER_EMIT[0] = emit_layer0


def emit_layer1(P, xsrc, xdst, src_keys=None, dst_keys=None):
    b = P.b
    ins = P.ins
    emit_outer_ln(P, xsrc, "l1o", src_keys)

    rows_h, rows_k = P.rows
    sqh, sqk = P.sq
    sgw = P.small("l1sgw", [128, 8], BF16)
    b.dma("POOL", sgw[0][:, :], ins["l1_sgwT2d"], (), (sgw[1],), tag="l1sgw")
    for t in range(NT):
        pr, prk = P.pb[6]
        for k in range(KC):
            b.pe(lambda e, pr=pr, k=k, t=t: e.matmul(
                pr[0:1, :TT], sgw[0][:, k:k + 1],
                P.xn[k][0][:, t * TT:(t + 1) * TT],
                start=(k == 0), stop=(k == KC - 1)),
                (sgw[1], P.xn[k][1]), (prk,), tag=f"l1sg{t}_{k}")
        b.act(lambda s, pr=pr, t=t: s.activation(
            rows_h[0:1, t * TT:(t + 1) * TT], pr[0:1, :TT], AF.Sigmoid),
            (prk,), (rows_k,), tag=f"l1sge{t}")
    wg_r = wtsrc(ins["l1_shgT"], 8)
    wu_r = wtsrc(ins["l1_shuT"], 8)
    slotsA = [(P.wbA[0][:, s * 1024:(s + 1) * 1024], P.wbA[1] + f":{s}")
              for s in range(2)]
    slotsB = [(P.wbB[0][:, s * 1024:(s + 1) * 1024], P.wbB[1] + f":{s}")
              for s in range(2)]
    rot = [(fbi_bf(P.fb[0][0], 0, T), P.fb[0][1]),
           (fbi_bf(P.fb[0][0], T, 2 * T), P.fb[0][1]),
           (fbi_bf(P.fb[2][0], 0, T), P.fb[2][1]),
           (fbi_bf(P.fb[2][0], T, 2 * T), P.fb[2][1])]

    def dma_sh(ji):
        wh, wk = slotsA[ji % 2]
        b.dma("POOL", wh, wg_r[:, ji, :], (), (wk,), tag=f"l1shg{ji}")
        wh2, wk2 = slotsB[ji % 2]
        b.dma("POOL", wh2, wu_r[:, ji, :], (), (wk2,), tag=f"l1shu{ji}")
    for ji in range(min(2, 22)):
        dma_sh(ji)
    for ji in range(22):
        wh, wk = slotsA[ji % 2]
        wh2, wk2 = slotsB[ji % 2]
        dh, dk = rot[ji % 2]
        for t in range(NT):
            ps, psk = P.pb[t % 2]
            ps2, psk2 = P.pb[2 + t % 2]
            for k in range(KC):
                b.pe(lambda e, ps=ps, wh=wh, k=k, t=t: e.matmul(
                    ps[:, :TT], wh[:, k * 128:(k + 1) * 128],
                    P.xn[k][0][:, t * TT:(t + 1) * TT],
                    start=(k == 0), stop=(k == KC - 1)),
                    (wk, P.xn[k][1]), (psk,), tag=f"l1sm{ji}_{t}_{k}")
                b.pe(lambda e, ps2=ps2, wh2=wh2, k=k, t=t: e.matmul(
                    ps2[:, :TT], wh2[:, k * 128:(k + 1) * 128],
                    P.xn[k][0][:, t * TT:(t + 1) * TT],
                    start=(k == 0), stop=(k == KC - 1)),
                    (wk2, P.xn[k][1]), (psk2,), tag=f"l1sm2{ji}_{t}_{k}")
            ss = (t % 2) * TT
            b.act(lambda s, ps=ps, ss=ss: s.activation(
                sqh[:, ss:ss + TT], ps[:, :TT], AF.Silu),
                (psk,), (sqk,), tag=f"l1ss{ji}_{t}")
            b.dve(lambda v, ps2=ps2, dh=dh, ss=ss, t=t: v.tensor_mul(
                dh[:, t * TT:(t + 1) * TT], ps2[:, :TT], sqh[:, ss:ss + TT]),
                (psk2, sqk), (dk,), tag=f"l1sh{ji}_{t}")
        if ji + 2 < 22:
            dma_sh(ji + 2)
        b.dma("ACT", P.ins["hscr"][ji * 128:(ji + 1) * 128, :], dh[:, :],
              (dk,), (f"hscr{ji}",), tag=f"l1so{ji}")
    shd_r = wtsrc(ins["l1_shdT"], 22)
    first = [True]
    for jp in ([0, 1], [2, 3], [4, 5], [6, 7]):
        wks = []
        for jj, j in enumerate(jp):
            s0 = jj * 4
            whj = P.wfull[0][:, s0 * 1024:s0 * 1024 + 22 * 128]
            wkj = tuple(P.wfull[1] + f":{s0 + i}" for i in range(3))
            b.dma("POOL", whj, shd_r[:, j, :], (), wkj,
                  tag=f"l1dw{j}")
            wks.append((whj, wkj))
        for k in range(22):
            rh, rk = rot[2 + k % 2]
            b.dma("ACT", rh[:, :], P.ins["hscr"][k * 128:(k + 1) * 128, :],
                  (f"hscr{k}",), (rk + "_s",), tag=f"l1dr{jp[0]}_{k}")
            for jj, j in enumerate(jp):
                whj, wkj = wks[jj]
                for t in range(NT):
                    ps, psk = P.pb[jj * 4 + t]
                    b.pe(lambda e2, ps=ps, whj=whj, k=k, t=t, rh=rh:
                         e2.matmul(ps[:, :TT], whj[:, k * 128:(k + 1) * 128],
                                   rh[:, t * TT:(t + 1) * TT],
                                   start=(k == 0), stop=(k == 21)),
                         (wkj[k // 8], rk + "_s"), (psk,),
                         tag=f"l1dm{j}_{k}_{t}")
        for jj, j in enumerate(jp):
            gh, gk = P.gb[j]
            for t in range(NT):
                ps, psk = P.pb[jj * 4 + t]
                ss = (t % 2) * TT
                b.act(lambda s, ps=ps, ss=ss: s.activation(
                    sqh[:, ss:ss + TT], ps[:, :TT], AF.Copy),
                    (psk,), (sqk,), tag=f"l1dc{j}_{t}")
                bcast_row(b, f"l1sb{j}_{t}",
                          (rows_h, rows_k, slice(t * TT, (t + 1) * TT)),
                          (ps, psk), (P.oner[0], P.oner[1]))
                b.dve(lambda v, ps=ps, gh=gh, ss=ss, t=t: v.tensor_mul(
                    gh[:, t * TT:(t + 1) * TT], sqh[:, ss:ss + TT],
                    ps[:, :TT]),
                    (sqk, psk, gk), (gk,), tag=f"l1de{j}_{t}")
    def epi_g1(j, t, ph, pk):
        b.dve(lambda v, ph=ph, t=t: v.tensor_copy(
            P.fb[1][0][0:16, t * TT:(t + 1) * TT], ph[0:16, :TT]),
            (pk,), (P.fb[1][1],), tag=f"l1gl{t}")
    mm_dense(b, "l1g", ins["l1_gateT"], D, 16, P.xn, P.pb[:6],
             [(P.wbB[0][:, 0:128], P.wbB[1] + ":0")], epi_g1, blk_cols=16)
    emit_topk_probs(P, "l1tk", P.fb[1], P.fb[1], 16, 2, eps=0.0)
    upslots = [(P.wbA[0][:, s * 1024:(s + 1) * 1024], P.wbA[1] + f":{s}")
               for s in range(2)] + \
              [(P.wbB[0][:, s * 1024:(s + 1) * 1024], P.wbB[1] + f":{s}")
               for s in range(2)]
    dnslots = [(P.wfull[0][:, s * 1024:(s + 1) * 1024], P.wfull[1] + f":{s}")
               for s in range(8)]
    wup_r = wtsrc_e(ins["l1_wgu"], 8)
    wdn_r = wtsrc_e(ins["l1_wdn"], 4)
    emit_moe_std(P, "l1m", 16, 512, wup_r, wdn_r, P.xn,
                 [P.bb[8 + k] for k in range(4)], P.gb, upslots, dnslots,
                 acc_init_first=False)
    for j in range(8):
        gh, gk = P.gb[j]
        b.act(lambda s, gh=gh: s.activation(gh[:, :], gh[:, :], AF.Silu),
              (gk,), (gk,), tag=f"l1gs{j}")

    adn = load_smallvec(P, "l1adn", "l1_adn2d", (128, 3))
    alog = (adn[0][:, 0:1], adn[1])
    dtb = (adn[0][:, 1:2], adn[1])
    nw = (adn[0][:, 2:3], adn[1])
    cvq = load_smallvec(P, "l1cvq", "l1_convq2d", (128, 32), F32, "POOL")
    cvk = load_smallvec(P, "l1cvk", "l1_convk2d", (128, 32), F32, "POOL")
    cvv = load_smallvec(P, "l1cvv", "l1_convv2d", (128, 64), F32, "POOL")
    ut01 = P.smalls["l0ut"] if "l0ut" in P.smalls else \
        load_smallvec(P, "l0ut", "c_ut01", (128, 128))
    ls01 = P.smalls["l0ls"] if "l0ls" in P.smalls else \
        load_smallvec(P, "l0ls", "c_ls01", (128, 128))
    radt = P.small("l0radt", [128, 128], F32)
    sst = P.small("l1s", [64, 128], BF16)
    eps6b = b.const(1e-6)
    eps5b = b.const(1e-5)
    one_cb = b.const(1.0)

    def epi_ba(j, t, ph, pk):
        if j == 0:
            b.act(lambda s, ph=ph, t=t: s.activation(
                P.fb[0][0][0:16, t * TT:(t + 1) * TT], ph[0:16, :TT],
                AF.Sigmoid), (pk,), (P.fb[0][1],), tag=f"l1be{t}")
        else:
            b.dve(lambda v, ph=ph, t=t: v.tensor_copy(
                P.fb[1][0][0:16, t * TT:(t + 1) * TT], ph[0:16, :TT]),
                (pk,), (P.fb[1][1],), tag=f"l1ae{t}")
    mm_dense(b, "l1ba", ins["l1_wba"], D, 256, P.xn, P.pb[:6],
             [(P.wbA[0][:, s * 1024:(s + 1) * 1024], P.wbA[1] + f":{s}")
              for s in range(2)], epi_ba)
    BEh, BEk = P.fb[0][0], P.fb[0][1]
    GAh, GAk = P.fb[1][0], P.fb[1][1]
    b.dve(lambda v: v.tensor_scalar_mul(BEh[0:16, :], BEh[0:16, :], 2.0),
          (BEk,), (BEk,), tag="l1b2")
    b.dve(lambda v: v.tensor_scalar_add(GAh[0:16, :], GAh[0:16, :],
                                        dtb[0][0:16, :]),
          (GAk, dtb[1]), (GAk,), tag="l1sdtb")
    for th in range(2):
        c = slice(th * 1024, (th + 1) * 1024)
        b.act(lambda s, c=c: s.activation(sqh[0:16, 0:1024], GAh[0:16, c],
                                          AF.Abs), (GAk,), (sqk,),
              tag=f"l1sab{th}")
        b.act(lambda s: s.activation(sqh[0:16, 0:1024], sqh[0:16, 0:1024],
                                     AF.Exp, scale=-1.0), (sqk,), (sqk,),
              tag=f"l1sex{th}")
        b.act(lambda s: s.activation(sqh[0:16, 0:1024], sqh[0:16, 0:1024],
                                     AF.Ln, bias=one_cb[0:16, :]),
              (sqk, "cpool"), (sqk,), tag=f"l1sln{th}")
        b.dve(lambda v, c=c: v.tensor_scalar_max(GAh[0:16, c], GAh[0:16, c],
                                                 0.0),
              (GAk,), (GAk,), tag=f"l1smx{th}")
        b.dve(lambda v, c=c: v.tensor_add(GAh[0:16, c], GAh[0:16, c],
                                          sqh[0:16, 0:1024]),
              (GAk, sqk), (GAk,), tag=f"l1sad{th}")
    b.act(lambda s: s.activation(P.stat[0][0:16, 0:1], alog[0][0:16, :],
                                 AF.Exp), (alog[1],), (P.stat[1],),
          tag="l1aexp")
    b.dve(lambda v: v.tensor_scalar_mul(GAh[0:16, :], GAh[0:16, :],
                                        P.stat[0][0:16, 0:1]),
          (GAk, P.stat[1]), (GAk,), tag="l1ga1")
    b.dve(lambda v: v.tensor_scalar_mul(GAh[0:16, :], GAh[0:16, :], -1.0),
          (GAk,), (GAk,), tag="l1ga2")
    for c in range(16):
        ph, pk = P.pb[6]
        b.pe(lambda e, c=c, ph=ph: e.transpose(
            ph[0:128, 0:16], GAh[0:16, c * 128:(c + 1) * 128],
            P.idf[0][0:16, 0:16]), (GAk, P.idf[1]), (pk,), tag=f"l1tg{c}")
        b.dve(lambda v, c=c, ph=ph: v.tensor_copy(
            sqh[:, c * 16:(c + 1) * 16], ph[0:128, 0:16]),
            (pk,), (sqk,), tag=f"l1tge{c}")
        ph2, pk2 = P.pb[7]
        b.pe(lambda e, c=c, ph2=ph2: e.transpose(
            ph2[0:128, 0:16], BEh[0:16, c * 128:(c + 1) * 128],
            P.idf[0][0:16, 0:16]), (BEk, P.idf[1]), (pk2,), tag=f"l1tb{c}")
        b.dve(lambda v, c=c, ph2=ph2: v.tensor_copy(
            sqh[:, 256 + c * 16:256 + (c + 1) * 16], ph2[0:128, 0:16]),
            (pk2,), (sqk,), tag=f"l1tbe{c}")

    wq_r = wtsrc(ins["l1_wq"], 8, 64)
    wk2_r = wtsrc(ins["l1_wk"], 8, 64)
    wv_r = wtsrc(ins["l1_wv"], 8)
    wgg_r = wtsrc(ins["l1_wg"], 8)
    wo_r = ins["l1_wout"].rearrange("(h p) n -> p h n", p=128)
    sacc = [(fbi_bf(P.fb[0][0], 0, T), P.fb[0][1]),
            (fbi_bf(P.fb[0][0], T, 2 * T), P.fb[0][1]),
            (fbi_bf(P.fb[1][0], 0, T), P.fb[1][1]),
            (fbi_bf(P.fb[1][0], T, 2 * T), P.fb[1][1]),
            (fbi_bf(P.fb[2][0], 0, T), P.fb[2][1]),
            (fbi_bf(P.fb[2][0], T, 2 * T), P.fb[2][1]),
            P.bb[13], P.bb[14]]
    wqsl = [(P.wbA[0][:, s * 1024:(s + 1) * 1024], P.wbA[1] + f":{s}")
            for s in range(2)]
    wksl = [(P.wbB[0][:, s * 1024:(s + 1) * 1024], P.wbB[1] + f":{s}")
            for s in range(2)]
    wvsl = [(P.wfull[0][:, s * 1024:(s + 1) * 1024], P.wfull[1] + f":{s}")
            for s in range(8)]

    def dma_h1(h):
        wh, wk = wqsl[h % 2]
        b.dma("POOL", wh[:, 0:512], wq_r[:, h, :], (), (wk,), tag=f"l1wq{h}")
        b.dma("POOL", wh[:, 512:1024], wk2_r[:, h, :], (), (wk,),
              tag=f"l1wk{h}")
        wh2, wk2 = wksl[h % 2]
        b.dma("POOL", wh2[:, 0:1024], wv_r[:, h, :], (), (wk2,),
              tag=f"l1wv{h}")
        wh3, wk3 = wvsl[h % 4]
        b.dma("POOL", wh3, wgg_r[:, h, :], (), (wk3,), tag=f"l1wg{h}")
        wh4, wk4 = wvsl[4 + h % 4]
        b.dma("POOL", wh4, wo_r[:, h, :], (), (wk4,), tag=f"l1wo{h}")
    dma_h1(0)
    for h in range(16):
        if h + 1 < 16:
            dma_h1(h + 1)
        wh, wk = wqsl[h % 2]
        wh2, wk2 = wksl[h % 2]
        wh3, wk3 = wvsl[h % 4]
        for (wsl_, wke_, cols, dstt, dbase, nrow) in (
                (wh[:, 0:512], wk, 64, P.bb[0], 0, 64),
                (wh[:, 512:1024], wk, 64, P.bb[7], 0, 64),
                (wh2[:, 0:1024], wk2, 128, P.bb[4], 0, 128)):
            for t in range(NT):
                ps, psk = P.pb[t % 2]
                for k in range(KC):
                    b.pe(lambda e, ps=ps, wsl_=wsl_, k=k, t=t, cols=cols:
                         e.matmul(ps[0:cols, :TT],
                                  wsl_[:, k * cols:(k + 1) * cols],
                                  P.xn[k][0][:, t * TT:(t + 1) * TT],
                                  start=(k == 0), stop=(k == KC - 1)),
                         (wke_, P.xn[k][1]), (psk,), tag=f"l1p{h}_{id(wsl_)}_{t}_{k}")
                dh, dk = dstt
                b.act(lambda s, ps=ps, dh=dh, t=t, cols=cols: s.activation(
                    dh[0:cols, t * TT:(t + 1) * TT], ps[0:cols, :TT],
                    AF.Copy), (psk,), (dk,), tag=f"l1pe{h}_{id(wsl_)}_{t}")
        for t in range(NT):
            ps, psk = P.pb[t % 2]
            for k in range(KC):
                b.pe(lambda e, ps=ps, wh3=wh3, k=k, t=t: e.matmul(
                    ps[:, :TT], wh3[:, k * 128:(k + 1) * 128],
                    P.xn[k][0][:, t * TT:(t + 1) * TT],
                    start=(k == 0), stop=(k == KC - 1)),
                    (wk3, P.xn[k][1]), (psk,), tag=f"l1pg{h}_{t}_{k}")
            b.act(lambda s, ps=ps, t=t: s.activation(
                P.bb[3][0][:, t * TT:(t + 1) * TT], ps[:, :TT], AF.Silu),
                (psk,), (P.bb[3][1],), tag=f"l1pge{h}_{t}")
        for (src_, sk_, cw, nr, dst_, dk_) in (
                (P.bb[0], P.bb[0][1], cvq, 64, P.bb[1], P.bb[1][1]),
                (P.bb[7], P.bb[7][1], cvk, 64, P.bb[2], P.bb[2][1]),
                (P.bb[4], P.bb[4][1], cvv, 128, P.bb[5], P.bb[5][1])):
            sh_, _ = src_
            dh_, _ = dst_
            ccol = lambda j, hh=h: cw[0][0:nr, (hh % (32 // 4 if nr == 64 else 16)) * 0 + 0:1]
            coff = (h % 2) * 0
            b.dve(lambda v, sh_=sh_, dh_=dh_, cw=cw, nr=nr, h=h: v.tensor_scalar_mul(
                dh_[0:nr, :], sh_[0:nr, :],
                cw[0][0:nr, conv_col(h, nr, 3):conv_col(h, nr, 3) + 1]),
                (sk_, cw[1]), (dk_,), tag=f"l1cv3{h}_{id(sh_)}")
            for j in (2, 1, 0):
                a = 3 - j
                b.dve(lambda v, sh_=sh_, dh_=dh_, cw=cw, nr=nr, a=a, j=j, h=h:
                      v.scalar_tensor_tensor(
                          dh_[0:nr, a:T], sh_[0:nr, 0:T - a],
                          cw[0][0:nr, conv_col(h, nr, j):conv_col(h, nr, j) + 1],
                          dh_[0:nr, a:T], ALU.mult, ALU.add),
                      (sk_, cw[1], dk_), (dk_,), tag=f"l1cv{j}_{h}_{id(sh_)}")
            b.act(lambda s, dh_=dh_, nr=nr: s.activation(
                dh_[0:nr, :], dh_[0:nr, :], AF.Silu),
                (dk_,), (dk_,), tag=f"l1cvs{h}_{id(sh_)}")
        for (th_, tk_) in ((P.bb[1][0], P.bb[1][1]), (P.bb[2][0], P.bb[2][1])):
            for t in range(NT):
                b.act(lambda s, th_=th_, t=t: s.activation(
                    sqh[:, 512:512 + TT][0:64, :], th_[0:64, t * TT:(t + 1) * TT],
                    AF.Square), (tk_,), (sqk,), tag=f"l1nq{h}_{t}_{id(th_)}")
                pr, prk = P.pb[6]
                b.pe(lambda e, pr=pr: e.matmul(
                    pr[0:1, :TT], P.onec[0][0:64, 0:1], sqh[0:64, 512:512 + TT],
                    start=True, stop=True),
                    (P.onec[1], sqk), (prk,), tag=f"l1nm{h}_{t}_{id(th_)}")
                b.dve(lambda v, pr=pr, t=t: v.tensor_copy(
                    rows_h[0:1, t * TT:(t + 1) * TT], pr[0:1, :TT]),
                    (prk,), (rows_k,), tag=f"l1nr{h}_{t}_{id(th_)}")
            b.act(lambda s: s.activation(rows_h[0:1, 0:T], rows_h[0:1, 0:T],
                                         AF.Ln, bias=eps6b[0:1, :]),
                  (rows_k, "cpool"), (rows_k,), tag=f"l1nl{h}_{id(th_)}")
            b.act(lambda s: s.activation(rows_h[0:1, 0:T], rows_h[0:1, 0:T],
                                         AF.Exp, scale=-0.5),
                  (rows_k,), (rows_k,), tag=f"l1ne{h}_{id(th_)}")
            for t in range(NT):
                bcast_row(b, f"l1nb{h}_{t}_{id(th_)}",
                          (rows_h, rows_k, slice(t * TT, (t + 1) * TT)),
                          P.pb[6], (P.oner[0], P.oner[1]))
                b.dve(lambda v, th_=th_, t=t: v.tensor_mul(
                    th_[0:64, t * TT:(t + 1) * TT],
                    th_[0:64, t * TT:(t + 1) * TT],
                    P.pb[6][0][0:64, :TT]),
                    (tk_, P.pb[6][1]), (tk_,), tag=f"l1nn{h}_{t}_{id(th_)}")
        for c in range(16):
            tph, tpk = P.pb[6 + (c % 2)]
            b.pe(lambda e, c=c, tph=tph: e.transpose(
                tph[:, :].bitcast(BF16)[:, 0:128],
                P.bb[5][0][:, c * 128:(c + 1) * 128], P.idb[0][:, :]),
                (P.bb[5][1], P.idb[1]), (tpk,), tag=f"l1vt{h}_{c}")
            b.act(lambda s, c=c, tph=tph: s.activation(
                P.bb[6][0][:, c * 128:(c + 1) * 128],
                tph[:, :].bitcast(BF16)[:, 0:128], AF.Copy),
                (tpk,), (P.bb[6][1],), tag=f"l1vte{h}_{c}")
            b.pe(lambda e, c=c, tph=tph: e.transpose(
                tph[:, :].bitcast(BF16)[:, 128:192],
                P.bb[2][0][0:64, c * 128:(c + 1) * 128], P.idb[0][0:64, 0:64]),
                (P.bb[2][1], P.idb[1]), (tpk,), tag=f"l1kt{h}_{c}")
            b.act(lambda s, c=c, tph=tph: s.activation(
                P.bb[8][0][:, c * 64:(c + 1) * 64],
                tph[:, :].bitcast(BF16)[:, 128:192], AF.Copy),
                (tpk,), (P.bb[8][1],), tag=f"l1kte{h}_{c}")
            b.pe(lambda e, c=c, tph=tph: e.transpose(
                tph[:, :].bitcast(BF16)[:, 192:256],
                P.bb[1][0][0:64, c * 128:(c + 1) * 128], P.idb[0][0:64, 0:64]),
                (P.bb[1][1], P.idb[1]), (tpk,), tag=f"l1qt{h}_{c}")
            b.act(lambda s, c=c, tph=tph: s.activation(
                P.bb[8][0][:, 1024 + c * 64:1024 + (c + 1) * 64],
                tph[:, :].bitcast(BF16)[:, 192:256], AF.Copy),
                (tpk,), (P.bb[8][1],), tag=f"l1qte{h}_{c}")
        b.dve(lambda v: v.memset(sst[0][:, :], 0.0), (), (sst[1],),
              tag=f"l1sz{h}")
        for c in range(16):
            gcol = sqh[:, c * 16 + h:c * 16 + h + 1]
            bcol = sqh[:, 256 + c * 16 + h:256 + c * 16 + h + 1]
            b.dve(lambda v, gcol=gcol: v.tensor_scalar_mul(
                radt[0][:, :], ut01[0][:, :], gcol),
                (ut01[1], sqk), (radt[1],), tag=f"l1cr{h}_{c}")
            psc, psck = P.pb[2]
            b.pe(lambda e, psc=psc: e.matmul(
                psc[:, 0:1], radt[0][:, :], P.onec[0][:, 0:1],
                start=True, stop=True),
                (radt[1], P.onec[1]), (psck,), tag=f"l1cc{h}_{c}")
            b.act(lambda s, psc=psc: s.activation(P.stat[0][:, 4:5],
                                                  psc[:, 0:1], AF.Exp),
                  (psck,), (P.stat[1],), tag=f"l1cce{h}_{c}")
            b.pe(lambda e, psc=psc, gcol=gcol: e.matmul(
                psc[:, 1:2], ls01[0][:, :], gcol, start=True, stop=True),
                (ls01[1], sqk), (psck,), tag=f"l1cw3{h}_{c}")
            b.act(lambda s, psc=psc: s.activation(P.stat[0][:, 5:6],
                                                  psc[:, 1:2], AF.Exp),
                  (psck,), (P.stat[1],), tag=f"l1cw3e{h}_{c}")
            b.pe(lambda e, psc=psc, gcol=gcol: e.matmul(
                psc[0:1, 2:3], P.onec[0][:, 0:1], gcol, start=True, stop=True),
                (P.onec[1], sqk), (psck,), tag=f"l1gs{h}_{c}")
            b.dve(lambda v, psc=psc: v.tensor_copy(P.stat[0][0:1, 6:7],
                                                   psc[0:1, 2:3]),
                  (psck,), (P.stat[1],), tag=f"l1gse{h}_{c}")
            b.pe(lambda e, psc=psc: e.matmul(
                psc[0:64, 3:4], P.oner[0][0:1, 0:64], P.stat[0][0:1, 6:7],
                start=True, stop=True),
                (P.oner[1], P.stat[1]), (psck,), tag=f"l1gb{h}_{c}")
            b.act(lambda s, psc=psc: s.activation(P.stat[0][0:64, 7:8],
                                                  psc[0:64, 3:4], AF.Exp),
                  (psck,), (P.stat[1],), tag=f"l1gbe{h}_{c}")
            psA, psAk = P.pb[0]
            b.pe(lambda e, psA=psA: e.matmul(psA[:, 0:128], radt[0][:, :],
                                             ls01[0][:, :], start=True,
                                             stop=False),
                 (radt[1], ls01[1]), (psAk,), tag=f"l1eA{h}_{c}")
            b.dve(lambda v: v.tensor_scalar_mul(radt[0][:, :],
                                                ut01[0][:, :], NEG_BIG),
                  (ut01[1],), (radt[1],), tag=f"l1mA{h}_{c}")
            b.pe(lambda e, psA=psA: e.matmul(psA[:, 0:128], P.idf[0][:, :],
                                             radt[0][:, :], start=False,
                                             stop=True),
                 (P.idf[1], radt[1]), (psAk,), tag=f"l1mA2{h}_{c}")
            b.act(lambda s, psA=psA: s.activation(
                P.bb[7][0][:, 1024:1152], psA[:, 0:128], AF.Exp),
                (psAk,), (P.bb[7][1],), tag=f"l1eAe{h}_{c}")
            b.dve(lambda v, gcol=gcol: v.tensor_scalar_mul(
                radt[0][:, :], ut01[0][:, :], gcol),
                (ut01[1], sqk), (radt[1],), tag=f"l1cr2{h}_{c}")
            psW, psWk = P.pb[1]
            b.pe(lambda e, psW=psW: e.matmul(psW[:, 0:128], ls01[0][:, :],
                                             radt[0][:, :], start=True,
                                             stop=False),
                 (ls01[1], radt[1]), (psWk,), tag=f"l1eW{h}_{c}")
            b.dve(lambda v: v.tensor_scalar_mul(radt[0][:, :],
                                                ls01[0][:, :], NEG_BIG),
                  (ls01[1],), (radt[1],), tag=f"l1mW{h}_{c}")
            b.pe(lambda e, psW=psW: e.matmul(psW[:, 0:128], P.idf[0][:, :],
                                             radt[0][:, :], start=False,
                                             stop=True),
                 (P.idf[1], radt[1]), (psWk,), tag=f"l1mW2{h}_{c}")
            b.act(lambda s, psW=psW: s.activation(
                P.bb[7][0][:, 1152:1280], psW[:, 0:128], AF.Exp),
                (psWk,), (P.bb[7][1],), tag=f"l1eWe{h}_{c}")
            psK, psKk = P.pb[3]
            b.pe(lambda e, psK=psK, c=c: e.matmul(
                psK[:, 0:128], P.bb[2][0][0:64, c * 128:(c + 1) * 128],
                P.bb[2][0][0:64, c * 128:(c + 1) * 128],
                start=True, stop=True),
                (P.bb[2][1],), (psKk,), tag=f"l1kk{h}_{c}")
            b.dve(lambda v, psK=psK, bcol=bcol: v.scalar_tensor_tensor(
                P.bb[7][0][:, 256:384], psK[:, 0:128], bcol,
                P.bb[7][0][:, 1024:1152], ALU.mult, ALU.mult),
                (psKk, sqk, P.bb[7][1]), (P.bb[7][1],), tag=f"l1A{h}_{c}")
            psQ, psQk = P.pb[4]
            b.pe(lambda e, psQ=psQ, c=c: e.matmul(
                psQ[:, 0:128], P.bb[2][0][0:64, c * 128:(c + 1) * 128],
                P.bb[1][0][0:64, c * 128:(c + 1) * 128],
                start=True, stop=True),
                (P.bb[2][1], P.bb[1][1]), (psQk,), tag=f"l1kq{h}_{c}")
            b.dve(lambda v, psQ=psQ: v.tensor_mul(
                P.bb[7][0][:, 896:1024], psQ[:, 0:128],
                P.bb[7][0][:, 1152:1280]),
                (psQk, P.bb[7][1]), (P.bb[7][1],), tag=f"l1pp{h}_{c}")
            psR, psRk = P.pb[5]
            b.pe(lambda e, psR=psR, c=c: e.matmul(
                psR[:, 0:128], P.bb[2][0][0:64, c * 128:(c + 1) * 128],
                sst[0][:, :], start=True, stop=True),
                (P.bb[2][1], sst[1]), (psRk,), tag=f"l1ks{h}_{c}")
            b.dve(lambda v, c=c, bcol=bcol: v.tensor_scalar_mul(
                P.bb[7][0][:, 768:896], P.bb[6][0][:, c * 128:(c + 1) * 128],
                bcol),
                (P.bb[6][1], sqk), (P.bb[7][1],), tag=f"l1bv{h}_{c}")
            b.dve(lambda v, bcol=bcol: v.tensor_mul(
                P.stat[0][:, 8:9], P.stat[0][:, 4:5], bcol),
                (P.stat[1], sqk), (P.stat[1],), tag=f"l1bcg{h}_{c}")
            b.dve(lambda v: v.tensor_scalar_mul(P.stat[0][:, 8:9],
                                                P.stat[0][:, 8:9], -1.0),
                  (P.stat[1],), (P.stat[1],), tag=f"l1bcg2{h}_{c}")
            b.dve(lambda v, psR=psR: v.scalar_tensor_tensor(
                P.bb[7][0][:, 640:768], psR[:, 0:128], P.stat[0][:, 8:9],
                P.bb[7][0][:, 768:896], ALU.mult, ALU.add),
                (psRk, P.stat[1], P.bb[7][1]), (P.bb[7][1],),
                tag=f"l1R{h}_{c}")
            tph, tpk = P.pb[6]
            b.pe(lambda e, tph=tph: e.transpose(
                tph[:, :].bitcast(BF16)[:, 0:128], P.bb[7][0][:, 256:384],
                P.idb[0][:, :]),
                (P.bb[7][1], P.idb[1]), (tpk,), tag=f"l1at{h}_{c}")
            b.act(lambda s, tph=tph: s.activation(
                P.bb[7][0][:, 512:640], tph[:, :].bitcast(BF16)[:, 0:128],
                AF.Copy), (tpk,), (P.bb[7][1],), tag=f"l1ate{h}_{c}")
            psX, psXk = P.pb[2]
            b.pe(lambda e, psX=psX: e.matmul(
                psX[:, 0:128], P.bb[7][0][:, 512:640],
                P.bb[7][0][:, 640:768], start=True, stop=True),
                (P.bb[7][1],), (psXk,), tag=f"l1x0{h}_{c}")
            b.dve(lambda v, psX=psX: v.tensor_sub(
                P.bb[7][0][:, 640:768], P.bb[7][0][:, 640:768],
                psX[:, 0:128]),
                (psXk, P.bb[7][1]), (P.bb[7][1],), tag=f"l1x0e{h}_{c}")
            for lev in range(6):
                psN, psNk = P.pb[2]
                nsrc = slice(256, 384) if lev == 0 else slice(384, 512)
                b.pe(lambda e, psN=psN, nsrc=nsrc: e.matmul(
                    psN[:, 0:128], P.bb[7][0][:, 512:640],
                    P.bb[7][0][:, nsrc], start=True, stop=True),
                    (P.bb[7][1],), (psNk,), tag=f"l1n{h}_{c}_{lev}")
                b.dve(lambda v, psN=psN: v.tensor_copy(
                    P.bb[7][0][:, 384:512], psN[:, 0:128]),
                    (psNk, P.bb[7][1]), (P.bb[7][1],), tag=f"l1ne{h}_{c}_{lev}")
                tph2, tpk2 = P.pb[6]
                b.pe(lambda e, tph2=tph2: e.transpose(
                    tph2[:, :].bitcast(BF16)[:, 128:256],
                    P.bb[7][0][:, 384:512], P.idb[0][:, :]),
                    (P.bb[7][1], P.idb[1]), (tpk2,), tag=f"l1nt{h}_{c}_{lev}")
                b.act(lambda s, tph2=tph2: s.activation(
                    P.bb[7][0][:, 512:640],
                    tph2[:, :].bitcast(BF16)[:, 128:256], AF.Copy),
                    (tpk2,), (P.bb[7][1],), tag=f"l1nte{h}_{c}_{lev}")
                psX2, psX2k = P.pb[2]
                b.pe(lambda e, psX2=psX2: e.matmul(
                    psX2[:, 0:128], P.bb[7][0][:, 512:640],
                    P.bb[7][0][:, 640:768], start=True, stop=True),
                    (P.bb[7][1],), (psX2k,), tag=f"l1xa{h}_{c}_{lev}")
                b.dve(lambda v, psX2=psX2: v.tensor_add(
                    P.bb[7][0][:, 640:768], P.bb[7][0][:, 640:768],
                    psX2[:, 0:128]),
                    (psX2k, P.bb[7][1]), (P.bb[7][1],),
                    tag=f"l1xae{h}_{c}_{lev}")
            b.dve(lambda v, c=c: v.tensor_scalar_mul(
                P.bb[7][0][:, 0:64], P.bb[8][0][:, 1024 + c * 64:1024 + (c + 1) * 64],
                P.stat[0][:, 4:5]),
                (P.bb[8][1], P.stat[1]), (P.bb[7][1],), tag=f"l1ex{h}_{c}")
            tph3, tpk3 = P.pb[6]
            b.pe(lambda e, tph3=tph3: e.transpose(
                tph3[:, :].bitcast(BF16)[0:64, 256:384],
                P.bb[7][0][:, 0:64], P.idb[0][:, :]),
                (P.bb[7][1], P.idb[1]), (tpk3,), tag=f"l1ext{h}_{c}")
            b.act(lambda s, tph3=tph3: s.activation(
                P.bb[7][0][0:64, 64:192],
                tph3[:, :].bitcast(BF16)[0:64, 256:384], AF.Copy),
                (tpk3,), (P.bb[7][1],), tag=f"l1exte{h}_{c}")
            psO, psOk = P.pb[3]
            b.pe(lambda e, psO=psO: e.matmul(
                psO[:, 0:128], P.bb[7][0][:, 640:768],
                P.bb[7][0][:, 896:1024], start=True, stop=False),
                (P.bb[7][1],), (psOk,), tag=f"l1o1{h}_{c}")
            b.pe(lambda e, psO=psO: e.matmul(
                psO[:, 0:128], sst[0][:, :], P.bb[7][0][0:64, 64:192],
                start=False, stop=True),
                (sst[1], P.bb[7][1]), (psOk,), tag=f"l1o2{h}_{c}")
            b.dve(lambda v, psO=psO, c=c: v.tensor_scalar_mul(
                P.bb[9][0][:, c * 128:(c + 1) * 128], psO[:, 0:128], 0.125),
                (psOk,), (P.bb[9][1],), tag=f"l1oe{h}_{c}")
            b.dve(lambda v, c=c: v.tensor_scalar_mul(
                P.bb[7][0][:, 192:256], P.bb[8][0][:, c * 64:(c + 1) * 64],
                P.stat[0][:, 5:6]),
                (P.bb[8][1], P.stat[1]), (P.bb[7][1],), tag=f"l1kw{h}_{c}")
            psS, psSk = P.pb[5]
            b.pe(lambda e, psS=psS: e.matmul(
                psS[0:64, 0:128], P.bb[7][0][:, 192:256],
                P.bb[7][0][:, 640:768], start=True, stop=True),
                (P.bb[7][1],), (psSk,), tag=f"l1ds{h}_{c}")
            b.dve(lambda v, psS=psS: v.scalar_tensor_tensor(
                sst[0][:, :], sst[0][:, :], P.stat[0][0:64, 7:8],
                psS[0:64, 0:128], ALU.mult, ALU.add),
                (sst[1], P.stat[1], psSk), (sst[1],), tag=f"l1su{h}_{c}")
        for t in range(NT):
            b.act(lambda s, t=t: s.activation(
                sqh[:, 512:512 + TT], P.bb[9][0][:, t * TT:(t + 1) * TT],
                AF.Square), (P.bb[9][1],), (sqk,), tag=f"l1or{h}_{t}")
            pr, prk = P.pb[6]
            b.pe(lambda e, pr=pr: e.matmul(
                pr[0:1, :TT], P.onec[0][:, 0:1], sqh[:, 512:512 + TT],
                start=True, stop=True),
                (P.onec[1], sqk), (prk,), tag=f"l1orm{h}_{t}")
            b.dve(lambda v, pr=pr, t=t: v.tensor_scalar_mul(
                rows_h[0:1, t * TT:(t + 1) * TT], pr[0:1, :TT], 1.0 / 128),
                (prk,), (rows_k,), tag=f"l1orr{h}_{t}")
        b.act(lambda s: s.activation(rows_h[0:1, 0:T], rows_h[0:1, 0:T],
                                     AF.Ln, bias=eps5b[0:1, :]),
              (rows_k, "cpool"), (rows_k,), tag=f"l1orl{h}")
        b.act(lambda s: s.activation(rows_h[0:1, 0:T], rows_h[0:1, 0:T],
                                     AF.Exp, scale=-0.5),
              (rows_k,), (rows_k,), tag=f"l1ore{h}")
        for t in range(NT):
            bcast_row(b, f"l1orb{h}_{t}",
                      (rows_h, rows_k, slice(t * TT, (t + 1) * TT)),
                      P.pb[6], (P.oner[0], P.oner[1]))
            b.dve(lambda v, t=t: v.scalar_tensor_tensor(
                P.bb[5][0][:, t * TT:(t + 1) * TT],
                P.bb[9][0][:, t * TT:(t + 1) * TT], nw[0][:, 0:1],
                P.pb[6][0][:, :TT], ALU.mult, ALU.mult),
                (P.bb[9][1], nw[1], P.pb[6][1]), (P.bb[5][1],),
                tag=f"l1on{h}_{t}")
            b.dve(lambda v, t=t: v.tensor_mul(
                P.bb[5][0][:, t * TT:(t + 1) * TT],
                P.bb[5][0][:, t * TT:(t + 1) * TT],
                P.bb[3][0][:, t * TT:(t + 1) * TT]),
                (P.bb[5][1], P.bb[3][1]), (P.bb[5][1],), tag=f"l1og{h}_{t}")
        wh4, wk4 = wvsl[4 + h % 4]
        for j in range(8):
            for t in range(NT):
                ps, psk = P.pb[(j % 2) * 2 + (t % 2)]
                b.pe(lambda e, ps=ps, wh4=wh4, j=j, t=t: e.matmul(
                    ps[:, :TT], wh4[:, j * 128:(j + 1) * 128],
                    P.bb[5][0][:, t * TT:(t + 1) * TT],
                    start=True, stop=True),
                    (wk4, P.bb[5][1]), (psk,), tag=f"l1so{h}_{j}_{t}")
                sh_, sk_ = sacc[j]
                if h == 0:
                    b.act(lambda s, ps=ps, sh_=sh_, t=t: s.activation(
                        sh_[:, t * TT:(t + 1) * TT], ps[:, :TT], AF.Copy),
                        (psk,), (sk_,), tag=f"l1si{h}_{j}_{t}")
                else:
                    b.dve(lambda v, ps=ps, sh_=sh_, t=t: v.tensor_add(
                        sh_[:, t * TT:(t + 1) * TT],
                        sh_[:, t * TT:(t + 1) * TT], ps[:, :TT]),
                        (psk, sk_), (sk_,), tag=f"l1sa{h}_{j}_{t}")

    for j in range(8):
        sh_, sk_ = sacc[j]
        for th in range(2):
            b.dma("SP", sqh[:, 0:1024],
                  xsrc[j * 128:(j + 1) * 128, th * 1024:(th + 1) * 1024],
                  (() if src_keys is None else (src_keys[j],)), (sqk,),
                  tag=f"l1fx{j}_{th}")
            for tt in range(2):
                t = th * 2 + tt
                pc, pck = P.pb[6]
                b.dve(lambda v, sh_=sh_, j=j, t=t, pc=pc: v.tensor_mul(
                    pc[:, :TT], sh_[:, t * TT:(t + 1) * TT],
                    P.gb[j][0][:, t * TT:(t + 1) * TT]),
                    (sk_, P.gb[j][1]), (pck,), tag=f"l1f1{j}_{t}")
                b.dve(lambda v, tt=tt, pc=pc: v.tensor_add(
                    sqh[:, tt * TT:(tt + 1) * TT],
                    sqh[:, tt * TT:(tt + 1) * TT], pc[:, :TT]),
                    (pck, sqk), (sqk,), tag=f"l1f2{j}_{t}")
            b.dma("SP", xdst[j * 128:(j + 1) * 128, th * 1024:(th + 1) * 1024],
                  sqh[:, 0:1024], (sqk,), (dst_keys[j],), tag=f"l1fw{j}_{th}")


def conv_col(h, nrows, j):
    if nrows == 64:
        return (h // 2) * 4 + j
    return h * 4 + j


def prep_l1(ins, d):
    C = np.ascontiguousarray
    d["l1_sgwT2d"] = _col2d(np.asarray(ins["L1_ffn_shared_gate_w"])[0])
    d["l1_shgT"] = _wt(np.asarray(ins["L1_ffn_shared_w_gate"]).T)
    d["l1_shuT"] = _wt(np.asarray(ins["L1_ffn_shared_w_up"]).T)
    d["l1_shdT"] = _wt(np.asarray(ins["L1_ffn_shared_w_down"]).T)
    d["l1_gateT"] = _wt(np.asarray(ins["L1_ffn_gate_w"]).T, 16)
    d["l1_wgu"] = _wte(np.asarray(ins["L1_ffn_w_gate_up"]))
    d["l1_wdn"] = _wte(np.asarray(ins["L1_ffn_w_down"]))
    d["l1_wq"] = _wt(np.asarray(ins["L1_attn_w_q"]), 64)
    d["l1_wk"] = _wt(np.asarray(ins["L1_attn_w_k"]), 64)
    d["l1_wv"] = _wt(np.asarray(ins["L1_attn_w_v"]))
    d["l1_wg"] = _wt(np.asarray(ins["L1_attn_w_g"]))
    d["l1_wout"] = np.asarray(ins["L1_attn_w_out"])
    wbA = np.asarray(ins["L1_attn_w_b"])
    waA = np.asarray(ins["L1_attn_w_a"])
    z112 = np.zeros((1024, 112), np.float32)
    d["l1_wba"] = _wt(np.concatenate([wbA, z112, waA, z112], 1))
    adn = np.zeros((128, 3), np.float32)
    adn[0:16, 0] = np.asarray(ins["L1_attn_A_log"])
    adn[0:16, 1] = np.asarray(ins["L1_attn_dt_bias"])
    adn[:, 2] = np.asarray(ins["L1_attn_norm_w"])
    d["l1_adn2d"] = adn
    cq = np.asarray(ins["L1_attn_conv_q"]).reshape(8, 128, 4)
    ck = np.asarray(ins["L1_attn_conv_k"]).reshape(8, 128, 4)
    cv = np.asarray(ins["L1_attn_conv_v"]).reshape(16, 128, 4)
    d["l1_convq2d"] = C(cq.transpose(1, 0, 2).reshape(128, 32))
    d["l1_convk2d"] = C(ck.transpose(1, 0, 2).reshape(128, 32))
    d["l1_convv2d"] = C(cv.transpose(1, 0, 2).reshape(128, 64))
    r = np.arange(128)
    d["c_ut01"] = C((r[:, None] <= r[None, :]).astype(np.float32))
    d["c_ls01"] = C((r[:, None] > r[None, :]).astype(np.float32))


L1_DEVINS = [
    ("l1_sgwT2d", [128, 8]), ("l1_shgT", [128, 22528]),
    ("l1_shuT", [128, 22528]), ("l1_shdT", [128, 22528]),
    ("l1_gateT", [128, 128]),
    ("l1_wgu", [16, 128, 8192]), ("l1_wdn", [16, 128, 4096]),
    ("l1_wq", [128, 8192]), ("l1_wk", [128, 8192]),
    ("l1_wv", [128, 16384]), ("l1_wg", [128, 16384]),
    ("l1_wout", [2048, 1024]), ("l1_wba", [128, 2048]),
    ("l1_adn2d", [128, 3]),
    ("l1_convq2d", [128, 32]), ("l1_convk2d", [128, 32]),
    ("l1_convv2d", [128, 64]),
    ("c_ut01", [128, 128]), ("c_ls01", [128, 128]),
]
LAYER_DEVINS[1] = L1_DEVINS
LAYER_PREP[1] = prep_l1
LAYER_EMIT[1] = emit_layer1
